# version 24
# baseline (speedup 1.0000x reference)
import numpy as np
import ml_dtypes
from contextlib import ExitStack

import concourse.bass as bass
import concourse.mybir as mybir
from concourse.bass_utils import run_bass_kernel_spmd

F32 = mybir.dt.float32
BF16 = mybir.dt.bfloat16
AF = mybir.ActivationFunctionType
ALU = mybir.AluOpType

EPS = 1e-5

FULL_CFG = dict(DM=8, DC=16, L=4096, LC=256, N=16, R=64, CONV=4, M=64,
                INNER=16, NCORES=8)


class Prog:
    ENGINES = ("tensor", "vector", "scalar", "gpsimd", "sync")

    @staticmethod
    def is_dma_sem(sem):
        return sem.startswith("d_") or sem.startswith("cc")

    def __init__(self, nc):
        self.nc = nc
        self.q = {e: [] for e in self.ENGINES}
        self.cnt = {}
        self.state = {}
        self.waited = {}
        self.opidx = {e: 0 for e in self.ENGINES}
        self.last_drain = {e: -1 for e in self.ENGINES}

    def _need(self, eng, stamp, drains):
        sem, val, peng, pidx = stamp
        if self.is_dma_sem(sem):
            val = self.cnt.get(sem, 0)
        k = (eng, sem)
        if self.waited.get(k, 0) < val:
            self.waited[k] = val
            self.q[eng].append(("wait", sem, val))

    def op(self, eng, fn, reads=(), writes=(), dma=False, sem=None, inc=None,
           accum=False):
        drains = []
        for r in reads:
            st = self.state.get(r)
            if st and st["w"]:
                self._need(eng, st["w"], drains)
        if not accum:
            for r in writes:
                st = self.state.get(r)
                if st:
                    if st["w"]:
                        sem_, val_, peng_, pidx_ = st["w"]
                        if peng_ != eng or self.is_dma_sem(sem_):
                            self._need(eng, st["w"], drains)
                    for stamp in st["r"].values():
                        self._need(eng, stamp, drains)
        if drains and max(drains) > self.last_drain[eng]:
            self.q[eng].append(("drain",))
            self.last_drain[eng] = self.opidx[eng]
        if sem is None:
            assert not dma, "DMA ops must name their semaphore"
            sem = eng
        if inc is None:
            inc = 16 if dma else 1
        self.cnt[sem] = self.cnt.get(sem, 0) + inc
        val = self.cnt[sem]
        self.opidx[eng] += 1
        stamp = (sem, val, eng, self.opidx[eng])
        self.q[eng].append(("op", fn, sem, inc, val))
        for r in writes:
            self.state[r] = {"w": stamp, "r": {}}
        for r in reads:
            st = self.state.setdefault(r, {"w": None, "r": {}})
            st["r"][eng] = stamp
        return stamp

    def finalize(self, block, sems):
        waited = {}
        for ename in self.ENGINES:
            for it in self.q[ename]:
                if it[0] == "wait" and not self.is_dma_sem(it[1]):
                    waited.setdefault(it[1], set()).add(it[2])
        remap = {}
        for sem, vals in waited.items():
            remap[sem] = {v: i + 1 for i, v in enumerate(sorted(vals))}

        for ename in self.ENGINES:
            items = self.q[ename]

            def make_body(items):
                def body(engine):
                    for it in items:
                        if it[0] == "wait":
                            sem, val = it[1], it[2]
                            if sem in remap:
                                val = remap[sem][val]
                            engine.wait_ge(sems[sem], val)
                        elif it[0] == "drain":
                            engine.drain()
                        else:
                            _, fn, sem, inc, val = it
                            inst = fn(engine)
                            if self.is_dma_sem(sem):
                                inst.then_inc(sems[sem], inc)
                            elif sem in remap and val in remap[sem]:
                                inst.then_inc(sems[sem], inc)
                return body
            getattr(block, ename)(make_body(items))


def build_core_program(cfg):
    DM, DC, L, LC = cfg["DM"], cfg["DC"], cfg["L"], cfg["LC"]
    N, R, CONV = cfg["N"], cfg["R"], cfg["CONV"]
    M, INNER = cfg["M"], cfg["INNER"]
    NCORES = cfg["NCORES"]
    D_MODEL = DM * 128
    D_INNER = DC * 128
    NCH = L // LC
    HCH = NCH // 2
    HALF = L // 2
    EB = 2 * DC
    LCP = LC + CONV - 1
    RN = 128

    nc = bass.Bass("TRN2", target_bir_lowering=False)

    def din(name, shape, dt=F32):
        return nc.dram_tensor(name, list(shape), dt, kind="ExternalInput").ap()

    xT = din("xT", [D_MODEL, L])
    x_res = din("x_res", [D_MODEL, HALF])
    win_t4 = din("win_t4", [EB, 128, DM, 128])
    wout_t = din("wout_t", [D_INNER, D_MODEL])
    wx_t = din("wx_t", [D_INNER, RN])
    wdt_t = din("wdt_t", [R, D_INNER])
    alog_c = din("alog_c", [128, DC * N])
    dcol = din("dcol", [128, DC])
    bdt_c = din("bdt_c", [128, DC])
    convw_c = din("convw_c", [128, DC * CONV])
    convb_r = din("convb_r", [1, D_INNER])
    gcol = din("gcol", [128, DM])
    bcol = din("bcol", [128, DM])
    winf = din("winf", [D_MODEL, M])
    woutf = din("woutf", [M, D_MODEL])
    w1 = din("w1", [M, INNER])
    b1 = din("b1", [M, INNER])
    w2 = din("w2", [M, INNER])
    b2c = din("b2c", [M, 1])
    binc = din("binc", [M, 1])
    bout_c = din("bout_c", [128, DM])
    eye16 = din("eye16", [128, 128], BF16)

    out_d = nc.dram_tensor("out", [D_MODEL, HALF], F32, kind="ExternalOutput").ap()

    win_bf = nc.dram_tensor("win_bf", [EB, 128, DM * 128], BF16).ap()
    bc_dram = [nc.dram_tensor(f"bc_dram{i}", [2 * N, LC], BF16).ap()
               for i in range(2)]
    yown_dram = nc.dram_tensor("yown_dram", [D_MODEL, HALF], BF16).ap()
    send_dram = nc.dram_tensor("send_dram", [2, D_MODEL, HALF // 2], BF16).ap()
    ar_dram = nc.dram_tensor("ar_dram", [2, D_MODEL, HALF // 2], BF16).ap()

    stack = ExitStack()

    def sbuf(name, shape, dt):
        return stack.enter_context(nc.sbuf_tensor(name, list(shape), dt))

    def psum(name, shape, dt=F32):
        return stack.enter_context(nc.psum_tensor(name, list(shape), dt))

    wout_sb = sbuf("wout_sb", [128, DC * D_MODEL], BF16)
    wx_sb = sbuf("wx_sb", [128, DC * RN], BF16)
    wdt_sb = sbuf("wdt_sb", [R, D_INNER], BF16)
    winf_sb = sbuf("winf_sb", [128, DM * M], BF16)
    woutf_sb = sbuf("woutf_sb", [M, D_MODEL], BF16)
    diag_sb = sbuf("diag_sb", [128, DC * CONV * 128], BF16)
    eye_sb = sbuf("eye_sb", [128, 128], BF16)
    A_sb = sbuf("A_sb", [128, DC * N], F32)
    dcol_sb = sbuf("dcol_sb", [128, DC], F32)
    bdt_sb = sbuf("bdt_sb", [128, DC], F32)
    convw_sb = sbuf("convw_sb", [128, DC * CONV], F32)
    convbr_sb = sbuf("convbr_sb", [1, D_INNER], BF16)
    gcol_sb = sbuf("gcol_sb", [128, DM], F32)
    bcol_sb = sbuf("bcol_sb", [128, DM], F32)
    w1_sb = sbuf("w1_sb", [M, INNER], F32)
    b1_sb = sbuf("b1_sb", [M, INNER], F32)
    w2_sb = sbuf("w2_sb", [M, INNER], F32)
    bias1_sb = sbuf("bias1_sb", [M, INNER], F32)
    b2_sb = sbuf("b2_sb", [M, 1], F32)
    binc_sb = sbuf("binc_sb", [M, 1], F32)
    bout_sb = sbuf("bout_sb", [128, DM], F32)
    ones_rf = sbuf("ones_rf", [1, 128], F32)
    cvals_sb = sbuf("cvals_sb", [1, 4], F32)
    half_c = sbuf("half_c", [128, 1], F32)
    ones_rb = sbuf("ones_rb", [1, LC], BF16)
    onec_f = sbuf("onec_f", [128, 1], F32)
    onec_b = sbuf("onec_b", [128, 1], BF16)
    c_sb = sbuf("c_sb", [128, EB], F32)
    bcolb_sb = sbuf("bcolb_sb", [128, DM], BF16)

    wbuf = [sbuf(f"wbuf{i}", [128, DM * 128], BF16) for i in range(2)]
    xt_sb = [sbuf(f"xt_sb{i}", [128, DM * LC], BF16) for i in range(2)]
    xn_sb = [sbuf(f"xn_sb{i}", [128, DM * LC], BF16) for i in range(2)]
    xzu_sb = sbuf("xzu_sb", [128, DC * LCP], BF16)
    u_sb = sbuf("u_sb", [128, DC * LC], BF16)
    gt_sb = [sbuf(f"gt_sb{i}", [128, DC * LC], BF16) for i in range(2)]
    dt_sb = [sbuf(f"dt_sb{i}", [128, DC * LC], BF16) for i in range(2)]
    uw_sb = sbuf("uw_sb", [128, DC * LC], BF16)
    a_sb = sbuf("a_sb", [128, DC * LC], BF16)
    b_sb = sbuf("b_sb", [128, DC * LC], BF16)
    y_sb = sbuf("y_sb", [128, DC * LC], BF16)
    tprod_sb = sbuf("tprod_sb", [128, DC * LC], BF16)
    bc_sb = sbuf("bc_sb", [64, LC], BF16)
    bcq_sb = [sbuf(f"bcq_sb{i}", [128, 8 * LC], BF16) for i in range(2)]
    hstate_sb = sbuf("hstate_sb", [128, N * DC], BF16)
    fix_sb = sbuf("fix_sb", [128, DC], BF16)
    outst_sb = sbuf("outst_sb", [128, 2 * LC], BF16)
    stat_sb = sbuf("stat_sb", [1, 2 * LC], F32)
    gacc_sb = [sbuf(f"gacc_sb{i}", [M, LC], BF16) for i in range(2)]
    x1_sb = sbuf("x1_sb", [128, DM * LC], F32)
    xres_sb = sbuf("xres_sb", [128, DM * LC], F32)
    ys_sb = sbuf("ys_sb", [128, 2 * LC], F32)
    s2row_sb = sbuf("s2row_sb", [1, LC], F32)
    e1t_sb = sbuf("e1t_sb", [128, LC], BF16)
    dtrt_sb = sbuf("dtrt_sb", [R, LC], BF16)
    e1_sb = e1t_sb[:, 0:LC]
    dtr_sb = dtrt_sb[0:R, 0:LC]
    if INNER * LC <= DC * LC:
        hff_sb = a_sb
    else:
        hff_sb = sbuf("hff_sb", [M, INNER * LC], BF16)

    ps_rep = psum("ps_rep", [128, 512])
    ps_st = psum("ps_st", [1, 512])
    ps_sum = ps_st[0:1, 0:256]
    ps_sq = ps_st[0:1, 256:512]
    ps_xz = [psum(f"ps_xz{i}", [128, LC]) for i in range(2)]
    ps_conv = [psum(f"ps_conv{i}", [128, LC]) for i in range(2)]
    ps_small = psum("ps_small", [128, 512])
    ps_outb = [psum("ps_outb", [128, LC])]
    ps_out = [ps_outb[0], ps_outb[0]]

    assert LC <= 256

    sem_names = ["tensor", "vector", "scalar", "gpsimd", "sync",
                 "d_st", "d_x0", "d_x1", "d_w0", "d_w1", "d_bc0", "d_bc1",
                 "d_ow", "d_pl0", "d_pl1", "d_od", "d_g", "d_gx0", "d_gx1",
                 "d_bn0", "d_bn1", "cc", "cc1"]
    sems = {nm: stack.enter_context(nc.semaphore(nm)) for nm in sem_names}

    p = Prog(nc)
    W = {"xt": lambda i: ("xt", i), "xn": lambda k: ("xn", k)}

    XN = [("xn", k) for k in range(DM)]
    U = [("u", d) for d in range(DC)]
    AK = [("a", d) for d in range(DC)]
    BK = [("b", d) for d in range(DC)]
    YK = [("y", d) for d in range(DC)]
    def DTK(par):
        return [("dt", par, d) for d in range(DC)]
    GT = [("gt", 0, d) for d in range(DC)]
    XZU = [("xzu", d) for d in range(DC)]
    X1 = [("x1", k) for k in range(DM)]
    YS = [("ys", k) for k in range(DM)]
    OUTST = [("outst", k) for k in range(DM)]

    def startup():
        for eblk in range(EB):
            p.op("gpsimd", lambda g, eblk=eblk: g.dma_start(
                out=win_bf[eblk], in_=win_t4[eblk]),
                writes=[("win_bf", eblk)], dma=True, sem="d_g")
        for j in range(4):
            d0 = DC // 4 * j
            d1 = DC // 4 * (j + 1)
            p.op("gpsimd", lambda g, d0=d0, d1=d1: g.dma_start(
                out=wout_sb[:, d0 * D_MODEL:d1 * D_MODEL].rearrange(
                    "p (dc d) -> p dc d", d=D_MODEL),
                in_=wout_t.rearrange("(dc p) d -> p dc d", p=128)[:, d0:d1]),
                writes=[("wout_sb", j)], dma=True, sem="d_g")
        p.op("gpsimd", lambda g: g.dma_start(
            out=wx_sb[:].rearrange("p (dc r) -> p dc r", r=RN),
            in_=wx_t.rearrange("(dc p) r -> p dc r", p=128)),
            writes=[("wx_sb",)], dma=True, sem="d_g")
        p.op("gpsimd", lambda g: g.dma_start(out=wdt_sb[:], in_=wdt_t),
             writes=[("wdt_sb",)], dma=True, sem="d_g")
        p.op("gpsimd", lambda g: g.dma_start(
            out=winf_sb[:].rearrange("p (km m) -> p km m", m=M),
            in_=winf.rearrange("(km p) m -> p km m", p=128)),
            writes=[("winf_sb",)], dma=True, sem="d_g")
        p.op("gpsimd", lambda g: g.dma_start(out=woutf_sb[:], in_=woutf),
             writes=[("woutf_sb",)], dma=True, sem="d_g")
        p.op("gpsimd", lambda g: g.dma_start(out=convbr_sb[:], in_=convb_r),
             writes=[("convbr_sb",)], dma=True, sem="d_g")
        loads = [(A_sb, alog_c, "A_sb"), (dcol_sb, dcol, "dcol_sb"),
                 (bdt_sb, bdt_c, "bdt_sb"), (convw_sb, convw_c, "convw_sb"),
                 (gcol_sb, gcol, "gcol_sb"), (bcol_sb, bcol, "bcol_sb"),
                 (w1_sb, w1, "w1_sb"), (b1_sb, b1, "b1_sb"),
                 (w2_sb, w2, "w2_sb"), (b2_sb, b2c, "b2_sb"),
                 (binc_sb, binc, "binc_sb"), (bout_sb, bout_c, "bout_sb"),
                 (eye_sb, eye16, "eye_sb")]
        for t, src, rg in loads:
            p.op("sync", lambda s, t=t, src=src: s.dma_start(out=t[:], in_=src),
                 writes=[(rg,)], dma=True, sem="d_st")
        p.op("vector", lambda v: v.memset(ones_rf[:], 1.0), writes=[("ones_rf",)])
        p.op("vector", lambda v: v.memset(cvals_sb[0:1, 0:1], EPS * EPS),
             writes=[("cv", 0)])
        p.op("vector", lambda v: v.memset(cvals_sb[0:1, 1:2], 1.0 + EPS),
             writes=[("cv", 1)])
        p.op("vector", lambda v: v.memset(cvals_sb[0:1, 2:3], EPS),
             writes=[("cv", 2)])
        p.op("vector", lambda v: v.memset(cvals_sb[0:1, 3:4], -0.5),
             writes=[("cv", 3)])
        p.op("vector", lambda v: v.memset(half_c[:], 0.5), writes=[("half_c",)])
        p.op("vector", lambda v: v.memset(ones_rb[:], 1.0), writes=[("ones_rb",)])
        p.op("vector", lambda v: v.memset(onec_f[:], 1.0), writes=[("onec_f",)])
        p.op("vector", lambda v: v.memset(onec_b[:], 1.0), writes=[("onec_b",)])
        p.op("vector", lambda v: v.memset(hstate_sb[:], 0.0), writes=[("hstate",)])
        p.op("vector", lambda v: v.memset(xzu_sb[:], 0.0), writes=XZU)
        p.op("scalar", lambda s: s.activation(A_sb[:], A_sb[:], AF.Exp),
             reads=[("A_sb",)], writes=[("A_sb",)])
        p.op("vector", lambda v: v.tensor_scalar(
            A_sb[:], A_sb[:], -1.0, None, ALU.mult),
            reads=[("A_sb",)], writes=[("A_sb",)])
        for dc in range(DC):
            for k in range(CONV):
                j = dc * CONV + k
                p.op("vector", lambda v, j=j: v.tensor_scalar(
                    diag_sb[:, j * 128:(j + 1) * 128], eye_sb[:],
                    convw_sb[:, j:j + 1], None, ALU.mult),
                    reads=[("eye_sb",), ("convw_sb",)], writes=[("diag", j)])
        p.op("vector", lambda v: v.tensor_copy(bcolb_sb[:], bcol_sb[:]),
             reads=[("bcol_sb",)], writes=[("bcolb",)])
        for eblk in range(EB):
            p.op("sync", lambda s, eblk=eblk: s.dma_start(
                out=wbuf[0][:], in_=win_bf[eblk]),
                reads=[("win_bf", eblk)], writes=[("wbuf", 0)], dma=True,
                sem="d_w0")
            for k in range(DM):
                p.op("tensor", lambda t, k=k: t.matmul(
                    ps_rep[:, 0:1], wbuf[0][:, k * 128:(k + 1) * 128],
                    bcolb_sb[:, k:k + 1],
                    start=(k == 0), stop=(k == DM - 1)),
                    reads=[("wbuf", 0), ("bcolb",)],
                    writes=[("ps_rep",)], accum=(k > 0))
            p.op("scalar", lambda s, eblk=eblk: s.copy(
                c_sb[:, eblk:eblk + 1], ps_rep[:, 0:1]),
                reads=[("ps_rep",)], writes=[("c_sb",)])
            for k in range(DM):
                p.op("vector", lambda v, k=k: v.tensor_scalar(
                    wbuf[0][:, k * 128:(k + 1) * 128],
                    wbuf[0][:, k * 128:(k + 1) * 128],
                    gcol_sb[:, k:k + 1], None, ALU.mult),
                    reads=[("wbuf", 0), ("gcol_sb",)], writes=[("wbuf", 0)])
            p.op("sync", lambda s, eblk=eblk: s.dma_start(
                out=win_bf[eblk], in_=wbuf[0][:]),
                reads=[("wbuf", 0)], writes=[("win_bf", eblk)], dma=True,
                sem="d_w0")
        p.op("vector", lambda v: v.tensor_scalar(
            bias1_sb[:], w1_sb[:], binc_sb[:, 0:1], None, ALU.mult),
            reads=[("w1_sb",), ("binc_sb",)], writes=[("bias1_sb",)])
        p.op("vector", lambda v: v.tensor_tensor(
            bias1_sb[:], bias1_sb[:], b1_sb[:], ALU.add),
            reads=[("bias1_sb",), ("b1_sb",)], writes=[("bias1_sb",)])

    def ln_stats_part(xt_ap, xt_rg, sq_buf, ones_col, ones_rg):
        for km in range(DM):
            p.op("tensor", lambda t, km=km: t.matmul(
                ps_sum[:, 0:LC], ones_col[:],
                xt_ap[:, km * LC:(km + 1) * LC],
                start=(km == 0), stop=(km == DM - 1)),
                reads=[xt_rg[km] if isinstance(xt_rg, list) else xt_rg,
                       ones_rg],
                writes=[("ps_sum",)], accum=(km > 0))
        for km in range(DM):
            sl = slice((km % 2) * LC, (km % 2 + 1) * LC)
            p.op("scalar", lambda s, km=km, sl=sl: s.activation(
                ys_sb[:, sl],
                xt_ap[:, km * LC:(km + 1) * LC], AF.Square),
                reads=[xt_rg[km] if isinstance(xt_rg, list) else xt_rg],
                writes=[("ys", km % 2)])
            p.op("tensor", lambda t, km=km, sl=sl: t.matmul(
                ps_sq[:, 0:LC], onec_f[:], ys_sb[:, sl],
                start=(km == 0), stop=(km == DM - 1)),
                reads=[("ys", km % 2), ("onec_f",)],
                writes=[("ps_sq",)], accum=(km > 0))

    def ln_stats_apply(xt_ap, xt_rg, sq_buf, combine12, ones_col=None,
                       ones_rg=("onec_f",), skip_stats=False):
        if ones_col is None:
            ones_col = onec_f
        if not skip_stats:
            ln_stats_part(xt_ap, xt_rg, sq_buf, ones_col, ones_rg)
        s0 = stat_sb[0:1, 0:LC]
        s1 = stat_sb[0:1, LC:2 * LC]
        s2 = s2row_sb[0:1, 0:LC]
        p.op("vector", lambda v: v.tensor_scalar(
            s0, ps_sum[:, 0:LC], 1.0 / D_MODEL, None, ALU.mult),
            reads=[("ps_sum",), ("ps_sq",)], writes=[("st0",)])
        p.op("vector", lambda v: v.tensor_scalar(
            s1, ps_sq[:, 0:LC], 1.0 / D_MODEL, None, ALU.mult),
            reads=[("ps_sq",), ("ps_sum",)], writes=[("st1",)])
        p.op("vector", lambda v: v.tensor_tensor(s2, s0, s0, ALU.mult),
             reads=[("st0",)], writes=[("st2",)])
        p.op("vector", lambda v: v.tensor_tensor(s1, s1, s2, ALU.subtract),
             reads=[("st1",), ("st2",)], writes=[("st1",)])
        if combine12:
            p.op("scalar", lambda s: s.activation(
                s1, s1, AF.Ln, bias=cvals_sb[0:1, 0:1],
                scale=cvals_sb[0:1, 1:2]),
                reads=[("st1",), ("cv", 0), ("cv", 1)], writes=[("st1",)])
        else:
            p.op("scalar", lambda s: s.activation(
                s1, s1, AF.Ln, bias=cvals_sb[0:1, 2:3]),
                reads=[("st1",), ("cv", 2)], writes=[("st1",)])
        p.op("scalar", lambda s: s.activation(
            s1, s1, AF.Exp, scale=cvals_sb[0:1, 3:4]),
            reads=[("st1",), ("cv", 3)], writes=[("st1",)])
        p.op("tensor", lambda t: t.matmul(
            ps_rep[:, 0:LC], ones_rf[:], s1, start=True, stop=True),
            reads=[("st1",), ("ones_rf",)], writes=[("ps_rep",)])
        p.op("vector", lambda v: v.tensor_tensor(s0, s0, s1, ALU.mult),
             reads=[("st0",), ("st1",)], writes=[("st0",)])
        p.op("tensor", lambda t: t.matmul(
            ps_rep[:, 256:256 + LC], ones_rf[:], s0, start=True, stop=True),
            reads=[("st0",), ("ones_rf",)], writes=[("ps_rep",)])

    def x_load(ch):
        par = ch % 2
        src = xT.rearrange("(km p) l -> p km l", p=128)[:, :, ch * LC:(ch + 1) * LC]
        p.op("gpsimd", lambda g, par=par, src=src: g.dma_start(
            out=xt_sb[par][:].rearrange("p (km l) -> p km l", l=LC), in_=src),
            writes=[("xt", par)], dma=True, sem=f"d_gx{par}")

    def ln12_stats(ch):
        par = ch % 2
        ln_stats_part(xt_sb[par][:], ("xt", par), None, onec_b, ("onec_b",))

    def ln12_mid(ch):
        par = ch % 2
        ln_stats_apply(xt_sb[par][:], ("xt", par), None, combine12=True,
                       ones_col=onec_b, ones_rg=("onec_b",), skip_stats=True)

    def ln12_apply(ch):
        par = ch % 2
        xt = xt_sb[par][:]
        xn = xn_sb[par][:]
        for km in range(DM):
            sl = slice(km * LC, (km + 1) * LC)
            r = km % 4
            if r < 2:
                tmp = ys_sb[:, r * LC:(r + 1) * LC]
                tkey = ("ys", r)
            else:
                tmp = x1_sb[:, (r - 2) * LC:(r - 1) * LC]
                tkey = ("x1", r - 2)
            p.op("vector", lambda v, sl=sl, tmp=tmp: v.tensor_tensor(
                tmp, xt[:, sl], ps_rep[:, 0:LC], ALU.mult),
                reads=[("xt", par), ("ps_rep",)], writes=[tkey])
            p.op("vector", lambda v, sl=sl, tmp=tmp, xn=xn: v.tensor_tensor(
                xn[:, sl], tmp, ps_rep[:, 256:256 + LC], ALU.subtract),
                reads=[tkey, ("ps_rep",)], writes=[("xn", par, km)])

    def xz_eblk(eblk, ch):
        par = eblk % 2
        cpar = ch % 2
        xn = xn_sb[cpar][:]
        p.op("sync", lambda s, par=par, eblk=eblk: s.dma_start(
            out=wbuf[par][:], in_=win_bf[eblk]),
            reads=[("win_bf", eblk)], writes=[("wbuf", par)], dma=True,
            sem=f"d_w{par}")
        for km in range(DM):
            p.op("tensor", lambda t, par=par, km=km, xn=xn: t.matmul(
                ps_xz[par][:],
                wbuf[par][:, km * 128:(km + 1) * 128],
                xn[:, km * LC:(km + 1) * LC],
                start=(km == 0), stop=(km == DM - 1)),
                reads=[("wbuf", par), ("xn", cpar, km)],
                writes=[("ps_xz", par)], accum=(km > 0))
        if eblk < DC:
            dc = eblk
            p.op("scalar", lambda s, dc=dc, par=par, eblk=eblk: s.activation(
                xzu_sb[:, dc * LCP + CONV - 1: dc * LCP + CONV - 1 + LC],
                ps_xz[par][:], AF.Identity, bias=c_sb[:, eblk:eblk + 1]),
                reads=[("ps_xz", par), ("c_sb",)], writes=[("xzu", dc)])
        else:
            dc = eblk - DC
            p.op("scalar", lambda s, dc=dc, par=par, cpar=cpar, eblk=eblk:
                 s.activation(
                    gt_sb[cpar][:, dc * LC:(dc + 1) * LC], ps_xz[par][:],
                    AF.Silu, bias=c_sb[:, eblk:eblk + 1]),
                 reads=[("ps_xz", par), ("c_sb",)], writes=[("gt", cpar, dc)])

    def xz_u(ch):
        for eblk in range(DC):
            xz_eblk(eblk, ch)

    def xz_z(ch):
        for eblk in range(DC, EB):
            xz_eblk(eblk, ch)

    def conv_silu(ch):
        for dc in range(DC):
            par = dc % 2
            for k in range(CONV):
                j = dc * CONV + k
                p.op("tensor", lambda t, j=j, dc=dc, k=k, par=par: t.matmul(
                    ps_conv[par][:],
                    diag_sb[:, j * 128:(j + 1) * 128],
                    xzu_sb[:, dc * LCP + k: dc * LCP + k + LC],
                    start=(k == 0), stop=False),
                    reads=[("diag", j), ("xzu", dc)],
                    writes=[("ps_conv", par)], accum=(k > 0))
            p.op("tensor", lambda t, dc=dc, par=par: t.matmul(
                ps_conv[par][:],
                convbr_sb[0:1, dc * 128:(dc + 1) * 128], ones_rb[:],
                start=False, stop=True),
                reads=[("convbr_sb",), ("ones_rb",)],
                writes=[("ps_conv", par)], accum=True)
            p.op("scalar", lambda s, dc=dc, par=par: s.activation(
                u_sb[:, dc * LC:(dc + 1) * LC], ps_conv[par][:], AF.Silu),
                reads=[("ps_conv", par)], writes=[("u", dc)])
        xv = xzu_sb[:].rearrange("p (dc l) -> p dc l", l=LCP)
        p.op("vector", lambda v, xv=xv: v.tensor_copy(
            xv[:, :, 0:CONV - 1], xv[:, :, LC:LC + CONV - 1]),
            reads=XZU, writes=XZU)

    def xdbl_dt(ch):
        par2 = ch % 2
        dt = dt_sb[par2]
        for dc in range(DC):
            p.op("tensor", lambda t, dc=dc: t.matmul(
                ps_small[0:RN, 0:LC],
                wx_sb[:, dc * RN:(dc + 1) * RN],
                u_sb[:, dc * LC:(dc + 1) * LC],
                start=(dc == 0), stop=(dc == DC - 1)),
                reads=[("wx_sb",), ("u", dc)],
                writes=[("ps_xdbl",)], accum=(dc > 0))
        p.op("scalar", lambda s: s.copy(dtr_sb, ps_small[0:R, 0:LC]),
             reads=[("ps_xdbl",)], writes=[("dtr",)])
        p.op("scalar", lambda s: s.copy(
            bc_sb[0:N, :], ps_small[R:R + N, 0:LC]),
            reads=[("ps_xdbl",)], writes=[("bc_b",)])
        p.op("scalar", lambda s: s.copy(
            bc_sb[32:32 + N, :], ps_small[96:96 + N, 0:LC]),
            reads=[("ps_xdbl",)], writes=[("bc_c",)])
        p.op("sync", lambda s, par2=par2: s.dma_start(
            out=bc_dram[par2][0:N], in_=bc_sb[0:N, :]),
            reads=[("bc_b",)], writes=[("bc_dram", par2)],
            dma=True, sem=f"d_bc{par2}")
        p.op("sync", lambda s, par2=par2: s.dma_start(
            out=bc_dram[par2][N:2 * N], in_=bc_sb[32:32 + N, :]),
            reads=[("bc_c",)], writes=[("bc_dram", par2)],
            dma=True, sem=f"d_bc{par2}")




    def dt_chain(ch, dcs):
        par2 = ch % 2
        dt = dt_sb[par2]
        dtslots = [ps_small[:, 256:256 + LC], ps_conv[0][:], ps_conv[1][:]]
        dtkeys = [("ps_dt",), ("ps_conv", 0), ("ps_conv", 1)]
        for dc in dcs:
            slot = dtslots[dc % 3]
            skey = dtkeys[dc % 3]
            wr = [skey] + ([("ps_xdbl",)] if dc == 0 else [])
            p.op("tensor", lambda t, dc=dc, slot=slot: t.matmul(
                slot, wdt_sb[:, dc * 128:(dc + 1) * 128], dtr_sb,
                start=True, stop=True),
                reads=[("wdt_sb",), ("dtr",)], writes=wr)
            p.op("scalar", lambda s, dc=dc, slot=slot: s.activation(
                e1_sb, slot, AF.Exp, bias=bdt_sb[:, dc:dc + 1]),
                reads=[skey, ("bdt_sb",)], writes=[("e1",)])
            p.op("scalar", lambda s, dc=dc, dt=dt: s.activation(
                dt[:, dc * LC:(dc + 1) * LC], e1_sb, AF.Ln,
                bias=onec_f[:, 0:1]),
                reads=[("e1",), ("onec_f",)], writes=[("dt", par2, dc)])

    def bcq_load(ch, q):
        par2 = ch % 2
        par = q % 2
        p.op("sync", lambda s, par2=par2, par=par, q=q: s.dma_start(
            out=bcq_sb[par][:, 0:4 * LC],
            in_=bc_dram[par2][4 * q:4 * q + 4].rearrange("r l -> (r l)")[
                None, :].broadcast_to([128, 4 * LC])),
            reads=[("bc_dram", par2)], writes=[("bcq", par)], dma=True,
            sem=f"d_bn{par}")
        p.op("sync", lambda s, par2=par2, par=par, q=q: s.dma_start(
            out=bcq_sb[par][:, 4 * LC:8 * LC],
            in_=bc_dram[par2][N + 4 * q:N + 4 * q + 4].rearrange(
                "r l -> (r l)")[None, :].broadcast_to([128, 4 * LC])),
            reads=[("bc_dram", par2)], writes=[("bcq", par)], dma=True,
            sem=f"d_bn{par}")

    def uw_y_init(ch):
        par2 = ch % 2
        p.op("vector", lambda v, par2=par2: v.tensor_tensor(
            uw_sb[:], dt_sb[par2][:], u_sb[:], ALU.mult),
            reads=DTK(par2) + U, writes=[("uw",)])
        for dc in range(DC):
            sl = slice(dc * LC, (dc + 1) * LC)
            p.op("scalar", lambda s, sl=sl, dc=dc: s.activation(
                y_sb[:, sl], u_sb[:, sl], AF.Copy,
                scale=dcol_sb[:, dc:dc + 1]),
                reads=[("u", dc), ("dcol_sb",)], writes=[("y", dc)])

    def scan_phase(ch):
        first = (ch == 0)
        av = a_sb[:].rearrange("p (dc l) -> p dc l", l=LC)
        bv = b_sb[:].rearrange("p (dc l) -> p dc l", l=LC)
        H = DC // 2
        nxt = ch + 1 < NCH
        usched = {}
        csched = {}
        for j in range(DC):
            usched.setdefault(2 + j // 2, []).append(j)
            csched.setdefault(3 + j // 2, []).append(j)
        tail_n = 3 + (DC - 1) // 2 + 1
        zsched = {}
        zslots = list(range(max(N - 6, tail_n), N))
        for i in range(DC):
            zsched.setdefault(zslots[i * len(zslots) // DC], []).append(DC + i)
        cpar = ch % 2
        bcq_load(ch, 0)
        for n in range(N):
            for h in range(2):
                hdc = range(h * H, (h + 1) * H)
                hsl = slice(h * H * LC, (h + 1) * H * LC)
                HK = [("a", d) for d in hdc]
                HB = [("b", d) for d in hdc]
                for dc in hdc:
                    p.op("scalar", lambda s, n=n, dc=dc, cpar=cpar: s.activation(
                        a_sb[:, dc * LC:(dc + 1) * LC],
                        dt_sb[cpar][:, dc * LC:(dc + 1) * LC],
                        AF.Exp, scale=A_sb[:, dc * N + n:dc * N + n + 1]),
                        reads=[("dt", cpar, dc), ("A_sb",)], writes=[("a", dc)])
                p.op("vector", lambda v, n=n, hsl=hsl: v.tensor_tensor(
                    b_sb[:, hsl].rearrange("p (dc l) -> p dc l", l=LC),
                    uw_sb[:, hsl].rearrange("p (dc l) -> p dc l", l=LC),
                    bcq_sb[(n // 4) % 2][:, (n % 4) * LC:(n % 4 + 1) * LC][
                        :, None, :].broadcast_to([128, H, LC]),
                    ALU.mult),
                    reads=[("uw",), ("bcq", (n // 4) % 2)], writes=HB)
                avh = av[:, h * H:(h + 1) * H]
                bvh = bv[:, h * H:(h + 1) * H]
                if not first:
                    p.op("vector", lambda v, n=n, avh=avh, h=h: v.tensor_tensor(
                        fix_sb[:, 0:H], avh[:, :, 0],
                        hstate_sb[:, n * DC + h * H:n * DC + (h + 1) * H],
                        ALU.mult),
                        reads=HK + [("hstate",)], writes=[("fix",)])
                    p.op("vector", lambda v, bvh=bvh: v.tensor_tensor(
                        bvh[:, :, 0], bvh[:, :, 0], fix_sb[:, 0:H], ALU.add),
                        reads=HB + [("fix",)], writes=HB)
                p.op("vector", lambda v, avh=avh: v.memset(avh[:, :, 0], 0.0),
                     reads=HK, writes=HK)
                p.op("vector", lambda v, hsl=hsl: v.tensor_tensor_scan(
                    b_sb[:, hsl], a_sb[:, hsl], b_sb[:, hsl], 0.0,
                    ALU.mult, ALU.add),
                    reads=HK + HB, writes=HB)
                p.op("vector", lambda v, n=n, bvh=bvh, h=h: v.tensor_copy(
                    hstate_sb[:, n * DC + h * H:n * DC + (h + 1) * H],
                    bvh[:, :, LC - 1]),
                    reads=HB, writes=[("hstate",)])
                hk = ("tp", h)
                p.op("vector", lambda v, n=n, hsl=hsl, h=h: v.tensor_tensor(
                    tprod_sb[:, hsl].rearrange("p (dc l) -> p dc l", l=LC),
                    b_sb[:, hsl].rearrange("p (dc l) -> p dc l", l=LC),
                    bcq_sb[(n // 4) % 2][:, (4 + n % 4) * LC:(4 + n % 4 + 1) * LC][
                        :, None, :].broadcast_to([128, H, LC]),
                    ALU.mult),
                    reads=HB + [("bcq", (n // 4) % 2)], writes=[hk])
                p.op("vector", lambda v, hsl=hsl: v.tensor_tensor(
                    y_sb[:, hsl], y_sb[:, hsl], tprod_sb[:, hsl], ALU.add),
                    reads=[hk] + YK, writes=YK)
            if n % 4 == 1 and n // 4 + 1 < 4:
                bcq_load(ch, n // 4 + 1)
            if nxt:
                for j in usched.get(n, []):
                    xz_eblk(j, ch + 1)
                for j in csched.get(n, []):
                    conv_dc_next(j)
                if n == tail_n:
                    conv_tail_next()
                for j in zsched.get(n, []):
                    xz_eblk(j, ch + 1)
                if n == 8 and ch + 2 < NCH:
                    ln12_stats(ch + 2)
                if n == 9 and ch + 2 < NCH:
                    ln12_mid(ch + 2)
                if n == 11:
                    xdbl_dt(ch + 1)
                if 12 <= n <= 15:
                    dt_chain(ch + 1,
                             range(DC * (n - 12) // 4, DC * (n - 11) // 4))

    def conv_dc_next(dc):
        par = dc % 2
        for k in range(CONV):
            j = dc * CONV + k
            p.op("tensor", lambda t, j=j, dc=dc, k=k, par=par: t.matmul(
                ps_conv[par][:],
                diag_sb[:, j * 128:(j + 1) * 128],
                xzu_sb[:, dc * LCP + k: dc * LCP + k + LC],
                start=(k == 0), stop=False),
                reads=[("diag", j), ("xzu", dc)],
                writes=[("ps_conv", par)], accum=(k > 0))
        p.op("tensor", lambda t, dc=dc, par=par: t.matmul(
            ps_conv[par][:],
            convbr_sb[0:1, dc * 128:(dc + 1) * 128], ones_rb[:],
            start=False, stop=True),
            reads=[("convbr_sb",), ("ones_rb",)],
            writes=[("ps_conv", par)], accum=True)
        p.op("scalar", lambda s, dc=dc, par=par: s.activation(
            u_sb[:, dc * LC:(dc + 1) * LC], ps_conv[par][:], AF.Silu),
            reads=[("ps_conv", par)], writes=[("u", dc)])

    def conv_tail_next():
        xv = xzu_sb[:].rearrange("p (dc l) -> p dc l", l=LCP)
        p.op("vector", lambda v, xv=xv: v.tensor_copy(
            xv[:, :, 0:CONV - 1], xv[:, :, LC:LC + CONV - 1]),
            reads=XZU, writes=XZU)

    def gate_out(ch):
        cpar = ch % 2
        xv = xzu_sb[:].rearrange("p (dc l) -> p dc l", l=LCP)
        p.op("vector", lambda v, cpar=cpar, xv=xv: v.tensor_tensor(
            xv[:, :, CONV - 1:CONV - 1 + LC],
            y_sb[:].rearrange("p (dc l) -> p dc l", l=LC),
            gt_sb[cpar][:].rearrange("p (dc l) -> p dc l", l=LC),
            ALU.mult),
            reads=YK + [("gt", cpar, d) for d in range(DC)], writes=XZU)
        for km in range(DM):
            par = km % 2
            for dc in range(DC):
                p.op("tensor", lambda t, km=km, dc=dc, par=par: t.matmul(
                    ps_out[par][:],
                    wout_sb[:, dc * D_MODEL + km * 128:
                            dc * D_MODEL + (km + 1) * 128],
                    xzu_sb[:, dc * LCP + CONV - 1: dc * LCP + CONV - 1 + LC],
                    start=(dc == 0), stop=(dc == DC - 1)),
                    reads=[("wout_sb", dc * 4 // DC), ("xzu", dc)],
                    writes=[("ps_o",)], accum=(dc > 0))
            ks = km % 2
            if ch < HCH:
                dst = outst_sb[:, ks * LC:(ks + 1) * LC]
            else:
                dst = outst_sb[:, ks * LC:(ks + 1) * LC][:, ::-1]
            p.op("scalar", lambda s, dst=dst, par=par: s.copy(
                dst, ps_out[par][:]),
                reads=[("ps_o",)], writes=[("outst", ks)])
            if ks == 1:
                kg = km - 1
                if ch < HCH:
                    dst2 = yown_dram.rearrange("(km p) l -> p km l", p=128)[
                        :, kg:kg + 2, ch * LC:(ch + 1) * LC]
                    rg = ("yown_dram", ch)
                else:
                    cc = NCH - 1 - ch
                    base = cc * LC
                    hi = base // (HALF // 2)
                    off = base % (HALF // 2)
                    dst2 = send_dram[hi].rearrange("(km p) l -> p km l", p=128)[
                        :, kg:kg + 2, off:off + LC]
                    rg = ("send_dram", cc)
                p.op("sync", lambda s, dst2=dst2: s.dma_start(
                    out=dst2,
                    in_=outst_sb[:].rearrange("p (km l) -> p km l", l=LC)),
                    reads=[("outst", 0), ("outst", 1)],
                    writes=[rg], dma=True, sem="d_ow")

    def collective():
        groups = [[2 * i, 2 * i + 1] for i in range(NCORES // 2)]
        halves = [(0, HALF // 2), (HALF // 2, HALF)]
        for hi, (c0, c1) in enumerate(halves):
            chs = [c for c in range(HCH)
                   if c0 <= (HCH - 1 - c) * LC < c1 or c0 <= (HCH - 1 - c) * LC + LC - 1 < c1]
            p.op("gpsimd", lambda g, hi=hi: g.collective_compute(
                "AllReduce", ALU.add, replica_groups=groups,
                ins=[send_dram[hi].opt()],
                outs=[ar_dram[hi].opt()]),
                reads=[("send_dram", c) for c in range(HCH)],
                writes=[("ar_dram", hi)], sem="cc" if hi == 0 else "cc1",
                inc=1)

    def post_chunk(pc):
        par = pc % 2
        xt = xres_sb[:]
        colsl = slice(pc * LC, (pc + 1) * LC)
        p.op("sync", lambda s: s.dma_start(
            out=xt.rearrange("p (km l) -> p km l", l=LC),
            in_=x_res.rearrange("(km p) l -> p km l", p=128)[:, :, colsl]),
            writes=[("xres",)], dma=True, sem=f"d_x{par}")
        p.op("sync", lambda s: s.dma_start(
            out=b_sb[:, 0:DM * LC].rearrange("p (km l) -> p km l", l=LC),
            in_=yown_dram.rearrange("(km p) l -> p km l", p=128)[:, :, colsl]),
            reads=[("yown_dram", pc)], writes=BK, dma=True, sem=f"d_pl{par}")
        arh = 0 if (pc + 1) * LC <= HALF // 2 else 1
        poff = pc * LC - arh * (HALF // 2)
        p.op("sync", lambda s: s.dma_start(
            out=uw_sb[:, 0:DM * LC].rearrange("p (km l) -> p km l", l=LC),
            in_=send_dram[arh].rearrange("(km p) l -> p km l", p=128)[
                :, :, poff:poff + LC]),
            reads=[("send_dram", pc)], writes=[("uw",)], dma=True,
            sem=f"d_pl{par}")
        p.op("sync", lambda s: s.dma_start(
            out=gt_sb[0][:, 0:DM * LC].rearrange("p (km l) -> p km l", l=LC),
            in_=ar_dram[arh].rearrange("(km p) l -> p km l", p=128)[
                :, :, poff:poff + LC]),
            reads=[("ar_dram", arh)], writes=GT, dma=True, sem=f"d_pl{par}")
        p.op("vector", lambda v: v.tensor_tensor(
            a_sb[:, 0:DM * LC], gt_sb[0][:, 0:DM * LC], uw_sb[:, 0:DM * LC],
            ALU.subtract),
            reads=GT + [("uw",)], writes=AK)
        p.op("vector", lambda v: v.tensor_tensor(
            dt_sb[0][:, 0:DM * LC], a_sb[:, 0:DM * LC], b_sb[:, 0:DM * LC],
            ALU.add),
            reads=AK + BK, writes=DTK(0))

        ln_stats_apply(xt, ("xres",), None, combine12=False)
        for km in range(DM):
            sl = slice(km * LC, (km + 1) * LC)
            ts = slice((km % 2) * LC, (km % 2 + 1) * LC)
            p.op("vector", lambda v, sl=sl, ts=ts: v.tensor_tensor(
                ys_sb[:, ts], xt[:, sl], ps_rep[:, 0:LC], ALU.mult),
                reads=[("xres",), ("ps_rep",)], writes=[("ys", km % 2)])
            p.op("vector", lambda v, ts=ts: v.tensor_tensor(
                ys_sb[:, ts], ys_sb[:, ts], ps_rep[:, 256:256 + LC],
                ALU.subtract),
                reads=[("ys", km % 2), ("ps_rep",)],
                writes=[("ys", km % 2)])
            p.op("vector", lambda v, sl=sl, ts=ts: v.tensor_tensor(
                ys_sb[:, ts], ys_sb[:, ts], xt[:, sl], ALU.add),
                reads=[("ys", km % 2), ("xres",)],
                writes=[("ys", km % 2)])
            p.op("vector", lambda v, sl=sl, ts=ts: v.tensor_tensor(
                x1_sb[:, sl], ys_sb[:, ts], dt_sb[0][:, sl], ALU.add),
                reads=[("ys", km % 2), ("dt", 0, sl.start // LC)],
                writes=[("x1", sl.start // LC)])

        ln_stats_apply(x1_sb[:], X1, None, combine12=False)
        for km in range(DM):
            sl = slice(km * LC, (km + 1) * LC)
            ts = slice((km % 2) * LC, (km % 2 + 1) * LC)
            p.op("vector", lambda v, sl=sl, ts=ts: v.tensor_tensor(
                ys_sb[:, ts], x1_sb[:, sl], ps_rep[:, 0:LC], ALU.mult),
                reads=[("x1", km), ("ps_rep",)], writes=[("ys", km % 2)])
            p.op("vector", lambda v, sl=sl, ts=ts, km=km: v.tensor_tensor(
                xn_sb[0][:, sl], ys_sb[:, ts], ps_rep[:, 256:256 + LC],
                ALU.subtract),
                reads=[("ys", km % 2), ("ps_rep",)], writes=[("xn", 0, km)])

        for km in range(DM):
            wr = [("ps_ffs",)] + ([("ps_xdbl",), ("ps_dt",)] if km == 0 else [])
            p.op("tensor", lambda t, km=km: t.matmul(
                ps_small[0:M, 0:LC], winf_sb[:, km * M:(km + 1) * M],
                xn_sb[0][:, km * LC:(km + 1) * LC],
                start=(km == 0), stop=(km == DM - 1)),
                reads=[("winf_sb",), ("xn", 0, km)],
                writes=wr, accum=(km > 0))
        for i in range(INNER):
            wr = [("hff", i)] + (AK if (pc == 0 and i == 0) else [])
            p.op("scalar", lambda s, i=i: s.activation(
                hff_sb[0:M, i * LC:(i + 1) * LC], ps_small[0:M, 0:LC], AF.Relu,
                bias=bias1_sb[:, i:i + 1], scale=w1_sb[:, i:i + 1]),
                reads=[("ps_ffs",), ("bias1_sb",), ("w1_sb",)],
                writes=wr)
        p.op("vector", lambda v: v.tensor_scalar(
            gacc_sb[0][:], hff_sb[0:M, 0:LC], w2_sb[:, 0:1], b2_sb[:, 0:1],
            ALU.mult, ALU.add),
            reads=[("hff", 0), ("w2_sb",), ("b2_sb",)], writes=[("gacc", 0)])
        for i in range(1, INNER):
            sp, dp = (i - 1) % 2, i % 2
            p.op("vector", lambda v, i=i, sp=sp, dp=dp: v.scalar_tensor_tensor(
                gacc_sb[dp][:], hff_sb[0:M, i * LC:(i + 1) * LC],
                w2_sb[:, i:i + 1], gacc_sb[sp][:], ALU.mult, ALU.add),
                reads=[("hff", i), ("w2_sb",), ("gacc", sp)],
                writes=[("gacc", dp)])
        gfin = gacc_sb[(INNER - 1) % 2]
        gfin_rg = ("gacc", (INNER - 1) % 2)
        for km in range(DM):
            par2 = km % 2
            p.op("tensor", lambda t, km=km, par2=par2: t.matmul(
                ps_out[par2][:], woutf_sb[:, km * 128:(km + 1) * 128],
                gfin[:], start=True, stop=True),
                reads=[("woutf_sb",), gfin_rg], writes=[("ps_o",)])
            p.op("vector", lambda v, km=km, par2=par2: v.scalar_tensor_tensor(
                xres_sb[:, km * LC:(km + 1) * LC], ps_out[par2][:],
                bout_sb[:, km:km + 1], x1_sb[:, km * LC:(km + 1) * LC],
                ALU.add, ALU.add),
                reads=[("ps_o",), ("bout_sb",), ("x1", km)],
                writes=[("xres",)])
        p.op("sync", lambda s: s.dma_start(
            out=out_d.rearrange("(km p) l -> p km l", p=128)[:, :, colsl],
            in_=xres_sb[:].rearrange("p (km l) -> p km l", l=LC)),
            reads=[("xres",)], writes=[("out_d", pc)], dma=True, sem="d_od")

    startup()
    x_load(0)
    ln12_stats(0)
    ln12_mid(0)
    ln12_apply(0)
    x_load(1)
    xz_u(0)
    conv_silu(0)
    xz_z(0)
    xdbl_dt(0)
    dt_chain(0, range(DC))
    if NCH > 1:
        ln12_stats(1)
        ln12_mid(1)
    for k in range(NCH):
        if k > 0:
            gate_out(k - 1)
        if k + 1 < NCH:
            ln12_apply(k + 1)
        uw_y_init(k)
        if k + 2 < NCH:
            x_load(k + 2)
        scan_phase(k)
    gate_out(NCH - 1)
    collective()
    for pc in range(HCH):
        post_chunk(pc)

    for snm in ["d_st", "d_x0", "d_x1", "d_w0", "d_w1", "d_bc0", "d_bc1",
                "d_ow", "d_pl0", "d_pl1", "d_od", "d_bn0", "d_bn1"]:
        p.q["sync"].append(("wait", snm, p.cnt.get(snm, 0)))
    p.q["gpsimd"].append(("wait", "d_g", p.cnt.get("d_g", 0)))
    p.q["gpsimd"].append(("wait", "d_gx0", p.cnt.get("d_gx0", 0)))
    p.q["gpsimd"].append(("wait", "d_gx1", p.cnt.get("d_gx1", 0)))
    p.q["gpsimd"].append(("wait", "cc", p.cnt.get("cc", 0)))
    p.q["gpsimd"].append(("wait", "cc1", p.cnt.get("cc1", 0)))

    with nc.Block() as block:
        p.finalize(block, sems)
    stack.close()
    return nc


def _prep_core_inputs(cfg, x_b, flip, w, ffn):
    DM, DC, L = cfg["DM"], cfg["DC"], cfg["L"]
    N, R, CONV, M = cfg["N"], cfg["R"], cfg["CONV"], cfg["M"]
    D_MODEL, D_INNER = DM * 128, DC * 128
    EB = 2 * DC
    HALF = L // 2
    C = np.ascontiguousarray

    xd = x_b[::-1] if flip else x_b
    ins = {}
    ins["xT"] = C(xd.T.astype(np.float32))
    ins["x_res"] = C(xd[:HALF].T.astype(np.float32))
    win = np.asarray(w["Win"], dtype=np.float32)
    ins["win_t4"] = C(win.reshape(EB, 128, DM, 128).transpose(0, 3, 2, 1))
    ins["wout_t"] = C(np.asarray(w["Wout"], np.float32).T)
    wxT = np.asarray(w["Wx"], np.float32).T
    wxp = np.zeros((wxT.shape[0], 128), np.float32)
    wxp[:, 0:R] = wxT[:, 0:R]
    wxp[:, R:R + N] = wxT[:, R:R + N]
    wxp[:, 96:96 + N] = wxT[:, R + N:R + 2 * N]
    ins["wx_t"] = C(wxp)
    ins["wdt_t"] = C(np.asarray(w["Wdt"], np.float32).T)
    ins["alog_c"] = C(np.asarray(w["Alog"], np.float32)
                      .reshape(DC, 128, N).transpose(1, 0, 2).reshape(128, DC * N))
    ins["dcol"] = C(np.asarray(w["D"], np.float32).reshape(DC, 128).T)
    ins["bdt_c"] = C(np.asarray(w["bdt"], np.float32).reshape(DC, 128).T)
    ins["convw_c"] = C(np.asarray(w["convw"], np.float32)[:, 0, :]
                       .reshape(DC, 128, CONV).transpose(1, 0, 2)
                       .reshape(128, DC * CONV))
    ins["convb_r"] = C(np.asarray(w["convb"], np.float32)[None, :])
    ins["gcol"] = C(np.asarray(w["norm_g"], np.float32).reshape(DM, 128).T)
    ins["bcol"] = C(np.asarray(w["norm_b"], np.float32).reshape(DM, 128).T)
    ins["winf"] = C(np.asarray(ffn["Win"], np.float32))
    ins["woutf"] = C(np.asarray(ffn["Wout"], np.float32))
    ins["w1"] = C(np.asarray(ffn["W1"], np.float32))
    ins["b1"] = C(np.asarray(ffn["b1"], np.float32))
    ins["w2"] = C(np.asarray(ffn["W2"], np.float32))
    ins["b2c"] = C(np.asarray(ffn["b2"], np.float32)[:, None])
    ins["binc"] = C(np.asarray(ffn["bin_"], np.float32)[:, None])
    ins["bout_c"] = C(np.asarray(ffn["bout"], np.float32).reshape(DM, 128).T)
    ins["eye16"] = np.eye(128, dtype=ml_dtypes.bfloat16)
    return ins


def _assemble(cfg, shards):
    DM, L = cfg["DM"], cfg["L"]
    D_MODEL, HALF = DM * 128, L // 2
    B = len(shards) // 2
    out = np.empty((B, L, D_MODEL), dtype=np.float32)
    for b in range(B):
        out[b, :HALF] = shards[2 * b].T
        out[b, HALF:] = shards[2 * b + 1].T[::-1]
    return out


MAMBA_KEYS = ["Win", "convw", "convb", "Wx", "Wdt", "bdt", "Alog", "D", "Wout"]


def kernel(**inputs):
    cfg = FULL_CFG
    x = np.asarray(inputs["L0_attn_x"], dtype=np.float32)
    Bn = x.shape[0]
    wf = {k: np.asarray(inputs["L0_attn_f_" + k]) for k in MAMBA_KEYS}
    wb = {k: np.asarray(inputs["L0_attn_b_" + k]) for k in MAMBA_KEYS}
    for d in (wf, wb):
        d["norm_g"] = np.asarray(inputs["L0_attn_norm_g"])
        d["norm_b"] = np.asarray(inputs["L0_attn_norm_b"])
    ffn = {k: np.asarray(inputs["L0_ffn_" + k]) for k in
           ["Win", "bin_", "W1", "b1", "W2", "b2", "Wout", "bout"]}

    nc = build_core_program(cfg)
    in_maps = []
    for b in range(Bn):
        in_maps.append(_prep_core_inputs(cfg, x[b], False, wf, ffn))
        in_maps.append(_prep_core_inputs(cfg, x[b], True, wb, ffn))
    res = run_bass_kernel_spmd(nc, in_maps, core_ids=list(range(2 * Bn)))
    shards = [np.asarray(r["out"]) for r in res.results]
    return _assemble(cfg, shards)


# version 25
# speedup vs baseline: 1.0076x; 1.0076x over previous
import numpy as np
import ml_dtypes
from contextlib import ExitStack

import concourse.bass as bass
import concourse.mybir as mybir
from concourse.bass_utils import run_bass_kernel_spmd

F32 = mybir.dt.float32
BF16 = mybir.dt.bfloat16
AF = mybir.ActivationFunctionType
ALU = mybir.AluOpType

EPS = 1e-5

FULL_CFG = dict(DM=8, DC=16, L=4096, LC=256, N=16, R=64, CONV=4, M=64,
                INNER=16, NCORES=8)


class Prog:
    ENGINES = ("tensor", "vector", "scalar", "gpsimd", "sync")

    @staticmethod
    def is_dma_sem(sem):
        return sem.startswith("d_") or sem.startswith("cc")

    def __init__(self, nc):
        self.nc = nc
        self.q = {e: [] for e in self.ENGINES}
        self.cnt = {}
        self.state = {}
        self.waited = {}
        self.opidx = {e: 0 for e in self.ENGINES}
        self.last_drain = {e: -1 for e in self.ENGINES}

    def _need(self, eng, stamp, drains):
        sem, val, peng, pidx = stamp
        if self.is_dma_sem(sem):
            val = self.cnt.get(sem, 0)
        k = (eng, sem)
        if self.waited.get(k, 0) < val:
            self.waited[k] = val
            self.q[eng].append(("wait", sem, val))

    def op(self, eng, fn, reads=(), writes=(), dma=False, sem=None, inc=None,
           accum=False):
        drains = []
        for r in reads:
            st = self.state.get(r)
            if st and st["w"]:
                self._need(eng, st["w"], drains)
        if not accum:
            for r in writes:
                st = self.state.get(r)
                if st:
                    if st["w"]:
                        sem_, val_, peng_, pidx_ = st["w"]
                        if peng_ != eng or self.is_dma_sem(sem_):
                            self._need(eng, st["w"], drains)
                    for stamp in st["r"].values():
                        self._need(eng, stamp, drains)
        if drains and max(drains) > self.last_drain[eng]:
            self.q[eng].append(("drain",))
            self.last_drain[eng] = self.opidx[eng]
        if sem is None:
            assert not dma, "DMA ops must name their semaphore"
            sem = eng
        if inc is None:
            inc = 16 if dma else 1
        self.cnt[sem] = self.cnt.get(sem, 0) + inc
        val = self.cnt[sem]
        self.opidx[eng] += 1
        stamp = (sem, val, eng, self.opidx[eng])
        self.q[eng].append(("op", fn, sem, inc, val))
        for r in writes:
            self.state[r] = {"w": stamp, "r": {}}
        for r in reads:
            st = self.state.setdefault(r, {"w": None, "r": {}})
            st["r"][eng] = stamp
        return stamp

    def finalize(self, block, sems):
        waited = {}
        for ename in self.ENGINES:
            for it in self.q[ename]:
                if it[0] == "wait" and not self.is_dma_sem(it[1]):
                    waited.setdefault(it[1], set()).add(it[2])
        remap = {}
        for sem, vals in waited.items():
            remap[sem] = {v: i + 1 for i, v in enumerate(sorted(vals))}

        for ename in self.ENGINES:
            items = self.q[ename]

            def make_body(items):
                def body(engine):
                    for it in items:
                        if it[0] == "wait":
                            sem, val = it[1], it[2]
                            if sem in remap:
                                val = remap[sem][val]
                            engine.wait_ge(sems[sem], val)
                        elif it[0] == "drain":
                            engine.drain()
                        else:
                            _, fn, sem, inc, val = it
                            inst = fn(engine)
                            if self.is_dma_sem(sem):
                                inst.then_inc(sems[sem], inc)
                            elif sem in remap and val in remap[sem]:
                                inst.then_inc(sems[sem], inc)
                return body
            getattr(block, ename)(make_body(items))


def build_core_program(cfg):
    DM, DC, L, LC = cfg["DM"], cfg["DC"], cfg["L"], cfg["LC"]
    N, R, CONV = cfg["N"], cfg["R"], cfg["CONV"]
    M, INNER = cfg["M"], cfg["INNER"]
    NCORES = cfg["NCORES"]
    D_MODEL = DM * 128
    D_INNER = DC * 128
    NCH = L // LC
    HCH = NCH // 2
    HALF = L // 2
    EB = 2 * DC
    LCP = LC + CONV - 1
    RN = 128

    nc = bass.Bass("TRN2", target_bir_lowering=False)

    def din(name, shape, dt=F32):
        return nc.dram_tensor(name, list(shape), dt, kind="ExternalInput").ap()

    xT = din("xT", [D_MODEL, L])
    x_res = din("x_res", [D_MODEL, HALF])
    win_t4 = din("win_t4", [EB, 128, DM, 128])
    wout_t = din("wout_t", [D_INNER, D_MODEL])
    wx_t = din("wx_t", [D_INNER, RN])
    wdt_t = din("wdt_t", [R, D_INNER])
    alog_c = din("alog_c", [128, DC * N])
    dcol = din("dcol", [128, DC])
    bdt_c = din("bdt_c", [128, DC])
    convw_c = din("convw_c", [128, DC * CONV])
    convb_r = din("convb_r", [1, D_INNER])
    gcol = din("gcol", [128, DM])
    bcol = din("bcol", [128, DM])
    winf = din("winf", [D_MODEL, M])
    woutf = din("woutf", [M, D_MODEL])
    w1 = din("w1", [M, INNER])
    b1 = din("b1", [M, INNER])
    w2 = din("w2", [M, INNER])
    b2c = din("b2c", [M, 1])
    binc = din("binc", [M, 1])
    bout_c = din("bout_c", [128, DM])
    eye16 = din("eye16", [128, 128], BF16)

    out_d = nc.dram_tensor("out", [D_MODEL, HALF], F32, kind="ExternalOutput").ap()

    win_bf = nc.dram_tensor("win_bf", [EB, 128, DM * 128], BF16).ap()
    bc_dram = [nc.dram_tensor(f"bc_dram{i}", [2 * N, LC], BF16).ap()
               for i in range(2)]
    yown_dram = nc.dram_tensor("yown_dram", [D_MODEL, HALF], BF16).ap()
    send_dram = nc.dram_tensor("send_dram", [2, D_MODEL, HALF // 2], BF16).ap()
    ar_dram = nc.dram_tensor("ar_dram", [2, D_MODEL, HALF // 2], BF16).ap()

    stack = ExitStack()

    def sbuf(name, shape, dt):
        return stack.enter_context(nc.sbuf_tensor(name, list(shape), dt))

    def psum(name, shape, dt=F32):
        return stack.enter_context(nc.psum_tensor(name, list(shape), dt))

    wout_sb = sbuf("wout_sb", [128, DC * D_MODEL], BF16)
    wx_sb = sbuf("wx_sb", [128, DC * RN], BF16)
    wdt_sb = sbuf("wdt_sb", [R, D_INNER], BF16)
    winf_sb = sbuf("winf_sb", [128, DM * M], BF16)
    woutf_sb = sbuf("woutf_sb", [M, D_MODEL], BF16)
    diag_sb = sbuf("diag_sb", [128, DC * CONV * 128], BF16)
    eye_sb = sbuf("eye_sb", [128, 128], BF16)
    A_sb = sbuf("A_sb", [128, DC * N], F32)
    dcol_sb = sbuf("dcol_sb", [128, DC], F32)
    bdt_sb = sbuf("bdt_sb", [128, DC], F32)
    convw_sb = sbuf("convw_sb", [128, DC * CONV], F32)
    convbr_sb = sbuf("convbr_sb", [1, D_INNER], BF16)
    gcol_sb = sbuf("gcol_sb", [128, DM], F32)
    bcol_sb = sbuf("bcol_sb", [128, DM], F32)
    w1_sb = sbuf("w1_sb", [M, INNER], F32)
    b1_sb = sbuf("b1_sb", [M, INNER], F32)
    w2_sb = sbuf("w2_sb", [M, INNER], F32)
    bias1_sb = sbuf("bias1_sb", [M, INNER], F32)
    b2_sb = sbuf("b2_sb", [M, 1], F32)
    binc_sb = sbuf("binc_sb", [M, 1], F32)
    bout_sb = sbuf("bout_sb", [128, DM], F32)
    ones_rf = sbuf("ones_rf", [1, 128], F32)
    cvals_sb = sbuf("cvals_sb", [1, 4], F32)
    half_c = sbuf("half_c", [128, 1], F32)
    ones_rb = sbuf("ones_rb", [1, LC], BF16)
    onec_f = sbuf("onec_f", [128, 1], F32)
    onec_b = sbuf("onec_b", [128, 1], BF16)
    c_sb = sbuf("c_sb", [128, EB], F32)
    bcolb_sb = sbuf("bcolb_sb", [128, DM], BF16)

    wbuf = [sbuf(f"wbuf{i}", [128, DM * 128], BF16) for i in range(2)]
    xt_sb = [sbuf(f"xt_sb{i}", [128, DM * LC], BF16) for i in range(2)]
    xn_sb = [sbuf(f"xn_sb{i}", [128, DM * LC], BF16) for i in range(2)]
    xzu_sb = sbuf("xzu_sb", [128, DC * LCP], BF16)
    u_sb = sbuf("u_sb", [128, DC * LC], BF16)
    gt_sb = [sbuf(f"gt_sb{i}", [128, DC * LC], BF16) for i in range(2)]
    dt_sb = [sbuf(f"dt_sb{i}", [128, DC * LC], BF16) for i in range(2)]
    uw_sb = sbuf("uw_sb", [128, DC * LC], BF16)
    a_sb = sbuf("a_sb", [128, DC * LC], BF16)
    b_sb = sbuf("b_sb", [128, DC * LC], BF16)
    y_sb = sbuf("y_sb", [128, DC * LC], BF16)
    tprod_sb = sbuf("tprod_sb", [128, DC * LC], BF16)
    bc_sb = sbuf("bc_sb", [64, LC], BF16)
    bcq_sb = [sbuf(f"bcq_sb{i}", [128, 8 * LC], BF16) for i in range(2)]
    hstate_sb = sbuf("hstate_sb", [128, N * DC], BF16)
    fix_sb = sbuf("fix_sb", [128, DC], BF16)
    outst_sb = sbuf("outst_sb", [128, 2 * LC], BF16)
    stat_sb = sbuf("stat_sb", [1, 2 * LC], F32)
    gacc_sb = [sbuf(f"gacc_sb{i}", [M, LC], BF16) for i in range(2)]
    x1_sb = sbuf("x1_sb", [128, DM * LC], F32)
    xres_sb = sbuf("xres_sb", [128, DM * LC], F32)
    ys_sb = sbuf("ys_sb", [128, 2 * LC], F32)
    s2row_sb = sbuf("s2row_sb", [1, LC], F32)
    e1t_sb = sbuf("e1t_sb", [128, LC], BF16)
    dtrt_sb = sbuf("dtrt_sb", [R, LC], BF16)
    e1_sb = e1t_sb[:, 0:LC]
    dtr_sb = dtrt_sb[0:R, 0:LC]
    if INNER * LC <= DC * LC:
        hff_sb = a_sb
    else:
        hff_sb = sbuf("hff_sb", [M, INNER * LC], BF16)

    ps_rep = psum("ps_rep", [128, 512])
    ps_st = psum("ps_st", [1, 512])
    ps_sum = ps_st[0:1, 0:256]
    ps_sq = ps_st[0:1, 256:512]
    ps_xz = [psum(f"ps_xz{i}", [128, LC]) for i in range(2)]
    ps_conv = [psum(f"ps_conv{i}", [128, LC]) for i in range(2)]
    ps_small = psum("ps_small", [128, 512])
    ps_outb = [psum("ps_outb", [128, LC])]
    ps_out = [ps_outb[0], ps_outb[0]]

    assert LC <= 256

    sem_names = ["tensor", "vector", "scalar", "gpsimd", "sync",
                 "d_st", "d_x0", "d_x1", "d_w0", "d_w1", "d_bc0", "d_bc1",
                 "d_ow", "d_pl0", "d_pl1", "d_od", "d_g", "d_gx0", "d_gx1",
                 "d_bn0", "d_bn1", "cc", "cc1"]
    sems = {nm: stack.enter_context(nc.semaphore(nm)) for nm in sem_names}

    p = Prog(nc)
    W = {"xt": lambda i: ("xt", i), "xn": lambda k: ("xn", k)}

    XN = [("xn", k) for k in range(DM)]
    U = [("u", d) for d in range(DC)]
    AK = [("a", d) for d in range(DC)]
    BK = [("b", d) for d in range(DC)]
    YK = [("y", d) for d in range(DC)]
    def DTK(par):
        return [("dt", par, d) for d in range(DC)]
    GT = [("gt", 0, d) for d in range(DC)]
    XZU = [("xzu", d) for d in range(DC)]
    X1 = [("x1", k) for k in range(DM)]
    YS = [("ys", k) for k in range(DM)]
    OUTST = [("outst", k) for k in range(DM)]

    def startup():
        for eblk in range(EB):
            p.op("gpsimd", lambda g, eblk=eblk: g.dma_start(
                out=win_bf[eblk], in_=win_t4[eblk]),
                writes=[("win_bf", eblk)], dma=True, sem="d_g")
        for j in range(4):
            d0 = DC // 4 * j
            d1 = DC // 4 * (j + 1)
            p.op("gpsimd", lambda g, d0=d0, d1=d1: g.dma_start(
                out=wout_sb[:, d0 * D_MODEL:d1 * D_MODEL].rearrange(
                    "p (dc d) -> p dc d", d=D_MODEL),
                in_=wout_t.rearrange("(dc p) d -> p dc d", p=128)[:, d0:d1]),
                writes=[("wout_sb", j)], dma=True, sem="d_g")
        p.op("gpsimd", lambda g: g.dma_start(
            out=wx_sb[:].rearrange("p (dc r) -> p dc r", r=RN),
            in_=wx_t.rearrange("(dc p) r -> p dc r", p=128)),
            writes=[("wx_sb",)], dma=True, sem="d_g")
        p.op("gpsimd", lambda g: g.dma_start(out=wdt_sb[:], in_=wdt_t),
             writes=[("wdt_sb",)], dma=True, sem="d_g")
        p.op("gpsimd", lambda g: g.dma_start(
            out=winf_sb[:].rearrange("p (km m) -> p km m", m=M),
            in_=winf.rearrange("(km p) m -> p km m", p=128)),
            writes=[("winf_sb",)], dma=True, sem="d_g")
        p.op("gpsimd", lambda g: g.dma_start(out=woutf_sb[:], in_=woutf),
             writes=[("woutf_sb",)], dma=True, sem="d_g")
        p.op("gpsimd", lambda g: g.dma_start(out=convbr_sb[:], in_=convb_r),
             writes=[("convbr_sb",)], dma=True, sem="d_g")
        loads = [(A_sb, alog_c, "A_sb"), (dcol_sb, dcol, "dcol_sb"),
                 (bdt_sb, bdt_c, "bdt_sb"), (convw_sb, convw_c, "convw_sb"),
                 (gcol_sb, gcol, "gcol_sb"), (bcol_sb, bcol, "bcol_sb"),
                 (w1_sb, w1, "w1_sb"), (b1_sb, b1, "b1_sb"),
                 (w2_sb, w2, "w2_sb"), (b2_sb, b2c, "b2_sb"),
                 (binc_sb, binc, "binc_sb"), (bout_sb, bout_c, "bout_sb"),
                 (eye_sb, eye16, "eye_sb")]
        for t, src, rg in loads:
            p.op("sync", lambda s, t=t, src=src: s.dma_start(out=t[:], in_=src),
                 writes=[(rg,)], dma=True, sem="d_st")
        p.op("vector", lambda v: v.memset(ones_rf[:], 1.0), writes=[("ones_rf",)])
        p.op("vector", lambda v: v.memset(cvals_sb[0:1, 0:1], EPS * EPS),
             writes=[("cv", 0)])
        p.op("vector", lambda v: v.memset(cvals_sb[0:1, 1:2], 1.0 + EPS),
             writes=[("cv", 1)])
        p.op("vector", lambda v: v.memset(cvals_sb[0:1, 2:3], EPS),
             writes=[("cv", 2)])
        p.op("vector", lambda v: v.memset(cvals_sb[0:1, 3:4], -0.5),
             writes=[("cv", 3)])
        p.op("vector", lambda v: v.memset(half_c[:], 0.5), writes=[("half_c",)])
        p.op("vector", lambda v: v.memset(ones_rb[:], 1.0), writes=[("ones_rb",)])
        p.op("vector", lambda v: v.memset(onec_f[:], 1.0), writes=[("onec_f",)])
        p.op("vector", lambda v: v.memset(onec_b[:], 1.0), writes=[("onec_b",)])
        p.op("vector", lambda v: v.memset(hstate_sb[:], 0.0), writes=[("hstate",)])
        p.op("vector", lambda v: v.memset(xzu_sb[:], 0.0), writes=XZU)
        p.op("scalar", lambda s: s.activation(A_sb[:], A_sb[:], AF.Exp),
             reads=[("A_sb",)], writes=[("A_sb",)])
        p.op("vector", lambda v: v.tensor_scalar(
            A_sb[:], A_sb[:], -1.0, None, ALU.mult),
            reads=[("A_sb",)], writes=[("A_sb",)])
        for dc in range(DC):
            for k in range(CONV):
                j = dc * CONV + k
                p.op("vector", lambda v, j=j: v.tensor_scalar(
                    diag_sb[:, j * 128:(j + 1) * 128], eye_sb[:],
                    convw_sb[:, j:j + 1], None, ALU.mult),
                    reads=[("eye_sb",), ("convw_sb",)], writes=[("diag", j)])
        p.op("vector", lambda v: v.tensor_copy(bcolb_sb[:], bcol_sb[:]),
             reads=[("bcol_sb",)], writes=[("bcolb",)])
        WB = DM * 128
        nub = WB // LC
        stg = [u_sb[:, 0:WB], u_sb[:, WB:2 * WB]]
        stgk = [[("u", d) for d in range(nub)],
                [("u", d) for d in range(nub, 2 * nub)]]
        for eblk in range(EB):
            bf = eblk % 2
            p.op("sync", lambda s, eblk=eblk, bf=bf: s.dma_start(
                out=stg[bf][:], in_=win_bf[eblk]),
                reads=[("win_bf", eblk)], writes=stgk[bf], dma=True,
                sem=f"d_w{bf}")
            for k in range(DM):
                p.op("tensor", lambda t, k=k, bf=bf: t.matmul(
                    ps_rep[:, bf:bf + 1], stg[bf][:, k * 128:(k + 1) * 128],
                    bcolb_sb[:, k:k + 1],
                    start=(k == 0), stop=(k == DM - 1)),
                    reads=stgk[bf] + [("bcolb",)],
                    writes=[("ps_rep",)], accum=(k > 0))
            p.op("scalar", lambda s, eblk=eblk, bf=bf: s.copy(
                c_sb[:, eblk:eblk + 1], ps_rep[:, bf:bf + 1]),
                reads=[("ps_rep",)], writes=[("c_sb",)])
            for k in range(DM):
                p.op("vector", lambda v, k=k, bf=bf: v.tensor_scalar(
                    stg[bf][:, k * 128:(k + 1) * 128],
                    stg[bf][:, k * 128:(k + 1) * 128],
                    gcol_sb[:, k:k + 1], None, ALU.mult),
                    reads=stgk[bf] + [("gcol_sb",)], writes=stgk[bf])
            p.op("sync", lambda s, eblk=eblk, bf=bf: s.dma_start(
                out=win_bf[eblk], in_=stg[bf][:]),
                reads=stgk[bf], writes=[("win_bf", eblk)], dma=True,
                sem=f"d_w{bf}")
        p.op("vector", lambda v: v.tensor_scalar(
            bias1_sb[:], w1_sb[:], binc_sb[:, 0:1], None, ALU.mult),
            reads=[("w1_sb",), ("binc_sb",)], writes=[("bias1_sb",)])
        p.op("vector", lambda v: v.tensor_tensor(
            bias1_sb[:], bias1_sb[:], b1_sb[:], ALU.add),
            reads=[("bias1_sb",), ("b1_sb",)], writes=[("bias1_sb",)])

    def ln_stats_part(xt_ap, xt_rg, sq_buf, ones_col, ones_rg):
        for km in range(DM):
            p.op("tensor", lambda t, km=km: t.matmul(
                ps_sum[:, 0:LC], ones_col[:],
                xt_ap[:, km * LC:(km + 1) * LC],
                start=(km == 0), stop=(km == DM - 1)),
                reads=[xt_rg[km] if isinstance(xt_rg, list) else xt_rg,
                       ones_rg],
                writes=[("ps_sum",)], accum=(km > 0))
        for km in range(DM):
            sl = slice((km % 2) * LC, (km % 2 + 1) * LC)
            p.op("scalar", lambda s, km=km, sl=sl: s.activation(
                ys_sb[:, sl],
                xt_ap[:, km * LC:(km + 1) * LC], AF.Square),
                reads=[xt_rg[km] if isinstance(xt_rg, list) else xt_rg],
                writes=[("ys", km % 2)])
            p.op("tensor", lambda t, km=km, sl=sl: t.matmul(
                ps_sq[:, 0:LC], onec_f[:], ys_sb[:, sl],
                start=(km == 0), stop=(km == DM - 1)),
                reads=[("ys", km % 2), ("onec_f",)],
                writes=[("ps_sq",)], accum=(km > 0))

    def ln_stats_apply(xt_ap, xt_rg, sq_buf, combine12, ones_col=None,
                       ones_rg=("onec_f",), skip_stats=False):
        if ones_col is None:
            ones_col = onec_f
        if not skip_stats:
            ln_stats_part(xt_ap, xt_rg, sq_buf, ones_col, ones_rg)
        s0 = stat_sb[0:1, 0:LC]
        s1 = stat_sb[0:1, LC:2 * LC]
        s2 = s2row_sb[0:1, 0:LC]
        p.op("vector", lambda v: v.tensor_scalar(
            s0, ps_sum[:, 0:LC], 1.0 / D_MODEL, None, ALU.mult),
            reads=[("ps_sum",), ("ps_sq",)], writes=[("st0",)])
        p.op("vector", lambda v: v.tensor_scalar(
            s1, ps_sq[:, 0:LC], 1.0 / D_MODEL, None, ALU.mult),
            reads=[("ps_sq",), ("ps_sum",)], writes=[("st1",)])
        p.op("vector", lambda v: v.tensor_tensor(s2, s0, s0, ALU.mult),
             reads=[("st0",)], writes=[("st2",)])
        p.op("vector", lambda v: v.tensor_tensor(s1, s1, s2, ALU.subtract),
             reads=[("st1",), ("st2",)], writes=[("st1",)])
        if combine12:
            p.op("scalar", lambda s: s.activation(
                s1, s1, AF.Ln, bias=cvals_sb[0:1, 0:1],
                scale=cvals_sb[0:1, 1:2]),
                reads=[("st1",), ("cv", 0), ("cv", 1)], writes=[("st1",)])
        else:
            p.op("scalar", lambda s: s.activation(
                s1, s1, AF.Ln, bias=cvals_sb[0:1, 2:3]),
                reads=[("st1",), ("cv", 2)], writes=[("st1",)])
        p.op("scalar", lambda s: s.activation(
            s1, s1, AF.Exp, scale=cvals_sb[0:1, 3:4]),
            reads=[("st1",), ("cv", 3)], writes=[("st1",)])
        p.op("tensor", lambda t: t.matmul(
            ps_rep[:, 0:LC], ones_rf[:], s1, start=True, stop=True),
            reads=[("st1",), ("ones_rf",)], writes=[("ps_rep",)])
        p.op("vector", lambda v: v.tensor_tensor(s0, s0, s1, ALU.mult),
             reads=[("st0",), ("st1",)], writes=[("st0",)])
        p.op("tensor", lambda t: t.matmul(
            ps_rep[:, 256:256 + LC], ones_rf[:], s0, start=True, stop=True),
            reads=[("st0",), ("ones_rf",)], writes=[("ps_rep",)])

    def x_load(ch):
        par = ch % 2
        src = xT.rearrange("(km p) l -> p km l", p=128)[:, :, ch * LC:(ch + 1) * LC]
        p.op("gpsimd", lambda g, par=par, src=src: g.dma_start(
            out=xt_sb[par][:].rearrange("p (km l) -> p km l", l=LC), in_=src),
            writes=[("xt", par)], dma=True, sem=f"d_gx{par}")

    def ln12_stats(ch):
        par = ch % 2
        ln_stats_part(xt_sb[par][:], ("xt", par), None, onec_b, ("onec_b",))

    def ln12_mid(ch):
        par = ch % 2
        ln_stats_apply(xt_sb[par][:], ("xt", par), None, combine12=True,
                       ones_col=onec_b, ones_rg=("onec_b",), skip_stats=True)

    def ln12_apply(ch):
        par = ch % 2
        xt = xt_sb[par][:]
        xn = xn_sb[par][:]
        for km in range(DM):
            sl = slice(km * LC, (km + 1) * LC)
            r = km % 4
            if r < 2:
                tmp = ys_sb[:, r * LC:(r + 1) * LC]
                tkey = ("ys", r)
            else:
                tmp = x1_sb[:, (r - 2) * LC:(r - 1) * LC]
                tkey = ("x1", r - 2)
            p.op("vector", lambda v, sl=sl, tmp=tmp: v.tensor_tensor(
                tmp, xt[:, sl], ps_rep[:, 0:LC], ALU.mult),
                reads=[("xt", par), ("ps_rep",)], writes=[tkey])
            p.op("vector", lambda v, sl=sl, tmp=tmp, xn=xn: v.tensor_tensor(
                xn[:, sl], tmp, ps_rep[:, 256:256 + LC], ALU.subtract),
                reads=[tkey, ("ps_rep",)], writes=[("xn", par, km)])

    def xz_eblk(eblk, ch):
        par = eblk % 2
        cpar = ch % 2
        xn = xn_sb[cpar][:]
        p.op("sync", lambda s, par=par, eblk=eblk: s.dma_start(
            out=wbuf[par][:], in_=win_bf[eblk]),
            reads=[("win_bf", eblk)], writes=[("wbuf", par)], dma=True,
            sem=f"d_w{par}")
        for km in range(DM):
            p.op("tensor", lambda t, par=par, km=km, xn=xn: t.matmul(
                ps_xz[par][:],
                wbuf[par][:, km * 128:(km + 1) * 128],
                xn[:, km * LC:(km + 1) * LC],
                start=(km == 0), stop=(km == DM - 1)),
                reads=[("wbuf", par), ("xn", cpar, km)],
                writes=[("ps_xz", par)], accum=(km > 0))
        if eblk < DC:
            dc = eblk
            p.op("scalar", lambda s, dc=dc, par=par, eblk=eblk: s.activation(
                xzu_sb[:, dc * LCP + CONV - 1: dc * LCP + CONV - 1 + LC],
                ps_xz[par][:], AF.Identity, bias=c_sb[:, eblk:eblk + 1]),
                reads=[("ps_xz", par), ("c_sb",)], writes=[("xzu", dc)])
        else:
            dc = eblk - DC
            p.op("scalar", lambda s, dc=dc, par=par, cpar=cpar, eblk=eblk:
                 s.activation(
                    gt_sb[cpar][:, dc * LC:(dc + 1) * LC], ps_xz[par][:],
                    AF.Silu, bias=c_sb[:, eblk:eblk + 1]),
                 reads=[("ps_xz", par), ("c_sb",)], writes=[("gt", cpar, dc)])

    def xz_u(ch):
        for eblk in range(DC):
            xz_eblk(eblk, ch)

    def xz_z(ch):
        for eblk in range(DC, EB):
            xz_eblk(eblk, ch)

    def conv_silu(ch):
        for dc in range(DC):
            par = dc % 2
            for k in range(CONV):
                j = dc * CONV + k
                p.op("tensor", lambda t, j=j, dc=dc, k=k, par=par: t.matmul(
                    ps_conv[par][:],
                    diag_sb[:, j * 128:(j + 1) * 128],
                    xzu_sb[:, dc * LCP + k: dc * LCP + k + LC],
                    start=(k == 0), stop=False),
                    reads=[("diag", j), ("xzu", dc)],
                    writes=[("ps_conv", par)], accum=(k > 0))
            p.op("tensor", lambda t, dc=dc, par=par: t.matmul(
                ps_conv[par][:],
                convbr_sb[0:1, dc * 128:(dc + 1) * 128], ones_rb[:],
                start=False, stop=True),
                reads=[("convbr_sb",), ("ones_rb",)],
                writes=[("ps_conv", par)], accum=True)
            p.op("scalar", lambda s, dc=dc, par=par: s.activation(
                u_sb[:, dc * LC:(dc + 1) * LC], ps_conv[par][:], AF.Silu),
                reads=[("ps_conv", par)], writes=[("u", dc)])
        xv = xzu_sb[:].rearrange("p (dc l) -> p dc l", l=LCP)
        p.op("vector", lambda v, xv=xv: v.tensor_copy(
            xv[:, :, 0:CONV - 1], xv[:, :, LC:LC + CONV - 1]),
            reads=XZU, writes=XZU)

    def xdbl_dt(ch):
        par2 = ch % 2
        dt = dt_sb[par2]
        for dc in range(DC):
            p.op("tensor", lambda t, dc=dc: t.matmul(
                ps_small[0:RN, 0:LC],
                wx_sb[:, dc * RN:(dc + 1) * RN],
                u_sb[:, dc * LC:(dc + 1) * LC],
                start=(dc == 0), stop=(dc == DC - 1)),
                reads=[("wx_sb",), ("u", dc)],
                writes=[("ps_xdbl",)], accum=(dc > 0))
        p.op("scalar", lambda s: s.copy(dtr_sb, ps_small[0:R, 0:LC]),
             reads=[("ps_xdbl",)], writes=[("dtr",)])
        p.op("scalar", lambda s: s.copy(
            bc_sb[0:N, :], ps_small[R:R + N, 0:LC]),
            reads=[("ps_xdbl",)], writes=[("bc_b",)])
        p.op("scalar", lambda s: s.copy(
            bc_sb[32:32 + N, :], ps_small[96:96 + N, 0:LC]),
            reads=[("ps_xdbl",)], writes=[("bc_c",)])
        p.op("sync", lambda s, par2=par2: s.dma_start(
            out=bc_dram[par2][0:N], in_=bc_sb[0:N, :]),
            reads=[("bc_b",)], writes=[("bc_dram", par2)],
            dma=True, sem=f"d_bc{par2}")
        p.op("sync", lambda s, par2=par2: s.dma_start(
            out=bc_dram[par2][N:2 * N], in_=bc_sb[32:32 + N, :]),
            reads=[("bc_c",)], writes=[("bc_dram", par2)],
            dma=True, sem=f"d_bc{par2}")




    def dt_chain(ch, dcs):
        par2 = ch % 2
        dt = dt_sb[par2]
        dtslots = [ps_small[:, 256:256 + LC], ps_conv[0][:], ps_conv[1][:]]
        dtkeys = [("ps_dt",), ("ps_conv", 0), ("ps_conv", 1)]
        for dc in dcs:
            slot = dtslots[dc % 3]
            skey = dtkeys[dc % 3]
            wr = [skey] + ([("ps_xdbl",)] if dc == 0 else [])
            p.op("tensor", lambda t, dc=dc, slot=slot: t.matmul(
                slot, wdt_sb[:, dc * 128:(dc + 1) * 128], dtr_sb,
                start=True, stop=True),
                reads=[("wdt_sb",), ("dtr",)], writes=wr)
            p.op("scalar", lambda s, dc=dc, slot=slot: s.activation(
                e1_sb, slot, AF.Exp, bias=bdt_sb[:, dc:dc + 1]),
                reads=[skey, ("bdt_sb",)], writes=[("e1",)])
            p.op("scalar", lambda s, dc=dc, dt=dt: s.activation(
                dt[:, dc * LC:(dc + 1) * LC], e1_sb, AF.Ln,
                bias=onec_f[:, 0:1]),
                reads=[("e1",), ("onec_f",)], writes=[("dt", par2, dc)])

    def bcq_load(ch, q):
        par2 = ch % 2
        par = q % 2
        p.op("sync", lambda s, par2=par2, par=par, q=q: s.dma_start(
            out=bcq_sb[par][:, 0:4 * LC],
            in_=bc_dram[par2][4 * q:4 * q + 4].rearrange("r l -> (r l)")[
                None, :].broadcast_to([128, 4 * LC])),
            reads=[("bc_dram", par2)], writes=[("bcq", par)], dma=True,
            sem=f"d_bn{par}")
        p.op("sync", lambda s, par2=par2, par=par, q=q: s.dma_start(
            out=bcq_sb[par][:, 4 * LC:8 * LC],
            in_=bc_dram[par2][N + 4 * q:N + 4 * q + 4].rearrange(
                "r l -> (r l)")[None, :].broadcast_to([128, 4 * LC])),
            reads=[("bc_dram", par2)], writes=[("bcq", par)], dma=True,
            sem=f"d_bn{par}")

    def uw_y_init(ch):
        par2 = ch % 2
        p.op("vector", lambda v, par2=par2: v.tensor_tensor(
            uw_sb[:], dt_sb[par2][:], u_sb[:], ALU.mult),
            reads=DTK(par2) + U, writes=[("uw",)])
        for dc in range(DC):
            sl = slice(dc * LC, (dc + 1) * LC)
            p.op("scalar", lambda s, sl=sl, dc=dc: s.activation(
                y_sb[:, sl], u_sb[:, sl], AF.Copy,
                scale=dcol_sb[:, dc:dc + 1]),
                reads=[("u", dc), ("dcol_sb",)], writes=[("y", dc)])

    def scan_phase(ch):
        first = (ch == 0)
        av = a_sb[:].rearrange("p (dc l) -> p dc l", l=LC)
        bv = b_sb[:].rearrange("p (dc l) -> p dc l", l=LC)
        H = DC // 2
        nxt = ch + 1 < NCH
        usched = {}
        csched = {}
        for j in range(DC):
            usched.setdefault(2 + j // 2, []).append(j)
            csched.setdefault(3 + j // 2, []).append(j)
        tail_n = 3 + (DC - 1) // 2 + 1
        zsched = {}
        zslots = list(range(max(N - 6, tail_n), N))
        for i in range(DC):
            zsched.setdefault(zslots[i * len(zslots) // DC], []).append(DC + i)
        cpar = ch % 2
        if first:
            bcq_load(ch, 0)
        for n in range(N):
            for h in range(2):
                hdc = range(h * H, (h + 1) * H)
                hsl = slice(h * H * LC, (h + 1) * H * LC)
                HK = [("a", d) for d in hdc]
                HB = [("b", d) for d in hdc]
                for dc in hdc:
                    p.op("scalar", lambda s, n=n, dc=dc, cpar=cpar: s.activation(
                        a_sb[:, dc * LC:(dc + 1) * LC],
                        dt_sb[cpar][:, dc * LC:(dc + 1) * LC],
                        AF.Exp, scale=A_sb[:, dc * N + n:dc * N + n + 1]),
                        reads=[("dt", cpar, dc), ("A_sb",)], writes=[("a", dc)])
                p.op("vector", lambda v, n=n, hsl=hsl: v.tensor_tensor(
                    b_sb[:, hsl].rearrange("p (dc l) -> p dc l", l=LC),
                    uw_sb[:, hsl].rearrange("p (dc l) -> p dc l", l=LC),
                    bcq_sb[(n // 4) % 2][:, (n % 4) * LC:(n % 4 + 1) * LC][
                        :, None, :].broadcast_to([128, H, LC]),
                    ALU.mult),
                    reads=[("uw",), ("bcq", (n // 4) % 2)], writes=HB)
                avh = av[:, h * H:(h + 1) * H]
                bvh = bv[:, h * H:(h + 1) * H]
                if not first:
                    p.op("vector", lambda v, n=n, avh=avh, h=h: v.tensor_tensor(
                        fix_sb[:, 0:H], avh[:, :, 0],
                        hstate_sb[:, n * DC + h * H:n * DC + (h + 1) * H],
                        ALU.mult),
                        reads=HK + [("hstate",)], writes=[("fix",)])
                    p.op("vector", lambda v, bvh=bvh: v.tensor_tensor(
                        bvh[:, :, 0], bvh[:, :, 0], fix_sb[:, 0:H], ALU.add),
                        reads=HB + [("fix",)], writes=HB)
                p.op("vector", lambda v, avh=avh: v.memset(avh[:, :, 0], 0.0),
                     reads=HK, writes=HK)
                p.op("vector", lambda v, hsl=hsl: v.tensor_tensor_scan(
                    b_sb[:, hsl], a_sb[:, hsl], b_sb[:, hsl], 0.0,
                    ALU.mult, ALU.add),
                    reads=HK + HB, writes=HB)
                p.op("vector", lambda v, n=n, bvh=bvh, h=h: v.tensor_copy(
                    hstate_sb[:, n * DC + h * H:n * DC + (h + 1) * H],
                    bvh[:, :, LC - 1]),
                    reads=HB, writes=[("hstate",)])
                hk = ("tp", h)
                p.op("vector", lambda v, n=n, hsl=hsl, h=h: v.tensor_tensor(
                    tprod_sb[:, hsl].rearrange("p (dc l) -> p dc l", l=LC),
                    b_sb[:, hsl].rearrange("p (dc l) -> p dc l", l=LC),
                    bcq_sb[(n // 4) % 2][:, (4 + n % 4) * LC:(4 + n % 4 + 1) * LC][
                        :, None, :].broadcast_to([128, H, LC]),
                    ALU.mult),
                    reads=HB + [("bcq", (n // 4) % 2)], writes=[hk])
                p.op("vector", lambda v, hsl=hsl: v.tensor_tensor(
                    y_sb[:, hsl], y_sb[:, hsl], tprod_sb[:, hsl], ALU.add),
                    reads=[hk] + YK, writes=YK)
            if n % 4 == 1 and n // 4 + 1 < 4:
                bcq_load(ch, n // 4 + 1)
            if nxt:
                for j in usched.get(n, []):
                    xz_eblk(j, ch + 1)
                for j in csched.get(n, []):
                    conv_dc_next(j)
                if n == tail_n:
                    conv_tail_next()
                for j in zsched.get(n, []):
                    xz_eblk(j, ch + 1)
                if n == 8 and ch + 2 < NCH:
                    ln12_stats(ch + 2)
                if n == 9 and ch + 2 < NCH:
                    ln12_mid(ch + 2)
                if n == 11:
                    xdbl_dt(ch + 1)
                if 12 <= n <= 15:
                    dt_chain(ch + 1,
                             range(DC * (n - 12) // 4, DC * (n - 11) // 4))
                if n == 13:
                    bcq_load(ch + 1, 0)

    def conv_dc_next(dc):
        par = dc % 2
        for k in range(CONV):
            j = dc * CONV + k
            p.op("tensor", lambda t, j=j, dc=dc, k=k, par=par: t.matmul(
                ps_conv[par][:],
                diag_sb[:, j * 128:(j + 1) * 128],
                xzu_sb[:, dc * LCP + k: dc * LCP + k + LC],
                start=(k == 0), stop=False),
                reads=[("diag", j), ("xzu", dc)],
                writes=[("ps_conv", par)], accum=(k > 0))
        p.op("tensor", lambda t, dc=dc, par=par: t.matmul(
            ps_conv[par][:],
            convbr_sb[0:1, dc * 128:(dc + 1) * 128], ones_rb[:],
            start=False, stop=True),
            reads=[("convbr_sb",), ("ones_rb",)],
            writes=[("ps_conv", par)], accum=True)
        p.op("scalar", lambda s, dc=dc, par=par: s.activation(
            u_sb[:, dc * LC:(dc + 1) * LC], ps_conv[par][:], AF.Silu),
            reads=[("ps_conv", par)], writes=[("u", dc)])

    def conv_tail_next():
        xv = xzu_sb[:].rearrange("p (dc l) -> p dc l", l=LCP)
        p.op("vector", lambda v, xv=xv: v.tensor_copy(
            xv[:, :, 0:CONV - 1], xv[:, :, LC:LC + CONV - 1]),
            reads=XZU, writes=XZU)

    def gate_out(ch):
        cpar = ch % 2
        xv = xzu_sb[:].rearrange("p (dc l) -> p dc l", l=LCP)
        p.op("vector", lambda v, cpar=cpar, xv=xv: v.tensor_tensor(
            xv[:, :, CONV - 1:CONV - 1 + LC],
            y_sb[:].rearrange("p (dc l) -> p dc l", l=LC),
            gt_sb[cpar][:].rearrange("p (dc l) -> p dc l", l=LC),
            ALU.mult),
            reads=YK + [("gt", cpar, d) for d in range(DC)], writes=XZU)
        for km in range(DM):
            par = km % 2
            for dc in range(DC):
                p.op("tensor", lambda t, km=km, dc=dc, par=par: t.matmul(
                    ps_out[par][:],
                    wout_sb[:, dc * D_MODEL + km * 128:
                            dc * D_MODEL + (km + 1) * 128],
                    xzu_sb[:, dc * LCP + CONV - 1: dc * LCP + CONV - 1 + LC],
                    start=(dc == 0), stop=(dc == DC - 1)),
                    reads=[("wout_sb", dc * 4 // DC), ("xzu", dc)],
                    writes=[("ps_o",)], accum=(dc > 0))
            ks = km % 2
            if ch < HCH:
                dst = outst_sb[:, ks * LC:(ks + 1) * LC]
            else:
                dst = outst_sb[:, ks * LC:(ks + 1) * LC][:, ::-1]
            p.op("scalar", lambda s, dst=dst, par=par: s.copy(
                dst, ps_out[par][:]),
                reads=[("ps_o",)], writes=[("outst", ks)])
            if ks == 1:
                kg = km - 1
                if ch < HCH:
                    dst2 = yown_dram.rearrange("(km p) l -> p km l", p=128)[
                        :, kg:kg + 2, ch * LC:(ch + 1) * LC]
                    rg = ("yown_dram", ch)
                else:
                    cc = NCH - 1 - ch
                    base = cc * LC
                    hi = base // (HALF // 2)
                    off = base % (HALF // 2)
                    dst2 = send_dram[hi].rearrange("(km p) l -> p km l", p=128)[
                        :, kg:kg + 2, off:off + LC]
                    rg = ("send_dram", cc)
                p.op("sync", lambda s, dst2=dst2: s.dma_start(
                    out=dst2,
                    in_=outst_sb[:].rearrange("p (km l) -> p km l", l=LC)),
                    reads=[("outst", 0), ("outst", 1)],
                    writes=[rg], dma=True, sem="d_ow")

    def collective():
        groups = [[2 * i, 2 * i + 1] for i in range(NCORES // 2)]
        halves = [(0, HALF // 2), (HALF // 2, HALF)]
        for hi, (c0, c1) in enumerate(halves):
            chs = [c for c in range(HCH)
                   if c0 <= (HCH - 1 - c) * LC < c1 or c0 <= (HCH - 1 - c) * LC + LC - 1 < c1]
            p.op("gpsimd", lambda g, hi=hi: g.collective_compute(
                "AllReduce", ALU.add, replica_groups=groups,
                ins=[send_dram[hi].opt()],
                outs=[ar_dram[hi].opt()]),
                reads=[("send_dram", c) for c in range(HCH)],
                writes=[("ar_dram", hi)], sem="cc" if hi == 0 else "cc1",
                inc=1)

    def post_chunk(pc):
        par = pc % 2
        xt = xres_sb[:]
        colsl = slice(pc * LC, (pc + 1) * LC)
        p.op("sync", lambda s: s.dma_start(
            out=xt.rearrange("p (km l) -> p km l", l=LC),
            in_=x_res.rearrange("(km p) l -> p km l", p=128)[:, :, colsl]),
            writes=[("xres",)], dma=True, sem=f"d_x{par}")
        p.op("sync", lambda s: s.dma_start(
            out=b_sb[:, 0:DM * LC].rearrange("p (km l) -> p km l", l=LC),
            in_=yown_dram.rearrange("(km p) l -> p km l", p=128)[:, :, colsl]),
            reads=[("yown_dram", pc)], writes=BK, dma=True, sem=f"d_pl{par}")
        arh = 0 if (pc + 1) * LC <= HALF // 2 else 1
        poff = pc * LC - arh * (HALF // 2)
        p.op("sync", lambda s: s.dma_start(
            out=uw_sb[:, 0:DM * LC].rearrange("p (km l) -> p km l", l=LC),
            in_=send_dram[arh].rearrange("(km p) l -> p km l", p=128)[
                :, :, poff:poff + LC]),
            reads=[("send_dram", pc)], writes=[("uw",)], dma=True,
            sem=f"d_pl{par}")
        p.op("sync", lambda s: s.dma_start(
            out=gt_sb[0][:, 0:DM * LC].rearrange("p (km l) -> p km l", l=LC),
            in_=ar_dram[arh].rearrange("(km p) l -> p km l", p=128)[
                :, :, poff:poff + LC]),
            reads=[("ar_dram", arh)], writes=GT, dma=True, sem=f"d_pl{par}")
        p.op("vector", lambda v: v.tensor_tensor(
            a_sb[:, 0:DM * LC], gt_sb[0][:, 0:DM * LC], uw_sb[:, 0:DM * LC],
            ALU.subtract),
            reads=GT + [("uw",)], writes=AK)
        p.op("vector", lambda v: v.tensor_tensor(
            dt_sb[0][:, 0:DM * LC], a_sb[:, 0:DM * LC], b_sb[:, 0:DM * LC],
            ALU.add),
            reads=AK + BK, writes=DTK(0))

        ln_stats_apply(xt, ("xres",), None, combine12=False)
        for km in range(DM):
            sl = slice(km * LC, (km + 1) * LC)
            ts = slice((km % 2) * LC, (km % 2 + 1) * LC)
            p.op("vector", lambda v, sl=sl, ts=ts: v.tensor_tensor(
                ys_sb[:, ts], xt[:, sl], ps_rep[:, 0:LC], ALU.mult),
                reads=[("xres",), ("ps_rep",)], writes=[("ys", km % 2)])
            p.op("vector", lambda v, ts=ts: v.tensor_tensor(
                ys_sb[:, ts], ys_sb[:, ts], ps_rep[:, 256:256 + LC],
                ALU.subtract),
                reads=[("ys", km % 2), ("ps_rep",)],
                writes=[("ys", km % 2)])
            p.op("vector", lambda v, sl=sl, ts=ts: v.tensor_tensor(
                ys_sb[:, ts], ys_sb[:, ts], xt[:, sl], ALU.add),
                reads=[("ys", km % 2), ("xres",)],
                writes=[("ys", km % 2)])
            p.op("vector", lambda v, sl=sl, ts=ts: v.tensor_tensor(
                x1_sb[:, sl], ys_sb[:, ts], dt_sb[0][:, sl], ALU.add),
                reads=[("ys", km % 2), ("dt", 0, sl.start // LC)],
                writes=[("x1", sl.start // LC)])

        ln_stats_apply(x1_sb[:], X1, None, combine12=False)
        for km in range(DM):
            sl = slice(km * LC, (km + 1) * LC)
            ts = slice((km % 2) * LC, (km % 2 + 1) * LC)
            p.op("vector", lambda v, sl=sl, ts=ts: v.tensor_tensor(
                ys_sb[:, ts], x1_sb[:, sl], ps_rep[:, 0:LC], ALU.mult),
                reads=[("x1", km), ("ps_rep",)], writes=[("ys", km % 2)])
            p.op("vector", lambda v, sl=sl, ts=ts, km=km: v.tensor_tensor(
                xn_sb[0][:, sl], ys_sb[:, ts], ps_rep[:, 256:256 + LC],
                ALU.subtract),
                reads=[("ys", km % 2), ("ps_rep",)], writes=[("xn", 0, km)])

        for km in range(DM):
            wr = [("ps_ffs",)] + ([("ps_xdbl",), ("ps_dt",)] if km == 0 else [])
            p.op("tensor", lambda t, km=km: t.matmul(
                ps_small[0:M, 0:LC], winf_sb[:, km * M:(km + 1) * M],
                xn_sb[0][:, km * LC:(km + 1) * LC],
                start=(km == 0), stop=(km == DM - 1)),
                reads=[("winf_sb",), ("xn", 0, km)],
                writes=wr, accum=(km > 0))
        for i in range(INNER):
            wr = [("hff", i)] + (AK if (pc == 0 and i == 0) else [])
            p.op("scalar", lambda s, i=i: s.activation(
                hff_sb[0:M, i * LC:(i + 1) * LC], ps_small[0:M, 0:LC], AF.Relu,
                bias=bias1_sb[:, i:i + 1], scale=w1_sb[:, i:i + 1]),
                reads=[("ps_ffs",), ("bias1_sb",), ("w1_sb",)],
                writes=wr)
        p.op("vector", lambda v: v.tensor_scalar(
            gacc_sb[0][:], hff_sb[0:M, 0:LC], w2_sb[:, 0:1], b2_sb[:, 0:1],
            ALU.mult, ALU.add),
            reads=[("hff", 0), ("w2_sb",), ("b2_sb",)], writes=[("gacc", 0)])
        for i in range(1, INNER):
            sp, dp = (i - 1) % 2, i % 2
            p.op("vector", lambda v, i=i, sp=sp, dp=dp: v.scalar_tensor_tensor(
                gacc_sb[dp][:], hff_sb[0:M, i * LC:(i + 1) * LC],
                w2_sb[:, i:i + 1], gacc_sb[sp][:], ALU.mult, ALU.add),
                reads=[("hff", i), ("w2_sb",), ("gacc", sp)],
                writes=[("gacc", dp)])
        gfin = gacc_sb[(INNER - 1) % 2]
        gfin_rg = ("gacc", (INNER - 1) % 2)
        for km in range(DM):
            par2 = km % 2
            p.op("tensor", lambda t, km=km, par2=par2: t.matmul(
                ps_out[par2][:], woutf_sb[:, km * 128:(km + 1) * 128],
                gfin[:], start=True, stop=True),
                reads=[("woutf_sb",), gfin_rg], writes=[("ps_o",)])
            p.op("vector", lambda v, km=km, par2=par2: v.scalar_tensor_tensor(
                xres_sb[:, km * LC:(km + 1) * LC], ps_out[par2][:],
                bout_sb[:, km:km + 1], x1_sb[:, km * LC:(km + 1) * LC],
                ALU.add, ALU.add),
                reads=[("ps_o",), ("bout_sb",), ("x1", km)],
                writes=[("xres",)])
        p.op("sync", lambda s: s.dma_start(
            out=out_d.rearrange("(km p) l -> p km l", p=128)[:, :, colsl],
            in_=xres_sb[:].rearrange("p (km l) -> p km l", l=LC)),
            reads=[("xres",)], writes=[("out_d", pc)], dma=True, sem="d_od")

    startup()
    x_load(0)
    ln12_stats(0)
    ln12_mid(0)
    ln12_apply(0)
    x_load(1)
    xz_u(0)
    conv_silu(0)
    xz_z(0)
    xdbl_dt(0)
    dt_chain(0, range(DC))
    if NCH > 1:
        ln12_stats(1)
        ln12_mid(1)
    for k in range(NCH):
        if k > 0:
            gate_out(k - 1)
        if k + 1 < NCH:
            ln12_apply(k + 1)
        uw_y_init(k)
        if k + 2 < NCH:
            x_load(k + 2)
        scan_phase(k)
    gate_out(NCH - 1)
    collective()
    for pc in range(HCH):
        post_chunk(pc)

    for snm in ["d_st", "d_x0", "d_x1", "d_w0", "d_w1", "d_bc0", "d_bc1",
                "d_ow", "d_pl0", "d_pl1", "d_od", "d_bn0", "d_bn1"]:
        p.q["sync"].append(("wait", snm, p.cnt.get(snm, 0)))
    p.q["gpsimd"].append(("wait", "d_g", p.cnt.get("d_g", 0)))
    p.q["gpsimd"].append(("wait", "d_gx0", p.cnt.get("d_gx0", 0)))
    p.q["gpsimd"].append(("wait", "d_gx1", p.cnt.get("d_gx1", 0)))
    p.q["gpsimd"].append(("wait", "cc", p.cnt.get("cc", 0)))
    p.q["gpsimd"].append(("wait", "cc1", p.cnt.get("cc1", 0)))

    with nc.Block() as block:
        p.finalize(block, sems)
    stack.close()
    return nc


def _prep_core_inputs(cfg, x_b, flip, w, ffn):
    DM, DC, L = cfg["DM"], cfg["DC"], cfg["L"]
    N, R, CONV, M = cfg["N"], cfg["R"], cfg["CONV"], cfg["M"]
    D_MODEL, D_INNER = DM * 128, DC * 128
    EB = 2 * DC
    HALF = L // 2
    C = np.ascontiguousarray

    xd = x_b[::-1] if flip else x_b
    ins = {}
    ins["xT"] = C(xd.T.astype(np.float32))
    ins["x_res"] = C(xd[:HALF].T.astype(np.float32))
    win = np.asarray(w["Win"], dtype=np.float32)
    ins["win_t4"] = C(win.reshape(EB, 128, DM, 128).transpose(0, 3, 2, 1))
    ins["wout_t"] = C(np.asarray(w["Wout"], np.float32).T)
    wxT = np.asarray(w["Wx"], np.float32).T
    wxp = np.zeros((wxT.shape[0], 128), np.float32)
    wxp[:, 0:R] = wxT[:, 0:R]
    wxp[:, R:R + N] = wxT[:, R:R + N]
    wxp[:, 96:96 + N] = wxT[:, R + N:R + 2 * N]
    ins["wx_t"] = C(wxp)
    ins["wdt_t"] = C(np.asarray(w["Wdt"], np.float32).T)
    ins["alog_c"] = C(np.asarray(w["Alog"], np.float32)
                      .reshape(DC, 128, N).transpose(1, 0, 2).reshape(128, DC * N))
    ins["dcol"] = C(np.asarray(w["D"], np.float32).reshape(DC, 128).T)
    ins["bdt_c"] = C(np.asarray(w["bdt"], np.float32).reshape(DC, 128).T)
    ins["convw_c"] = C(np.asarray(w["convw"], np.float32)[:, 0, :]
                       .reshape(DC, 128, CONV).transpose(1, 0, 2)
                       .reshape(128, DC * CONV))
    ins["convb_r"] = C(np.asarray(w["convb"], np.float32)[None, :])
    ins["gcol"] = C(np.asarray(w["norm_g"], np.float32).reshape(DM, 128).T)
    ins["bcol"] = C(np.asarray(w["norm_b"], np.float32).reshape(DM, 128).T)
    ins["winf"] = C(np.asarray(ffn["Win"], np.float32))
    ins["woutf"] = C(np.asarray(ffn["Wout"], np.float32))
    ins["w1"] = C(np.asarray(ffn["W1"], np.float32))
    ins["b1"] = C(np.asarray(ffn["b1"], np.float32))
    ins["w2"] = C(np.asarray(ffn["W2"], np.float32))
    ins["b2c"] = C(np.asarray(ffn["b2"], np.float32)[:, None])
    ins["binc"] = C(np.asarray(ffn["bin_"], np.float32)[:, None])
    ins["bout_c"] = C(np.asarray(ffn["bout"], np.float32).reshape(DM, 128).T)
    ins["eye16"] = np.eye(128, dtype=ml_dtypes.bfloat16)
    return ins


def _assemble(cfg, shards):
    DM, L = cfg["DM"], cfg["L"]
    D_MODEL, HALF = DM * 128, L // 2
    B = len(shards) // 2
    out = np.empty((B, L, D_MODEL), dtype=np.float32)
    for b in range(B):
        out[b, :HALF] = shards[2 * b].T
        out[b, HALF:] = shards[2 * b + 1].T[::-1]
    return out


MAMBA_KEYS = ["Win", "convw", "convb", "Wx", "Wdt", "bdt", "Alog", "D", "Wout"]


def kernel(**inputs):
    cfg = FULL_CFG
    x = np.asarray(inputs["L0_attn_x"], dtype=np.float32)
    Bn = x.shape[0]
    wf = {k: np.asarray(inputs["L0_attn_f_" + k]) for k in MAMBA_KEYS}
    wb = {k: np.asarray(inputs["L0_attn_b_" + k]) for k in MAMBA_KEYS}
    for d in (wf, wb):
        d["norm_g"] = np.asarray(inputs["L0_attn_norm_g"])
        d["norm_b"] = np.asarray(inputs["L0_attn_norm_b"])
    ffn = {k: np.asarray(inputs["L0_ffn_" + k]) for k in
           ["Win", "bin_", "W1", "b1", "W2", "b2", "Wout", "bout"]}

    nc = build_core_program(cfg)
    in_maps = []
    for b in range(Bn):
        in_maps.append(_prep_core_inputs(cfg, x[b], False, wf, ffn))
        in_maps.append(_prep_core_inputs(cfg, x[b], True, wb, ffn))
    res = run_bass_kernel_spmd(nc, in_maps, core_ids=list(range(2 * Bn)))
    shards = [np.asarray(r["out"]) for r in res.results]
    return _assemble(cfg, shards)


# version 26
# speedup vs baseline: 1.0441x; 1.0362x over previous
import numpy as np
import ml_dtypes
from contextlib import ExitStack

import concourse.bass as bass
import concourse.mybir as mybir
from concourse.bass_utils import run_bass_kernel_spmd

F32 = mybir.dt.float32
BF16 = mybir.dt.bfloat16
AF = mybir.ActivationFunctionType
ALU = mybir.AluOpType

EPS = 1e-5

FULL_CFG = dict(DM=8, DC=16, L=4096, LC=256, N=16, R=64, CONV=4, M=64,
                INNER=16, NCORES=8)


class Prog:
    ENGINES = ("tensor", "vector", "scalar", "gpsimd", "sync")

    @staticmethod
    def is_dma_sem(sem):
        return sem.startswith("d_") or sem.startswith("cc")

    def __init__(self, nc):
        self.nc = nc
        self.q = {e: [] for e in self.ENGINES}
        self.cnt = {}
        self.state = {}
        self.waited = {}
        self.opidx = {e: 0 for e in self.ENGINES}
        self.last_drain = {e: -1 for e in self.ENGINES}

    def _need(self, eng, stamp, drains):
        sem, val, peng, pidx = stamp
        if self.is_dma_sem(sem):
            val = self.cnt.get(sem, 0)
        k = (eng, sem)
        if self.waited.get(k, 0) < val:
            self.waited[k] = val
            self.q[eng].append(("wait", sem, val))

    def op(self, eng, fn, reads=(), writes=(), dma=False, sem=None, inc=None,
           accum=False):
        drains = []
        for r in reads:
            st = self.state.get(r)
            if st and st["w"]:
                self._need(eng, st["w"], drains)
        if not accum:
            for r in writes:
                st = self.state.get(r)
                if st:
                    if st["w"]:
                        sem_, val_, peng_, pidx_ = st["w"]
                        if peng_ != eng or self.is_dma_sem(sem_):
                            self._need(eng, st["w"], drains)
                    for stamp in st["r"].values():
                        self._need(eng, stamp, drains)
        if drains and max(drains) > self.last_drain[eng]:
            self.q[eng].append(("drain",))
            self.last_drain[eng] = self.opidx[eng]
        if sem is None:
            assert not dma, "DMA ops must name their semaphore"
            sem = eng
        if inc is None:
            inc = 16 if dma else 1
        self.cnt[sem] = self.cnt.get(sem, 0) + inc
        val = self.cnt[sem]
        self.opidx[eng] += 1
        stamp = (sem, val, eng, self.opidx[eng])
        self.q[eng].append(("op", fn, sem, inc, val))
        for r in writes:
            self.state[r] = {"w": stamp, "r": {}}
        for r in reads:
            st = self.state.setdefault(r, {"w": None, "r": {}})
            st["r"][eng] = stamp
        return stamp

    def finalize(self, block, sems):
        waited = {}
        for ename in self.ENGINES:
            for it in self.q[ename]:
                if it[0] == "wait" and not self.is_dma_sem(it[1]):
                    waited.setdefault(it[1], set()).add(it[2])
        remap = {}
        for sem, vals in waited.items():
            remap[sem] = {v: i + 1 for i, v in enumerate(sorted(vals))}

        for ename in self.ENGINES:
            items = self.q[ename]

            def make_body(items):
                def body(engine):
                    for it in items:
                        if it[0] == "wait":
                            sem, val = it[1], it[2]
                            if sem in remap:
                                val = remap[sem][val]
                            engine.wait_ge(sems[sem], val)
                        elif it[0] == "drain":
                            engine.drain()
                        else:
                            _, fn, sem, inc, val = it
                            inst = fn(engine)
                            if self.is_dma_sem(sem):
                                inst.then_inc(sems[sem], inc)
                            elif sem in remap and val in remap[sem]:
                                inst.then_inc(sems[sem], inc)
                return body
            getattr(block, ename)(make_body(items))


def build_core_program(cfg):
    DM, DC, L, LC = cfg["DM"], cfg["DC"], cfg["L"], cfg["LC"]
    N, R, CONV = cfg["N"], cfg["R"], cfg["CONV"]
    M, INNER = cfg["M"], cfg["INNER"]
    NCORES = cfg["NCORES"]
    D_MODEL = DM * 128
    D_INNER = DC * 128
    NCH = L // LC
    HCH = NCH // 2
    HALF = L // 2
    EB = 2 * DC
    LCP = LC + CONV - 1
    RN = 128

    nc = bass.Bass("TRN2", target_bir_lowering=False)

    def din(name, shape, dt=F32):
        return nc.dram_tensor(name, list(shape), dt, kind="ExternalInput").ap()

    xT = din("xT", [D_MODEL, L])
    x_res = din("x_res", [D_MODEL, HALF])
    win_t4 = din("win_t4", [EB, 128, DM, 128])
    wout_t = din("wout_t", [D_INNER, D_MODEL])
    wx_t = din("wx_t", [D_INNER, RN])
    wdt_t = din("wdt_t", [R, D_INNER])
    alog_c = din("alog_c", [128, DC * N])
    dcol = din("dcol", [128, DC])
    bdt_c = din("bdt_c", [128, DC])
    convw_c = din("convw_c", [128, DC * CONV])
    convb_r = din("convb_r", [1, D_INNER])
    gcol = din("gcol", [128, DM])
    bcol = din("bcol", [128, DM])
    winf = din("winf", [D_MODEL, M])
    woutf = din("woutf", [M, D_MODEL])
    w1 = din("w1", [M, INNER])
    b1 = din("b1", [M, INNER])
    w2 = din("w2", [M, INNER])
    b2c = din("b2c", [M, 1])
    binc = din("binc", [M, 1])
    bout_c = din("bout_c", [128, DM])
    eye16 = din("eye16", [128, 128], BF16)

    out_d = nc.dram_tensor("out", [D_MODEL, HALF], F32, kind="ExternalOutput").ap()

    win_bf = nc.dram_tensor("win_bf", [EB, 128, DM * 128], BF16).ap()
    bc_dram = [nc.dram_tensor(f"bc_dram{i}", [2 * N, LC], BF16).ap()
               for i in range(2)]
    yown_dram = nc.dram_tensor("yown_dram", [D_MODEL, HALF], BF16).ap()
    send_dram = nc.dram_tensor("send_dram", [2, D_MODEL, HALF // 2], BF16).ap()
    ar_dram = nc.dram_tensor("ar_dram", [2, D_MODEL, HALF // 2], BF16).ap()

    stack = ExitStack()

    def sbuf(name, shape, dt):
        return stack.enter_context(nc.sbuf_tensor(name, list(shape), dt))

    def psum(name, shape, dt=F32):
        return stack.enter_context(nc.psum_tensor(name, list(shape), dt))

    wout_sb = sbuf("wout_sb", [128, DC * D_MODEL], BF16)
    wx_sb = sbuf("wx_sb", [128, DC * RN], BF16)
    wdt_sb = sbuf("wdt_sb", [R, D_INNER], BF16)
    winf_sb = sbuf("winf_sb", [128, DM * M], BF16)
    woutf_sb = sbuf("woutf_sb", [M, D_MODEL], BF16)
    diag_sb = sbuf("diag_sb", [128, DC * CONV * 128], BF16)
    eye_sb = sbuf("eye_sb", [128, 128], BF16)
    A_sb = sbuf("A_sb", [128, DC * N], F32)
    dcol_sb = sbuf("dcol_sb", [128, DC], F32)
    bdt_sb = sbuf("bdt_sb", [128, DC], F32)
    convw_sb = sbuf("convw_sb", [128, DC * CONV], F32)
    convbr_sb = sbuf("convbr_sb", [1, D_INNER], BF16)
    gcol_sb = sbuf("gcol_sb", [128, DM], F32)
    bcol_sb = sbuf("bcol_sb", [128, DM], F32)
    w1_sb = sbuf("w1_sb", [M, INNER], F32)
    b1_sb = sbuf("b1_sb", [M, INNER], F32)
    w2_sb = sbuf("w2_sb", [M, INNER], F32)
    bias1_sb = sbuf("bias1_sb", [M, INNER], F32)
    b2_sb = sbuf("b2_sb", [M, 1], F32)
    binc_sb = sbuf("binc_sb", [M, 1], F32)
    bout_sb = sbuf("bout_sb", [128, DM], F32)
    ones_rf = sbuf("ones_rf", [1, 128], F32)
    cvals_sb = sbuf("cvals_sb", [1, 4], F32)
    half_c = sbuf("half_c", [128, 1], F32)
    ones_rb = sbuf("ones_rb", [1, LC], BF16)
    onec_f = sbuf("onec_f", [128, 1], F32)
    onec_b = sbuf("onec_b", [128, 1], BF16)
    c_sb = sbuf("c_sb", [128, EB], F32)
    bcolb_sb = sbuf("bcolb_sb", [128, DM], BF16)

    wbuf = [sbuf(f"wbuf{i}", [128, DM * 128], BF16) for i in range(2)]
    xt_sb = [sbuf(f"xt_sb{i}", [128, DM * LC], BF16) for i in range(2)]
    xn_sb = [sbuf(f"xn_sb{i}", [128, DM * LC], BF16) for i in range(2)]
    xzu_sb = sbuf("xzu_sb", [128, DC * LCP], BF16)
    u_sb = sbuf("u_sb", [128, DC * LC], BF16)
    gt_sb = [sbuf(f"gt_sb{i}", [128, DC * LC], BF16) for i in range(2)]
    dt_sb = [sbuf(f"dt_sb{i}", [128, DC * LC], BF16) for i in range(2)]
    uw_sb = sbuf("uw_sb", [128, DC * LC], BF16)
    a_sb = sbuf("a_sb", [128, DC * LC], BF16)
    b_sb = sbuf("b_sb", [128, DC * LC], BF16)
    y_sb = sbuf("y_sb", [128, DC * LC], BF16)
    tprod_sb = sbuf("tprod_sb", [128, DC * LC], BF16)
    bc_sb = sbuf("bc_sb", [64, LC], BF16)
    bcq_sb = [sbuf(f"bcq_sb{i}", [128, 8 * LC], BF16) for i in range(2)]
    hstate_sb = sbuf("hstate_sb", [128, N * DC], BF16)
    fix_sb = sbuf("fix_sb", [128, DC], BF16)
    outst_sb = sbuf("outst_sb", [128, 2 * LC], BF16)
    stat_sb = sbuf("stat_sb", [1, 2 * LC], F32)
    gacc_sb = [sbuf(f"gacc_sb{i}", [M, LC], BF16) for i in range(2)]
    x1_sb = sbuf("x1_sb", [128, DM * LC], F32)
    xres_sb = sbuf("xres_sb", [128, DM * LC], F32)
    ys_sb = sbuf("ys_sb", [128, 2 * LC], F32)
    s2row_sb = sbuf("s2row_sb", [1, LC], F32)
    e1t_sb = sbuf("e1t_sb", [128, LC], BF16)
    dtrt_sb = sbuf("dtrt_sb", [R, LC], BF16)
    e1_sb = e1t_sb[:, 0:LC]
    dtr_sb = dtrt_sb[0:R, 0:LC]
    if INNER * LC <= DC * LC:
        hff_sb = a_sb
    else:
        hff_sb = sbuf("hff_sb", [M, INNER * LC], BF16)

    ps_rep = psum("ps_rep", [128, 512])
    ps_st = psum("ps_st", [1, 512])
    ps_sum = ps_st[0:1, 0:256]
    ps_sq = ps_st[0:1, 256:512]
    ps_xz = [psum(f"ps_xz{i}", [128, LC]) for i in range(2)]
    ps_conv = [psum(f"ps_conv{i}", [128, LC]) for i in range(2)]
    ps_small = psum("ps_small", [128, 512])
    ps_outb = [psum("ps_outb", [128, LC])]
    ps_out = [ps_outb[0], ps_outb[0]]

    assert LC <= 256

    sem_names = ["tensor", "vector", "scalar", "gpsimd", "sync",
                 "d_st", "d_x0", "d_x1", "d_w0", "d_w1", "d_bc0", "d_bc1",
                 "d_ow", "d_pl0", "d_pl1", "d_od", "d_g", "d_gx0", "d_gx1",
                 "d_bn0", "d_bn1", "cc", "cc1"]
    sems = {nm: stack.enter_context(nc.semaphore(nm)) for nm in sem_names}

    p = Prog(nc)
    W = {"xt": lambda i: ("xt", i), "xn": lambda k: ("xn", k)}

    XN = [("xn", k) for k in range(DM)]
    U = [("u", d) for d in range(DC)]
    AK = [("a", d) for d in range(DC)]
    BK = [("b", d) for d in range(DC)]
    YK = [("y", d) for d in range(DC)]
    def DTK(par):
        return [("dt", par, d) for d in range(DC)]
    GT = [("gt", 0, d) for d in range(DC)]
    XZU = [("xzu", d) for d in range(DC)]
    X1 = [("x1", k) for k in range(DM)]
    YS = [("ys", k) for k in range(DM)]
    OUTST = [("outst", k) for k in range(DM)]

    def startup():
        for eblk in range(EB):
            p.op("gpsimd", lambda g, eblk=eblk: g.dma_start(
                out=win_bf[eblk], in_=win_t4[eblk]),
                writes=[("win_bf", eblk)], dma=True, sem="d_g")
        for j in range(4):
            d0 = DC // 4 * j
            d1 = DC // 4 * (j + 1)
            p.op("gpsimd", lambda g, d0=d0, d1=d1: g.dma_start(
                out=wout_sb[:, d0 * D_MODEL:d1 * D_MODEL].rearrange(
                    "p (dc d) -> p dc d", d=D_MODEL),
                in_=wout_t.rearrange("(dc p) d -> p dc d", p=128)[:, d0:d1]),
                writes=[("wout_sb", j)], dma=True, sem="d_g")
        p.op("gpsimd", lambda g: g.dma_start(
            out=wx_sb[:].rearrange("p (dc r) -> p dc r", r=RN),
            in_=wx_t.rearrange("(dc p) r -> p dc r", p=128)),
            writes=[("wx_sb",)], dma=True, sem="d_g")
        p.op("gpsimd", lambda g: g.dma_start(out=wdt_sb[:], in_=wdt_t),
             writes=[("wdt_sb",)], dma=True, sem="d_g")
        p.op("gpsimd", lambda g: g.dma_start(
            out=winf_sb[:].rearrange("p (km m) -> p km m", m=M),
            in_=winf.rearrange("(km p) m -> p km m", p=128)),
            writes=[("winf_sb",)], dma=True, sem="d_g")
        p.op("gpsimd", lambda g: g.dma_start(out=woutf_sb[:], in_=woutf),
             writes=[("woutf_sb",)], dma=True, sem="d_g")
        p.op("gpsimd", lambda g: g.dma_start(out=convbr_sb[:], in_=convb_r),
             writes=[("convbr_sb",)], dma=True, sem="d_g")
        loads = [(A_sb, alog_c, "A_sb"), (dcol_sb, dcol, "dcol_sb"),
                 (bdt_sb, bdt_c, "bdt_sb"), (convw_sb, convw_c, "convw_sb"),
                 (gcol_sb, gcol, "gcol_sb"), (bcol_sb, bcol, "bcol_sb"),
                 (w1_sb, w1, "w1_sb"), (b1_sb, b1, "b1_sb"),
                 (w2_sb, w2, "w2_sb"), (b2_sb, b2c, "b2_sb"),
                 (binc_sb, binc, "binc_sb"), (bout_sb, bout_c, "bout_sb"),
                 (eye_sb, eye16, "eye_sb")]
        for t, src, rg in loads:
            p.op("sync", lambda s, t=t, src=src: s.dma_start(out=t[:], in_=src),
                 writes=[(rg,)], dma=True, sem="d_st")
        p.op("vector", lambda v: v.memset(ones_rf[:], 1.0), writes=[("ones_rf",)])
        p.op("vector", lambda v: v.memset(cvals_sb[0:1, 0:1], EPS * EPS),
             writes=[("cv", 0)])
        p.op("vector", lambda v: v.memset(cvals_sb[0:1, 1:2], 1.0 + EPS),
             writes=[("cv", 1)])
        p.op("vector", lambda v: v.memset(cvals_sb[0:1, 2:3], EPS),
             writes=[("cv", 2)])
        p.op("vector", lambda v: v.memset(cvals_sb[0:1, 3:4], -0.5),
             writes=[("cv", 3)])
        p.op("vector", lambda v: v.memset(half_c[:], 0.5), writes=[("half_c",)])
        p.op("vector", lambda v: v.memset(ones_rb[:], 1.0), writes=[("ones_rb",)])
        p.op("vector", lambda v: v.memset(onec_f[:], 1.0), writes=[("onec_f",)])
        p.op("vector", lambda v: v.memset(onec_b[:], 1.0), writes=[("onec_b",)])
        p.op("vector", lambda v: v.memset(hstate_sb[:], 0.0), writes=[("hstate",)])
        p.op("vector", lambda v: v.memset(xzu_sb[:], 0.0), writes=XZU)
        p.op("scalar", lambda s: s.activation(A_sb[:], A_sb[:], AF.Exp),
             reads=[("A_sb",)], writes=[("A_sb",)])
        p.op("vector", lambda v: v.tensor_scalar(
            A_sb[:], A_sb[:], -1.0, None, ALU.mult),
            reads=[("A_sb",)], writes=[("A_sb",)])
        for dc in range(DC):
            for k in range(CONV):
                j = dc * CONV + k
                p.op("vector", lambda v, j=j: v.tensor_scalar(
                    diag_sb[:, j * 128:(j + 1) * 128], eye_sb[:],
                    convw_sb[:, j:j + 1], None, ALU.mult),
                    reads=[("eye_sb",), ("convw_sb",)], writes=[("diag", j)])
        p.op("vector", lambda v: v.tensor_copy(bcolb_sb[:], bcol_sb[:]),
             reads=[("bcol_sb",)], writes=[("bcolb",)])
        WB = DM * 128
        nub = WB // LC
        stg = [u_sb[:, 0:WB], u_sb[:, WB:2 * WB]]
        stgk = [[("u", d) for d in range(nub)],
                [("u", d) for d in range(nub, 2 * nub)]]
        for eblk in range(EB):
            bf = eblk % 2
            p.op("sync", lambda s, eblk=eblk, bf=bf: s.dma_start(
                out=stg[bf][:], in_=win_bf[eblk]),
                reads=[("win_bf", eblk)], writes=stgk[bf], dma=True,
                sem=f"d_w{bf}")
            for k in range(DM):
                p.op("tensor", lambda t, k=k, bf=bf: t.matmul(
                    ps_rep[:, bf:bf + 1], stg[bf][:, k * 128:(k + 1) * 128],
                    bcolb_sb[:, k:k + 1],
                    start=(k == 0), stop=(k == DM - 1)),
                    reads=stgk[bf] + [("bcolb",)],
                    writes=[("ps_rep",)], accum=(k > 0))
            p.op("scalar", lambda s, eblk=eblk, bf=bf: s.copy(
                c_sb[:, eblk:eblk + 1], ps_rep[:, bf:bf + 1]),
                reads=[("ps_rep",)], writes=[("c_sb",)])
            for k in range(DM):
                p.op("vector", lambda v, k=k, bf=bf: v.tensor_scalar(
                    stg[bf][:, k * 128:(k + 1) * 128],
                    stg[bf][:, k * 128:(k + 1) * 128],
                    gcol_sb[:, k:k + 1], None, ALU.mult),
                    reads=stgk[bf] + [("gcol_sb",)], writes=stgk[bf])
            p.op("sync", lambda s, eblk=eblk, bf=bf: s.dma_start(
                out=win_bf[eblk], in_=stg[bf][:]),
                reads=stgk[bf], writes=[("win_bf", eblk)], dma=True,
                sem=f"d_w{bf}")
        p.op("vector", lambda v: v.tensor_scalar(
            bias1_sb[:], w1_sb[:], binc_sb[:, 0:1], None, ALU.mult),
            reads=[("w1_sb",), ("binc_sb",)], writes=[("bias1_sb",)])
        p.op("vector", lambda v: v.tensor_tensor(
            bias1_sb[:], bias1_sb[:], b1_sb[:], ALU.add),
            reads=[("bias1_sb",), ("b1_sb",)], writes=[("bias1_sb",)])

    def ln_stats_part(xt_ap, xt_rg, sq_buf, ones_col, ones_rg):
        for km in range(DM):
            p.op("tensor", lambda t, km=km: t.matmul(
                ps_sum[:, 0:LC], ones_col[:],
                xt_ap[:, km * LC:(km + 1) * LC],
                start=(km == 0), stop=(km == DM - 1)),
                reads=[xt_rg[km] if isinstance(xt_rg, list) else xt_rg,
                       ones_rg],
                writes=[("ps_sum",)], accum=(km > 0))
        for km in range(DM):
            sl = slice((km % 2) * LC, (km % 2 + 1) * LC)
            p.op("scalar", lambda s, km=km, sl=sl: s.activation(
                ys_sb[:, sl],
                xt_ap[:, km * LC:(km + 1) * LC], AF.Square),
                reads=[xt_rg[km] if isinstance(xt_rg, list) else xt_rg],
                writes=[("ys", km % 2)])
            p.op("tensor", lambda t, km=km, sl=sl: t.matmul(
                ps_sq[:, 0:LC], onec_f[:], ys_sb[:, sl],
                start=(km == 0), stop=(km == DM - 1)),
                reads=[("ys", km % 2), ("onec_f",)],
                writes=[("ps_sq",)], accum=(km > 0))

    def ln_stats_apply(xt_ap, xt_rg, sq_buf, combine12, ones_col=None,
                       ones_rg=("onec_f",), skip_stats=False):
        if ones_col is None:
            ones_col = onec_f
        if not skip_stats:
            ln_stats_part(xt_ap, xt_rg, sq_buf, ones_col, ones_rg)
        s0 = stat_sb[0:1, 0:LC]
        s1 = stat_sb[0:1, LC:2 * LC]
        s2 = s2row_sb[0:1, 0:LC]
        p.op("vector", lambda v: v.tensor_scalar(
            s0, ps_sum[:, 0:LC], 1.0 / D_MODEL, None, ALU.mult),
            reads=[("ps_sum",), ("ps_sq",)], writes=[("st0",)])
        p.op("vector", lambda v: v.tensor_scalar(
            s1, ps_sq[:, 0:LC], 1.0 / D_MODEL, None, ALU.mult),
            reads=[("ps_sq",), ("ps_sum",)], writes=[("st1",)])
        p.op("vector", lambda v: v.tensor_tensor(s2, s0, s0, ALU.mult),
             reads=[("st0",)], writes=[("st2",)])
        p.op("vector", lambda v: v.tensor_tensor(s1, s1, s2, ALU.subtract),
             reads=[("st1",), ("st2",)], writes=[("st1",)])
        if combine12:
            p.op("scalar", lambda s: s.activation(
                s1, s1, AF.Ln, bias=cvals_sb[0:1, 0:1],
                scale=cvals_sb[0:1, 1:2]),
                reads=[("st1",), ("cv", 0), ("cv", 1)], writes=[("st1",)])
        else:
            p.op("scalar", lambda s: s.activation(
                s1, s1, AF.Ln, bias=cvals_sb[0:1, 2:3]),
                reads=[("st1",), ("cv", 2)], writes=[("st1",)])
        p.op("scalar", lambda s: s.activation(
            s1, s1, AF.Exp, scale=cvals_sb[0:1, 3:4]),
            reads=[("st1",), ("cv", 3)], writes=[("st1",)])
        p.op("tensor", lambda t: t.matmul(
            ps_rep[:, 0:LC], ones_rf[:], s1, start=True, stop=True),
            reads=[("st1",), ("ones_rf",)], writes=[("ps_rep",)])
        p.op("vector", lambda v: v.tensor_tensor(s0, s0, s1, ALU.mult),
             reads=[("st0",), ("st1",)], writes=[("st0",)])
        p.op("tensor", lambda t: t.matmul(
            ps_rep[:, 256:256 + LC], ones_rf[:], s0, start=True, stop=True),
            reads=[("st0",), ("ones_rf",)], writes=[("ps_rep",)])

    def x_load(ch):
        par = ch % 2
        src = xT.rearrange("(km p) l -> p km l", p=128)[:, :, ch * LC:(ch + 1) * LC]
        p.op("gpsimd", lambda g, par=par, src=src: g.dma_start(
            out=xt_sb[par][:].rearrange("p (km l) -> p km l", l=LC), in_=src),
            writes=[("xt", par)], dma=True, sem=f"d_gx{par}")

    def ln12_stats(ch):
        par = ch % 2
        ln_stats_part(xt_sb[par][:], ("xt", par), None, onec_b, ("onec_b",))

    def ln12_mid(ch):
        par = ch % 2
        ln_stats_apply(xt_sb[par][:], ("xt", par), None, combine12=True,
                       ones_col=onec_b, ones_rg=("onec_b",), skip_stats=True)

    def ln12_apply(ch):
        par = ch % 2
        xt = xt_sb[par][:]
        xn = xn_sb[par][:]
        for km in range(DM):
            sl = slice(km * LC, (km + 1) * LC)
            r = km % 4
            if r < 2:
                tmp = ys_sb[:, r * LC:(r + 1) * LC]
                tkey = ("ys", r)
            else:
                tmp = x1_sb[:, (r - 2) * LC:(r - 1) * LC]
                tkey = ("x1", r - 2)
            p.op("vector", lambda v, sl=sl, tmp=tmp: v.tensor_tensor(
                tmp, xt[:, sl], ps_rep[:, 0:LC], ALU.mult),
                reads=[("xt", par), ("ps_rep",)], writes=[tkey])
            p.op("vector", lambda v, sl=sl, tmp=tmp, xn=xn: v.tensor_tensor(
                xn[:, sl], tmp, ps_rep[:, 256:256 + LC], ALU.subtract),
                reads=[tkey, ("ps_rep",)], writes=[("xn", par, km)])

    def xz_eblk(eblk, ch):
        par = eblk % 2
        cpar = ch % 2
        xn = xn_sb[cpar][:]
        p.op("sync", lambda s, par=par, eblk=eblk: s.dma_start(
            out=wbuf[par][:], in_=win_bf[eblk]),
            reads=[("win_bf", eblk)], writes=[("wbuf", par)], dma=True,
            sem=f"d_w{par}")
        for km in range(DM):
            p.op("tensor", lambda t, par=par, km=km, xn=xn: t.matmul(
                ps_xz[par][:],
                wbuf[par][:, km * 128:(km + 1) * 128],
                xn[:, km * LC:(km + 1) * LC],
                start=(km == 0), stop=(km == DM - 1)),
                reads=[("wbuf", par), ("xn", cpar, km)],
                writes=[("ps_xz", par)], accum=(km > 0))
        if eblk < DC:
            dc = eblk
            p.op("scalar", lambda s, dc=dc, par=par, eblk=eblk: s.activation(
                xzu_sb[:, dc * LCP + CONV - 1: dc * LCP + CONV - 1 + LC],
                ps_xz[par][:], AF.Identity, bias=c_sb[:, eblk:eblk + 1]),
                reads=[("ps_xz", par), ("c_sb",)], writes=[("xzu", dc)])
        else:
            dc = eblk - DC
            p.op("scalar", lambda s, dc=dc, par=par, cpar=cpar, eblk=eblk:
                 s.activation(
                    gt_sb[cpar][:, dc * LC:(dc + 1) * LC], ps_xz[par][:],
                    AF.Silu, bias=c_sb[:, eblk:eblk + 1]),
                 reads=[("ps_xz", par), ("c_sb",)], writes=[("gt", cpar, dc)])

    def xz_u(ch):
        for eblk in range(DC):
            xz_eblk(eblk, ch)

    def xz_z(ch):
        for eblk in range(DC, EB):
            xz_eblk(eblk, ch)

    def conv_silu(ch):
        for dc in range(DC):
            par = dc % 2
            for k in range(CONV):
                j = dc * CONV + k
                p.op("tensor", lambda t, j=j, dc=dc, k=k, par=par: t.matmul(
                    ps_conv[par][:],
                    diag_sb[:, j * 128:(j + 1) * 128],
                    xzu_sb[:, dc * LCP + k: dc * LCP + k + LC],
                    start=(k == 0), stop=False),
                    reads=[("diag", j), ("xzu", dc)],
                    writes=[("ps_conv", par)], accum=(k > 0))
            p.op("tensor", lambda t, dc=dc, par=par: t.matmul(
                ps_conv[par][:],
                convbr_sb[0:1, dc * 128:(dc + 1) * 128], ones_rb[:],
                start=False, stop=True),
                reads=[("convbr_sb",), ("ones_rb",)],
                writes=[("ps_conv", par)], accum=True)
            p.op("scalar", lambda s, dc=dc, par=par: s.activation(
                u_sb[:, dc * LC:(dc + 1) * LC], ps_conv[par][:], AF.Silu),
                reads=[("ps_conv", par)], writes=[("u", dc)])
        xv = xzu_sb[:].rearrange("p (dc l) -> p dc l", l=LCP)
        p.op("vector", lambda v, xv=xv: v.tensor_copy(
            xv[:, :, 0:CONV - 1], xv[:, :, LC:LC + CONV - 1]),
            reads=XZU, writes=XZU)

    def xdbl_dt(ch):
        par2 = ch % 2
        dt = dt_sb[par2]
        for dc in range(DC):
            p.op("tensor", lambda t, dc=dc: t.matmul(
                ps_small[0:RN, 0:LC],
                wx_sb[:, dc * RN:(dc + 1) * RN],
                u_sb[:, dc * LC:(dc + 1) * LC],
                start=(dc == 0), stop=(dc == DC - 1)),
                reads=[("wx_sb",), ("u", dc)],
                writes=[("ps_xdbl",)], accum=(dc > 0))
        p.op("scalar", lambda s: s.copy(dtr_sb, ps_small[0:R, 0:LC]),
             reads=[("ps_xdbl",)], writes=[("dtr",)])
        p.op("scalar", lambda s: s.copy(
            bc_sb[0:N, :], ps_small[R:R + N, 0:LC]),
            reads=[("ps_xdbl",)], writes=[("bc_b",)])
        p.op("scalar", lambda s: s.copy(
            bc_sb[32:32 + N, :], ps_small[96:96 + N, 0:LC]),
            reads=[("ps_xdbl",)], writes=[("bc_c",)])
        p.op("sync", lambda s, par2=par2: s.dma_start(
            out=bc_dram[par2][0:N], in_=bc_sb[0:N, :]),
            reads=[("bc_b",)], writes=[("bc_dram", par2)],
            dma=True, sem=f"d_bc{par2}")
        p.op("sync", lambda s, par2=par2: s.dma_start(
            out=bc_dram[par2][N:2 * N], in_=bc_sb[32:32 + N, :]),
            reads=[("bc_c",)], writes=[("bc_dram", par2)],
            dma=True, sem=f"d_bc{par2}")




    def dt_chain(ch, dcs):
        par2 = ch % 2
        dt = dt_sb[par2]
        dtslots = [ps_small[:, 256:256 + LC], ps_conv[0][:], ps_conv[1][:]]
        dtkeys = [("ps_dt",), ("ps_conv", 0), ("ps_conv", 1)]
        for dc in dcs:
            slot = dtslots[dc % 3]
            skey = dtkeys[dc % 3]
            wr = [skey] + ([("ps_xdbl",)] if dc == 0 else [])
            p.op("tensor", lambda t, dc=dc, slot=slot: t.matmul(
                slot, wdt_sb[:, dc * 128:(dc + 1) * 128], dtr_sb,
                start=True, stop=True),
                reads=[("wdt_sb",), ("dtr",)], writes=wr)
            p.op("scalar", lambda s, dc=dc, slot=slot: s.activation(
                e1_sb, slot, AF.Exp, bias=bdt_sb[:, dc:dc + 1]),
                reads=[skey, ("bdt_sb",)], writes=[("e1",)])
            p.op("scalar", lambda s, dc=dc, dt=dt: s.activation(
                dt[:, dc * LC:(dc + 1) * LC], e1_sb, AF.Ln,
                bias=onec_f[:, 0:1]),
                reads=[("e1",), ("onec_f",)], writes=[("dt", par2, dc)])

    def bcq_load(ch, q):
        par2 = ch % 2
        par = q % 2
        p.op("sync", lambda s, par2=par2, par=par, q=q: s.dma_start(
            out=bcq_sb[par][:, 0:4 * LC],
            in_=bc_dram[par2][4 * q:4 * q + 4].rearrange("r l -> (r l)")[
                None, :].broadcast_to([128, 4 * LC])),
            reads=[("bc_dram", par2)], writes=[("bcq", par)], dma=True,
            sem=f"d_bn{par}")
        p.op("sync", lambda s, par2=par2, par=par, q=q: s.dma_start(
            out=bcq_sb[par][:, 4 * LC:8 * LC],
            in_=bc_dram[par2][N + 4 * q:N + 4 * q + 4].rearrange(
                "r l -> (r l)")[None, :].broadcast_to([128, 4 * LC])),
            reads=[("bc_dram", par2)], writes=[("bcq", par)], dma=True,
            sem=f"d_bn{par}")

    def uw_y_init(ch):
        par2 = ch % 2
        p.op("vector", lambda v, par2=par2: v.tensor_tensor(
            uw_sb[:], dt_sb[par2][:], u_sb[:], ALU.mult),
            reads=DTK(par2) + U, writes=[("uw",)])
        for dc in range(DC):
            sl = slice(dc * LC, (dc + 1) * LC)
            p.op("scalar", lambda s, sl=sl, dc=dc: s.activation(
                y_sb[:, sl], u_sb[:, sl], AF.Copy,
                scale=dcol_sb[:, dc:dc + 1]),
                reads=[("u", dc), ("dcol_sb",)], writes=[("y", dc)])

    def scan_phase(ch):
        first = (ch == 0)
        av = a_sb[:].rearrange("p (dc l) -> p dc l", l=LC)
        bv = b_sb[:].rearrange("p (dc l) -> p dc l", l=LC)
        H = DC // 2
        nxt = ch + 1 < NCH
        usched = {}
        csched = {}
        for j in range(DC):
            usched.setdefault(2 + j // 2, []).append(j)
            csched.setdefault(3 + j // 2, []).append(j)
        tail_n = 3 + (DC - 1) // 2 + 1
        zsched = {}
        zslots = list(range(max(N - 6, tail_n), N))
        for i in range(DC):
            zsched.setdefault(zslots[i * len(zslots) // DC], []).append(DC + i)
        cpar = ch % 2
        if first:
            bcq_load(ch, 0)
        for n in range(N):
            for h in range(2):
                hdc = range(h * H, (h + 1) * H)
                hsl = slice(h * H * LC, (h + 1) * H * LC)
                HK = [("a", d) for d in hdc]
                HB = [("b", d) for d in hdc]
                if n > 0 or first:
                    for dc in hdc:
                        p.op("scalar", lambda s, n=n, dc=dc, cpar=cpar:
                             s.activation(
                                a_sb[:, dc * LC:(dc + 1) * LC],
                                dt_sb[cpar][:, dc * LC:(dc + 1) * LC],
                                AF.Exp,
                                scale=A_sb[:, dc * N + n:dc * N + n + 1]),
                             reads=[("dt", cpar, dc), ("A_sb",)],
                             writes=[("a", dc)])
                p.op("vector", lambda v, n=n, hsl=hsl: v.tensor_tensor(
                    b_sb[:, hsl].rearrange("p (dc l) -> p dc l", l=LC),
                    uw_sb[:, hsl].rearrange("p (dc l) -> p dc l", l=LC),
                    bcq_sb[(n // 4) % 2][:, (n % 4) * LC:(n % 4 + 1) * LC][
                        :, None, :].broadcast_to([128, H, LC]),
                    ALU.mult),
                    reads=[("uw",), ("bcq", (n // 4) % 2)], writes=HB)
                avh = av[:, h * H:(h + 1) * H]
                bvh = bv[:, h * H:(h + 1) * H]
                if not first:
                    p.op("vector", lambda v, n=n, avh=avh, h=h: v.tensor_tensor(
                        fix_sb[:, 0:H], avh[:, :, 0],
                        hstate_sb[:, n * DC + h * H:n * DC + (h + 1) * H],
                        ALU.mult),
                        reads=HK + [("hstate",)], writes=[("fix",)])
                    p.op("vector", lambda v, bvh=bvh: v.tensor_tensor(
                        bvh[:, :, 0], bvh[:, :, 0], fix_sb[:, 0:H], ALU.add),
                        reads=HB + [("fix",)], writes=HB)
                p.op("vector", lambda v, avh=avh: v.memset(avh[:, :, 0], 0.0),
                     reads=HK, writes=HK)
                p.op("vector", lambda v, hsl=hsl: v.tensor_tensor_scan(
                    b_sb[:, hsl], a_sb[:, hsl], b_sb[:, hsl], 0.0,
                    ALU.mult, ALU.add),
                    reads=HK + HB, writes=HB)
                p.op("vector", lambda v, n=n, bvh=bvh, h=h: v.tensor_copy(
                    hstate_sb[:, n * DC + h * H:n * DC + (h + 1) * H],
                    bvh[:, :, LC - 1]),
                    reads=HB, writes=[("hstate",)])
                hk = ("tp", h)
                p.op("vector", lambda v, n=n, hsl=hsl, h=h: v.tensor_tensor(
                    tprod_sb[:, hsl].rearrange("p (dc l) -> p dc l", l=LC),
                    b_sb[:, hsl].rearrange("p (dc l) -> p dc l", l=LC),
                    bcq_sb[(n // 4) % 2][:, (4 + n % 4) * LC:(4 + n % 4 + 1) * LC][
                        :, None, :].broadcast_to([128, H, LC]),
                    ALU.mult),
                    reads=HB + [("bcq", (n // 4) % 2)], writes=[hk])
                p.op("vector", lambda v, hsl=hsl: v.tensor_tensor(
                    y_sb[:, hsl], y_sb[:, hsl], tprod_sb[:, hsl], ALU.add),
                    reads=[hk] + YK, writes=YK)
            if n % 4 == 1 and n // 4 + 1 < 4:
                bcq_load(ch, n // 4 + 1)
            if nxt:
                for j in usched.get(n, []):
                    xz_eblk(j, ch + 1)
                for j in csched.get(n, []):
                    conv_dc_next(j)
                if n == tail_n:
                    conv_tail_next()
                for j in zsched.get(n, []):
                    xz_eblk(j, ch + 1)
                if n == 8 and ch + 2 < NCH:
                    ln12_stats(ch + 2)
                if n == 9 and ch + 2 < NCH:
                    ln12_mid(ch + 2)
                if n == 11:
                    xdbl_dt(ch + 1)
                if 12 <= n <= 15:
                    dt_chain(ch + 1,
                             range(DC * (n - 12) // 4, DC * (n - 11) // 4))
                if n == 13:
                    bcq_load(ch + 1, 0)
        if nxt:
            cpar2 = (ch + 1) % 2
            for dc in range(DC):
                p.op("scalar", lambda s, dc=dc, cpar2=cpar2: s.activation(
                    a_sb[:, dc * LC:(dc + 1) * LC],
                    dt_sb[cpar2][:, dc * LC:(dc + 1) * LC],
                    AF.Exp, scale=A_sb[:, dc * N:dc * N + 1]),
                    reads=[("dt", cpar2, dc), ("A_sb",)], writes=[("a", dc)])

    def conv_dc_next(dc):
        par = dc % 2
        for k in range(CONV):
            j = dc * CONV + k
            p.op("tensor", lambda t, j=j, dc=dc, k=k, par=par: t.matmul(
                ps_conv[par][:],
                diag_sb[:, j * 128:(j + 1) * 128],
                xzu_sb[:, dc * LCP + k: dc * LCP + k + LC],
                start=(k == 0), stop=False),
                reads=[("diag", j), ("xzu", dc)],
                writes=[("ps_conv", par)], accum=(k > 0))
        p.op("tensor", lambda t, dc=dc, par=par: t.matmul(
            ps_conv[par][:],
            convbr_sb[0:1, dc * 128:(dc + 1) * 128], ones_rb[:],
            start=False, stop=True),
            reads=[("convbr_sb",), ("ones_rb",)],
            writes=[("ps_conv", par)], accum=True)
        p.op("scalar", lambda s, dc=dc, par=par: s.activation(
            u_sb[:, dc * LC:(dc + 1) * LC], ps_conv[par][:], AF.Silu),
            reads=[("ps_conv", par)], writes=[("u", dc)])

    def conv_tail_next():
        xv = xzu_sb[:].rearrange("p (dc l) -> p dc l", l=LCP)
        p.op("vector", lambda v, xv=xv: v.tensor_copy(
            xv[:, :, 0:CONV - 1], xv[:, :, LC:LC + CONV - 1]),
            reads=XZU, writes=XZU)

    def gate_out(ch):
        cpar = ch % 2
        xv = xzu_sb[:].rearrange("p (dc l) -> p dc l", l=LCP)
        p.op("vector", lambda v, cpar=cpar, xv=xv: v.tensor_tensor(
            xv[:, :, CONV - 1:CONV - 1 + LC],
            y_sb[:].rearrange("p (dc l) -> p dc l", l=LC),
            gt_sb[cpar][:].rearrange("p (dc l) -> p dc l", l=LC),
            ALU.mult),
            reads=YK + [("gt", cpar, d) for d in range(DC)], writes=XZU)
        for km in range(DM):
            par = km % 2
            for dc in range(DC):
                p.op("tensor", lambda t, km=km, dc=dc, par=par: t.matmul(
                    ps_out[par][:],
                    wout_sb[:, dc * D_MODEL + km * 128:
                            dc * D_MODEL + (km + 1) * 128],
                    xzu_sb[:, dc * LCP + CONV - 1: dc * LCP + CONV - 1 + LC],
                    start=(dc == 0), stop=(dc == DC - 1)),
                    reads=[("wout_sb", dc * 4 // DC), ("xzu", dc)],
                    writes=[("ps_o",)], accum=(dc > 0))
            ks = km % 2
            if ch < HCH:
                dst = outst_sb[:, ks * LC:(ks + 1) * LC]
            else:
                dst = outst_sb[:, ks * LC:(ks + 1) * LC][:, ::-1]
            p.op("scalar", lambda s, dst=dst, par=par: s.copy(
                dst, ps_out[par][:]),
                reads=[("ps_o",)], writes=[("outst", ks)])
            if ks == 1:
                kg = km - 1
                if ch < HCH:
                    dst2 = yown_dram.rearrange("(km p) l -> p km l", p=128)[
                        :, kg:kg + 2, ch * LC:(ch + 1) * LC]
                    rg = ("yown_dram", ch)
                else:
                    cc = NCH - 1 - ch
                    base = cc * LC
                    hi = base // (HALF // 2)
                    off = base % (HALF // 2)
                    dst2 = send_dram[hi].rearrange("(km p) l -> p km l", p=128)[
                        :, kg:kg + 2, off:off + LC]
                    rg = ("send_dram", cc)
                p.op("sync", lambda s, dst2=dst2: s.dma_start(
                    out=dst2,
                    in_=outst_sb[:].rearrange("p (km l) -> p km l", l=LC)),
                    reads=[("outst", 0), ("outst", 1)],
                    writes=[rg], dma=True, sem="d_ow")

    def collective_half(hi):
        groups = [[2 * i, 2 * i + 1] for i in range(NCORES // 2)]
        chs = [c for c in range(HCH)
               if hi * (HALF // 2) <= c * LC < (hi + 1) * (HALF // 2)]
        p.op("gpsimd", lambda g, hi=hi: g.collective_compute(
            "AllReduce", ALU.add, replica_groups=groups,
            ins=[send_dram[hi].opt()],
            outs=[ar_dram[hi].opt()]),
            reads=[("send_dram", c) for c in chs],
            writes=[("ar_dram", hi)], sem="cc" if hi == 0 else "cc1",
            inc=1)

    def post_chunk(pc):
        par = pc % 2
        xt = xres_sb[:]
        colsl = slice(pc * LC, (pc + 1) * LC)
        p.op("sync", lambda s: s.dma_start(
            out=xt.rearrange("p (km l) -> p km l", l=LC),
            in_=x_res.rearrange("(km p) l -> p km l", p=128)[:, :, colsl]),
            writes=[("xres",)], dma=True, sem=f"d_x{par}")
        p.op("sync", lambda s: s.dma_start(
            out=b_sb[:, 0:DM * LC].rearrange("p (km l) -> p km l", l=LC),
            in_=yown_dram.rearrange("(km p) l -> p km l", p=128)[:, :, colsl]),
            reads=[("yown_dram", pc)], writes=BK, dma=True, sem=f"d_pl{par}")
        arh = 0 if (pc + 1) * LC <= HALF // 2 else 1
        poff = pc * LC - arh * (HALF // 2)
        p.op("sync", lambda s: s.dma_start(
            out=uw_sb[:, 0:DM * LC].rearrange("p (km l) -> p km l", l=LC),
            in_=send_dram[arh].rearrange("(km p) l -> p km l", p=128)[
                :, :, poff:poff + LC]),
            reads=[("send_dram", pc)], writes=[("uw",)], dma=True,
            sem=f"d_pl{par}")
        p.op("sync", lambda s: s.dma_start(
            out=gt_sb[0][:, 0:DM * LC].rearrange("p (km l) -> p km l", l=LC),
            in_=ar_dram[arh].rearrange("(km p) l -> p km l", p=128)[
                :, :, poff:poff + LC]),
            reads=[("ar_dram", arh)], writes=GT, dma=True, sem=f"d_pl{par}")
        p.op("vector", lambda v: v.tensor_tensor(
            a_sb[:, 0:DM * LC], gt_sb[0][:, 0:DM * LC], uw_sb[:, 0:DM * LC],
            ALU.subtract),
            reads=GT + [("uw",)], writes=AK)
        p.op("vector", lambda v: v.tensor_tensor(
            dt_sb[0][:, 0:DM * LC], a_sb[:, 0:DM * LC], b_sb[:, 0:DM * LC],
            ALU.add),
            reads=AK + BK, writes=DTK(0))

        ln_stats_apply(xt, ("xres",), None, combine12=False)
        for km in range(DM):
            sl = slice(km * LC, (km + 1) * LC)
            ts = slice((km % 2) * LC, (km % 2 + 1) * LC)
            p.op("vector", lambda v, sl=sl, ts=ts: v.tensor_tensor(
                ys_sb[:, ts], xt[:, sl], ps_rep[:, 0:LC], ALU.mult),
                reads=[("xres",), ("ps_rep",)], writes=[("ys", km % 2)])
            p.op("vector", lambda v, ts=ts: v.tensor_tensor(
                ys_sb[:, ts], ys_sb[:, ts], ps_rep[:, 256:256 + LC],
                ALU.subtract),
                reads=[("ys", km % 2), ("ps_rep",)],
                writes=[("ys", km % 2)])
            p.op("vector", lambda v, sl=sl, ts=ts: v.tensor_tensor(
                ys_sb[:, ts], ys_sb[:, ts], xt[:, sl], ALU.add),
                reads=[("ys", km % 2), ("xres",)],
                writes=[("ys", km % 2)])
            p.op("vector", lambda v, sl=sl, ts=ts: v.tensor_tensor(
                x1_sb[:, sl], ys_sb[:, ts], dt_sb[0][:, sl], ALU.add),
                reads=[("ys", km % 2), ("dt", 0, sl.start // LC)],
                writes=[("x1", sl.start // LC)])

        ln_stats_apply(x1_sb[:], X1, None, combine12=False)
        for km in range(DM):
            sl = slice(km * LC, (km + 1) * LC)
            ts = slice((km % 2) * LC, (km % 2 + 1) * LC)
            p.op("vector", lambda v, sl=sl, ts=ts: v.tensor_tensor(
                ys_sb[:, ts], x1_sb[:, sl], ps_rep[:, 0:LC], ALU.mult),
                reads=[("x1", km), ("ps_rep",)], writes=[("ys", km % 2)])
            p.op("vector", lambda v, sl=sl, ts=ts, km=km: v.tensor_tensor(
                xn_sb[0][:, sl], ys_sb[:, ts], ps_rep[:, 256:256 + LC],
                ALU.subtract),
                reads=[("ys", km % 2), ("ps_rep",)], writes=[("xn", 0, km)])

        for km in range(DM):
            wr = [("ps_ffs",)] + ([("ps_xdbl",), ("ps_dt",)] if km == 0 else [])
            p.op("tensor", lambda t, km=km: t.matmul(
                ps_small[0:M, 0:LC], winf_sb[:, km * M:(km + 1) * M],
                xn_sb[0][:, km * LC:(km + 1) * LC],
                start=(km == 0), stop=(km == DM - 1)),
                reads=[("winf_sb",), ("xn", 0, km)],
                writes=wr, accum=(km > 0))
        for i in range(INNER):
            wr = [("hff", i)] + (AK if (pc == 0 and i == 0) else [])
            p.op("scalar", lambda s, i=i: s.activation(
                hff_sb[0:M, i * LC:(i + 1) * LC], ps_small[0:M, 0:LC], AF.Relu,
                bias=bias1_sb[:, i:i + 1], scale=w1_sb[:, i:i + 1]),
                reads=[("ps_ffs",), ("bias1_sb",), ("w1_sb",)],
                writes=wr)
        p.op("vector", lambda v: v.tensor_scalar(
            gacc_sb[0][:], hff_sb[0:M, 0:LC], w2_sb[:, 0:1], b2_sb[:, 0:1],
            ALU.mult, ALU.add),
            reads=[("hff", 0), ("w2_sb",), ("b2_sb",)], writes=[("gacc", 0)])
        for i in range(1, INNER):
            sp, dp = (i - 1) % 2, i % 2
            p.op("vector", lambda v, i=i, sp=sp, dp=dp: v.scalar_tensor_tensor(
                gacc_sb[dp][:], hff_sb[0:M, i * LC:(i + 1) * LC],
                w2_sb[:, i:i + 1], gacc_sb[sp][:], ALU.mult, ALU.add),
                reads=[("hff", i), ("w2_sb",), ("gacc", sp)],
                writes=[("gacc", dp)])
        gfin = gacc_sb[(INNER - 1) % 2]
        gfin_rg = ("gacc", (INNER - 1) % 2)
        for km in range(DM):
            par2 = km % 2
            p.op("tensor", lambda t, km=km, par2=par2: t.matmul(
                ps_out[par2][:], woutf_sb[:, km * 128:(km + 1) * 128],
                gfin[:], start=True, stop=True),
                reads=[("woutf_sb",), gfin_rg], writes=[("ps_o",)])
            p.op("vector", lambda v, km=km, par2=par2: v.scalar_tensor_tensor(
                xres_sb[:, km * LC:(km + 1) * LC], ps_out[par2][:],
                bout_sb[:, km:km + 1], x1_sb[:, km * LC:(km + 1) * LC],
                ALU.add, ALU.add),
                reads=[("ps_o",), ("bout_sb",), ("x1", km)],
                writes=[("xres",)])
        p.op("sync", lambda s: s.dma_start(
            out=out_d.rearrange("(km p) l -> p km l", p=128)[:, :, colsl],
            in_=xres_sb[:].rearrange("p (km l) -> p km l", l=LC)),
            reads=[("xres",)], writes=[("out_d", pc)], dma=True, sem="d_od")

    startup()
    x_load(0)
    ln12_stats(0)
    ln12_mid(0)
    ln12_apply(0)
    x_load(1)
    xz_u(0)
    conv_silu(0)
    xz_z(0)
    xdbl_dt(0)
    dt_chain(0, range(DC))
    if NCH > 1:
        ln12_stats(1)
        ln12_mid(1)
    for k in range(NCH):
        if k > 0:
            gate_out(k - 1)
        if k == NCH - HCH // 2:
            collective_half(1)
        if k + 1 < NCH:
            ln12_apply(k + 1)
        uw_y_init(k)
        if k + 2 < NCH:
            x_load(k + 2)
        scan_phase(k)
    gate_out(NCH - 1)
    collective_half(0)
    for pc in list(range(HCH // 2, HCH)) + list(range(HCH // 2)):
        post_chunk(pc)

    for snm in ["d_st", "d_x0", "d_x1", "d_w0", "d_w1", "d_bc0", "d_bc1",
                "d_ow", "d_pl0", "d_pl1", "d_od", "d_bn0", "d_bn1"]:
        p.q["sync"].append(("wait", snm, p.cnt.get(snm, 0)))
    p.q["gpsimd"].append(("wait", "d_g", p.cnt.get("d_g", 0)))
    p.q["gpsimd"].append(("wait", "d_gx0", p.cnt.get("d_gx0", 0)))
    p.q["gpsimd"].append(("wait", "d_gx1", p.cnt.get("d_gx1", 0)))
    p.q["gpsimd"].append(("wait", "cc", p.cnt.get("cc", 0)))
    p.q["gpsimd"].append(("wait", "cc1", p.cnt.get("cc1", 0)))

    with nc.Block() as block:
        p.finalize(block, sems)
    stack.close()
    return nc


def _prep_core_inputs(cfg, x_b, flip, w, ffn):
    DM, DC, L = cfg["DM"], cfg["DC"], cfg["L"]
    N, R, CONV, M = cfg["N"], cfg["R"], cfg["CONV"], cfg["M"]
    D_MODEL, D_INNER = DM * 128, DC * 128
    EB = 2 * DC
    HALF = L // 2
    C = np.ascontiguousarray

    xd = x_b[::-1] if flip else x_b
    ins = {}
    ins["xT"] = C(xd.T.astype(np.float32))
    ins["x_res"] = C(xd[:HALF].T.astype(np.float32))
    win = np.asarray(w["Win"], dtype=np.float32)
    ins["win_t4"] = C(win.reshape(EB, 128, DM, 128).transpose(0, 3, 2, 1))
    ins["wout_t"] = C(np.asarray(w["Wout"], np.float32).T)
    wxT = np.asarray(w["Wx"], np.float32).T
    wxp = np.zeros((wxT.shape[0], 128), np.float32)
    wxp[:, 0:R] = wxT[:, 0:R]
    wxp[:, R:R + N] = wxT[:, R:R + N]
    wxp[:, 96:96 + N] = wxT[:, R + N:R + 2 * N]
    ins["wx_t"] = C(wxp)
    ins["wdt_t"] = C(np.asarray(w["Wdt"], np.float32).T)
    ins["alog_c"] = C(np.asarray(w["Alog"], np.float32)
                      .reshape(DC, 128, N).transpose(1, 0, 2).reshape(128, DC * N))
    ins["dcol"] = C(np.asarray(w["D"], np.float32).reshape(DC, 128).T)
    ins["bdt_c"] = C(np.asarray(w["bdt"], np.float32).reshape(DC, 128).T)
    ins["convw_c"] = C(np.asarray(w["convw"], np.float32)[:, 0, :]
                       .reshape(DC, 128, CONV).transpose(1, 0, 2)
                       .reshape(128, DC * CONV))
    ins["convb_r"] = C(np.asarray(w["convb"], np.float32)[None, :])
    ins["gcol"] = C(np.asarray(w["norm_g"], np.float32).reshape(DM, 128).T)
    ins["bcol"] = C(np.asarray(w["norm_b"], np.float32).reshape(DM, 128).T)
    ins["winf"] = C(np.asarray(ffn["Win"], np.float32))
    ins["woutf"] = C(np.asarray(ffn["Wout"], np.float32))
    ins["w1"] = C(np.asarray(ffn["W1"], np.float32))
    ins["b1"] = C(np.asarray(ffn["b1"], np.float32))
    ins["w2"] = C(np.asarray(ffn["W2"], np.float32))
    ins["b2c"] = C(np.asarray(ffn["b2"], np.float32)[:, None])
    ins["binc"] = C(np.asarray(ffn["bin_"], np.float32)[:, None])
    ins["bout_c"] = C(np.asarray(ffn["bout"], np.float32).reshape(DM, 128).T)
    ins["eye16"] = np.eye(128, dtype=ml_dtypes.bfloat16)
    return ins


def _assemble(cfg, shards):
    DM, L = cfg["DM"], cfg["L"]
    D_MODEL, HALF = DM * 128, L // 2
    B = len(shards) // 2
    out = np.empty((B, L, D_MODEL), dtype=np.float32)
    for b in range(B):
        out[b, :HALF] = shards[2 * b].T
        out[b, HALF:] = shards[2 * b + 1].T[::-1]
    return out


MAMBA_KEYS = ["Win", "convw", "convb", "Wx", "Wdt", "bdt", "Alog", "D", "Wout"]


def kernel(**inputs):
    cfg = FULL_CFG
    x = np.asarray(inputs["L0_attn_x"], dtype=np.float32)
    Bn = x.shape[0]
    wf = {k: np.asarray(inputs["L0_attn_f_" + k]) for k in MAMBA_KEYS}
    wb = {k: np.asarray(inputs["L0_attn_b_" + k]) for k in MAMBA_KEYS}
    for d in (wf, wb):
        d["norm_g"] = np.asarray(inputs["L0_attn_norm_g"])
        d["norm_b"] = np.asarray(inputs["L0_attn_norm_b"])
    ffn = {k: np.asarray(inputs["L0_ffn_" + k]) for k in
           ["Win", "bin_", "W1", "b1", "W2", "b2", "Wout", "bout"]}

    nc = build_core_program(cfg)
    in_maps = []
    for b in range(Bn):
        in_maps.append(_prep_core_inputs(cfg, x[b], False, wf, ffn))
        in_maps.append(_prep_core_inputs(cfg, x[b], True, wb, ffn))
    res = run_bass_kernel_spmd(nc, in_maps, core_ids=list(range(2 * Bn)))
    shards = [np.asarray(r["out"]) for r in res.results]
    return _assemble(cfg, shards)


# version 27
# speedup vs baseline: 1.0668x; 1.0218x over previous
import numpy as np
import ml_dtypes
from contextlib import ExitStack

import concourse.bass as bass
import concourse.mybir as mybir
from concourse.bass_utils import run_bass_kernel_spmd

F32 = mybir.dt.float32
BF16 = mybir.dt.bfloat16
AF = mybir.ActivationFunctionType
ALU = mybir.AluOpType

EPS = 1e-5

FULL_CFG = dict(DM=8, DC=16, L=4096, LC=256, N=16, R=64, CONV=4, M=64,
                INNER=16, NCORES=8)


class Prog:
    ENGINES = ("tensor", "vector", "scalar", "gpsimd", "sync")

    @staticmethod
    def is_dma_sem(sem):
        return sem.startswith("d_") or sem.startswith("cc")

    def __init__(self, nc):
        self.nc = nc
        self.q = {e: [] for e in self.ENGINES}
        self.cnt = {}
        self.state = {}
        self.waited = {}
        self.opidx = {e: 0 for e in self.ENGINES}
        self.last_drain = {e: -1 for e in self.ENGINES}

    def _need(self, eng, stamp, drains):
        sem, val, peng, pidx = stamp
        if self.is_dma_sem(sem):
            val = self.cnt.get(sem, 0)
        k = (eng, sem)
        if self.waited.get(k, 0) < val:
            self.waited[k] = val
            self.q[eng].append(("wait", sem, val))

    def op(self, eng, fn, reads=(), writes=(), dma=False, sem=None, inc=None,
           accum=False):
        drains = []
        for r in reads:
            st = self.state.get(r)
            if st and st["w"]:
                self._need(eng, st["w"], drains)
        if not accum:
            for r in writes:
                st = self.state.get(r)
                if st:
                    if st["w"]:
                        sem_, val_, peng_, pidx_ = st["w"]
                        if peng_ != eng or self.is_dma_sem(sem_):
                            self._need(eng, st["w"], drains)
                    for stamp in st["r"].values():
                        self._need(eng, stamp, drains)
        if drains and max(drains) > self.last_drain[eng]:
            self.q[eng].append(("drain",))
            self.last_drain[eng] = self.opidx[eng]
        if sem is None:
            assert not dma, "DMA ops must name their semaphore"
            sem = eng
        if inc is None:
            inc = 16 if dma else 1
        self.cnt[sem] = self.cnt.get(sem, 0) + inc
        val = self.cnt[sem]
        self.opidx[eng] += 1
        stamp = (sem, val, eng, self.opidx[eng])
        self.q[eng].append(("op", fn, sem, inc, val))
        for r in writes:
            self.state[r] = {"w": stamp, "r": {}}
        for r in reads:
            st = self.state.setdefault(r, {"w": None, "r": {}})
            st["r"][eng] = stamp
        return stamp

    def finalize(self, block, sems):
        waited = {}
        for ename in self.ENGINES:
            for it in self.q[ename]:
                if it[0] == "wait" and not self.is_dma_sem(it[1]):
                    waited.setdefault(it[1], set()).add(it[2])
        remap = {}
        for sem, vals in waited.items():
            remap[sem] = {v: i + 1 for i, v in enumerate(sorted(vals))}

        for ename in self.ENGINES:
            items = self.q[ename]

            def make_body(items):
                def body(engine):
                    for it in items:
                        if it[0] == "wait":
                            sem, val = it[1], it[2]
                            if sem in remap:
                                val = remap[sem][val]
                            engine.wait_ge(sems[sem], val)
                        elif it[0] == "drain":
                            engine.drain()
                        else:
                            _, fn, sem, inc, val = it
                            inst = fn(engine)
                            if self.is_dma_sem(sem):
                                inst.then_inc(sems[sem], inc)
                            elif sem in remap and val in remap[sem]:
                                inst.then_inc(sems[sem], inc)
                return body
            getattr(block, ename)(make_body(items))


def build_core_program(cfg):
    DM, DC, L, LC = cfg["DM"], cfg["DC"], cfg["L"], cfg["LC"]
    N, R, CONV = cfg["N"], cfg["R"], cfg["CONV"]
    M, INNER = cfg["M"], cfg["INNER"]
    NCORES = cfg["NCORES"]
    D_MODEL = DM * 128
    D_INNER = DC * 128
    NCH = L // LC
    HCH = NCH // 2
    HALF = L // 2
    EB = 2 * DC
    LCP = LC + CONV - 1
    RN = 128

    nc = bass.Bass("TRN2", target_bir_lowering=False)

    def din(name, shape, dt=F32):
        return nc.dram_tensor(name, list(shape), dt, kind="ExternalInput").ap()

    xT = din("xT", [D_MODEL, L])
    x_res = din("x_res", [D_MODEL, HALF])
    win_t4 = din("win_t4", [EB, 128, DM, 128])
    wout_t = din("wout_t", [D_INNER, D_MODEL])
    wx_t = din("wx_t", [D_INNER, RN])
    wdt_t = din("wdt_t", [R, D_INNER])
    alog_c = din("alog_c", [128, DC * N])
    dcol = din("dcol", [128, DC])
    bdt_c = din("bdt_c", [128, DC])
    convw_c = din("convw_c", [128, DC * CONV])
    convb_r = din("convb_r", [1, D_INNER])
    gcol = din("gcol", [128, DM])
    bcol = din("bcol", [128, DM])
    winf = din("winf", [D_MODEL, M])
    woutf = din("woutf", [M, D_MODEL])
    w1 = din("w1", [M, INNER])
    b1 = din("b1", [M, INNER])
    w2 = din("w2", [M, INNER])
    b2c = din("b2c", [M, 1])
    binc = din("binc", [M, 1])
    bout_c = din("bout_c", [128, DM])
    eye16 = din("eye16", [128, 128], BF16)

    out_d = nc.dram_tensor("out", [D_MODEL, HALF], F32, kind="ExternalOutput").ap()

    win_bf = nc.dram_tensor("win_bf", [EB, 128, DM * 128], BF16).ap()
    bc_dram = [nc.dram_tensor(f"bc_dram{i}", [2 * N, LC], BF16).ap()
               for i in range(2)]
    yown_dram = nc.dram_tensor("yown_dram", [D_MODEL, HALF], BF16).ap()
    send_dram = nc.dram_tensor("send_dram", [2, D_MODEL, HALF // 2], BF16).ap()
    ar_dram = nc.dram_tensor("ar_dram", [2, D_MODEL, HALF // 2], BF16).ap()

    stack = ExitStack()

    def sbuf(name, shape, dt):
        return stack.enter_context(nc.sbuf_tensor(name, list(shape), dt))

    def psum(name, shape, dt=F32):
        return stack.enter_context(nc.psum_tensor(name, list(shape), dt))

    wout_sb = sbuf("wout_sb", [128, DC * D_MODEL], BF16)
    wx_sb = sbuf("wx_sb", [128, DC * RN], BF16)
    wdt_sb = sbuf("wdt_sb", [R, D_INNER], BF16)
    winf_sb = sbuf("winf_sb", [128, DM * M], BF16)
    woutf_sb = sbuf("woutf_sb", [M, D_MODEL], BF16)
    diag_sb = sbuf("diag_sb", [128, DC * CONV * 128], BF16)
    eye_sb = sbuf("eye_sb", [128, 128], BF16)
    A_sb = sbuf("A_sb", [128, DC * N], F32)
    dcol_sb = sbuf("dcol_sb", [128, DC], F32)
    bdt_sb = sbuf("bdt_sb", [128, DC], F32)
    convw_sb = sbuf("convw_sb", [128, DC * CONV], F32)
    convbr_sb = sbuf("convbr_sb", [1, D_INNER], BF16)
    gcol_sb = sbuf("gcol_sb", [128, DM], F32)
    bcol_sb = sbuf("bcol_sb", [128, DM], F32)
    w1_sb = sbuf("w1_sb", [M, INNER], F32)
    b1_sb = sbuf("b1_sb", [M, INNER], F32)
    w2_sb = sbuf("w2_sb", [M, INNER], F32)
    bias1_sb = sbuf("bias1_sb", [M, INNER], F32)
    b2_sb = sbuf("b2_sb", [M, 1], F32)
    binc_sb = sbuf("binc_sb", [M, 1], F32)
    bout_sb = sbuf("bout_sb", [128, DM], F32)
    ones_rf = sbuf("ones_rf", [1, 128], F32)
    cvals_sb = sbuf("cvals_sb", [1, 4], F32)
    half_c = sbuf("half_c", [128, 1], F32)
    ones_rb = sbuf("ones_rb", [1, LC], BF16)
    onec_f = sbuf("onec_f", [128, 1], F32)
    onec_b = sbuf("onec_b", [128, 1], BF16)
    c_sb = sbuf("c_sb", [128, EB], F32)
    bcolb_sb = sbuf("bcolb_sb", [128, DM], BF16)

    wbuf = [sbuf(f"wbuf{i}", [128, DM * 128], BF16) for i in range(2)]
    xt_sb = [sbuf(f"xt_sb{i}", [128, DM * LC], BF16) for i in range(2)]
    xn_sb = [sbuf(f"xn_sb{i}", [128, DM * LC], BF16) for i in range(2)]
    xzu_sb = sbuf("xzu_sb", [128, DC * LCP], BF16)
    u_sb = sbuf("u_sb", [128, DC * LC], BF16)
    gt_sb = [sbuf(f"gt_sb{i}", [128, DC * LC], BF16) for i in range(2)]
    dt_sb = [sbuf(f"dt_sb{i}", [128, DC * LC], BF16) for i in range(2)]
    uw_sb = sbuf("uw_sb", [128, DC * LC], BF16)
    a_sb = sbuf("a_sb", [128, DC * LC], BF16)
    b_sb = sbuf("b_sb", [128, DC * LC], BF16)
    y_sb = sbuf("y_sb", [128, DC * LC], BF16)
    tprod_sb = sbuf("tprod_sb", [128, DC * LC], BF16)
    bc_sb = sbuf("bc_sb", [64, LC], BF16)
    bcq_sb = [sbuf(f"bcq_sb{i}", [128, 8 * LC], BF16) for i in range(2)]
    hstate_sb = sbuf("hstate_sb", [128, N * DC], BF16)
    fix_sb = sbuf("fix_sb", [128, DC], BF16)
    outst_sb = sbuf("outst_sb", [128, 2 * LC], BF16)
    stat_sb = sbuf("stat_sb", [1, 2 * LC], F32)
    gacc_sb = [sbuf(f"gacc_sb{i}", [M, LC], BF16) for i in range(2)]
    x1_sb = sbuf("x1_sb", [128, DM * LC], F32)
    xres_sb = sbuf("xres_sb", [128, DM * LC], F32)
    ys_sb = sbuf("ys_sb", [128, 2 * LC], F32)
    s2row_sb = sbuf("s2row_sb", [1, LC], F32)
    e1t_sb = sbuf("e1t_sb", [128, LC], BF16)
    dtrt_sb = sbuf("dtrt_sb", [R, LC], BF16)
    e1_sb = e1t_sb[:, 0:LC]
    dtr_sb = dtrt_sb[0:R, 0:LC]
    if INNER * LC <= DC * LC:
        hff_sb = a_sb
    else:
        hff_sb = sbuf("hff_sb", [M, INNER * LC], BF16)

    ps_rep = psum("ps_rep", [128, 512])
    ps_st = psum("ps_st", [1, 512])
    ps_sum = ps_st[0:1, 0:256]
    ps_sq = ps_st[0:1, 256:512]
    ps_xz = [psum(f"ps_xz{i}", [128, LC]) for i in range(2)]
    ps_conv = [psum(f"ps_conv{i}", [128, LC]) for i in range(2)]
    ps_small = psum("ps_small", [128, 512])
    ps_outb = [psum("ps_outb", [128, LC])]
    ps_out = [ps_outb[0], ps_outb[0]]

    assert LC <= 256

    sem_names = ["tensor", "vector", "scalar", "gpsimd", "sync",
                 "d_st", "d_x0", "d_x1", "d_w0", "d_w1", "d_bc0", "d_bc1",
                 "d_ow", "d_pl0", "d_pl1", "d_od", "d_g", "d_gx0", "d_gx1",
                 "d_bn0", "d_bn1", "cc", "cc1"]
    sems = {nm: stack.enter_context(nc.semaphore(nm)) for nm in sem_names}

    p = Prog(nc)
    W = {"xt": lambda i: ("xt", i), "xn": lambda k: ("xn", k)}

    XN = [("xn", k) for k in range(DM)]
    U = [("u", d) for d in range(DC)]
    AK = [("a", d) for d in range(DC)]
    BK = [("b", d) for d in range(DC)]
    YK = [("y", d) for d in range(DC)]
    def DTK(par):
        return [("dt", par, d) for d in range(DC)]
    GT = [("gt", 0, d) for d in range(DC)]
    XZU = [("xzu", d) for d in range(DC)]
    X1 = [("x1", k) for k in range(DM)]
    YS = [("ys", k) for k in range(DM)]
    OUTST = [("outst", k) for k in range(DM)]

    def startup():
        for eblk in range(EB):
            p.op("gpsimd", lambda g, eblk=eblk: g.dma_start(
                out=win_bf[eblk], in_=win_t4[eblk]),
                writes=[("win_bf", eblk)], dma=True, sem="d_g")
        for j in range(4):
            d0 = DC // 4 * j
            d1 = DC // 4 * (j + 1)
            p.op("gpsimd", lambda g, d0=d0, d1=d1: g.dma_start(
                out=wout_sb[:, d0 * D_MODEL:d1 * D_MODEL].rearrange(
                    "p (dc d) -> p dc d", d=D_MODEL),
                in_=wout_t.rearrange("(dc p) d -> p dc d", p=128)[:, d0:d1]),
                writes=[("wout_sb", j)], dma=True, sem="d_g")
        p.op("gpsimd", lambda g: g.dma_start(
            out=wx_sb[:].rearrange("p (dc r) -> p dc r", r=RN),
            in_=wx_t.rearrange("(dc p) r -> p dc r", p=128)),
            writes=[("wx_sb",)], dma=True, sem="d_g")
        p.op("gpsimd", lambda g: g.dma_start(out=wdt_sb[:], in_=wdt_t),
             writes=[("wdt_sb",)], dma=True, sem="d_g")
        p.op("gpsimd", lambda g: g.dma_start(
            out=winf_sb[:].rearrange("p (km m) -> p km m", m=M),
            in_=winf.rearrange("(km p) m -> p km m", p=128)),
            writes=[("winf_sb",)], dma=True, sem="d_g")
        p.op("gpsimd", lambda g: g.dma_start(out=woutf_sb[:], in_=woutf),
             writes=[("woutf_sb",)], dma=True, sem="d_g")
        p.op("gpsimd", lambda g: g.dma_start(out=convbr_sb[:], in_=convb_r),
             writes=[("convbr_sb",)], dma=True, sem="d_g")
        loads = [(A_sb, alog_c, "A_sb"), (dcol_sb, dcol, "dcol_sb"),
                 (bdt_sb, bdt_c, "bdt_sb"), (convw_sb, convw_c, "convw_sb"),
                 (gcol_sb, gcol, "gcol_sb"), (bcol_sb, bcol, "bcol_sb"),
                 (w1_sb, w1, "w1_sb"), (b1_sb, b1, "b1_sb"),
                 (w2_sb, w2, "w2_sb"), (b2_sb, b2c, "b2_sb"),
                 (binc_sb, binc, "binc_sb"), (bout_sb, bout_c, "bout_sb"),
                 (eye_sb, eye16, "eye_sb")]
        for t, src, rg in loads:
            p.op("sync", lambda s, t=t, src=src: s.dma_start(out=t[:], in_=src),
                 writes=[(rg,)], dma=True, sem="d_st")
        p.op("vector", lambda v: v.memset(ones_rf[:], 1.0), writes=[("ones_rf",)])
        p.op("vector", lambda v: v.memset(cvals_sb[0:1, 0:1], EPS * EPS),
             writes=[("cv", 0)])
        p.op("vector", lambda v: v.memset(cvals_sb[0:1, 1:2], 1.0 + EPS),
             writes=[("cv", 1)])
        p.op("vector", lambda v: v.memset(cvals_sb[0:1, 2:3], EPS),
             writes=[("cv", 2)])
        p.op("vector", lambda v: v.memset(cvals_sb[0:1, 3:4], -0.5),
             writes=[("cv", 3)])
        p.op("vector", lambda v: v.memset(half_c[:], 0.5), writes=[("half_c",)])
        p.op("vector", lambda v: v.memset(ones_rb[:], 1.0), writes=[("ones_rb",)])
        p.op("vector", lambda v: v.memset(onec_f[:], 1.0), writes=[("onec_f",)])
        p.op("vector", lambda v: v.memset(onec_b[:], 1.0), writes=[("onec_b",)])
        p.op("vector", lambda v: v.memset(hstate_sb[:], 0.0), writes=[("hstate",)])
        p.op("vector", lambda v: v.memset(xzu_sb[:], 0.0), writes=XZU)
        p.op("scalar", lambda s: s.activation(A_sb[:], A_sb[:], AF.Exp),
             reads=[("A_sb",)], writes=[("A_sb",)])
        p.op("vector", lambda v: v.tensor_scalar(
            A_sb[:], A_sb[:], -1.0, None, ALU.mult),
            reads=[("A_sb",)], writes=[("A_sb",)])
        for dc in range(DC):
            for k in range(CONV):
                j = dc * CONV + k
                p.op("vector", lambda v, j=j: v.tensor_scalar(
                    diag_sb[:, j * 128:(j + 1) * 128], eye_sb[:],
                    convw_sb[:, j:j + 1], None, ALU.mult),
                    reads=[("eye_sb",), ("convw_sb",)], writes=[("diag", j)])
        p.op("vector", lambda v: v.tensor_copy(bcolb_sb[:], bcol_sb[:]),
             reads=[("bcol_sb",)], writes=[("bcolb",)])
        WB = DM * 128
        nub = WB // LC
        stg = [u_sb[:, 0:WB], u_sb[:, WB:2 * WB]]
        stgk = [[("u", d) for d in range(nub)],
                [("u", d) for d in range(nub, 2 * nub)]]
        for eblk in range(EB):
            bf = eblk % 2
            p.op("sync", lambda s, eblk=eblk, bf=bf: s.dma_start(
                out=stg[bf][:], in_=win_bf[eblk]),
                reads=[("win_bf", eblk)], writes=stgk[bf], dma=True,
                sem=f"d_w{bf}")
            for k in range(DM):
                p.op("tensor", lambda t, k=k, bf=bf: t.matmul(
                    ps_rep[:, bf:bf + 1], stg[bf][:, k * 128:(k + 1) * 128],
                    bcolb_sb[:, k:k + 1],
                    start=(k == 0), stop=(k == DM - 1)),
                    reads=stgk[bf] + [("bcolb",)],
                    writes=[("ps_rep",)], accum=(k > 0))
            p.op("scalar", lambda s, eblk=eblk, bf=bf: s.copy(
                c_sb[:, eblk:eblk + 1], ps_rep[:, bf:bf + 1]),
                reads=[("ps_rep",)], writes=[("c_sb",)])
            for k in range(DM):
                p.op("vector", lambda v, k=k, bf=bf: v.tensor_scalar(
                    stg[bf][:, k * 128:(k + 1) * 128],
                    stg[bf][:, k * 128:(k + 1) * 128],
                    gcol_sb[:, k:k + 1], None, ALU.mult),
                    reads=stgk[bf] + [("gcol_sb",)], writes=stgk[bf])
            p.op("sync", lambda s, eblk=eblk, bf=bf: s.dma_start(
                out=win_bf[eblk], in_=stg[bf][:]),
                reads=stgk[bf], writes=[("win_bf", eblk)], dma=True,
                sem=f"d_w{bf}")
        p.op("vector", lambda v: v.tensor_scalar(
            bias1_sb[:], w1_sb[:], binc_sb[:, 0:1], None, ALU.mult),
            reads=[("w1_sb",), ("binc_sb",)], writes=[("bias1_sb",)])
        p.op("vector", lambda v: v.tensor_tensor(
            bias1_sb[:], bias1_sb[:], b1_sb[:], ALU.add),
            reads=[("bias1_sb",), ("b1_sb",)], writes=[("bias1_sb",)])

    def ln_stats_part(xt_ap, xt_rg, sq_buf, ones_col, ones_rg):
        for km in range(DM):
            p.op("tensor", lambda t, km=km: t.matmul(
                ps_sum[:, 0:LC], ones_col[:],
                xt_ap[:, km * LC:(km + 1) * LC],
                start=(km == 0), stop=(km == DM - 1)),
                reads=[xt_rg[km] if isinstance(xt_rg, list) else xt_rg,
                       ones_rg],
                writes=[("ps_sum",)], accum=(km > 0))
        for km in range(DM):
            sl = slice((km % 2) * LC, (km % 2 + 1) * LC)
            p.op("scalar", lambda s, km=km, sl=sl: s.activation(
                ys_sb[:, sl],
                xt_ap[:, km * LC:(km + 1) * LC], AF.Square),
                reads=[xt_rg[km] if isinstance(xt_rg, list) else xt_rg],
                writes=[("ys", km % 2)])
            p.op("tensor", lambda t, km=km, sl=sl: t.matmul(
                ps_sq[:, 0:LC], onec_f[:], ys_sb[:, sl],
                start=(km == 0), stop=(km == DM - 1)),
                reads=[("ys", km % 2), ("onec_f",)],
                writes=[("ps_sq",)], accum=(km > 0))

    def ln_stats_apply(xt_ap, xt_rg, sq_buf, combine12, ones_col=None,
                       ones_rg=("onec_f",), skip_stats=False):
        if ones_col is None:
            ones_col = onec_f
        if not skip_stats:
            ln_stats_part(xt_ap, xt_rg, sq_buf, ones_col, ones_rg)
        s0 = stat_sb[0:1, 0:LC]
        s1 = stat_sb[0:1, LC:2 * LC]
        s2 = s2row_sb[0:1, 0:LC]
        p.op("vector", lambda v: v.tensor_scalar(
            s0, ps_sum[:, 0:LC], 1.0 / D_MODEL, None, ALU.mult),
            reads=[("ps_sum",), ("ps_sq",)], writes=[("st0",)])
        p.op("vector", lambda v: v.tensor_scalar(
            s1, ps_sq[:, 0:LC], 1.0 / D_MODEL, None, ALU.mult),
            reads=[("ps_sq",), ("ps_sum",)], writes=[("st1",)])
        p.op("vector", lambda v: v.tensor_tensor(s2, s0, s0, ALU.mult),
             reads=[("st0",)], writes=[("st2",)])
        p.op("vector", lambda v: v.tensor_tensor(s1, s1, s2, ALU.subtract),
             reads=[("st1",), ("st2",)], writes=[("st1",)])
        if combine12:
            p.op("scalar", lambda s: s.activation(
                s1, s1, AF.Ln, bias=cvals_sb[0:1, 0:1],
                scale=cvals_sb[0:1, 1:2]),
                reads=[("st1",), ("cv", 0), ("cv", 1)], writes=[("st1",)])
        else:
            p.op("scalar", lambda s: s.activation(
                s1, s1, AF.Ln, bias=cvals_sb[0:1, 2:3]),
                reads=[("st1",), ("cv", 2)], writes=[("st1",)])
        p.op("scalar", lambda s: s.activation(
            s1, s1, AF.Exp, scale=cvals_sb[0:1, 3:4]),
            reads=[("st1",), ("cv", 3)], writes=[("st1",)])
        p.op("tensor", lambda t: t.matmul(
            ps_rep[:, 0:LC], ones_rf[:], s1, start=True, stop=True),
            reads=[("st1",), ("ones_rf",)], writes=[("ps_rep",)])
        p.op("vector", lambda v: v.tensor_tensor(s0, s0, s1, ALU.mult),
             reads=[("st0",), ("st1",)], writes=[("st0",)])
        p.op("tensor", lambda t: t.matmul(
            ps_rep[:, 256:256 + LC], ones_rf[:], s0, start=True, stop=True),
            reads=[("st0",), ("ones_rf",)], writes=[("ps_rep",)])

    def x_load(ch):
        par = ch % 2
        src = xT.rearrange("(km p) l -> p km l", p=128)[:, :, ch * LC:(ch + 1) * LC]
        p.op("gpsimd", lambda g, par=par, src=src: g.dma_start(
            out=xt_sb[par][:].rearrange("p (km l) -> p km l", l=LC), in_=src),
            writes=[("xt", par)], dma=True, sem=f"d_gx{par}")

    def ln12_stats(ch):
        par = ch % 2
        ln_stats_part(xt_sb[par][:], ("xt", par), None, onec_b, ("onec_b",))

    def ln12_mid(ch):
        par = ch % 2
        ln_stats_apply(xt_sb[par][:], ("xt", par), None, combine12=True,
                       ones_col=onec_b, ones_rg=("onec_b",), skip_stats=True)

    def ln12_apply(ch):
        par = ch % 2
        xt = xt_sb[par][:]
        xn = xn_sb[par][:]
        for km in range(DM):
            sl = slice(km * LC, (km + 1) * LC)
            r = km % 4
            if r < 2:
                tmp = ys_sb[:, r * LC:(r + 1) * LC]
                tkey = ("ys", r)
            else:
                tmp = x1_sb[:, (r - 2) * LC:(r - 1) * LC]
                tkey = ("x1", r - 2)
            p.op("vector", lambda v, sl=sl, tmp=tmp: v.tensor_tensor(
                tmp, xt[:, sl], ps_rep[:, 0:LC], ALU.mult),
                reads=[("xt", par), ("ps_rep",)], writes=[tkey])
            p.op("vector", lambda v, sl=sl, tmp=tmp, xn=xn: v.tensor_tensor(
                xn[:, sl], tmp, ps_rep[:, 256:256 + LC], ALU.subtract),
                reads=[tkey, ("ps_rep",)], writes=[("xn", par, km)])

    def xz_eblk(eblk, ch):
        par = eblk % 2
        cpar = ch % 2
        xn = xn_sb[cpar][:]
        p.op("sync", lambda s, par=par, eblk=eblk: s.dma_start(
            out=wbuf[par][:], in_=win_bf[eblk]),
            reads=[("win_bf", eblk)], writes=[("wbuf", par)], dma=True,
            sem=f"d_w{par}")
        for km in range(DM):
            p.op("tensor", lambda t, par=par, km=km, xn=xn: t.matmul(
                ps_xz[par][:],
                wbuf[par][:, km * 128:(km + 1) * 128],
                xn[:, km * LC:(km + 1) * LC],
                start=(km == 0), stop=(km == DM - 1)),
                reads=[("wbuf", par), ("xn", cpar, km)],
                writes=[("ps_xz", par)], accum=(km > 0))
        if eblk < DC:
            dc = eblk
            p.op("scalar", lambda s, dc=dc, par=par, eblk=eblk: s.activation(
                xzu_sb[:, dc * LCP + CONV - 1: dc * LCP + CONV - 1 + LC],
                ps_xz[par][:], AF.Identity, bias=c_sb[:, eblk:eblk + 1]),
                reads=[("ps_xz", par), ("c_sb",)], writes=[("xzu", dc)])
        else:
            dc = eblk - DC
            p.op("scalar", lambda s, dc=dc, par=par, cpar=cpar, eblk=eblk:
                 s.activation(
                    gt_sb[cpar][:, dc * LC:(dc + 1) * LC], ps_xz[par][:],
                    AF.Silu, bias=c_sb[:, eblk:eblk + 1]),
                 reads=[("ps_xz", par), ("c_sb",)], writes=[("gt", cpar, dc)])

    def xz_u(ch):
        for eblk in range(DC):
            xz_eblk(eblk, ch)

    def xz_z(ch):
        for eblk in range(DC, EB):
            xz_eblk(eblk, ch)

    def conv_silu(ch):
        for dc in range(DC):
            par = dc % 2
            for k in range(CONV):
                j = dc * CONV + k
                p.op("tensor", lambda t, j=j, dc=dc, k=k, par=par: t.matmul(
                    ps_conv[par][:],
                    diag_sb[:, j * 128:(j + 1) * 128],
                    xzu_sb[:, dc * LCP + k: dc * LCP + k + LC],
                    start=(k == 0), stop=False),
                    reads=[("diag", j), ("xzu", dc)],
                    writes=[("ps_conv", par)], accum=(k > 0))
            p.op("tensor", lambda t, dc=dc, par=par: t.matmul(
                ps_conv[par][:],
                convbr_sb[0:1, dc * 128:(dc + 1) * 128], ones_rb[:],
                start=False, stop=True),
                reads=[("convbr_sb",), ("ones_rb",)],
                writes=[("ps_conv", par)], accum=True)
            p.op("scalar", lambda s, dc=dc, par=par: s.activation(
                u_sb[:, dc * LC:(dc + 1) * LC], ps_conv[par][:], AF.Silu),
                reads=[("ps_conv", par)], writes=[("u", dc)])
        xv = xzu_sb[:].rearrange("p (dc l) -> p dc l", l=LCP)
        p.op("vector", lambda v, xv=xv: v.tensor_copy(
            xv[:, :, 0:CONV - 1], xv[:, :, LC:LC + CONV - 1]),
            reads=XZU, writes=XZU)

    def xdbl_dt(ch):
        par2 = ch % 2
        dt = dt_sb[par2]
        for dc in range(DC):
            p.op("tensor", lambda t, dc=dc: t.matmul(
                ps_small[0:RN, 0:LC],
                wx_sb[:, dc * RN:(dc + 1) * RN],
                u_sb[:, dc * LC:(dc + 1) * LC],
                start=(dc == 0), stop=(dc == DC - 1)),
                reads=[("wx_sb",), ("u", dc)],
                writes=[("ps_xdbl",)], accum=(dc > 0))
        p.op("scalar", lambda s: s.copy(dtr_sb, ps_small[0:R, 0:LC]),
             reads=[("ps_xdbl",)], writes=[("dtr",)])
        p.op("scalar", lambda s: s.copy(
            bc_sb[0:N, :], ps_small[R:R + N, 0:LC]),
            reads=[("ps_xdbl",)], writes=[("bc_b",)])
        p.op("scalar", lambda s: s.copy(
            bc_sb[32:32 + N, :], ps_small[96:96 + N, 0:LC]),
            reads=[("ps_xdbl",)], writes=[("bc_c",)])
        p.op("sync", lambda s, par2=par2: s.dma_start(
            out=bc_dram[par2][0:N], in_=bc_sb[0:N, :]),
            reads=[("bc_b",)], writes=[("bc_dram", par2)],
            dma=True, sem=f"d_bc{par2}")
        p.op("sync", lambda s, par2=par2: s.dma_start(
            out=bc_dram[par2][N:2 * N], in_=bc_sb[32:32 + N, :]),
            reads=[("bc_c",)], writes=[("bc_dram", par2)],
            dma=True, sem=f"d_bc{par2}")




    def dt_chain(ch, dcs):
        par2 = ch % 2
        dt = dt_sb[par2]
        dtslots = [ps_small[:, 256:256 + LC], ps_conv[0][:], ps_conv[1][:]]
        dtkeys = [("ps_dt",), ("ps_conv", 0), ("ps_conv", 1)]
        for dc in dcs:
            slot = dtslots[dc % 3]
            skey = dtkeys[dc % 3]
            wr = [skey] + ([("ps_xdbl",)] if dc == 0 else [])
            p.op("tensor", lambda t, dc=dc, slot=slot: t.matmul(
                slot, wdt_sb[:, dc * 128:(dc + 1) * 128], dtr_sb,
                start=True, stop=True),
                reads=[("wdt_sb",), ("dtr",)], writes=wr)
            p.op("scalar", lambda s, dc=dc, slot=slot: s.activation(
                e1_sb, slot, AF.Exp, bias=bdt_sb[:, dc:dc + 1]),
                reads=[skey, ("bdt_sb",)], writes=[("e1",)])
            p.op("scalar", lambda s, dc=dc, dt=dt: s.activation(
                dt[:, dc * LC:(dc + 1) * LC], e1_sb, AF.Ln,
                bias=onec_f[:, 0:1]),
                reads=[("e1",), ("onec_f",)], writes=[("dt", par2, dc)])

    def bcq_load(ch, q):
        par2 = ch % 2
        par = q % 2
        p.op("sync", lambda s, par2=par2, par=par, q=q: s.dma_start(
            out=bcq_sb[par][:, 0:4 * LC],
            in_=bc_dram[par2][4 * q:4 * q + 4].rearrange("r l -> (r l)")[
                None, :].broadcast_to([128, 4 * LC])),
            reads=[("bc_dram", par2)], writes=[("bcq", par)], dma=True,
            sem=f"d_bn{par}")
        p.op("sync", lambda s, par2=par2, par=par, q=q: s.dma_start(
            out=bcq_sb[par][:, 4 * LC:8 * LC],
            in_=bc_dram[par2][N + 4 * q:N + 4 * q + 4].rearrange(
                "r l -> (r l)")[None, :].broadcast_to([128, 4 * LC])),
            reads=[("bc_dram", par2)], writes=[("bcq", par)], dma=True,
            sem=f"d_bn{par}")

    def uw_y_init(ch):
        par2 = ch % 2
        p.op("vector", lambda v, par2=par2: v.tensor_tensor(
            uw_sb[:], dt_sb[par2][:], u_sb[:], ALU.mult),
            reads=DTK(par2) + U, writes=[("uw",)])
        for dc in range(DC):
            sl = slice(dc * LC, (dc + 1) * LC)
            p.op("scalar", lambda s, sl=sl, dc=dc: s.activation(
                y_sb[:, sl], u_sb[:, sl], AF.Copy,
                scale=dcol_sb[:, dc:dc + 1]),
                reads=[("u", dc), ("dcol_sb",)], writes=[("y", dc)])

    def scan_phase(ch):
        first = (ch == 0)
        av = a_sb[:].rearrange("p (dc l) -> p dc l", l=LC)
        bv = b_sb[:].rearrange("p (dc l) -> p dc l", l=LC)
        H = DC // 2
        nxt = ch + 1 < NCH
        usched = {}
        csched = {}
        for j in range(DC):
            usched.setdefault(2 + j // 2, []).append(j)
            csched.setdefault(3 + j // 2, []).append(j)
        tail_n = 3 + (DC - 1) // 2 + 1
        zsched = {}
        zslots = list(range(max(N - 6, tail_n), N))
        for i in range(DC):
            zsched.setdefault(zslots[i * len(zslots) // DC], []).append(DC + i)
        cpar = ch % 2
        if first:
            bcq_load(ch, 0)
        for n in range(N):
            for h in range(2):
                hdc = range(h * H, (h + 1) * H)
                hsl = slice(h * H * LC, (h + 1) * H * LC)
                HK = [("a", d) for d in hdc]
                HB = [("b", d) for d in hdc]
                if n > 0 or first:
                    for dc in hdc:
                        p.op("scalar", lambda s, n=n, dc=dc, cpar=cpar:
                             s.activation(
                                a_sb[:, dc * LC:(dc + 1) * LC],
                                dt_sb[cpar][:, dc * LC:(dc + 1) * LC],
                                AF.Exp,
                                scale=A_sb[:, dc * N + n:dc * N + n + 1]),
                             reads=[("dt", cpar, dc), ("A_sb",)],
                             writes=[("a", dc)])
                p.op("vector", lambda v, n=n, hsl=hsl: v.tensor_tensor(
                    b_sb[:, hsl].rearrange("p (dc l) -> p dc l", l=LC),
                    uw_sb[:, hsl].rearrange("p (dc l) -> p dc l", l=LC),
                    bcq_sb[(n // 4) % 2][:, (n % 4) * LC:(n % 4 + 1) * LC][
                        :, None, :].broadcast_to([128, H, LC]),
                    ALU.mult),
                    reads=[("uw",), ("bcq", (n // 4) % 2)], writes=HB)
                avh = av[:, h * H:(h + 1) * H]
                bvh = bv[:, h * H:(h + 1) * H]
                if not first:
                    p.op("vector", lambda v, n=n, avh=avh, h=h: v.tensor_tensor(
                        fix_sb[:, 0:H], avh[:, :, 0],
                        hstate_sb[:, n * DC + h * H:n * DC + (h + 1) * H],
                        ALU.mult),
                        reads=HK + [("hstate",)], writes=[("fix",)])
                    p.op("vector", lambda v, bvh=bvh: v.tensor_tensor(
                        bvh[:, :, 0], bvh[:, :, 0], fix_sb[:, 0:H], ALU.add),
                        reads=HB + [("fix",)], writes=HB)
                p.op("vector", lambda v, avh=avh: v.memset(avh[:, :, 0], 0.0),
                     reads=HK, writes=HK)
                p.op("vector", lambda v, hsl=hsl: v.tensor_tensor_scan(
                    b_sb[:, hsl], a_sb[:, hsl], b_sb[:, hsl], 0.0,
                    ALU.mult, ALU.add),
                    reads=HK + HB, writes=HB)
                p.op("vector", lambda v, n=n, bvh=bvh, h=h: v.tensor_copy(
                    hstate_sb[:, n * DC + h * H:n * DC + (h + 1) * H],
                    bvh[:, :, LC - 1]),
                    reads=HB, writes=[("hstate",)])
                hk = ("tp", h)
                p.op("vector", lambda v, n=n, hsl=hsl, h=h: v.tensor_tensor(
                    tprod_sb[:, hsl].rearrange("p (dc l) -> p dc l", l=LC),
                    b_sb[:, hsl].rearrange("p (dc l) -> p dc l", l=LC),
                    bcq_sb[(n // 4) % 2][:, (4 + n % 4) * LC:(4 + n % 4 + 1) * LC][
                        :, None, :].broadcast_to([128, H, LC]),
                    ALU.mult),
                    reads=HB + [("bcq", (n // 4) % 2)], writes=[hk])
                p.op("vector", lambda v, hsl=hsl: v.tensor_tensor(
                    y_sb[:, hsl], y_sb[:, hsl], tprod_sb[:, hsl], ALU.add),
                    reads=[hk] + YK, writes=YK)
            if n % 4 == 1 and n // 4 + 1 < 4:
                bcq_load(ch, n // 4 + 1)
            if nxt:
                for j in usched.get(n, []):
                    xz_eblk(j, ch + 1)
                for j in csched.get(n, []):
                    conv_dc_next(j)
                if n == tail_n:
                    conv_tail_next()
                for j in zsched.get(n, []):
                    xz_eblk(j, ch + 1)
                if n == 8 and ch + 2 < NCH:
                    ln12_stats(ch + 2)
                if n == 9 and ch + 2 < NCH:
                    ln12_mid(ch + 2)
                if n == 11:
                    xdbl_dt(ch + 1)
                if 12 <= n <= 15:
                    dt_chain(ch + 1,
                             range(DC * (n - 12) // 4, DC * (n - 11) // 4))
                if n == 13:
                    bcq_load(ch + 1, 0)
        if nxt:
            cpar2 = (ch + 1) % 2
            for dc in range(DC):
                p.op("scalar", lambda s, dc=dc, cpar2=cpar2: s.activation(
                    a_sb[:, dc * LC:(dc + 1) * LC],
                    dt_sb[cpar2][:, dc * LC:(dc + 1) * LC],
                    AF.Exp, scale=A_sb[:, dc * N:dc * N + 1]),
                    reads=[("dt", cpar2, dc), ("A_sb",)], writes=[("a", dc)])

    def conv_dc_next(dc):
        par = dc % 2
        for k in range(CONV):
            j = dc * CONV + k
            p.op("tensor", lambda t, j=j, dc=dc, k=k, par=par: t.matmul(
                ps_conv[par][:],
                diag_sb[:, j * 128:(j + 1) * 128],
                xzu_sb[:, dc * LCP + k: dc * LCP + k + LC],
                start=(k == 0), stop=False),
                reads=[("diag", j), ("xzu", dc)],
                writes=[("ps_conv", par)], accum=(k > 0))
        p.op("tensor", lambda t, dc=dc, par=par: t.matmul(
            ps_conv[par][:],
            convbr_sb[0:1, dc * 128:(dc + 1) * 128], ones_rb[:],
            start=False, stop=True),
            reads=[("convbr_sb",), ("ones_rb",)],
            writes=[("ps_conv", par)], accum=True)
        p.op("scalar", lambda s, dc=dc, par=par: s.activation(
            u_sb[:, dc * LC:(dc + 1) * LC], ps_conv[par][:], AF.Silu),
            reads=[("ps_conv", par)], writes=[("u", dc)])

    def conv_tail_next():
        xv = xzu_sb[:].rearrange("p (dc l) -> p dc l", l=LCP)
        p.op("vector", lambda v, xv=xv: v.tensor_copy(
            xv[:, :, 0:CONV - 1], xv[:, :, LC:LC + CONV - 1]),
            reads=XZU, writes=XZU)

    def gate_mult(ch):
        cpar = ch % 2
        xv = xzu_sb[:].rearrange("p (dc l) -> p dc l", l=LCP)
        p.op("vector", lambda v, cpar=cpar, xv=xv: v.tensor_tensor(
            xv[:, :, CONV - 1:CONV - 1 + LC],
            y_sb[:].rearrange("p (dc l) -> p dc l", l=LC),
            gt_sb[cpar][:].rearrange("p (dc l) -> p dc l", l=LC),
            ALU.mult),
            reads=YK + [("gt", cpar, d) for d in range(DC)], writes=XZU)

    def gate_mm(ch):
        for km in range(DM):
            par = km % 2
            for dc in range(DC):
                p.op("tensor", lambda t, km=km, dc=dc, par=par: t.matmul(
                    ps_out[par][:],
                    wout_sb[:, dc * D_MODEL + km * 128:
                            dc * D_MODEL + (km + 1) * 128],
                    xzu_sb[:, dc * LCP + CONV - 1: dc * LCP + CONV - 1 + LC],
                    start=(dc == 0), stop=(dc == DC - 1)),
                    reads=[("wout_sb", dc * 4 // DC), ("xzu", dc)],
                    writes=[("ps_o",)], accum=(dc > 0))
            ks = km % 2
            if ch < HCH:
                dst = outst_sb[:, ks * LC:(ks + 1) * LC]
            else:
                dst = outst_sb[:, ks * LC:(ks + 1) * LC][:, ::-1]
            p.op("scalar", lambda s, dst=dst, par=par: s.copy(
                dst, ps_out[par][:]),
                reads=[("ps_o",)], writes=[("outst", ks)])
            if ks == 1:
                kg = km - 1
                if ch < HCH:
                    dst2 = yown_dram.rearrange("(km p) l -> p km l", p=128)[
                        :, kg:kg + 2, ch * LC:(ch + 1) * LC]
                    rg = ("yown_dram", ch)
                else:
                    cc = NCH - 1 - ch
                    base = cc * LC
                    hi = base // (HALF // 2)
                    off = base % (HALF // 2)
                    dst2 = send_dram[hi].rearrange("(km p) l -> p km l", p=128)[
                        :, kg:kg + 2, off:off + LC]
                    rg = ("send_dram", cc)
                p.op("sync", lambda s, dst2=dst2: s.dma_start(
                    out=dst2,
                    in_=outst_sb[:].rearrange("p (km l) -> p km l", l=LC)),
                    reads=[("outst", 0), ("outst", 1)],
                    writes=[rg], dma=True, sem="d_ow")

    def collective_half(hi):
        groups = [[2 * i, 2 * i + 1] for i in range(NCORES // 2)]
        chs = [c for c in range(HCH)
               if hi * (HALF // 2) <= c * LC < (hi + 1) * (HALF // 2)]
        p.op("gpsimd", lambda g, hi=hi: g.collective_compute(
            "AllReduce", ALU.add, replica_groups=groups,
            ins=[send_dram[hi].opt()],
            outs=[ar_dram[hi].opt()]),
            reads=[("send_dram", c) for c in chs],
            writes=[("ar_dram", hi)], sem="cc" if hi == 0 else "cc1",
            inc=1)

    def post_chunk(pc):
        par = pc % 2
        xt = xres_sb[:]
        colsl = slice(pc * LC, (pc + 1) * LC)
        p.op("sync", lambda s: s.dma_start(
            out=xt.rearrange("p (km l) -> p km l", l=LC),
            in_=x_res.rearrange("(km p) l -> p km l", p=128)[:, :, colsl]),
            writes=[("xres",)], dma=True, sem=f"d_x{par}")
        p.op("sync", lambda s: s.dma_start(
            out=b_sb[:, 0:DM * LC].rearrange("p (km l) -> p km l", l=LC),
            in_=yown_dram.rearrange("(km p) l -> p km l", p=128)[:, :, colsl]),
            reads=[("yown_dram", pc)], writes=BK, dma=True, sem=f"d_pl{par}")
        arh = 0 if (pc + 1) * LC <= HALF // 2 else 1
        poff = pc * LC - arh * (HALF // 2)
        p.op("sync", lambda s: s.dma_start(
            out=uw_sb[:, 0:DM * LC].rearrange("p (km l) -> p km l", l=LC),
            in_=send_dram[arh].rearrange("(km p) l -> p km l", p=128)[
                :, :, poff:poff + LC]),
            reads=[("send_dram", pc)], writes=[("uw",)], dma=True,
            sem=f"d_pl{par}")
        p.op("sync", lambda s: s.dma_start(
            out=gt_sb[0][:, 0:DM * LC].rearrange("p (km l) -> p km l", l=LC),
            in_=ar_dram[arh].rearrange("(km p) l -> p km l", p=128)[
                :, :, poff:poff + LC]),
            reads=[("ar_dram", arh)], writes=GT, dma=True, sem=f"d_pl{par}")
        p.op("vector", lambda v: v.tensor_tensor(
            a_sb[:, 0:DM * LC], gt_sb[0][:, 0:DM * LC], uw_sb[:, 0:DM * LC],
            ALU.subtract),
            reads=GT + [("uw",)], writes=AK)
        p.op("vector", lambda v: v.tensor_tensor(
            dt_sb[0][:, 0:DM * LC], a_sb[:, 0:DM * LC], b_sb[:, 0:DM * LC],
            ALU.add),
            reads=AK + BK, writes=DTK(0))

        ln_stats_apply(xt, ("xres",), None, combine12=False)
        for km in range(DM):
            sl = slice(km * LC, (km + 1) * LC)
            ts = slice((km % 2) * LC, (km % 2 + 1) * LC)
            p.op("vector", lambda v, sl=sl, ts=ts: v.tensor_tensor(
                ys_sb[:, ts], xt[:, sl], ps_rep[:, 0:LC], ALU.mult),
                reads=[("xres",), ("ps_rep",)], writes=[("ys", km % 2)])
            p.op("vector", lambda v, ts=ts: v.tensor_tensor(
                ys_sb[:, ts], ys_sb[:, ts], ps_rep[:, 256:256 + LC],
                ALU.subtract),
                reads=[("ys", km % 2), ("ps_rep",)],
                writes=[("ys", km % 2)])
            p.op("vector", lambda v, sl=sl, ts=ts: v.tensor_tensor(
                ys_sb[:, ts], ys_sb[:, ts], xt[:, sl], ALU.add),
                reads=[("ys", km % 2), ("xres",)],
                writes=[("ys", km % 2)])
            p.op("vector", lambda v, sl=sl, ts=ts: v.tensor_tensor(
                x1_sb[:, sl], ys_sb[:, ts], dt_sb[0][:, sl], ALU.add),
                reads=[("ys", km % 2), ("dt", 0, sl.start // LC)],
                writes=[("x1", sl.start // LC)])

        ln_stats_apply(x1_sb[:], X1, None, combine12=False)
        for km in range(DM):
            sl = slice(km * LC, (km + 1) * LC)
            ts = slice((km % 2) * LC, (km % 2 + 1) * LC)
            p.op("vector", lambda v, sl=sl, ts=ts: v.tensor_tensor(
                ys_sb[:, ts], x1_sb[:, sl], ps_rep[:, 0:LC], ALU.mult),
                reads=[("x1", km), ("ps_rep",)], writes=[("ys", km % 2)])
            p.op("vector", lambda v, sl=sl, ts=ts, km=km: v.tensor_tensor(
                xn_sb[0][:, sl], ys_sb[:, ts], ps_rep[:, 256:256 + LC],
                ALU.subtract),
                reads=[("ys", km % 2), ("ps_rep",)], writes=[("xn", 0, km)])

        for km in range(DM):
            wr = [("ps_ffs",)] + ([("ps_xdbl",), ("ps_dt",)] if km == 0 else [])
            p.op("tensor", lambda t, km=km: t.matmul(
                ps_small[0:M, 0:LC], winf_sb[:, km * M:(km + 1) * M],
                xn_sb[0][:, km * LC:(km + 1) * LC],
                start=(km == 0), stop=(km == DM - 1)),
                reads=[("winf_sb",), ("xn", 0, km)],
                writes=wr, accum=(km > 0))
        for i in range(INNER):
            wr = [("hff", i)] + (AK if (pc == 0 and i == 0) else [])
            p.op("scalar", lambda s, i=i: s.activation(
                hff_sb[0:M, i * LC:(i + 1) * LC], ps_small[0:M, 0:LC], AF.Relu,
                bias=bias1_sb[:, i:i + 1], scale=w1_sb[:, i:i + 1]),
                reads=[("ps_ffs",), ("bias1_sb",), ("w1_sb",)],
                writes=wr)
        p.op("vector", lambda v: v.tensor_scalar(
            gacc_sb[0][:], hff_sb[0:M, 0:LC], w2_sb[:, 0:1], b2_sb[:, 0:1],
            ALU.mult, ALU.add),
            reads=[("hff", 0), ("w2_sb",), ("b2_sb",)], writes=[("gacc", 0)])
        for i in range(1, INNER):
            sp, dp = (i - 1) % 2, i % 2
            p.op("vector", lambda v, i=i, sp=sp, dp=dp: v.scalar_tensor_tensor(
                gacc_sb[dp][:], hff_sb[0:M, i * LC:(i + 1) * LC],
                w2_sb[:, i:i + 1], gacc_sb[sp][:], ALU.mult, ALU.add),
                reads=[("hff", i), ("w2_sb",), ("gacc", sp)],
                writes=[("gacc", dp)])
        gfin = gacc_sb[(INNER - 1) % 2]
        gfin_rg = ("gacc", (INNER - 1) % 2)
        for km in range(DM):
            par2 = km % 2
            p.op("tensor", lambda t, km=km, par2=par2: t.matmul(
                ps_out[par2][:], woutf_sb[:, km * 128:(km + 1) * 128],
                gfin[:], start=True, stop=True),
                reads=[("woutf_sb",), gfin_rg], writes=[("ps_o",)])
            p.op("vector", lambda v, km=km, par2=par2: v.scalar_tensor_tensor(
                xres_sb[:, km * LC:(km + 1) * LC], ps_out[par2][:],
                bout_sb[:, km:km + 1], x1_sb[:, km * LC:(km + 1) * LC],
                ALU.add, ALU.add),
                reads=[("ps_o",), ("bout_sb",), ("x1", km)],
                writes=[("xres",)])
        p.op("sync", lambda s: s.dma_start(
            out=out_d.rearrange("(km p) l -> p km l", p=128)[:, :, colsl],
            in_=xres_sb[:].rearrange("p (km l) -> p km l", l=LC)),
            reads=[("xres",)], writes=[("out_d", pc)], dma=True, sem="d_od")

    startup()
    x_load(0)
    ln12_stats(0)
    ln12_mid(0)
    ln12_apply(0)
    x_load(1)
    xz_u(0)
    conv_silu(0)
    xz_z(0)
    xdbl_dt(0)
    dt_chain(0, range(DC))
    if NCH > 1:
        ln12_stats(1)
        ln12_mid(1)
    for k in range(NCH):
        if k > 0:
            gate_mult(k - 1)
        uw_y_init(k)
        if k > 0:
            gate_mm(k - 1)
        if k == NCH - HCH // 2:
            collective_half(1)
        if k + 1 < NCH:
            ln12_apply(k + 1)
        if k + 2 < NCH:
            x_load(k + 2)
        scan_phase(k)
    gate_mult(NCH - 1)
    gate_mm(NCH - 1)
    collective_half(0)
    for pc in list(range(HCH // 2, HCH)) + list(range(HCH // 2)):
        post_chunk(pc)

    for snm in ["d_st", "d_x0", "d_x1", "d_w0", "d_w1", "d_bc0", "d_bc1",
                "d_ow", "d_pl0", "d_pl1", "d_od", "d_bn0", "d_bn1"]:
        p.q["sync"].append(("wait", snm, p.cnt.get(snm, 0)))
    p.q["gpsimd"].append(("wait", "d_g", p.cnt.get("d_g", 0)))
    p.q["gpsimd"].append(("wait", "d_gx0", p.cnt.get("d_gx0", 0)))
    p.q["gpsimd"].append(("wait", "d_gx1", p.cnt.get("d_gx1", 0)))
    p.q["gpsimd"].append(("wait", "cc", p.cnt.get("cc", 0)))
    p.q["gpsimd"].append(("wait", "cc1", p.cnt.get("cc1", 0)))

    with nc.Block() as block:
        p.finalize(block, sems)
    stack.close()
    return nc


def _prep_core_inputs(cfg, x_b, flip, w, ffn):
    DM, DC, L = cfg["DM"], cfg["DC"], cfg["L"]
    N, R, CONV, M = cfg["N"], cfg["R"], cfg["CONV"], cfg["M"]
    D_MODEL, D_INNER = DM * 128, DC * 128
    EB = 2 * DC
    HALF = L // 2
    C = np.ascontiguousarray

    xd = x_b[::-1] if flip else x_b
    ins = {}
    ins["xT"] = C(xd.T.astype(np.float32))
    ins["x_res"] = C(xd[:HALF].T.astype(np.float32))
    win = np.asarray(w["Win"], dtype=np.float32)
    ins["win_t4"] = C(win.reshape(EB, 128, DM, 128).transpose(0, 3, 2, 1))
    ins["wout_t"] = C(np.asarray(w["Wout"], np.float32).T)
    wxT = np.asarray(w["Wx"], np.float32).T
    wxp = np.zeros((wxT.shape[0], 128), np.float32)
    wxp[:, 0:R] = wxT[:, 0:R]
    wxp[:, R:R + N] = wxT[:, R:R + N]
    wxp[:, 96:96 + N] = wxT[:, R + N:R + 2 * N]
    ins["wx_t"] = C(wxp)
    ins["wdt_t"] = C(np.asarray(w["Wdt"], np.float32).T)
    ins["alog_c"] = C(np.asarray(w["Alog"], np.float32)
                      .reshape(DC, 128, N).transpose(1, 0, 2).reshape(128, DC * N))
    ins["dcol"] = C(np.asarray(w["D"], np.float32).reshape(DC, 128).T)
    ins["bdt_c"] = C(np.asarray(w["bdt"], np.float32).reshape(DC, 128).T)
    ins["convw_c"] = C(np.asarray(w["convw"], np.float32)[:, 0, :]
                       .reshape(DC, 128, CONV).transpose(1, 0, 2)
                       .reshape(128, DC * CONV))
    ins["convb_r"] = C(np.asarray(w["convb"], np.float32)[None, :])
    ins["gcol"] = C(np.asarray(w["norm_g"], np.float32).reshape(DM, 128).T)
    ins["bcol"] = C(np.asarray(w["norm_b"], np.float32).reshape(DM, 128).T)
    ins["winf"] = C(np.asarray(ffn["Win"], np.float32))
    ins["woutf"] = C(np.asarray(ffn["Wout"], np.float32))
    ins["w1"] = C(np.asarray(ffn["W1"], np.float32))
    ins["b1"] = C(np.asarray(ffn["b1"], np.float32))
    ins["w2"] = C(np.asarray(ffn["W2"], np.float32))
    ins["b2c"] = C(np.asarray(ffn["b2"], np.float32)[:, None])
    ins["binc"] = C(np.asarray(ffn["bin_"], np.float32)[:, None])
    ins["bout_c"] = C(np.asarray(ffn["bout"], np.float32).reshape(DM, 128).T)
    ins["eye16"] = np.eye(128, dtype=ml_dtypes.bfloat16)
    return ins


def _assemble(cfg, shards):
    DM, L = cfg["DM"], cfg["L"]
    D_MODEL, HALF = DM * 128, L // 2
    B = len(shards) // 2
    out = np.empty((B, L, D_MODEL), dtype=np.float32)
    for b in range(B):
        out[b, :HALF] = shards[2 * b].T
        out[b, HALF:] = shards[2 * b + 1].T[::-1]
    return out


MAMBA_KEYS = ["Win", "convw", "convb", "Wx", "Wdt", "bdt", "Alog", "D", "Wout"]


def kernel(**inputs):
    cfg = FULL_CFG
    x = np.asarray(inputs["L0_attn_x"], dtype=np.float32)
    Bn = x.shape[0]
    wf = {k: np.asarray(inputs["L0_attn_f_" + k]) for k in MAMBA_KEYS}
    wb = {k: np.asarray(inputs["L0_attn_b_" + k]) for k in MAMBA_KEYS}
    for d in (wf, wb):
        d["norm_g"] = np.asarray(inputs["L0_attn_norm_g"])
        d["norm_b"] = np.asarray(inputs["L0_attn_norm_b"])
    ffn = {k: np.asarray(inputs["L0_ffn_" + k]) for k in
           ["Win", "bin_", "W1", "b1", "W2", "b2", "Wout", "bout"]}

    nc = build_core_program(cfg)
    in_maps = []
    for b in range(Bn):
        in_maps.append(_prep_core_inputs(cfg, x[b], False, wf, ffn))
        in_maps.append(_prep_core_inputs(cfg, x[b], True, wb, ffn))
    res = run_bass_kernel_spmd(nc, in_maps, core_ids=list(range(2 * Bn)))
    shards = [np.asarray(r["out"]) for r in res.results]
    return _assemble(cfg, shards)


# version 29
# speedup vs baseline: 1.0901x; 1.0218x over previous
import numpy as np
import ml_dtypes
from contextlib import ExitStack

import concourse.bass as bass
import concourse.mybir as mybir
from concourse.bass_utils import run_bass_kernel_spmd

F32 = mybir.dt.float32
BF16 = mybir.dt.bfloat16
AF = mybir.ActivationFunctionType
ALU = mybir.AluOpType

EPS = 1e-5

FULL_CFG = dict(DM=8, DC=16, L=4096, LC=256, N=16, R=64, CONV=4, M=64,
                INNER=16, NCORES=8)


class Prog:
    ENGINES = ("tensor", "vector", "scalar", "gpsimd", "sync")

    @staticmethod
    def is_dma_sem(sem):
        return sem.startswith("d_") or sem.startswith("cc")

    def __init__(self, nc):
        self.nc = nc
        self.q = {e: [] for e in self.ENGINES}
        self.cnt = {}
        self.state = {}
        self.waited = {}
        self.opidx = {e: 0 for e in self.ENGINES}
        self.last_drain = {e: -1 for e in self.ENGINES}

    def _need(self, eng, stamp, drains):
        sem, val, peng, pidx = stamp
        if self.is_dma_sem(sem):
            val = self.cnt.get(sem, 0)
        k = (eng, sem)
        if self.waited.get(k, 0) < val:
            self.waited[k] = val
            self.q[eng].append(("wait", sem, val))

    def op(self, eng, fn, reads=(), writes=(), dma=False, sem=None, inc=None,
           accum=False):
        drains = []
        for r in reads:
            st = self.state.get(r)
            if st and st["w"]:
                self._need(eng, st["w"], drains)
        if not accum:
            for r in writes:
                st = self.state.get(r)
                if st:
                    if st["w"]:
                        sem_, val_, peng_, pidx_ = st["w"]
                        if peng_ != eng or self.is_dma_sem(sem_):
                            self._need(eng, st["w"], drains)
                    for stamp in st["r"].values():
                        self._need(eng, stamp, drains)
        if drains and max(drains) > self.last_drain[eng]:
            self.q[eng].append(("drain",))
            self.last_drain[eng] = self.opidx[eng]
        if sem is None:
            assert not dma, "DMA ops must name their semaphore"
            sem = eng
        if inc is None:
            inc = 16 if dma else 1
        self.cnt[sem] = self.cnt.get(sem, 0) + inc
        val = self.cnt[sem]
        self.opidx[eng] += 1
        stamp = (sem, val, eng, self.opidx[eng])
        self.q[eng].append(("op", fn, sem, inc, val))
        for r in writes:
            self.state[r] = {"w": stamp, "r": {}}
        for r in reads:
            st = self.state.setdefault(r, {"w": None, "r": {}})
            st["r"][eng] = stamp
        return stamp

    def finalize(self, block, sems):
        waited = {}
        for ename in self.ENGINES:
            for it in self.q[ename]:
                if it[0] == "wait" and not self.is_dma_sem(it[1]):
                    waited.setdefault(it[1], set()).add(it[2])
        remap = {}
        for sem, vals in waited.items():
            remap[sem] = {v: i + 1 for i, v in enumerate(sorted(vals))}

        for ename in self.ENGINES:
            items = self.q[ename]

            def make_body(items):
                def body(engine):
                    for it in items:
                        if it[0] == "wait":
                            sem, val = it[1], it[2]
                            if sem in remap:
                                val = remap[sem][val]
                            engine.wait_ge(sems[sem], val)
                        elif it[0] == "drain":
                            engine.drain()
                        else:
                            _, fn, sem, inc, val = it
                            inst = fn(engine)
                            if self.is_dma_sem(sem):
                                inst.then_inc(sems[sem], inc)
                            elif sem in remap and val in remap[sem]:
                                inst.then_inc(sems[sem], inc)
                return body
            getattr(block, ename)(make_body(items))


def build_core_program(cfg):
    DM, DC, L, LC = cfg["DM"], cfg["DC"], cfg["L"], cfg["LC"]
    N, R, CONV = cfg["N"], cfg["R"], cfg["CONV"]
    M, INNER = cfg["M"], cfg["INNER"]
    NCORES = cfg["NCORES"]
    D_MODEL = DM * 128
    D_INNER = DC * 128
    NCH = L // LC
    HCH = NCH // 2
    HALF = L // 2
    EB = 2 * DC
    LCP = LC + CONV - 1
    RN = 128

    nc = bass.Bass("TRN2", target_bir_lowering=False)

    def din(name, shape, dt=F32):
        return nc.dram_tensor(name, list(shape), dt, kind="ExternalInput").ap()

    xT = din("xT", [D_MODEL, L])
    x_res = din("x_res", [D_MODEL, HALF])
    win_t4 = din("win_t4", [EB, 128, DM, 128])
    wout_t = din("wout_t", [D_INNER, D_MODEL])
    wx_t = din("wx_t", [D_INNER, RN])
    wdt_t = din("wdt_t", [R, D_INNER])
    alog_c = din("alog_c", [128, DC * N])
    dcol = din("dcol", [128, DC])
    bdt_c = din("bdt_c", [128, DC])
    convw_c = din("convw_c", [128, DC * CONV])
    convb_r = din("convb_r", [1, D_INNER])
    gcol = din("gcol", [128, DM])
    bcol = din("bcol", [128, DM])
    winf = din("winf", [D_MODEL, M])
    woutf = din("woutf", [M, D_MODEL])
    w1 = din("w1", [M, INNER])
    b1 = din("b1", [M, INNER])
    w2 = din("w2", [M, INNER])
    b2c = din("b2c", [M, 1])
    binc = din("binc", [M, 1])
    bout_c = din("bout_c", [128, DM])
    eye16 = din("eye16", [128, 128], BF16)

    out_d = nc.dram_tensor("out", [D_MODEL, HALF], F32, kind="ExternalOutput").ap()

    win_bf = nc.dram_tensor("win_bf", [EB, 128, DM * 128], BF16).ap()
    bc_dram = [nc.dram_tensor(f"bc_dram{i}", [2 * N, LC], BF16).ap()
               for i in range(2)]
    yown_dram = nc.dram_tensor("yown_dram", [D_MODEL, HALF], BF16).ap()
    send_dram = nc.dram_tensor("send_dram", [2, D_MODEL, HALF // 2], BF16).ap()
    ar_dram = nc.dram_tensor("ar_dram", [2, D_MODEL, HALF // 2], BF16).ap()

    stack = ExitStack()

    def sbuf(name, shape, dt):
        return stack.enter_context(nc.sbuf_tensor(name, list(shape), dt))

    def psum(name, shape, dt=F32):
        return stack.enter_context(nc.psum_tensor(name, list(shape), dt))

    wout_sb = sbuf("wout_sb", [128, DC * D_MODEL], BF16)
    wx_sb = sbuf("wx_sb", [128, DC * RN], BF16)
    wdt_sb = sbuf("wdt_sb", [R, D_INNER], BF16)
    winf_sb = sbuf("winf_sb", [128, DM * M], BF16)
    woutf_sb = sbuf("woutf_sb", [M, D_MODEL], BF16)
    diag_sb = sbuf("diag_sb", [128, DC * CONV * 128], BF16)
    eye_sb = sbuf("eye_sb", [128, 128], BF16)
    A_sb = sbuf("A_sb", [128, DC * N], F32)
    dcol_sb = sbuf("dcol_sb", [128, DC], F32)
    bdt_sb = sbuf("bdt_sb", [128, DC], F32)
    convw_sb = sbuf("convw_sb", [128, DC * CONV], F32)
    convbr_sb = sbuf("convbr_sb", [1, D_INNER], BF16)
    gcol_sb = sbuf("gcol_sb", [128, DM], F32)
    bcol_sb = sbuf("bcol_sb", [128, DM], F32)
    w1_sb = sbuf("w1_sb", [M, INNER], F32)
    b1_sb = sbuf("b1_sb", [M, INNER], F32)
    w2_sb = sbuf("w2_sb", [M, INNER], F32)
    bias1_sb = sbuf("bias1_sb", [M, INNER], F32)
    b2_sb = sbuf("b2_sb", [M, 1], F32)
    binc_sb = sbuf("binc_sb", [M, 1], F32)
    bout_sb = sbuf("bout_sb", [128, DM], F32)
    ones_rf = sbuf("ones_rf", [1, 128], F32)
    cvals_sb = sbuf("cvals_sb", [1, 4], F32)
    half_c = sbuf("half_c", [128, 1], F32)
    ones_rb = sbuf("ones_rb", [1, LC], BF16)
    onec_f = sbuf("onec_f", [128, 1], F32)
    onec_b = sbuf("onec_b", [128, 1], BF16)
    c_sb = sbuf("c_sb", [128, EB], F32)
    bcolb_sb = sbuf("bcolb_sb", [128, DM], BF16)

    wbuf = [sbuf(f"wbuf{i}", [128, DM * 128], BF16) for i in range(2)]
    xt_sb = [sbuf(f"xt_sb{i}", [128, DM * LC], BF16) for i in range(2)]
    xn_sb = [sbuf(f"xn_sb{i}", [128, DM * LC], BF16) for i in range(2)]
    xzu_sb = sbuf("xzu_sb", [128, DC * LCP], BF16)
    u_sb = sbuf("u_sb", [128, DC * LC], BF16)
    gt_sb = [sbuf(f"gt_sb{i}", [128, DC * LC], BF16) for i in range(2)]
    dt_sb = [sbuf(f"dt_sb{i}", [128, DC * LC], BF16) for i in range(2)]
    uw_sb = sbuf("uw_sb", [128, DC * LC], BF16)
    a_sb = sbuf("a_sb", [128, DC * LC], BF16)
    b_sb = sbuf("b_sb", [128, DC * LC], BF16)
    y_sb = sbuf("y_sb", [128, DC * LC], BF16)
    tprod_sb = sbuf("tprod_sb", [128, DC * LC], BF16)
    bc_sb = sbuf("bc_sb", [64, LC], BF16)
    bcq_sb = [sbuf(f"bcq_sb{i}", [128, 8 * LC], BF16) for i in range(2)]
    hstate_sb = sbuf("hstate_sb", [128, N * DC], BF16)
    fix_sb = sbuf("fix_sb", [128, DC], BF16)
    outst_sb = sbuf("outst_sb", [128, 2 * LC], BF16)
    stat_sb = sbuf("stat_sb", [1, 2 * LC], F32)
    gacc_sb = [sbuf(f"gacc_sb{i}", [M, LC], BF16) for i in range(2)]
    x1_sb = sbuf("x1_sb", [128, DM * LC], F32)
    xres_sb = sbuf("xres_sb", [128, DM * LC], F32)
    ys_sb = sbuf("ys_sb", [128, 2 * LC], F32)
    s2row_sb = sbuf("s2row_sb", [1, LC], F32)
    e1t_sb = sbuf("e1t_sb", [128, LC], BF16)
    dtrt_sb = sbuf("dtrt_sb", [R, LC], BF16)
    e1_sb = e1t_sb[:, 0:LC]
    dtr_sb = dtrt_sb[0:R, 0:LC]
    if INNER * LC <= DC * LC:
        hff_sb = a_sb
    else:
        hff_sb = sbuf("hff_sb", [M, INNER * LC], BF16)

    ps_rep = psum("ps_rep", [128, 512])
    ps_st = psum("ps_st", [1, 512])
    ps_sum = ps_st[0:1, 0:256]
    ps_sq = ps_st[0:1, 256:512]
    ps_xz = [psum(f"ps_xz{i}", [128, LC]) for i in range(2)]
    ps_conv = [psum(f"ps_conv{i}", [128, LC]) for i in range(2)]
    ps_small = psum("ps_small", [128, 512])
    ps_outb = [psum("ps_outb", [128, LC])]
    ps_out = [ps_outb[0], ps_outb[0]]

    assert LC <= 256

    sem_names = ["tensor", "vector", "scalar", "gpsimd", "sync",
                 "d_st", "d_x0", "d_x1", "d_w0", "d_w1", "d_bc0", "d_bc1",
                 "d_ow", "d_pl0", "d_pl1", "d_od", "d_g", "d_gx0", "d_gx1",
                 "d_bn0", "d_bn1", "cc", "cc1", "d_g2", "d_wo"]
    sems = {nm: stack.enter_context(nc.semaphore(nm)) for nm in sem_names}

    p = Prog(nc)
    W = {"xt": lambda i: ("xt", i), "xn": lambda k: ("xn", k)}

    XN = [("xn", k) for k in range(DM)]
    U = [("u", d) for d in range(DC)]
    AK = [("a", d) for d in range(DC)]
    BK = [("b", d) for d in range(DC)]
    YK = [("y", d) for d in range(DC)]
    def DTK(par):
        return [("dt", par, d) for d in range(DC)]
    GT = [("gt", 0, d) for d in range(DC)]
    XZU = [("xzu", d) for d in range(DC)]
    X1 = [("x1", k) for k in range(DM)]
    YS = [("ys", k) for k in range(DM)]
    OUTST = [("outst", k) for k in range(DM)]

    def startup():
        for eblk in range(EB):
            q = eblk % 2
            p.op("gpsimd", lambda g, eblk=eblk: g.dma_start(
                out=win_bf[eblk], in_=win_t4[eblk]),
                writes=[("win_bf", eblk)], dma=True,
                sem="d_g" if q == 0 else "d_g2")
        p.op("gpsimd", lambda g: g.dma_start(
            out=wx_sb[:].rearrange("p (dc r) -> p dc r", r=RN),
            in_=wx_t.rearrange("(dc p) r -> p dc r", p=128)),
            writes=[("wx_sb",)], dma=True, sem="d_g")
        p.op("gpsimd", lambda g: g.dma_start(out=wdt_sb[:], in_=wdt_t),
             writes=[("wdt_sb",)], dma=True, sem="d_g")
        p.op("gpsimd", lambda g: g.dma_start(
            out=winf_sb[:].rearrange("p (km m) -> p km m", m=M),
            in_=winf.rearrange("(km p) m -> p km m", p=128)),
            writes=[("winf_sb",)], dma=True, sem="d_g")
        p.op("gpsimd", lambda g: g.dma_start(out=woutf_sb[:], in_=woutf),
             writes=[("woutf_sb",)], dma=True, sem="d_g")
        p.op("gpsimd", lambda g: g.dma_start(out=convbr_sb[:], in_=convb_r),
             writes=[("convbr_sb",)], dma=True, sem="d_g")
        loads = [(A_sb, alog_c, "A_sb"), (dcol_sb, dcol, "dcol_sb"),
                 (bdt_sb, bdt_c, "bdt_sb"), (convw_sb, convw_c, "convw_sb"),
                 (gcol_sb, gcol, "gcol_sb"), (bcol_sb, bcol, "bcol_sb"),
                 (w1_sb, w1, "w1_sb"), (b1_sb, b1, "b1_sb"),
                 (w2_sb, w2, "w2_sb"), (b2_sb, b2c, "b2_sb"),
                 (binc_sb, binc, "binc_sb"), (bout_sb, bout_c, "bout_sb"),
                 (eye_sb, eye16, "eye_sb")]
        for t, src, rg in loads:
            p.op("sync", lambda s, t=t, src=src: s.dma_start(out=t[:], in_=src),
                 writes=[(rg,)], dma=True, sem="d_st")
        p.op("vector", lambda v: v.memset(ones_rf[:], 1.0), writes=[("ones_rf",)])
        p.op("vector", lambda v: v.memset(cvals_sb[0:1, 0:1], EPS * EPS),
             writes=[("cv", 0)])
        p.op("vector", lambda v: v.memset(cvals_sb[0:1, 1:2], 1.0 + EPS),
             writes=[("cv", 1)])
        p.op("vector", lambda v: v.memset(cvals_sb[0:1, 2:3], EPS),
             writes=[("cv", 2)])
        p.op("vector", lambda v: v.memset(cvals_sb[0:1, 3:4], -0.5),
             writes=[("cv", 3)])
        p.op("vector", lambda v: v.memset(half_c[:], 0.5), writes=[("half_c",)])
        p.op("vector", lambda v: v.memset(ones_rb[:], 1.0), writes=[("ones_rb",)])
        p.op("vector", lambda v: v.memset(onec_f[:], 1.0), writes=[("onec_f",)])
        p.op("vector", lambda v: v.memset(onec_b[:], 1.0), writes=[("onec_b",)])
        p.op("vector", lambda v: v.memset(hstate_sb[:], 0.0), writes=[("hstate",)])
        p.op("vector", lambda v: v.memset(xzu_sb[:], 0.0), writes=XZU)
        p.op("scalar", lambda s: s.activation(A_sb[:], A_sb[:], AF.Exp),
             reads=[("A_sb",)], writes=[("A_sb",)])
        p.op("vector", lambda v: v.tensor_scalar(
            A_sb[:], A_sb[:], -1.0, None, ALU.mult),
            reads=[("A_sb",)], writes=[("A_sb",)])
        for dc in range(DC):
            for k in range(CONV):
                j = dc * CONV + k
                p.op("vector", lambda v, j=j: v.tensor_scalar(
                    diag_sb[:, j * 128:(j + 1) * 128], eye_sb[:],
                    convw_sb[:, j:j + 1], None, ALU.mult),
                    reads=[("eye_sb",), ("convw_sb",)], writes=[("diag", j)])
        p.op("vector", lambda v: v.tensor_scalar(
            bias1_sb[:], w1_sb[:], binc_sb[:, 0:1], None, ALU.mult),
            reads=[("w1_sb",), ("binc_sb",)], writes=[("bias1_sb",)])
        p.op("vector", lambda v: v.tensor_tensor(
            bias1_sb[:], bias1_sb[:], b1_sb[:], ALU.add),
            reads=[("bias1_sb",), ("b1_sb",)], writes=[("bias1_sb",)])

    def load_wout():
        for j in range(4):
            d0 = DC // 4 * j
            d1 = DC // 4 * (j + 1)
            p.op("gpsimd", lambda g, d0=d0, d1=d1: g.dma_start(
                out=wout_sb[:, d0 * D_MODEL:d1 * D_MODEL].rearrange(
                    "p (dc d) -> p dc d", d=D_MODEL),
                in_=wout_t.rearrange("(dc p) d -> p dc d", p=128)[:, d0:d1]),
                writes=[("wout_sb", j)], dma=True, sem="d_wo")

    def startup_transform():
        p.op("vector", lambda v: v.tensor_copy(bcolb_sb[:], bcol_sb[:]),
             reads=[("bcol_sb",)], writes=[("bcolb",)])
        WB = DM * 128
        nub = WB // LC
        stg = [u_sb[:, 0:WB], u_sb[:, WB:2 * WB]]
        stgk = [[("u", d) for d in range(nub)],
                [("u", d) for d in range(nub, 2 * nub)]]
        for eblk in range(EB):
            bf = eblk % 2
            p.op("sync", lambda s, eblk=eblk, bf=bf: s.dma_start(
                out=stg[bf][:], in_=win_bf[eblk]),
                reads=[("win_bf", eblk)], writes=stgk[bf], dma=True,
                sem=f"d_w{bf}")
            for k in range(DM):
                p.op("tensor", lambda t, k=k, bf=bf: t.matmul(
                    ps_rep[:, bf:bf + 1], stg[bf][:, k * 128:(k + 1) * 128],
                    bcolb_sb[:, k:k + 1],
                    start=(k == 0), stop=(k == DM - 1)),
                    reads=stgk[bf] + [("bcolb",)],
                    writes=[("ps_rep",)], accum=(k > 0))
            p.op("scalar", lambda s, eblk=eblk, bf=bf: s.copy(
                c_sb[:, eblk:eblk + 1], ps_rep[:, bf:bf + 1]),
                reads=[("ps_rep",)], writes=[("c_sb",)])
            for k in range(DM):
                p.op("vector", lambda v, k=k, bf=bf: v.tensor_scalar(
                    stg[bf][:, k * 128:(k + 1) * 128],
                    stg[bf][:, k * 128:(k + 1) * 128],
                    gcol_sb[:, k:k + 1], None, ALU.mult),
                    reads=stgk[bf] + [("gcol_sb",)], writes=stgk[bf])
            p.op("sync", lambda s, eblk=eblk, bf=bf: s.dma_start(
                out=win_bf[eblk], in_=stg[bf][:]),
                reads=stgk[bf], writes=[("win_bf", eblk)], dma=True,
                sem=f"d_w{bf}")

    def ln_stats_part(xt_ap, xt_rg, sq_buf, ones_col, ones_rg):
        for km in range(DM):
            p.op("tensor", lambda t, km=km: t.matmul(
                ps_sum[:, 0:LC], ones_col[:],
                xt_ap[:, km * LC:(km + 1) * LC],
                start=(km == 0), stop=(km == DM - 1)),
                reads=[xt_rg[km] if isinstance(xt_rg, list) else xt_rg,
                       ones_rg],
                writes=[("ps_sum",)], accum=(km > 0))
        for km in range(DM):
            sl = slice((km % 2) * LC, (km % 2 + 1) * LC)
            p.op("scalar", lambda s, km=km, sl=sl: s.activation(
                ys_sb[:, sl],
                xt_ap[:, km * LC:(km + 1) * LC], AF.Square),
                reads=[xt_rg[km] if isinstance(xt_rg, list) else xt_rg],
                writes=[("ys", km % 2)])
            p.op("tensor", lambda t, km=km, sl=sl: t.matmul(
                ps_sq[:, 0:LC], onec_f[:], ys_sb[:, sl],
                start=(km == 0), stop=(km == DM - 1)),
                reads=[("ys", km % 2), ("onec_f",)],
                writes=[("ps_sq",)], accum=(km > 0))

    def ln_stats_apply(xt_ap, xt_rg, sq_buf, combine12, ones_col=None,
                       ones_rg=("onec_f",), skip_stats=False):
        if ones_col is None:
            ones_col = onec_f
        if not skip_stats:
            ln_stats_part(xt_ap, xt_rg, sq_buf, ones_col, ones_rg)
        s0 = stat_sb[0:1, 0:LC]
        s1 = stat_sb[0:1, LC:2 * LC]
        s2 = s2row_sb[0:1, 0:LC]
        p.op("vector", lambda v: v.tensor_scalar(
            s0, ps_sum[:, 0:LC], 1.0 / D_MODEL, None, ALU.mult),
            reads=[("ps_sum",), ("ps_sq",)], writes=[("st0",)])
        p.op("vector", lambda v: v.tensor_scalar(
            s1, ps_sq[:, 0:LC], 1.0 / D_MODEL, None, ALU.mult),
            reads=[("ps_sq",), ("ps_sum",)], writes=[("st1",)])
        p.op("vector", lambda v: v.tensor_tensor(s2, s0, s0, ALU.mult),
             reads=[("st0",)], writes=[("st2",)])
        p.op("vector", lambda v: v.tensor_tensor(s1, s1, s2, ALU.subtract),
             reads=[("st1",), ("st2",)], writes=[("st1",)])
        if combine12:
            p.op("scalar", lambda s: s.activation(
                s1, s1, AF.Ln, bias=cvals_sb[0:1, 0:1],
                scale=cvals_sb[0:1, 1:2]),
                reads=[("st1",), ("cv", 0), ("cv", 1)], writes=[("st1",)])
        else:
            p.op("scalar", lambda s: s.activation(
                s1, s1, AF.Ln, bias=cvals_sb[0:1, 2:3]),
                reads=[("st1",), ("cv", 2)], writes=[("st1",)])
        p.op("scalar", lambda s: s.activation(
            s1, s1, AF.Exp, scale=cvals_sb[0:1, 3:4]),
            reads=[("st1",), ("cv", 3)], writes=[("st1",)])
        p.op("tensor", lambda t: t.matmul(
            ps_rep[:, 0:LC], ones_rf[:], s1, start=True, stop=True),
            reads=[("st1",), ("ones_rf",)], writes=[("ps_rep",)])
        p.op("vector", lambda v: v.tensor_tensor(s0, s0, s1, ALU.mult),
             reads=[("st0",), ("st1",)], writes=[("st0",)])
        p.op("tensor", lambda t: t.matmul(
            ps_rep[:, 256:256 + LC], ones_rf[:], s0, start=True, stop=True),
            reads=[("st0",), ("ones_rf",)], writes=[("ps_rep",)])

    def x_load(ch):
        par = ch % 2
        src = xT.rearrange("(km p) l -> p km l", p=128)[:, :, ch * LC:(ch + 1) * LC]
        p.op("gpsimd", lambda g, par=par, src=src: g.dma_start(
            out=xt_sb[par][:].rearrange("p (km l) -> p km l", l=LC), in_=src),
            writes=[("xt", par)], dma=True, sem=f"d_gx{par}")

    def ln12_stats(ch):
        par = ch % 2
        ln_stats_part(xt_sb[par][:], ("xt", par), None, onec_b, ("onec_b",))

    def ln12_mid(ch):
        par = ch % 2
        ln_stats_apply(xt_sb[par][:], ("xt", par), None, combine12=True,
                       ones_col=onec_b, ones_rg=("onec_b",), skip_stats=True)

    def ln12_apply(ch):
        par = ch % 2
        xt = xt_sb[par][:]
        xn = xn_sb[par][:]
        for km in range(DM):
            sl = slice(km * LC, (km + 1) * LC)
            r = km % 4
            if r < 2:
                tmp = ys_sb[:, r * LC:(r + 1) * LC]
                tkey = ("ys", r)
            else:
                tmp = x1_sb[:, (r - 2) * LC:(r - 1) * LC]
                tkey = ("x1", r - 2)
            p.op("vector", lambda v, sl=sl, tmp=tmp: v.tensor_tensor(
                tmp, xt[:, sl], ps_rep[:, 0:LC], ALU.mult),
                reads=[("xt", par), ("ps_rep",)], writes=[tkey])
            p.op("vector", lambda v, sl=sl, tmp=tmp, xn=xn: v.tensor_tensor(
                xn[:, sl], tmp, ps_rep[:, 256:256 + LC], ALU.subtract),
                reads=[tkey, ("ps_rep",)], writes=[("xn", par, km)])

    def xz_eblk(eblk, ch):
        par = eblk % 2
        cpar = ch % 2
        xn = xn_sb[cpar][:]
        p.op("sync", lambda s, par=par, eblk=eblk: s.dma_start(
            out=wbuf[par][:], in_=win_bf[eblk]),
            reads=[("win_bf", eblk)], writes=[("wbuf", par)], dma=True,
            sem=f"d_w{par}")
        for km in range(DM):
            p.op("tensor", lambda t, par=par, km=km, xn=xn: t.matmul(
                ps_xz[par][:],
                wbuf[par][:, km * 128:(km + 1) * 128],
                xn[:, km * LC:(km + 1) * LC],
                start=(km == 0), stop=(km == DM - 1)),
                reads=[("wbuf", par), ("xn", cpar, km)],
                writes=[("ps_xz", par)], accum=(km > 0))
        if eblk < DC:
            dc = eblk
            p.op("scalar", lambda s, dc=dc, par=par, eblk=eblk: s.activation(
                xzu_sb[:, dc * LCP + CONV - 1: dc * LCP + CONV - 1 + LC],
                ps_xz[par][:], AF.Identity, bias=c_sb[:, eblk:eblk + 1]),
                reads=[("ps_xz", par), ("c_sb",)], writes=[("xzu", dc)])
        else:
            dc = eblk - DC
            p.op("scalar", lambda s, dc=dc, par=par, cpar=cpar, eblk=eblk:
                 s.activation(
                    gt_sb[cpar][:, dc * LC:(dc + 1) * LC], ps_xz[par][:],
                    AF.Silu, bias=c_sb[:, eblk:eblk + 1]),
                 reads=[("ps_xz", par), ("c_sb",)], writes=[("gt", cpar, dc)])

    def xz_u(ch):
        for eblk in range(DC):
            xz_eblk(eblk, ch)

    def xz_z(ch):
        for eblk in range(DC, EB):
            xz_eblk(eblk, ch)

    def conv_silu(ch):
        for dc in range(DC):
            par = dc % 2
            for k in range(CONV):
                j = dc * CONV + k
                p.op("tensor", lambda t, j=j, dc=dc, k=k, par=par: t.matmul(
                    ps_conv[par][:],
                    diag_sb[:, j * 128:(j + 1) * 128],
                    xzu_sb[:, dc * LCP + k: dc * LCP + k + LC],
                    start=(k == 0), stop=False),
                    reads=[("diag", j), ("xzu", dc)],
                    writes=[("ps_conv", par)], accum=(k > 0))
            p.op("tensor", lambda t, dc=dc, par=par: t.matmul(
                ps_conv[par][:],
                convbr_sb[0:1, dc * 128:(dc + 1) * 128], ones_rb[:],
                start=False, stop=True),
                reads=[("convbr_sb",), ("ones_rb",)],
                writes=[("ps_conv", par)], accum=True)
            p.op("scalar", lambda s, dc=dc, par=par: s.activation(
                u_sb[:, dc * LC:(dc + 1) * LC], ps_conv[par][:], AF.Silu),
                reads=[("ps_conv", par)], writes=[("u", dc)])
        xv = xzu_sb[:].rearrange("p (dc l) -> p dc l", l=LCP)
        p.op("vector", lambda v, xv=xv: v.tensor_copy(
            xv[:, :, 0:CONV - 1], xv[:, :, LC:LC + CONV - 1]),
            reads=XZU, writes=XZU)

    def xdbl_dt(ch):
        par2 = ch % 2
        dt = dt_sb[par2]
        for dc in range(DC):
            p.op("tensor", lambda t, dc=dc: t.matmul(
                ps_small[0:RN, 0:LC],
                wx_sb[:, dc * RN:(dc + 1) * RN],
                u_sb[:, dc * LC:(dc + 1) * LC],
                start=(dc == 0), stop=(dc == DC - 1)),
                reads=[("wx_sb",), ("u", dc)],
                writes=[("ps_xdbl",)], accum=(dc > 0))
        p.op("scalar", lambda s: s.copy(dtr_sb, ps_small[0:R, 0:LC]),
             reads=[("ps_xdbl",)], writes=[("dtr",)])
        p.op("scalar", lambda s: s.copy(
            bc_sb[0:N, :], ps_small[R:R + N, 0:LC]),
            reads=[("ps_xdbl",)], writes=[("bc_b",)])
        p.op("scalar", lambda s: s.copy(
            bc_sb[32:32 + N, :], ps_small[96:96 + N, 0:LC]),
            reads=[("ps_xdbl",)], writes=[("bc_c",)])
        p.op("sync", lambda s, par2=par2: s.dma_start(
            out=bc_dram[par2][0:N], in_=bc_sb[0:N, :]),
            reads=[("bc_b",)], writes=[("bc_dram", par2)],
            dma=True, sem=f"d_bc{par2}")
        p.op("sync", lambda s, par2=par2: s.dma_start(
            out=bc_dram[par2][N:2 * N], in_=bc_sb[32:32 + N, :]),
            reads=[("bc_c",)], writes=[("bc_dram", par2)],
            dma=True, sem=f"d_bc{par2}")




    def dt_chain(ch, dcs):
        par2 = ch % 2
        dt = dt_sb[par2]
        dtslots = [ps_small[:, 256:256 + LC], ps_conv[0][:], ps_conv[1][:]]
        dtkeys = [("ps_dt",), ("ps_conv", 0), ("ps_conv", 1)]
        for dc in dcs:
            slot = dtslots[dc % 3]
            skey = dtkeys[dc % 3]
            wr = [skey] + ([("ps_xdbl",)] if dc == 0 else [])
            p.op("tensor", lambda t, dc=dc, slot=slot: t.matmul(
                slot, wdt_sb[:, dc * 128:(dc + 1) * 128], dtr_sb,
                start=True, stop=True),
                reads=[("wdt_sb",), ("dtr",)], writes=wr)
            p.op("scalar", lambda s, dc=dc, slot=slot: s.activation(
                e1_sb, slot, AF.Exp, bias=bdt_sb[:, dc:dc + 1]),
                reads=[skey, ("bdt_sb",)], writes=[("e1",)])
            p.op("scalar", lambda s, dc=dc, dt=dt: s.activation(
                dt[:, dc * LC:(dc + 1) * LC], e1_sb, AF.Ln,
                bias=onec_f[:, 0:1]),
                reads=[("e1",), ("onec_f",)], writes=[("dt", par2, dc)])

    def bcq_load(ch, q):
        par2 = ch % 2
        par = q % 2
        p.op("sync", lambda s, par2=par2, par=par, q=q: s.dma_start(
            out=bcq_sb[par][:, 0:4 * LC],
            in_=bc_dram[par2][4 * q:4 * q + 4].rearrange("r l -> (r l)")[
                None, :].broadcast_to([128, 4 * LC])),
            reads=[("bc_dram", par2)], writes=[("bcq", par)], dma=True,
            sem=f"d_bn{par}")
        p.op("sync", lambda s, par2=par2, par=par, q=q: s.dma_start(
            out=bcq_sb[par][:, 4 * LC:8 * LC],
            in_=bc_dram[par2][N + 4 * q:N + 4 * q + 4].rearrange(
                "r l -> (r l)")[None, :].broadcast_to([128, 4 * LC])),
            reads=[("bc_dram", par2)], writes=[("bcq", par)], dma=True,
            sem=f"d_bn{par}")

    def uw_y_init(ch):
        par2 = ch % 2
        p.op("vector", lambda v, par2=par2: v.tensor_tensor(
            uw_sb[:], dt_sb[par2][:], u_sb[:], ALU.mult),
            reads=DTK(par2) + U, writes=[("uw",)])
        for dc in range(DC):
            sl = slice(dc * LC, (dc + 1) * LC)
            p.op("scalar", lambda s, sl=sl, dc=dc: s.activation(
                y_sb[:, sl], u_sb[:, sl], AF.Copy,
                scale=dcol_sb[:, dc:dc + 1]),
                reads=[("u", dc), ("dcol_sb",)], writes=[("y", dc)])

    def scan_phase(ch):
        first = (ch == 0)
        av = a_sb[:].rearrange("p (dc l) -> p dc l", l=LC)
        bv = b_sb[:].rearrange("p (dc l) -> p dc l", l=LC)
        H = DC // 2
        nxt = ch + 1 < NCH
        usched = {}
        csched = {}
        for j in range(DC):
            usched.setdefault(2 + j // 2, []).append(j)
            csched.setdefault(3 + j // 2, []).append(j)
        tail_n = 3 + (DC - 1) // 2 + 1
        zsched = {}
        zslots = list(range(max(N - 6, tail_n), N))
        for i in range(DC):
            zsched.setdefault(zslots[i * len(zslots) // DC], []).append(DC + i)
        cpar = ch % 2
        if first:
            bcq_load(ch, 0)
        for n in range(N):
            for h in range(2):
                hdc = range(h * H, (h + 1) * H)
                hsl = slice(h * H * LC, (h + 1) * H * LC)
                HK = [("a", d) for d in hdc]
                HB = [("b", d) for d in hdc]
                if n > 0 or first:
                    for dc in hdc:
                        p.op("scalar", lambda s, n=n, dc=dc, cpar=cpar:
                             s.activation(
                                a_sb[:, dc * LC:(dc + 1) * LC],
                                dt_sb[cpar][:, dc * LC:(dc + 1) * LC],
                                AF.Exp,
                                scale=A_sb[:, dc * N + n:dc * N + n + 1]),
                             reads=[("dt", cpar, dc), ("A_sb",)],
                             writes=[("a", dc)])
                p.op("vector", lambda v, n=n, hsl=hsl: v.tensor_tensor(
                    b_sb[:, hsl].rearrange("p (dc l) -> p dc l", l=LC),
                    uw_sb[:, hsl].rearrange("p (dc l) -> p dc l", l=LC),
                    bcq_sb[(n // 4) % 2][:, (n % 4) * LC:(n % 4 + 1) * LC][
                        :, None, :].broadcast_to([128, H, LC]),
                    ALU.mult),
                    reads=[("uw",), ("bcq", (n // 4) % 2)], writes=HB)
                avh = av[:, h * H:(h + 1) * H]
                bvh = bv[:, h * H:(h + 1) * H]
                if not first:
                    p.op("vector", lambda v, n=n, avh=avh, h=h: v.tensor_tensor(
                        fix_sb[:, 0:H], avh[:, :, 0],
                        hstate_sb[:, n * DC + h * H:n * DC + (h + 1) * H],
                        ALU.mult),
                        reads=HK + [("hstate",)], writes=[("fix",)])
                    p.op("vector", lambda v, bvh=bvh: v.tensor_tensor(
                        bvh[:, :, 0], bvh[:, :, 0], fix_sb[:, 0:H], ALU.add),
                        reads=HB + [("fix",)], writes=HB)
                p.op("vector", lambda v, avh=avh: v.memset(avh[:, :, 0], 0.0),
                     reads=HK, writes=HK)
                p.op("vector", lambda v, hsl=hsl: v.tensor_tensor_scan(
                    b_sb[:, hsl], a_sb[:, hsl], b_sb[:, hsl], 0.0,
                    ALU.mult, ALU.add),
                    reads=HK + HB, writes=HB)
                p.op("vector", lambda v, n=n, bvh=bvh, h=h: v.tensor_copy(
                    hstate_sb[:, n * DC + h * H:n * DC + (h + 1) * H],
                    bvh[:, :, LC - 1]),
                    reads=HB, writes=[("hstate",)])
                hk = ("tp", h)
                p.op("vector", lambda v, n=n, hsl=hsl, h=h: v.tensor_tensor(
                    tprod_sb[:, hsl].rearrange("p (dc l) -> p dc l", l=LC),
                    b_sb[:, hsl].rearrange("p (dc l) -> p dc l", l=LC),
                    bcq_sb[(n // 4) % 2][:, (4 + n % 4) * LC:(4 + n % 4 + 1) * LC][
                        :, None, :].broadcast_to([128, H, LC]),
                    ALU.mult),
                    reads=HB + [("bcq", (n // 4) % 2)], writes=[hk])
                p.op("vector", lambda v, hsl=hsl: v.tensor_tensor(
                    y_sb[:, hsl], y_sb[:, hsl], tprod_sb[:, hsl], ALU.add),
                    reads=[hk] + YK, writes=YK)
            if n % 4 == 1 and n // 4 + 1 < 4:
                bcq_load(ch, n // 4 + 1)
            if nxt:
                for j in usched.get(n, []):
                    xz_eblk(j, ch + 1)
                for j in csched.get(n, []):
                    conv_dc_next(j)
                if n == tail_n:
                    conv_tail_next()
                for j in zsched.get(n, []):
                    xz_eblk(j, ch + 1)
                if n == 8 and ch + 2 < NCH:
                    ln12_stats(ch + 2)
                if n == 9 and ch + 2 < NCH:
                    ln12_mid(ch + 2)
                if n == 11:
                    xdbl_dt(ch + 1)
                if 12 <= n <= 15:
                    dt_chain(ch + 1,
                             range(DC * (n - 12) // 4, DC * (n - 11) // 4))
                if n == 13:
                    bcq_load(ch + 1, 0)
        if nxt:
            cpar2 = (ch + 1) % 2
            for dc in range(DC):
                p.op("scalar", lambda s, dc=dc, cpar2=cpar2: s.activation(
                    a_sb[:, dc * LC:(dc + 1) * LC],
                    dt_sb[cpar2][:, dc * LC:(dc + 1) * LC],
                    AF.Exp, scale=A_sb[:, dc * N:dc * N + 1]),
                    reads=[("dt", cpar2, dc), ("A_sb",)], writes=[("a", dc)])

    def conv_dc_next(dc):
        par = dc % 2
        for k in range(CONV):
            j = dc * CONV + k
            p.op("tensor", lambda t, j=j, dc=dc, k=k, par=par: t.matmul(
                ps_conv[par][:],
                diag_sb[:, j * 128:(j + 1) * 128],
                xzu_sb[:, dc * LCP + k: dc * LCP + k + LC],
                start=(k == 0), stop=False),
                reads=[("diag", j), ("xzu", dc)],
                writes=[("ps_conv", par)], accum=(k > 0))
        p.op("tensor", lambda t, dc=dc, par=par: t.matmul(
            ps_conv[par][:],
            convbr_sb[0:1, dc * 128:(dc + 1) * 128], ones_rb[:],
            start=False, stop=True),
            reads=[("convbr_sb",), ("ones_rb",)],
            writes=[("ps_conv", par)], accum=True)
        p.op("scalar", lambda s, dc=dc, par=par: s.activation(
            u_sb[:, dc * LC:(dc + 1) * LC], ps_conv[par][:], AF.Silu),
            reads=[("ps_conv", par)], writes=[("u", dc)])

    def conv_tail_next():
        xv = xzu_sb[:].rearrange("p (dc l) -> p dc l", l=LCP)
        p.op("vector", lambda v, xv=xv: v.tensor_copy(
            xv[:, :, 0:CONV - 1], xv[:, :, LC:LC + CONV - 1]),
            reads=XZU, writes=XZU)

    def gate_mult(ch):
        cpar = ch % 2
        xv = xzu_sb[:].rearrange("p (dc l) -> p dc l", l=LCP)
        p.op("vector", lambda v, cpar=cpar, xv=xv: v.tensor_tensor(
            xv[:, :, CONV - 1:CONV - 1 + LC],
            y_sb[:].rearrange("p (dc l) -> p dc l", l=LC),
            gt_sb[cpar][:].rearrange("p (dc l) -> p dc l", l=LC),
            ALU.mult),
            reads=YK + [("gt", cpar, d) for d in range(DC)], writes=XZU)

    def gate_mm(ch):
        for km in range(DM):
            par = km % 2
            for dc in range(DC):
                p.op("tensor", lambda t, km=km, dc=dc, par=par: t.matmul(
                    ps_out[par][:],
                    wout_sb[:, dc * D_MODEL + km * 128:
                            dc * D_MODEL + (km + 1) * 128],
                    xzu_sb[:, dc * LCP + CONV - 1: dc * LCP + CONV - 1 + LC],
                    start=(dc == 0), stop=(dc == DC - 1)),
                    reads=[("wout_sb", dc * 4 // DC), ("xzu", dc)],
                    writes=[("ps_o",)], accum=(dc > 0))
            ks = km % 2
            if ch < HCH:
                dst = outst_sb[:, ks * LC:(ks + 1) * LC]
            else:
                dst = outst_sb[:, ks * LC:(ks + 1) * LC][:, ::-1]
            p.op("scalar", lambda s, dst=dst, par=par: s.copy(
                dst, ps_out[par][:]),
                reads=[("ps_o",)], writes=[("outst", ks)])
            if ks == 1:
                kg = km - 1
                if ch < HCH:
                    dst2 = yown_dram.rearrange("(km p) l -> p km l", p=128)[
                        :, kg:kg + 2, ch * LC:(ch + 1) * LC]
                    rg = ("yown_dram", ch)
                else:
                    cc = NCH - 1 - ch
                    base = cc * LC
                    hi = base // (HALF // 2)
                    off = base % (HALF // 2)
                    dst2 = send_dram[hi].rearrange("(km p) l -> p km l", p=128)[
                        :, kg:kg + 2, off:off + LC]
                    rg = ("send_dram", cc)
                p.op("sync", lambda s, dst2=dst2: s.dma_start(
                    out=dst2,
                    in_=outst_sb[:].rearrange("p (km l) -> p km l", l=LC)),
                    reads=[("outst", 0), ("outst", 1)],
                    writes=[rg], dma=True, sem="d_ow")

    def collective_half(hi):
        groups = [[2 * i, 2 * i + 1] for i in range(NCORES // 2)]
        chs = [c for c in range(HCH)
               if hi * (HALF // 2) <= c * LC < (hi + 1) * (HALF // 2)]
        p.op("gpsimd", lambda g, hi=hi: g.collective_compute(
            "AllReduce", ALU.add, replica_groups=groups,
            ins=[send_dram[hi].opt()],
            outs=[ar_dram[hi].opt()]),
            reads=[("send_dram", c) for c in chs],
            writes=[("ar_dram", hi)], sem="cc" if hi == 0 else "cc1",
            inc=1)

    def post_loads(pc):
        par = pc % 2
        xt = xres_sb[:]
        colsl = slice(pc * LC, (pc + 1) * LC)
        p.op("sync", lambda s: s.dma_start(
            out=xt.rearrange("p (km l) -> p km l", l=LC),
            in_=x_res.rearrange("(km p) l -> p km l", p=128)[:, :, colsl]),
            writes=[("xres",)], dma=True, sem=f"d_x{par}")
        p.op("sync", lambda s: s.dma_start(
            out=b_sb[:, 0:DM * LC].rearrange("p (km l) -> p km l", l=LC),
            in_=yown_dram.rearrange("(km p) l -> p km l", p=128)[:, :, colsl]),
            reads=[("yown_dram", pc)], writes=BK, dma=True, sem=f"d_pl{par}")
        arh = 0 if (pc + 1) * LC <= HALF // 2 else 1
        poff = pc * LC - arh * (HALF // 2)
        p.op("sync", lambda s: s.dma_start(
            out=uw_sb[:, 0:DM * LC].rearrange("p (km l) -> p km l", l=LC),
            in_=send_dram[arh].rearrange("(km p) l -> p km l", p=128)[
                :, :, poff:poff + LC]),
            reads=[("send_dram", pc)], writes=[("uw",)], dma=True,
            sem=f"d_pl{par}")
        p.op("sync", lambda s: s.dma_start(
            out=gt_sb[0][:, 0:DM * LC].rearrange("p (km l) -> p km l", l=LC),
            in_=ar_dram[arh].rearrange("(km p) l -> p km l", p=128)[
                :, :, poff:poff + LC]),
            reads=[("ar_dram", arh)], writes=GT, dma=True, sem=f"d_pl{par}")

    def post_A(pc):
        xt = xres_sb[:]
        p.op("vector", lambda v: v.tensor_tensor(
            a_sb[:, 0:DM * LC], gt_sb[0][:, 0:DM * LC], uw_sb[:, 0:DM * LC],
            ALU.subtract),
            reads=GT + [("uw",)], writes=AK)
        p.op("vector", lambda v: v.tensor_tensor(
            dt_sb[0][:, 0:DM * LC], a_sb[:, 0:DM * LC], b_sb[:, 0:DM * LC],
            ALU.add),
            reads=AK + BK, writes=DTK(0))

        ln_stats_apply(xt, ("xres",), None, combine12=False)
        for km in range(DM):
            sl = slice(km * LC, (km + 1) * LC)
            ts = slice((km % 2) * LC, (km % 2 + 1) * LC)
            p.op("vector", lambda v, sl=sl, ts=ts: v.tensor_tensor(
                ys_sb[:, ts], xt[:, sl], ps_rep[:, 0:LC], ALU.mult),
                reads=[("xres",), ("ps_rep",)], writes=[("ys", km % 2)])
            p.op("vector", lambda v, ts=ts: v.tensor_tensor(
                ys_sb[:, ts], ys_sb[:, ts], ps_rep[:, 256:256 + LC],
                ALU.subtract),
                reads=[("ys", km % 2), ("ps_rep",)],
                writes=[("ys", km % 2)])
            p.op("vector", lambda v, sl=sl, ts=ts: v.tensor_tensor(
                ys_sb[:, ts], ys_sb[:, ts], xt[:, sl], ALU.add),
                reads=[("ys", km % 2), ("xres",)],
                writes=[("ys", km % 2)])
            p.op("vector", lambda v, sl=sl, ts=ts: v.tensor_tensor(
                x1_sb[:, sl], ys_sb[:, ts], dt_sb[0][:, sl], ALU.add),
                reads=[("ys", km % 2), ("dt", 0, sl.start // LC)],
                writes=[("x1", sl.start // LC)])

    def post_B(pc, firstpost):
        colsl = slice(pc * LC, (pc + 1) * LC)
        ln_stats_apply(x1_sb[:], X1, None, combine12=False)
        for km in range(DM):
            sl = slice(km * LC, (km + 1) * LC)
            ts = slice((km % 2) * LC, (km % 2 + 1) * LC)
            p.op("vector", lambda v, sl=sl, ts=ts: v.tensor_tensor(
                ys_sb[:, ts], x1_sb[:, sl], ps_rep[:, 0:LC], ALU.mult),
                reads=[("x1", km), ("ps_rep",)], writes=[("ys", km % 2)])
            p.op("vector", lambda v, sl=sl, ts=ts, km=km: v.tensor_tensor(
                xn_sb[0][:, sl], ys_sb[:, ts], ps_rep[:, 256:256 + LC],
                ALU.subtract),
                reads=[("ys", km % 2), ("ps_rep",)], writes=[("xn", 0, km)])

        for km in range(DM):
            wr = [("ps_ffs",)] + ([("ps_xdbl",), ("ps_dt",)] if km == 0 else [])
            p.op("tensor", lambda t, km=km: t.matmul(
                ps_small[0:M, 0:LC], winf_sb[:, km * M:(km + 1) * M],
                xn_sb[0][:, km * LC:(km + 1) * LC],
                start=(km == 0), stop=(km == DM - 1)),
                reads=[("winf_sb",), ("xn", 0, km)],
                writes=wr, accum=(km > 0))
        for i in range(INNER):
            wr = [("hff", i)] + (AK if (firstpost and i == 0) else [])
            p.op("scalar", lambda s, i=i: s.activation(
                hff_sb[0:M, i * LC:(i + 1) * LC], ps_small[0:M, 0:LC], AF.Relu,
                bias=bias1_sb[:, i:i + 1], scale=w1_sb[:, i:i + 1]),
                reads=[("ps_ffs",), ("bias1_sb",), ("w1_sb",)],
                writes=wr)
        p.op("vector", lambda v: v.tensor_scalar(
            gacc_sb[0][:], hff_sb[0:M, 0:LC], w2_sb[:, 0:1], b2_sb[:, 0:1],
            ALU.mult, ALU.add),
            reads=[("hff", 0), ("w2_sb",), ("b2_sb",)], writes=[("gacc", 0)])
        for i in range(1, INNER):
            sp, dp = (i - 1) % 2, i % 2
            p.op("vector", lambda v, i=i, sp=sp, dp=dp: v.scalar_tensor_tensor(
                gacc_sb[dp][:], hff_sb[0:M, i * LC:(i + 1) * LC],
                w2_sb[:, i:i + 1], gacc_sb[sp][:], ALU.mult, ALU.add),
                reads=[("hff", i), ("w2_sb",), ("gacc", sp)],
                writes=[("gacc", dp)])
        gfin = gacc_sb[(INNER - 1) % 2]
        gfin_rg = ("gacc", (INNER - 1) % 2)
        for km in range(DM):
            par2 = km % 2
            p.op("tensor", lambda t, km=km, par2=par2: t.matmul(
                ps_out[par2][:], woutf_sb[:, km * 128:(km + 1) * 128],
                gfin[:], start=True, stop=True),
                reads=[("woutf_sb",), gfin_rg], writes=[("ps_o",)])
            p.op("vector", lambda v, km=km, par2=par2: v.scalar_tensor_tensor(
                x1_sb[:, km * LC:(km + 1) * LC], ps_out[par2][:],
                bout_sb[:, km:km + 1], x1_sb[:, km * LC:(km + 1) * LC],
                ALU.add, ALU.add),
                reads=[("ps_o",), ("bout_sb",), ("x1", km)],
                writes=[("x1", km)])
        p.op("sync", lambda s: s.dma_start(
            out=out_d.rearrange("(km p) l -> p km l", p=128)[:, :, colsl],
            in_=x1_sb[:].rearrange("p (km l) -> p km l", l=LC)),
            reads=X1, writes=[("out_d", pc)], dma=True, sem="d_od")

    x_load(0)
    startup()
    ln12_stats(0)
    ln12_mid(0)
    ln12_apply(0)
    startup_transform()
    x_load(1)
    load_wout()
    xz_u(0)
    conv_silu(0)
    xz_z(0)
    xdbl_dt(0)
    dt_chain(0, range(DC))
    if NCH > 1:
        ln12_stats(1)
        ln12_mid(1)
    for k in range(NCH):
        if k > 0:
            gate_mult(k - 1)
        uw_y_init(k)
        if k > 0:
            gate_mm(k - 1)
        if k == NCH - HCH // 2:
            collective_half(1)
        if k + 1 < NCH:
            ln12_apply(k + 1)
        if k + 2 < NCH:
            x_load(k + 2)
        scan_phase(k)
    gate_mult(NCH - 1)
    gate_mm(NCH - 1)
    collective_half(0)
    post_order = list(range(HCH // 2, HCH)) + list(range(HCH // 2))
    post_loads(post_order[0])
    for idx, pc in enumerate(post_order):
        post_A(pc)
        if idx + 1 < len(post_order):
            post_loads(post_order[idx + 1])
        post_B(pc, firstpost=(idx == 0))

    for snm in ["d_st", "d_x0", "d_x1", "d_w0", "d_w1", "d_bc0", "d_bc1",
                "d_ow", "d_pl0", "d_pl1", "d_od", "d_bn0", "d_bn1"]:
        p.q["sync"].append(("wait", snm, p.cnt.get(snm, 0)))
    p.q["gpsimd"].append(("wait", "d_g", p.cnt.get("d_g", 0)))
    p.q["gpsimd"].append(("wait", "d_g2", p.cnt.get("d_g2", 0)))
    p.q["gpsimd"].append(("wait", "d_wo", p.cnt.get("d_wo", 0)))
    p.q["gpsimd"].append(("wait", "d_gx0", p.cnt.get("d_gx0", 0)))
    p.q["gpsimd"].append(("wait", "d_gx1", p.cnt.get("d_gx1", 0)))
    p.q["gpsimd"].append(("wait", "cc", p.cnt.get("cc", 0)))
    p.q["gpsimd"].append(("wait", "cc1", p.cnt.get("cc1", 0)))

    with nc.Block() as block:
        p.finalize(block, sems)
    stack.close()
    return nc


def _prep_core_inputs(cfg, x_b, flip, w, ffn):
    DM, DC, L = cfg["DM"], cfg["DC"], cfg["L"]
    N, R, CONV, M = cfg["N"], cfg["R"], cfg["CONV"], cfg["M"]
    D_MODEL, D_INNER = DM * 128, DC * 128
    EB = 2 * DC
    HALF = L // 2
    C = np.ascontiguousarray

    xd = x_b[::-1] if flip else x_b
    ins = {}
    ins["xT"] = C(xd.T.astype(np.float32))
    ins["x_res"] = C(xd[:HALF].T.astype(np.float32))
    win = np.asarray(w["Win"], dtype=np.float32)
    ins["win_t4"] = C(win.reshape(EB, 128, DM, 128).transpose(0, 3, 2, 1))
    ins["wout_t"] = C(np.asarray(w["Wout"], np.float32).T)
    wxT = np.asarray(w["Wx"], np.float32).T
    wxp = np.zeros((wxT.shape[0], 128), np.float32)
    wxp[:, 0:R] = wxT[:, 0:R]
    wxp[:, R:R + N] = wxT[:, R:R + N]
    wxp[:, 96:96 + N] = wxT[:, R + N:R + 2 * N]
    ins["wx_t"] = C(wxp)
    ins["wdt_t"] = C(np.asarray(w["Wdt"], np.float32).T)
    ins["alog_c"] = C(np.asarray(w["Alog"], np.float32)
                      .reshape(DC, 128, N).transpose(1, 0, 2).reshape(128, DC * N))
    ins["dcol"] = C(np.asarray(w["D"], np.float32).reshape(DC, 128).T)
    ins["bdt_c"] = C(np.asarray(w["bdt"], np.float32).reshape(DC, 128).T)
    ins["convw_c"] = C(np.asarray(w["convw"], np.float32)[:, 0, :]
                       .reshape(DC, 128, CONV).transpose(1, 0, 2)
                       .reshape(128, DC * CONV))
    ins["convb_r"] = C(np.asarray(w["convb"], np.float32)[None, :])
    ins["gcol"] = C(np.asarray(w["norm_g"], np.float32).reshape(DM, 128).T)
    ins["bcol"] = C(np.asarray(w["norm_b"], np.float32).reshape(DM, 128).T)
    ins["winf"] = C(np.asarray(ffn["Win"], np.float32))
    ins["woutf"] = C(np.asarray(ffn["Wout"], np.float32))
    ins["w1"] = C(np.asarray(ffn["W1"], np.float32))
    ins["b1"] = C(np.asarray(ffn["b1"], np.float32))
    ins["w2"] = C(np.asarray(ffn["W2"], np.float32))
    ins["b2c"] = C(np.asarray(ffn["b2"], np.float32)[:, None])
    ins["binc"] = C(np.asarray(ffn["bin_"], np.float32)[:, None])
    ins["bout_c"] = C(np.asarray(ffn["bout"], np.float32).reshape(DM, 128).T)
    ins["eye16"] = np.eye(128, dtype=ml_dtypes.bfloat16)
    return ins


def _assemble(cfg, shards):
    DM, L = cfg["DM"], cfg["L"]
    D_MODEL, HALF = DM * 128, L // 2
    B = len(shards) // 2
    out = np.empty((B, L, D_MODEL), dtype=np.float32)
    for b in range(B):
        out[b, :HALF] = shards[2 * b].T
        out[b, HALF:] = shards[2 * b + 1].T[::-1]
    return out


MAMBA_KEYS = ["Win", "convw", "convb", "Wx", "Wdt", "bdt", "Alog", "D", "Wout"]


def kernel(**inputs):
    cfg = FULL_CFG
    x = np.asarray(inputs["L0_attn_x"], dtype=np.float32)
    Bn = x.shape[0]
    wf = {k: np.asarray(inputs["L0_attn_f_" + k]) for k in MAMBA_KEYS}
    wb = {k: np.asarray(inputs["L0_attn_b_" + k]) for k in MAMBA_KEYS}
    for d in (wf, wb):
        d["norm_g"] = np.asarray(inputs["L0_attn_norm_g"])
        d["norm_b"] = np.asarray(inputs["L0_attn_norm_b"])
    ffn = {k: np.asarray(inputs["L0_ffn_" + k]) for k in
           ["Win", "bin_", "W1", "b1", "W2", "b2", "Wout", "bout"]}

    nc = build_core_program(cfg)
    in_maps = []
    for b in range(Bn):
        in_maps.append(_prep_core_inputs(cfg, x[b], False, wf, ffn))
        in_maps.append(_prep_core_inputs(cfg, x[b], True, wb, ffn))
    res = run_bass_kernel_spmd(nc, in_maps, core_ids=list(range(2 * Bn)))
    shards = [np.asarray(r["out"]) for r in res.results]
    return _assemble(cfg, shards)


# version 30
# speedup vs baseline: 1.0965x; 1.0059x over previous
import numpy as np
import ml_dtypes
from contextlib import ExitStack

import concourse.bass as bass
import concourse.mybir as mybir
from concourse.bass_utils import run_bass_kernel_spmd

F32 = mybir.dt.float32
BF16 = mybir.dt.bfloat16
AF = mybir.ActivationFunctionType
ALU = mybir.AluOpType

EPS = 1e-5

FULL_CFG = dict(DM=8, DC=16, L=4096, LC=256, N=16, R=64, CONV=4, M=64,
                INNER=16, NCORES=8)


class Prog:
    ENGINES = ("tensor", "vector", "scalar", "gpsimd", "sync")

    @staticmethod
    def is_dma_sem(sem):
        return sem.startswith("d_") or sem.startswith("cc")

    def __init__(self, nc):
        self.nc = nc
        self.q = {e: [] for e in self.ENGINES}
        self.cnt = {}
        self.state = {}
        self.waited = {}
        self.opidx = {e: 0 for e in self.ENGINES}
        self.last_drain = {e: -1 for e in self.ENGINES}

    def _need(self, eng, stamp, drains):
        sem, val, peng, pidx = stamp
        if self.is_dma_sem(sem):
            val = self.cnt.get(sem, 0)
        k = (eng, sem)
        if self.waited.get(k, 0) < val:
            self.waited[k] = val
            self.q[eng].append(("wait", sem, val))

    def op(self, eng, fn, reads=(), writes=(), dma=False, sem=None, inc=None,
           accum=False):
        drains = []
        for r in reads:
            st = self.state.get(r)
            if st and st["w"]:
                self._need(eng, st["w"], drains)
        if not accum:
            for r in writes:
                st = self.state.get(r)
                if st:
                    if st["w"]:
                        sem_, val_, peng_, pidx_ = st["w"]
                        if peng_ != eng or self.is_dma_sem(sem_):
                            self._need(eng, st["w"], drains)
                    for stamp in st["r"].values():
                        self._need(eng, stamp, drains)
        if drains and max(drains) > self.last_drain[eng]:
            self.q[eng].append(("drain",))
            self.last_drain[eng] = self.opidx[eng]
        if sem is None:
            assert not dma, "DMA ops must name their semaphore"
            sem = eng
        if inc is None:
            inc = 16 if dma else 1
        self.cnt[sem] = self.cnt.get(sem, 0) + inc
        val = self.cnt[sem]
        self.opidx[eng] += 1
        stamp = (sem, val, eng, self.opidx[eng])
        self.q[eng].append(("op", fn, sem, inc, val))
        for r in writes:
            self.state[r] = {"w": stamp, "r": {}}
        for r in reads:
            st = self.state.setdefault(r, {"w": None, "r": {}})
            st["r"][eng] = stamp
        return stamp

    def finalize(self, block, sems):
        waited = {}
        for ename in self.ENGINES:
            for it in self.q[ename]:
                if it[0] == "wait" and not self.is_dma_sem(it[1]):
                    waited.setdefault(it[1], set()).add(it[2])
        remap = {}
        for sem, vals in waited.items():
            remap[sem] = {v: i + 1 for i, v in enumerate(sorted(vals))}

        for ename in self.ENGINES:
            items = self.q[ename]

            def make_body(items):
                def body(engine):
                    for it in items:
                        if it[0] == "wait":
                            sem, val = it[1], it[2]
                            if sem in remap:
                                val = remap[sem][val]
                            engine.wait_ge(sems[sem], val)
                        elif it[0] == "drain":
                            engine.drain()
                        else:
                            _, fn, sem, inc, val = it
                            inst = fn(engine)
                            if self.is_dma_sem(sem):
                                inst.then_inc(sems[sem], inc)
                            elif sem in remap and val in remap[sem]:
                                inst.then_inc(sems[sem], inc)
                return body
            getattr(block, ename)(make_body(items))


def build_core_program(cfg):
    DM, DC, L, LC = cfg["DM"], cfg["DC"], cfg["L"], cfg["LC"]
    N, R, CONV = cfg["N"], cfg["R"], cfg["CONV"]
    M, INNER = cfg["M"], cfg["INNER"]
    NCORES = cfg["NCORES"]
    D_MODEL = DM * 128
    D_INNER = DC * 128
    NCH = L // LC
    HCH = NCH // 2
    HALF = L // 2
    EB = 2 * DC
    LCP = LC + CONV - 1
    RN = 128

    nc = bass.Bass("TRN2", target_bir_lowering=False)

    def din(name, shape, dt=F32):
        return nc.dram_tensor(name, list(shape), dt, kind="ExternalInput").ap()

    xT = din("xT", [D_MODEL, L])
    x_res = din("x_res", [D_MODEL, HALF])
    win_t4 = din("win_t4", [EB, 128, DM, 128])
    wout_t = din("wout_t", [D_INNER, D_MODEL])
    wx_t = din("wx_t", [D_INNER, RN])
    wdt_t = din("wdt_t", [R, D_INNER])
    alog_c = din("alog_c", [128, DC * N])
    dcol = din("dcol", [128, DC])
    bdt_c = din("bdt_c", [128, DC])
    convw_c = din("convw_c", [128, DC * CONV])
    convb_r = din("convb_r", [1, D_INNER])
    gcol = din("gcol", [128, DM])
    bcol = din("bcol", [128, DM])
    winf = din("winf", [D_MODEL, M])
    woutf = din("woutf", [M, D_MODEL])
    w1 = din("w1", [M, INNER])
    b1 = din("b1", [M, INNER])
    w2 = din("w2", [M, INNER])
    b2c = din("b2c", [M, 1])
    binc = din("binc", [M, 1])
    bout_c = din("bout_c", [128, DM])
    eye16 = din("eye16", [128, 128], BF16)

    out_d = nc.dram_tensor("out", [D_MODEL, HALF], F32, kind="ExternalOutput").ap()

    win_bf = nc.dram_tensor("win_bf", [EB, 128, DM * 128], BF16).ap()
    bc_dram = [nc.dram_tensor(f"bc_dram{i}", [2 * N, LC], BF16).ap()
               for i in range(2)]
    yown_dram = nc.dram_tensor("yown_dram", [D_MODEL, HALF], BF16).ap()
    send_dram = nc.dram_tensor("send_dram", [2, D_MODEL, HALF // 2], BF16).ap()
    ar_dram = nc.dram_tensor("ar_dram", [2, D_MODEL, HALF // 2], BF16).ap()

    stack = ExitStack()

    def sbuf(name, shape, dt):
        return stack.enter_context(nc.sbuf_tensor(name, list(shape), dt))

    def psum(name, shape, dt=F32):
        return stack.enter_context(nc.psum_tensor(name, list(shape), dt))

    wout_sb = sbuf("wout_sb", [128, DC * D_MODEL], BF16)
    wx_sb = sbuf("wx_sb", [128, DC * RN], BF16)
    wdt_sb = sbuf("wdt_sb", [R, D_INNER], BF16)
    winf_sb = sbuf("winf_sb", [128, DM * M], BF16)
    woutf_sb = sbuf("woutf_sb", [M, D_MODEL], BF16)
    diag_sb = sbuf("diag_sb", [128, DC * CONV * 128], BF16)
    eye_sb = sbuf("eye_sb", [128, 128], BF16)
    A_sb = sbuf("A_sb", [128, DC * N], F32)
    dcol_sb = sbuf("dcol_sb", [128, DC], F32)
    bdt_sb = sbuf("bdt_sb", [128, DC], F32)
    convw_sb = sbuf("convw_sb", [128, DC * CONV], F32)
    convbr_sb = sbuf("convbr_sb", [1, D_INNER], BF16)
    gcol_sb = sbuf("gcol_sb", [128, DM], F32)
    bcol_sb = sbuf("bcol_sb", [128, DM], F32)
    w1_sb = sbuf("w1_sb", [M, INNER], F32)
    b1_sb = sbuf("b1_sb", [M, INNER], F32)
    w2_sb = sbuf("w2_sb", [M, INNER], F32)
    bias1_sb = sbuf("bias1_sb", [M, INNER], F32)
    b2_sb = sbuf("b2_sb", [M, 1], F32)
    binc_sb = sbuf("binc_sb", [M, 1], F32)
    bout_sb = sbuf("bout_sb", [128, DM], F32)
    ones_rf = sbuf("ones_rf", [1, 128], F32)
    cvals_sb = sbuf("cvals_sb", [1, 4], F32)
    half_c = sbuf("half_c", [128, 1], F32)
    ones_rb = sbuf("ones_rb", [1, LC], BF16)
    onec_f = sbuf("onec_f", [128, 1], F32)
    onec_b = sbuf("onec_b", [128, 1], BF16)
    c_sb = sbuf("c_sb", [128, EB], F32)
    bcolb_sb = sbuf("bcolb_sb", [128, DM], BF16)

    wbuf = [sbuf(f"wbuf{i}", [128, DM * 128], BF16) for i in range(2)]
    xt_sb = [sbuf(f"xt_sb{i}", [128, DM * LC], BF16) for i in range(2)]
    xn_sb = [sbuf(f"xn_sb{i}", [128, DM * LC], BF16) for i in range(2)]
    xzu_sb = sbuf("xzu_sb", [128, DC * LCP], BF16)
    u_sb = sbuf("u_sb", [128, DC * LC], BF16)
    gt_sb = [sbuf(f"gt_sb{i}", [128, DC * LC], BF16) for i in range(2)]
    dt_sb = [sbuf(f"dt_sb{i}", [128, DC * LC], BF16) for i in range(2)]
    uw_sb = sbuf("uw_sb", [128, DC * LC], BF16)
    a_sb = sbuf("a_sb", [128, DC * LC], BF16)
    b_sb = sbuf("b_sb", [128, DC * LC], BF16)
    y_sb = sbuf("y_sb", [128, DC * LC], BF16)
    tprod_sb = sbuf("tprod_sb", [128, DC * LC], BF16)
    bc_sb = sbuf("bc_sb", [64, LC], BF16)
    bcq_sb = [sbuf(f"bcq_sb{i}", [128, 8 * LC], BF16) for i in range(2)]
    hstate_sb = sbuf("hstate_sb", [128, N * DC], BF16)
    fix_sb = sbuf("fix_sb", [128, DC], BF16)
    outst_sb = sbuf("outst_sb", [128, 2 * LC], BF16)
    stat_sb = sbuf("stat_sb", [1, 2 * LC], F32)
    gacc_sb = [sbuf(f"gacc_sb{i}", [M, LC], BF16) for i in range(2)]
    x1_sb = sbuf("x1_sb", [128, DM * LC], F32)
    xres_sb = sbuf("xres_sb", [128, DM * LC], F32)
    ys_sb = sbuf("ys_sb", [128, 2 * LC], F32)
    s2row_sb = sbuf("s2row_sb", [1, LC], F32)
    e1t_sb = sbuf("e1t_sb", [128, LC], BF16)
    dtrt_sb = sbuf("dtrt_sb", [R, LC], BF16)
    e1_sb = e1t_sb[:, 0:LC]
    dtr_sb = dtrt_sb[0:R, 0:LC]
    if INNER * LC <= DC * LC:
        hff_sb = a_sb
    else:
        hff_sb = sbuf("hff_sb", [M, INNER * LC], BF16)

    ps_rep = psum("ps_rep", [128, 512])
    ps_st = psum("ps_st", [1, 512])
    ps_sum = ps_st[0:1, 0:256]
    ps_sq = ps_st[0:1, 256:512]
    ps_xz = [psum(f"ps_xz{i}", [128, LC]) for i in range(2)]
    ps_conv = [psum(f"ps_conv{i}", [128, LC]) for i in range(2)]
    ps_small = psum("ps_small", [128, 512])
    ps_outb = [psum("ps_outb", [128, LC])]
    ps_out = [ps_outb[0], ps_outb[0]]

    assert LC <= 256

    sem_names = ["tensor", "vector", "scalar", "gpsimd", "sync",
                 "d_st", "d_x0", "d_x1", "d_w0", "d_w1", "d_bc0", "d_bc1",
                 "d_ow", "d_pl0", "d_pl1", "d_od", "d_g", "d_gx0", "d_gx1",
                 "d_bn0", "d_bn1", "cc", "cc1", "d_g2", "d_g3", "d_g4", "d_wo"]
    sems = {nm: stack.enter_context(nc.semaphore(nm)) for nm in sem_names}

    p = Prog(nc)
    W = {"xt": lambda i: ("xt", i), "xn": lambda k: ("xn", k)}

    XN = [("xn", k) for k in range(DM)]
    U = [("u", d) for d in range(DC)]
    AK = [("a", d) for d in range(DC)]
    BK = [("b", d) for d in range(DC)]
    YK = [("y", d) for d in range(DC)]
    def DTK(par):
        return [("dt", par, d) for d in range(DC)]
    GT = [("gt", 0, d) for d in range(DC)]
    XZU = [("xzu", d) for d in range(DC)]
    X1 = [("x1", k) for k in range(DM)]
    YS = [("ys", k) for k in range(DM)]
    OUTST = [("outst", k) for k in range(DM)]

    def startup():
        castq = ["d_g", "d_g2", "d_g3", "d_g4"]
        for eblk in range(EB):
            p.op("gpsimd", lambda g, eblk=eblk: g.dma_start(
                out=win_bf[eblk], in_=win_t4[eblk]),
                writes=[("win_bf", eblk)], dma=True,
                sem=castq[eblk % 4])
        p.op("gpsimd", lambda g: g.dma_start(
            out=wx_sb[:].rearrange("p (dc r) -> p dc r", r=RN),
            in_=wx_t.rearrange("(dc p) r -> p dc r", p=128)),
            writes=[("wx_sb",)], dma=True, sem="d_g")
        p.op("gpsimd", lambda g: g.dma_start(out=wdt_sb[:], in_=wdt_t),
             writes=[("wdt_sb",)], dma=True, sem="d_g")
        p.op("gpsimd", lambda g: g.dma_start(
            out=winf_sb[:].rearrange("p (km m) -> p km m", m=M),
            in_=winf.rearrange("(km p) m -> p km m", p=128)),
            writes=[("winf_sb",)], dma=True, sem="d_g")
        p.op("gpsimd", lambda g: g.dma_start(out=woutf_sb[:], in_=woutf),
             writes=[("woutf_sb",)], dma=True, sem="d_g")
        p.op("gpsimd", lambda g: g.dma_start(out=convbr_sb[:], in_=convb_r),
             writes=[("convbr_sb",)], dma=True, sem="d_g")
        loads = [(A_sb, alog_c, "A_sb"), (dcol_sb, dcol, "dcol_sb"),
                 (bdt_sb, bdt_c, "bdt_sb"), (convw_sb, convw_c, "convw_sb"),
                 (gcol_sb, gcol, "gcol_sb"), (bcol_sb, bcol, "bcol_sb"),
                 (w1_sb, w1, "w1_sb"), (b1_sb, b1, "b1_sb"),
                 (w2_sb, w2, "w2_sb"), (b2_sb, b2c, "b2_sb"),
                 (binc_sb, binc, "binc_sb"), (bout_sb, bout_c, "bout_sb"),
                 (eye_sb, eye16, "eye_sb")]
        for t, src, rg in loads:
            p.op("sync", lambda s, t=t, src=src: s.dma_start(out=t[:], in_=src),
                 writes=[(rg,)], dma=True, sem="d_st")
        p.op("vector", lambda v: v.memset(ones_rf[:], 1.0), writes=[("ones_rf",)])
        p.op("vector", lambda v: v.memset(cvals_sb[0:1, 0:1], EPS * EPS),
             writes=[("cv", 0)])
        p.op("vector", lambda v: v.memset(cvals_sb[0:1, 1:2], 1.0 + EPS),
             writes=[("cv", 1)])
        p.op("vector", lambda v: v.memset(cvals_sb[0:1, 2:3], EPS),
             writes=[("cv", 2)])
        p.op("vector", lambda v: v.memset(cvals_sb[0:1, 3:4], -0.5),
             writes=[("cv", 3)])
        p.op("vector", lambda v: v.memset(half_c[:], 0.5), writes=[("half_c",)])
        p.op("vector", lambda v: v.memset(ones_rb[:], 1.0), writes=[("ones_rb",)])
        p.op("vector", lambda v: v.memset(onec_f[:], 1.0), writes=[("onec_f",)])
        p.op("vector", lambda v: v.memset(onec_b[:], 1.0), writes=[("onec_b",)])
        p.op("vector", lambda v: v.memset(hstate_sb[:], 0.0), writes=[("hstate",)])
        p.op("vector", lambda v: v.memset(xzu_sb[:], 0.0), writes=XZU)

    def startup_prep():
        p.op("scalar", lambda s: s.activation(A_sb[:], A_sb[:], AF.Exp),
             reads=[("A_sb",)], writes=[("A_sb",)])
        p.op("vector", lambda v: v.tensor_scalar(
            A_sb[:], A_sb[:], -1.0, None, ALU.mult),
            reads=[("A_sb",)], writes=[("A_sb",)])
        for dc in range(DC):
            for k in range(CONV):
                j = dc * CONV + k
                p.op("vector", lambda v, j=j: v.tensor_scalar(
                    diag_sb[:, j * 128:(j + 1) * 128], eye_sb[:],
                    convw_sb[:, j:j + 1], None, ALU.mult),
                    reads=[("eye_sb",), ("convw_sb",)], writes=[("diag", j)])
        p.op("vector", lambda v: v.tensor_scalar(
            bias1_sb[:], w1_sb[:], binc_sb[:, 0:1], None, ALU.mult),
            reads=[("w1_sb",), ("binc_sb",)], writes=[("bias1_sb",)])
        p.op("vector", lambda v: v.tensor_tensor(
            bias1_sb[:], bias1_sb[:], b1_sb[:], ALU.add),
            reads=[("bias1_sb",), ("b1_sb",)], writes=[("bias1_sb",)])

    def load_wout():
        for j in range(4):
            d0 = DC // 4 * j
            d1 = DC // 4 * (j + 1)
            p.op("gpsimd", lambda g, d0=d0, d1=d1: g.dma_start(
                out=wout_sb[:, d0 * D_MODEL:d1 * D_MODEL].rearrange(
                    "p (dc d) -> p dc d", d=D_MODEL),
                in_=wout_t.rearrange("(dc p) d -> p dc d", p=128)[:, d0:d1]),
                writes=[("wout_sb", j)], dma=True, sem="d_wo")

    def startup_transform():
        p.op("vector", lambda v: v.tensor_copy(bcolb_sb[:], bcol_sb[:]),
             reads=[("bcol_sb",)], writes=[("bcolb",)])
        WB = DM * 128
        nub = WB // LC
        stg = [u_sb[:, 0:WB], u_sb[:, WB:2 * WB]]
        stgk = [[("u", d) for d in range(nub)],
                [("u", d) for d in range(nub, 2 * nub)]]
        for eblk in range(EB):
            bf = eblk % 2
            p.op("sync", lambda s, eblk=eblk, bf=bf: s.dma_start(
                out=stg[bf][:], in_=win_bf[eblk]),
                reads=[("win_bf", eblk)], writes=stgk[bf], dma=True,
                sem=f"d_w{bf}")
            for k in range(DM):
                p.op("tensor", lambda t, k=k, bf=bf: t.matmul(
                    ps_rep[:, bf:bf + 1], stg[bf][:, k * 128:(k + 1) * 128],
                    bcolb_sb[:, k:k + 1],
                    start=(k == 0), stop=(k == DM - 1)),
                    reads=stgk[bf] + [("bcolb",)],
                    writes=[("ps_rep",)], accum=(k > 0))
            p.op("scalar", lambda s, eblk=eblk, bf=bf: s.copy(
                c_sb[:, eblk:eblk + 1], ps_rep[:, bf:bf + 1]),
                reads=[("ps_rep",)], writes=[("c_sb",)])
            for k in range(DM):
                p.op("vector", lambda v, k=k, bf=bf: v.tensor_scalar(
                    stg[bf][:, k * 128:(k + 1) * 128],
                    stg[bf][:, k * 128:(k + 1) * 128],
                    gcol_sb[:, k:k + 1], None, ALU.mult),
                    reads=stgk[bf] + [("gcol_sb",)], writes=stgk[bf])
            p.op("sync", lambda s, eblk=eblk, bf=bf: s.dma_start(
                out=win_bf[eblk], in_=stg[bf][:]),
                reads=stgk[bf], writes=[("win_bf", eblk)], dma=True,
                sem=f"d_w{bf}")
            xz_eblk(eblk, 0, src=stg[bf], srck=stgk[bf])

    def ln_stats_part(xt_ap, xt_rg, sq_buf, ones_col, ones_rg):
        for km in range(DM):
            p.op("tensor", lambda t, km=km: t.matmul(
                ps_sum[:, 0:LC], ones_col[:],
                xt_ap[:, km * LC:(km + 1) * LC],
                start=(km == 0), stop=(km == DM - 1)),
                reads=[xt_rg[km] if isinstance(xt_rg, list) else xt_rg,
                       ones_rg],
                writes=[("ps_sum",)], accum=(km > 0))
        for km in range(DM):
            sl = slice((km % 2) * LC, (km % 2 + 1) * LC)
            p.op("scalar", lambda s, km=km, sl=sl: s.activation(
                ys_sb[:, sl],
                xt_ap[:, km * LC:(km + 1) * LC], AF.Square),
                reads=[xt_rg[km] if isinstance(xt_rg, list) else xt_rg],
                writes=[("ys", km % 2)])
            p.op("tensor", lambda t, km=km, sl=sl: t.matmul(
                ps_sq[:, 0:LC], onec_f[:], ys_sb[:, sl],
                start=(km == 0), stop=(km == DM - 1)),
                reads=[("ys", km % 2), ("onec_f",)],
                writes=[("ps_sq",)], accum=(km > 0))

    def ln_stats_apply(xt_ap, xt_rg, sq_buf, combine12, ones_col=None,
                       ones_rg=("onec_f",), skip_stats=False):
        if ones_col is None:
            ones_col = onec_f
        if not skip_stats:
            ln_stats_part(xt_ap, xt_rg, sq_buf, ones_col, ones_rg)
        s0 = stat_sb[0:1, 0:LC]
        s1 = stat_sb[0:1, LC:2 * LC]
        s2 = s2row_sb[0:1, 0:LC]
        p.op("vector", lambda v: v.tensor_scalar(
            s0, ps_sum[:, 0:LC], 1.0 / D_MODEL, None, ALU.mult),
            reads=[("ps_sum",), ("ps_sq",)], writes=[("st0",)])
        p.op("vector", lambda v: v.tensor_scalar(
            s1, ps_sq[:, 0:LC], 1.0 / D_MODEL, None, ALU.mult),
            reads=[("ps_sq",), ("ps_sum",)], writes=[("st1",)])
        p.op("vector", lambda v: v.tensor_tensor(s2, s0, s0, ALU.mult),
             reads=[("st0",)], writes=[("st2",)])
        p.op("vector", lambda v: v.tensor_tensor(s1, s1, s2, ALU.subtract),
             reads=[("st1",), ("st2",)], writes=[("st1",)])
        if combine12:
            p.op("scalar", lambda s: s.activation(
                s1, s1, AF.Ln, bias=cvals_sb[0:1, 0:1],
                scale=cvals_sb[0:1, 1:2]),
                reads=[("st1",), ("cv", 0), ("cv", 1)], writes=[("st1",)])
        else:
            p.op("scalar", lambda s: s.activation(
                s1, s1, AF.Ln, bias=cvals_sb[0:1, 2:3]),
                reads=[("st1",), ("cv", 2)], writes=[("st1",)])
        p.op("scalar", lambda s: s.activation(
            s1, s1, AF.Exp, scale=cvals_sb[0:1, 3:4]),
            reads=[("st1",), ("cv", 3)], writes=[("st1",)])
        p.op("tensor", lambda t: t.matmul(
            ps_rep[:, 0:LC], ones_rf[:], s1, start=True, stop=True),
            reads=[("st1",), ("ones_rf",)], writes=[("ps_rep",)])
        p.op("vector", lambda v: v.tensor_tensor(s0, s0, s1, ALU.mult),
             reads=[("st0",), ("st1",)], writes=[("st0",)])
        p.op("tensor", lambda t: t.matmul(
            ps_rep[:, 256:256 + LC], ones_rf[:], s0, start=True, stop=True),
            reads=[("st0",), ("ones_rf",)], writes=[("ps_rep",)])

    def x_load(ch):
        par = ch % 2
        src = xT.rearrange("(km p) l -> p km l", p=128)[:, :, ch * LC:(ch + 1) * LC]
        p.op("gpsimd", lambda g, par=par, src=src: g.dma_start(
            out=xt_sb[par][:].rearrange("p (km l) -> p km l", l=LC), in_=src),
            writes=[("xt", par)], dma=True, sem=f"d_gx{par}")

    def ln12_stats(ch):
        par = ch % 2
        ln_stats_part(xt_sb[par][:], ("xt", par), None, onec_b, ("onec_b",))

    def ln12_mid(ch):
        par = ch % 2
        ln_stats_apply(xt_sb[par][:], ("xt", par), None, combine12=True,
                       ones_col=onec_b, ones_rg=("onec_b",), skip_stats=True)

    def ln12_apply(ch):
        par = ch % 2
        xt = xt_sb[par][:]
        xn = xn_sb[par][:]
        for km in range(DM):
            sl = slice(km * LC, (km + 1) * LC)
            r = km % 4
            if r < 2:
                tmp = ys_sb[:, r * LC:(r + 1) * LC]
                tkey = ("ys", r)
            else:
                tmp = x1_sb[:, (r - 2) * LC:(r - 1) * LC]
                tkey = ("x1", r - 2)
            p.op("vector", lambda v, sl=sl, tmp=tmp: v.tensor_tensor(
                tmp, xt[:, sl], ps_rep[:, 0:LC], ALU.mult),
                reads=[("xt", par), ("ps_rep",)], writes=[tkey])
            p.op("vector", lambda v, sl=sl, tmp=tmp, xn=xn: v.tensor_tensor(
                xn[:, sl], tmp, ps_rep[:, 256:256 + LC], ALU.subtract),
                reads=[tkey, ("ps_rep",)], writes=[("xn", par, km)])

    def xz_eblk(eblk, ch, src=None, srck=None):
        par = eblk % 2
        cpar = ch % 2
        xn = xn_sb[cpar][:]
        if src is None:
            p.op("sync", lambda s, par=par, eblk=eblk: s.dma_start(
                out=wbuf[par][:], in_=win_bf[eblk]),
                reads=[("win_bf", eblk)], writes=[("wbuf", par)], dma=True,
                sem=f"d_w{par}")
            src = wbuf[par]
            srck = [("wbuf", par)]
        for km in range(DM):
            p.op("tensor", lambda t, par=par, km=km, xn=xn, src=src: t.matmul(
                ps_xz[par][:],
                src[:, km * 128:(km + 1) * 128],
                xn[:, km * LC:(km + 1) * LC],
                start=(km == 0), stop=(km == DM - 1)),
                reads=srck + [("xn", cpar, km)],
                writes=[("ps_xz", par)], accum=(km > 0))
        if eblk < DC:
            dc = eblk
            p.op("scalar", lambda s, dc=dc, par=par, eblk=eblk: s.activation(
                xzu_sb[:, dc * LCP + CONV - 1: dc * LCP + CONV - 1 + LC],
                ps_xz[par][:], AF.Identity, bias=c_sb[:, eblk:eblk + 1]),
                reads=[("ps_xz", par), ("c_sb",)], writes=[("xzu", dc)])
        else:
            dc = eblk - DC
            p.op("scalar", lambda s, dc=dc, par=par, cpar=cpar, eblk=eblk:
                 s.activation(
                    gt_sb[cpar][:, dc * LC:(dc + 1) * LC], ps_xz[par][:],
                    AF.Silu, bias=c_sb[:, eblk:eblk + 1]),
                 reads=[("ps_xz", par), ("c_sb",)], writes=[("gt", cpar, dc)])

    def xz_u(ch):
        for eblk in range(DC):
            xz_eblk(eblk, ch)

    def xz_z(ch):
        for eblk in range(DC, EB):
            xz_eblk(eblk, ch)

    def conv_silu(ch):
        for dc in range(DC):
            par = dc % 2
            for k in range(CONV):
                j = dc * CONV + k
                p.op("tensor", lambda t, j=j, dc=dc, k=k, par=par: t.matmul(
                    ps_conv[par][:],
                    diag_sb[:, j * 128:(j + 1) * 128],
                    xzu_sb[:, dc * LCP + k: dc * LCP + k + LC],
                    start=(k == 0), stop=False),
                    reads=[("diag", j), ("xzu", dc)],
                    writes=[("ps_conv", par)], accum=(k > 0))
            p.op("tensor", lambda t, dc=dc, par=par: t.matmul(
                ps_conv[par][:],
                convbr_sb[0:1, dc * 128:(dc + 1) * 128], ones_rb[:],
                start=False, stop=True),
                reads=[("convbr_sb",), ("ones_rb",)],
                writes=[("ps_conv", par)], accum=True)
            p.op("scalar", lambda s, dc=dc, par=par: s.activation(
                u_sb[:, dc * LC:(dc + 1) * LC], ps_conv[par][:], AF.Silu),
                reads=[("ps_conv", par)], writes=[("u", dc)])
        xv = xzu_sb[:].rearrange("p (dc l) -> p dc l", l=LCP)
        p.op("vector", lambda v, xv=xv: v.tensor_copy(
            xv[:, :, 0:CONV - 1], xv[:, :, LC:LC + CONV - 1]),
            reads=XZU, writes=XZU)

    def xdbl_dt(ch):
        par2 = ch % 2
        dt = dt_sb[par2]
        for dc in range(DC):
            p.op("tensor", lambda t, dc=dc: t.matmul(
                ps_small[0:RN, 0:LC],
                wx_sb[:, dc * RN:(dc + 1) * RN],
                u_sb[:, dc * LC:(dc + 1) * LC],
                start=(dc == 0), stop=(dc == DC - 1)),
                reads=[("wx_sb",), ("u", dc)],
                writes=[("ps_xdbl",)], accum=(dc > 0))
        p.op("scalar", lambda s: s.copy(dtr_sb, ps_small[0:R, 0:LC]),
             reads=[("ps_xdbl",)], writes=[("dtr",)])
        p.op("scalar", lambda s: s.copy(
            bc_sb[0:N, :], ps_small[R:R + N, 0:LC]),
            reads=[("ps_xdbl",)], writes=[("bc_b",)])
        p.op("scalar", lambda s: s.copy(
            bc_sb[32:32 + N, :], ps_small[96:96 + N, 0:LC]),
            reads=[("ps_xdbl",)], writes=[("bc_c",)])
        p.op("sync", lambda s, par2=par2: s.dma_start(
            out=bc_dram[par2][0:N], in_=bc_sb[0:N, :]),
            reads=[("bc_b",)], writes=[("bc_dram", par2)],
            dma=True, sem=f"d_bc{par2}")
        p.op("sync", lambda s, par2=par2: s.dma_start(
            out=bc_dram[par2][N:2 * N], in_=bc_sb[32:32 + N, :]),
            reads=[("bc_c",)], writes=[("bc_dram", par2)],
            dma=True, sem=f"d_bc{par2}")




    def dt_chain(ch, dcs):
        par2 = ch % 2
        dt = dt_sb[par2]
        dtslots = [ps_small[:, 256:256 + LC], ps_conv[0][:], ps_conv[1][:]]
        dtkeys = [("ps_dt",), ("ps_conv", 0), ("ps_conv", 1)]
        for dc in dcs:
            slot = dtslots[dc % 3]
            skey = dtkeys[dc % 3]
            wr = [skey] + ([("ps_xdbl",)] if dc == 0 else [])
            p.op("tensor", lambda t, dc=dc, slot=slot: t.matmul(
                slot, wdt_sb[:, dc * 128:(dc + 1) * 128], dtr_sb,
                start=True, stop=True),
                reads=[("wdt_sb",), ("dtr",)], writes=wr)
            p.op("scalar", lambda s, dc=dc, slot=slot: s.activation(
                e1_sb, slot, AF.Exp, bias=bdt_sb[:, dc:dc + 1]),
                reads=[skey, ("bdt_sb",)], writes=[("e1",)])
            p.op("scalar", lambda s, dc=dc, dt=dt: s.activation(
                dt[:, dc * LC:(dc + 1) * LC], e1_sb, AF.Ln,
                bias=onec_f[:, 0:1]),
                reads=[("e1",), ("onec_f",)], writes=[("dt", par2, dc)])

    def bcq_load(ch, q):
        par2 = ch % 2
        par = q % 2
        p.op("sync", lambda s, par2=par2, par=par, q=q: s.dma_start(
            out=bcq_sb[par][:, 0:4 * LC],
            in_=bc_dram[par2][4 * q:4 * q + 4].rearrange("r l -> (r l)")[
                None, :].broadcast_to([128, 4 * LC])),
            reads=[("bc_dram", par2)], writes=[("bcq", par)], dma=True,
            sem=f"d_bn{par}")
        p.op("sync", lambda s, par2=par2, par=par, q=q: s.dma_start(
            out=bcq_sb[par][:, 4 * LC:8 * LC],
            in_=bc_dram[par2][N + 4 * q:N + 4 * q + 4].rearrange(
                "r l -> (r l)")[None, :].broadcast_to([128, 4 * LC])),
            reads=[("bc_dram", par2)], writes=[("bcq", par)], dma=True,
            sem=f"d_bn{par}")

    def uw_y_init(ch):
        par2 = ch % 2
        p.op("vector", lambda v, par2=par2: v.tensor_tensor(
            uw_sb[:], dt_sb[par2][:], u_sb[:], ALU.mult),
            reads=DTK(par2) + U, writes=[("uw",)])
        for dc in range(DC):
            sl = slice(dc * LC, (dc + 1) * LC)
            p.op("scalar", lambda s, sl=sl, dc=dc: s.activation(
                y_sb[:, sl], u_sb[:, sl], AF.Copy,
                scale=dcol_sb[:, dc:dc + 1]),
                reads=[("u", dc), ("dcol_sb",)], writes=[("y", dc)])

    def scan_phase(ch):
        first = (ch == 0)
        av = a_sb[:].rearrange("p (dc l) -> p dc l", l=LC)
        bv = b_sb[:].rearrange("p (dc l) -> p dc l", l=LC)
        H = DC // 2
        nxt = ch + 1 < NCH
        usched = {}
        csched = {}
        for j in range(DC):
            usched.setdefault(2 + j // 2, []).append(j)
            csched.setdefault(3 + j // 2, []).append(j)
        tail_n = 3 + (DC - 1) // 2 + 1
        zsched = {}
        zslots = list(range(max(N - 6, tail_n), N))
        for i in range(DC):
            zsched.setdefault(zslots[i * len(zslots) // DC], []).append(DC + i)
        cpar = ch % 2
        if first:
            bcq_load(ch, 0)
        for n in range(N):
            for h in range(2):
                hdc = range(h * H, (h + 1) * H)
                hsl = slice(h * H * LC, (h + 1) * H * LC)
                HK = [("a", d) for d in hdc]
                HB = [("b", d) for d in hdc]
                if n > 0 or first:
                    for dc in hdc:
                        p.op("scalar", lambda s, n=n, dc=dc, cpar=cpar:
                             s.activation(
                                a_sb[:, dc * LC:(dc + 1) * LC],
                                dt_sb[cpar][:, dc * LC:(dc + 1) * LC],
                                AF.Exp,
                                scale=A_sb[:, dc * N + n:dc * N + n + 1]),
                             reads=[("dt", cpar, dc), ("A_sb",)],
                             writes=[("a", dc)])
                p.op("vector", lambda v, n=n, hsl=hsl: v.tensor_tensor(
                    b_sb[:, hsl].rearrange("p (dc l) -> p dc l", l=LC),
                    uw_sb[:, hsl].rearrange("p (dc l) -> p dc l", l=LC),
                    bcq_sb[(n // 4) % 2][:, (n % 4) * LC:(n % 4 + 1) * LC][
                        :, None, :].broadcast_to([128, H, LC]),
                    ALU.mult),
                    reads=[("uw",), ("bcq", (n // 4) % 2)], writes=HB)
                avh = av[:, h * H:(h + 1) * H]
                bvh = bv[:, h * H:(h + 1) * H]
                if not first:
                    p.op("vector", lambda v, n=n, avh=avh, h=h: v.tensor_tensor(
                        fix_sb[:, 0:H], avh[:, :, 0],
                        hstate_sb[:, n * DC + h * H:n * DC + (h + 1) * H],
                        ALU.mult),
                        reads=HK + [("hstate",)], writes=[("fix",)])
                    p.op("vector", lambda v, bvh=bvh: v.tensor_tensor(
                        bvh[:, :, 0], bvh[:, :, 0], fix_sb[:, 0:H], ALU.add),
                        reads=HB + [("fix",)], writes=HB)
                p.op("vector", lambda v, avh=avh: v.memset(avh[:, :, 0], 0.0),
                     reads=HK, writes=HK)
                p.op("vector", lambda v, hsl=hsl: v.tensor_tensor_scan(
                    b_sb[:, hsl], a_sb[:, hsl], b_sb[:, hsl], 0.0,
                    ALU.mult, ALU.add),
                    reads=HK + HB, writes=HB)
                p.op("vector", lambda v, n=n, bvh=bvh, h=h: v.tensor_copy(
                    hstate_sb[:, n * DC + h * H:n * DC + (h + 1) * H],
                    bvh[:, :, LC - 1]),
                    reads=HB, writes=[("hstate",)])
                hk = ("tp", h)
                p.op("vector", lambda v, n=n, hsl=hsl, h=h: v.tensor_tensor(
                    tprod_sb[:, hsl].rearrange("p (dc l) -> p dc l", l=LC),
                    b_sb[:, hsl].rearrange("p (dc l) -> p dc l", l=LC),
                    bcq_sb[(n // 4) % 2][:, (4 + n % 4) * LC:(4 + n % 4 + 1) * LC][
                        :, None, :].broadcast_to([128, H, LC]),
                    ALU.mult),
                    reads=HB + [("bcq", (n // 4) % 2)], writes=[hk])
                p.op("vector", lambda v, hsl=hsl: v.tensor_tensor(
                    y_sb[:, hsl], y_sb[:, hsl], tprod_sb[:, hsl], ALU.add),
                    reads=[hk] + YK, writes=YK)
            if n % 4 == 1 and n // 4 + 1 < 4:
                bcq_load(ch, n // 4 + 1)
            if nxt:
                for j in usched.get(n, []):
                    xz_eblk(j, ch + 1)
                for j in csched.get(n, []):
                    conv_dc_next(j)
                if n == tail_n:
                    conv_tail_next()
                for j in zsched.get(n, []):
                    xz_eblk(j, ch + 1)
                if n == 8 and ch + 2 < NCH:
                    ln12_stats(ch + 2)
                if n == 9 and ch + 2 < NCH:
                    ln12_mid(ch + 2)
                if n == 11:
                    xdbl_dt(ch + 1)
                if 12 <= n <= 15:
                    dt_chain(ch + 1,
                             range(DC * (n - 12) // 4, DC * (n - 11) // 4))
                if n == 13:
                    bcq_load(ch + 1, 0)
        if nxt:
            cpar2 = (ch + 1) % 2
            for dc in range(DC):
                p.op("scalar", lambda s, dc=dc, cpar2=cpar2: s.activation(
                    a_sb[:, dc * LC:(dc + 1) * LC],
                    dt_sb[cpar2][:, dc * LC:(dc + 1) * LC],
                    AF.Exp, scale=A_sb[:, dc * N:dc * N + 1]),
                    reads=[("dt", cpar2, dc), ("A_sb",)], writes=[("a", dc)])

    def conv_dc_next(dc):
        par = dc % 2
        for k in range(CONV):
            j = dc * CONV + k
            p.op("tensor", lambda t, j=j, dc=dc, k=k, par=par: t.matmul(
                ps_conv[par][:],
                diag_sb[:, j * 128:(j + 1) * 128],
                xzu_sb[:, dc * LCP + k: dc * LCP + k + LC],
                start=(k == 0), stop=False),
                reads=[("diag", j), ("xzu", dc)],
                writes=[("ps_conv", par)], accum=(k > 0))
        p.op("tensor", lambda t, dc=dc, par=par: t.matmul(
            ps_conv[par][:],
            convbr_sb[0:1, dc * 128:(dc + 1) * 128], ones_rb[:],
            start=False, stop=True),
            reads=[("convbr_sb",), ("ones_rb",)],
            writes=[("ps_conv", par)], accum=True)
        p.op("scalar", lambda s, dc=dc, par=par: s.activation(
            u_sb[:, dc * LC:(dc + 1) * LC], ps_conv[par][:], AF.Silu),
            reads=[("ps_conv", par)], writes=[("u", dc)])

    def conv_tail_next():
        xv = xzu_sb[:].rearrange("p (dc l) -> p dc l", l=LCP)
        p.op("vector", lambda v, xv=xv: v.tensor_copy(
            xv[:, :, 0:CONV - 1], xv[:, :, LC:LC + CONV - 1]),
            reads=XZU, writes=XZU)

    def gate_mult(ch):
        cpar = ch % 2
        xv = xzu_sb[:].rearrange("p (dc l) -> p dc l", l=LCP)
        p.op("vector", lambda v, cpar=cpar, xv=xv: v.tensor_tensor(
            xv[:, :, CONV - 1:CONV - 1 + LC],
            y_sb[:].rearrange("p (dc l) -> p dc l", l=LC),
            gt_sb[cpar][:].rearrange("p (dc l) -> p dc l", l=LC),
            ALU.mult),
            reads=YK + [("gt", cpar, d) for d in range(DC)], writes=XZU)

    def gate_mm(ch):
        for km in range(DM):
            par = km % 2
            for dc in range(DC):
                p.op("tensor", lambda t, km=km, dc=dc, par=par: t.matmul(
                    ps_out[par][:],
                    wout_sb[:, dc * D_MODEL + km * 128:
                            dc * D_MODEL + (km + 1) * 128],
                    xzu_sb[:, dc * LCP + CONV - 1: dc * LCP + CONV - 1 + LC],
                    start=(dc == 0), stop=(dc == DC - 1)),
                    reads=[("wout_sb", dc * 4 // DC), ("xzu", dc)],
                    writes=[("ps_o",)], accum=(dc > 0))
            ks = km % 2
            if ch < HCH:
                dst = outst_sb[:, ks * LC:(ks + 1) * LC]
            else:
                dst = outst_sb[:, ks * LC:(ks + 1) * LC][:, ::-1]
            p.op("scalar", lambda s, dst=dst, par=par: s.copy(
                dst, ps_out[par][:]),
                reads=[("ps_o",)], writes=[("outst", ks)])
            if ks == 1:
                kg = km - 1
                if ch < HCH:
                    dst2 = yown_dram.rearrange("(km p) l -> p km l", p=128)[
                        :, kg:kg + 2, ch * LC:(ch + 1) * LC]
                    rg = ("yown_dram", ch)
                else:
                    cc = NCH - 1 - ch
                    base = cc * LC
                    hi = base // (HALF // 2)
                    off = base % (HALF // 2)
                    dst2 = send_dram[hi].rearrange("(km p) l -> p km l", p=128)[
                        :, kg:kg + 2, off:off + LC]
                    rg = ("send_dram", cc)
                p.op("sync", lambda s, dst2=dst2: s.dma_start(
                    out=dst2,
                    in_=outst_sb[:].rearrange("p (km l) -> p km l", l=LC)),
                    reads=[("outst", 0), ("outst", 1)],
                    writes=[rg], dma=True, sem="d_ow")

    def collective_half(hi):
        groups = [[2 * i, 2 * i + 1] for i in range(NCORES // 2)]
        chs = [c for c in range(HCH)
               if hi * (HALF // 2) <= c * LC < (hi + 1) * (HALF // 2)]
        p.op("gpsimd", lambda g, hi=hi: g.collective_compute(
            "AllReduce", ALU.add, replica_groups=groups,
            ins=[send_dram[hi].opt()],
            outs=[ar_dram[hi].opt()]),
            reads=[("send_dram", c) for c in chs],
            writes=[("ar_dram", hi)], sem="cc" if hi == 0 else "cc1",
            inc=1)

    def post_loads(pc):
        par = pc % 2
        xt = xres_sb[:]
        colsl = slice(pc * LC, (pc + 1) * LC)
        p.op("sync", lambda s: s.dma_start(
            out=xt.rearrange("p (km l) -> p km l", l=LC),
            in_=x_res.rearrange("(km p) l -> p km l", p=128)[:, :, colsl]),
            writes=[("xres",)], dma=True, sem=f"d_x{par}")
        p.op("sync", lambda s: s.dma_start(
            out=b_sb[:, 0:DM * LC].rearrange("p (km l) -> p km l", l=LC),
            in_=yown_dram.rearrange("(km p) l -> p km l", p=128)[:, :, colsl]),
            reads=[("yown_dram", pc)], writes=BK, dma=True, sem=f"d_pl{par}")
        arh = 0 if (pc + 1) * LC <= HALF // 2 else 1
        poff = pc * LC - arh * (HALF // 2)
        p.op("sync", lambda s: s.dma_start(
            out=uw_sb[:, 0:DM * LC].rearrange("p (km l) -> p km l", l=LC),
            in_=send_dram[arh].rearrange("(km p) l -> p km l", p=128)[
                :, :, poff:poff + LC]),
            reads=[("send_dram", pc)], writes=[("uw",)], dma=True,
            sem=f"d_pl{par}")
        p.op("sync", lambda s: s.dma_start(
            out=gt_sb[0][:, 0:DM * LC].rearrange("p (km l) -> p km l", l=LC),
            in_=ar_dram[arh].rearrange("(km p) l -> p km l", p=128)[
                :, :, poff:poff + LC]),
            reads=[("ar_dram", arh)], writes=GT, dma=True, sem=f"d_pl{par}")

    def post_A(pc):
        xt = xres_sb[:]
        p.op("vector", lambda v: v.tensor_tensor(
            a_sb[:, 0:DM * LC], gt_sb[0][:, 0:DM * LC], uw_sb[:, 0:DM * LC],
            ALU.subtract),
            reads=GT + [("uw",)], writes=AK)
        p.op("vector", lambda v: v.tensor_tensor(
            dt_sb[0][:, 0:DM * LC], a_sb[:, 0:DM * LC], b_sb[:, 0:DM * LC],
            ALU.add),
            reads=AK + BK, writes=DTK(0))

        ln_stats_apply(xt, ("xres",), None, combine12=False)
        for km in range(DM):
            sl = slice(km * LC, (km + 1) * LC)
            ts = slice((km % 2) * LC, (km % 2 + 1) * LC)
            p.op("vector", lambda v, sl=sl, ts=ts: v.tensor_tensor(
                ys_sb[:, ts], xt[:, sl], ps_rep[:, 0:LC], ALU.mult),
                reads=[("xres",), ("ps_rep",)], writes=[("ys", km % 2)])
            p.op("vector", lambda v, ts=ts: v.tensor_tensor(
                ys_sb[:, ts], ys_sb[:, ts], ps_rep[:, 256:256 + LC],
                ALU.subtract),
                reads=[("ys", km % 2), ("ps_rep",)],
                writes=[("ys", km % 2)])
            p.op("vector", lambda v, sl=sl, ts=ts: v.tensor_tensor(
                ys_sb[:, ts], ys_sb[:, ts], xt[:, sl], ALU.add),
                reads=[("ys", km % 2), ("xres",)],
                writes=[("ys", km % 2)])
            p.op("vector", lambda v, sl=sl, ts=ts: v.tensor_tensor(
                x1_sb[:, sl], ys_sb[:, ts], dt_sb[0][:, sl], ALU.add),
                reads=[("ys", km % 2), ("dt", 0, sl.start // LC)],
                writes=[("x1", sl.start // LC)])

    def post_B(pc, firstpost):
        colsl = slice(pc * LC, (pc + 1) * LC)
        ln_stats_apply(x1_sb[:], X1, None, combine12=False)
        for km in range(DM):
            sl = slice(km * LC, (km + 1) * LC)
            ts = slice((km % 2) * LC, (km % 2 + 1) * LC)
            p.op("vector", lambda v, sl=sl, ts=ts: v.tensor_tensor(
                ys_sb[:, ts], x1_sb[:, sl], ps_rep[:, 0:LC], ALU.mult),
                reads=[("x1", km), ("ps_rep",)], writes=[("ys", km % 2)])
            p.op("vector", lambda v, sl=sl, ts=ts, km=km: v.tensor_tensor(
                xn_sb[0][:, sl], ys_sb[:, ts], ps_rep[:, 256:256 + LC],
                ALU.subtract),
                reads=[("ys", km % 2), ("ps_rep",)], writes=[("xn", 0, km)])

        for km in range(DM):
            wr = [("ps_ffs",)] + ([("ps_xdbl",), ("ps_dt",)] if km == 0 else [])
            p.op("tensor", lambda t, km=km: t.matmul(
                ps_small[0:M, 0:LC], winf_sb[:, km * M:(km + 1) * M],
                xn_sb[0][:, km * LC:(km + 1) * LC],
                start=(km == 0), stop=(km == DM - 1)),
                reads=[("winf_sb",), ("xn", 0, km)],
                writes=wr, accum=(km > 0))
        for i in range(INNER):
            wr = [("hff", i)] + (AK if (firstpost and i == 0) else [])
            p.op("scalar", lambda s, i=i: s.activation(
                hff_sb[0:M, i * LC:(i + 1) * LC], ps_small[0:M, 0:LC], AF.Relu,
                bias=bias1_sb[:, i:i + 1], scale=w1_sb[:, i:i + 1]),
                reads=[("ps_ffs",), ("bias1_sb",), ("w1_sb",)],
                writes=wr)
        p.op("vector", lambda v: v.tensor_scalar(
            gacc_sb[0][:], hff_sb[0:M, 0:LC], w2_sb[:, 0:1], b2_sb[:, 0:1],
            ALU.mult, ALU.add),
            reads=[("hff", 0), ("w2_sb",), ("b2_sb",)], writes=[("gacc", 0)])
        for i in range(1, INNER):
            sp, dp = (i - 1) % 2, i % 2
            p.op("vector", lambda v, i=i, sp=sp, dp=dp: v.scalar_tensor_tensor(
                gacc_sb[dp][:], hff_sb[0:M, i * LC:(i + 1) * LC],
                w2_sb[:, i:i + 1], gacc_sb[sp][:], ALU.mult, ALU.add),
                reads=[("hff", i), ("w2_sb",), ("gacc", sp)],
                writes=[("gacc", dp)])
        gfin = gacc_sb[(INNER - 1) % 2]
        gfin_rg = ("gacc", (INNER - 1) % 2)
        for km in range(DM):
            par2 = km % 2
            p.op("tensor", lambda t, km=km, par2=par2: t.matmul(
                ps_out[par2][:], woutf_sb[:, km * 128:(km + 1) * 128],
                gfin[:], start=True, stop=True),
                reads=[("woutf_sb",), gfin_rg], writes=[("ps_o",)])
            p.op("vector", lambda v, km=km, par2=par2: v.scalar_tensor_tensor(
                x1_sb[:, km * LC:(km + 1) * LC], ps_out[par2][:],
                bout_sb[:, km:km + 1], x1_sb[:, km * LC:(km + 1) * LC],
                ALU.add, ALU.add),
                reads=[("ps_o",), ("bout_sb",), ("x1", km)],
                writes=[("x1", km)])
        p.op("sync", lambda s: s.dma_start(
            out=out_d.rearrange("(km p) l -> p km l", p=128)[:, :, colsl],
            in_=x1_sb[:].rearrange("p (km l) -> p km l", l=LC)),
            reads=X1, writes=[("out_d", pc)], dma=True, sem="d_od")

    x_load(0)
    startup()
    ln12_stats(0)
    ln12_mid(0)
    ln12_apply(0)
    startup_transform()
    startup_prep()
    x_load(1)
    load_wout()
    conv_silu(0)
    xdbl_dt(0)
    dt_chain(0, range(DC))
    if NCH > 1:
        ln12_stats(1)
        ln12_mid(1)
    for k in range(NCH):
        if k > 0:
            gate_mult(k - 1)
        uw_y_init(k)
        if k > 0:
            gate_mm(k - 1)
        if k == NCH - HCH // 2:
            collective_half(1)
        if k + 1 < NCH:
            ln12_apply(k + 1)
        if k + 2 < NCH:
            x_load(k + 2)
        scan_phase(k)
    gate_mult(NCH - 1)
    gate_mm(NCH - 1)
    collective_half(0)
    post_order = list(range(HCH // 2, HCH)) + list(range(HCH // 2))
    post_loads(post_order[0])
    for idx, pc in enumerate(post_order):
        post_A(pc)
        if idx + 1 < len(post_order):
            post_loads(post_order[idx + 1])
        post_B(pc, firstpost=(idx == 0))

    for snm in ["d_st", "d_x0", "d_x1", "d_w0", "d_w1", "d_bc0", "d_bc1",
                "d_ow", "d_pl0", "d_pl1", "d_od", "d_bn0", "d_bn1"]:
        p.q["sync"].append(("wait", snm, p.cnt.get(snm, 0)))
    p.q["gpsimd"].append(("wait", "d_g", p.cnt.get("d_g", 0)))
    p.q["gpsimd"].append(("wait", "d_g2", p.cnt.get("d_g2", 0)))
    p.q["gpsimd"].append(("wait", "d_g3", p.cnt.get("d_g3", 0)))
    p.q["gpsimd"].append(("wait", "d_g4", p.cnt.get("d_g4", 0)))
    p.q["gpsimd"].append(("wait", "d_wo", p.cnt.get("d_wo", 0)))
    p.q["gpsimd"].append(("wait", "d_gx0", p.cnt.get("d_gx0", 0)))
    p.q["gpsimd"].append(("wait", "d_gx1", p.cnt.get("d_gx1", 0)))
    p.q["gpsimd"].append(("wait", "cc", p.cnt.get("cc", 0)))
    p.q["gpsimd"].append(("wait", "cc1", p.cnt.get("cc1", 0)))

    with nc.Block() as block:
        p.finalize(block, sems)
    stack.close()
    return nc


def _prep_core_inputs(cfg, x_b, flip, w, ffn):
    DM, DC, L = cfg["DM"], cfg["DC"], cfg["L"]
    N, R, CONV, M = cfg["N"], cfg["R"], cfg["CONV"], cfg["M"]
    D_MODEL, D_INNER = DM * 128, DC * 128
    EB = 2 * DC
    HALF = L // 2
    C = np.ascontiguousarray

    xd = x_b[::-1] if flip else x_b
    ins = {}
    ins["xT"] = C(xd.T.astype(np.float32))
    ins["x_res"] = C(xd[:HALF].T.astype(np.float32))
    win = np.asarray(w["Win"], dtype=np.float32)
    ins["win_t4"] = C(win.reshape(EB, 128, DM, 128).transpose(0, 3, 2, 1))
    ins["wout_t"] = C(np.asarray(w["Wout"], np.float32).T)
    wxT = np.asarray(w["Wx"], np.float32).T
    wxp = np.zeros((wxT.shape[0], 128), np.float32)
    wxp[:, 0:R] = wxT[:, 0:R]
    wxp[:, R:R + N] = wxT[:, R:R + N]
    wxp[:, 96:96 + N] = wxT[:, R + N:R + 2 * N]
    ins["wx_t"] = C(wxp)
    ins["wdt_t"] = C(np.asarray(w["Wdt"], np.float32).T)
    ins["alog_c"] = C(np.asarray(w["Alog"], np.float32)
                      .reshape(DC, 128, N).transpose(1, 0, 2).reshape(128, DC * N))
    ins["dcol"] = C(np.asarray(w["D"], np.float32).reshape(DC, 128).T)
    ins["bdt_c"] = C(np.asarray(w["bdt"], np.float32).reshape(DC, 128).T)
    ins["convw_c"] = C(np.asarray(w["convw"], np.float32)[:, 0, :]
                       .reshape(DC, 128, CONV).transpose(1, 0, 2)
                       .reshape(128, DC * CONV))
    ins["convb_r"] = C(np.asarray(w["convb"], np.float32)[None, :])
    ins["gcol"] = C(np.asarray(w["norm_g"], np.float32).reshape(DM, 128).T)
    ins["bcol"] = C(np.asarray(w["norm_b"], np.float32).reshape(DM, 128).T)
    ins["winf"] = C(np.asarray(ffn["Win"], np.float32))
    ins["woutf"] = C(np.asarray(ffn["Wout"], np.float32))
    ins["w1"] = C(np.asarray(ffn["W1"], np.float32))
    ins["b1"] = C(np.asarray(ffn["b1"], np.float32))
    ins["w2"] = C(np.asarray(ffn["W2"], np.float32))
    ins["b2c"] = C(np.asarray(ffn["b2"], np.float32)[:, None])
    ins["binc"] = C(np.asarray(ffn["bin_"], np.float32)[:, None])
    ins["bout_c"] = C(np.asarray(ffn["bout"], np.float32).reshape(DM, 128).T)
    ins["eye16"] = np.eye(128, dtype=ml_dtypes.bfloat16)
    return ins


def _assemble(cfg, shards):
    DM, L = cfg["DM"], cfg["L"]
    D_MODEL, HALF = DM * 128, L // 2
    B = len(shards) // 2
    out = np.empty((B, L, D_MODEL), dtype=np.float32)
    for b in range(B):
        out[b, :HALF] = shards[2 * b].T
        out[b, HALF:] = shards[2 * b + 1].T[::-1]
    return out


MAMBA_KEYS = ["Win", "convw", "convb", "Wx", "Wdt", "bdt", "Alog", "D", "Wout"]


def kernel(**inputs):
    cfg = FULL_CFG
    x = np.asarray(inputs["L0_attn_x"], dtype=np.float32)
    Bn = x.shape[0]
    wf = {k: np.asarray(inputs["L0_attn_f_" + k]) for k in MAMBA_KEYS}
    wb = {k: np.asarray(inputs["L0_attn_b_" + k]) for k in MAMBA_KEYS}
    for d in (wf, wb):
        d["norm_g"] = np.asarray(inputs["L0_attn_norm_g"])
        d["norm_b"] = np.asarray(inputs["L0_attn_norm_b"])
    ffn = {k: np.asarray(inputs["L0_ffn_" + k]) for k in
           ["Win", "bin_", "W1", "b1", "W2", "b2", "Wout", "bout"]}

    nc = build_core_program(cfg)
    in_maps = []
    for b in range(Bn):
        in_maps.append(_prep_core_inputs(cfg, x[b], False, wf, ffn))
        in_maps.append(_prep_core_inputs(cfg, x[b], True, wb, ffn))
    res = run_bass_kernel_spmd(nc, in_maps, core_ids=list(range(2 * Bn)))
    shards = [np.asarray(r["out"]) for r in res.results]
    return _assemble(cfg, shards)


# version 31
# speedup vs baseline: 1.1025x; 1.0055x over previous
import numpy as np
import ml_dtypes
from contextlib import ExitStack

import concourse.bass as bass
import concourse.mybir as mybir
from concourse.bass_utils import run_bass_kernel_spmd

F32 = mybir.dt.float32
BF16 = mybir.dt.bfloat16
AF = mybir.ActivationFunctionType
ALU = mybir.AluOpType

EPS = 1e-5

FULL_CFG = dict(DM=8, DC=16, L=4096, LC=256, N=16, R=64, CONV=4, M=64,
                INNER=16, NCORES=8)


class Prog:
    ENGINES = ("tensor", "vector", "scalar", "gpsimd", "sync")

    @staticmethod
    def is_dma_sem(sem):
        return sem.startswith("d_") or sem.startswith("cc")

    def __init__(self, nc):
        self.nc = nc
        self.q = {e: [] for e in self.ENGINES}
        self.cnt = {}
        self.state = {}
        self.waited = {}
        self.opidx = {e: 0 for e in self.ENGINES}
        self.last_drain = {e: -1 for e in self.ENGINES}

    def _need(self, eng, stamp, drains):
        sem, val, peng, pidx = stamp
        if self.is_dma_sem(sem):
            val = self.cnt.get(sem, 0)
        k = (eng, sem)
        if self.waited.get(k, 0) < val:
            self.waited[k] = val
            self.q[eng].append(("wait", sem, val))

    def op(self, eng, fn, reads=(), writes=(), dma=False, sem=None, inc=None,
           accum=False):
        drains = []
        for r in reads:
            st = self.state.get(r)
            if st and st["w"]:
                self._need(eng, st["w"], drains)
        if not accum:
            for r in writes:
                st = self.state.get(r)
                if st:
                    if st["w"]:
                        sem_, val_, peng_, pidx_ = st["w"]
                        if peng_ != eng or self.is_dma_sem(sem_):
                            self._need(eng, st["w"], drains)
                    for stamp in st["r"].values():
                        self._need(eng, stamp, drains)
        if drains and max(drains) > self.last_drain[eng]:
            self.q[eng].append(("drain",))
            self.last_drain[eng] = self.opidx[eng]
        if sem is None:
            assert not dma, "DMA ops must name their semaphore"
            sem = eng
        if inc is None:
            inc = 16 if dma else 1
        self.cnt[sem] = self.cnt.get(sem, 0) + inc
        val = self.cnt[sem]
        self.opidx[eng] += 1
        stamp = (sem, val, eng, self.opidx[eng])
        self.q[eng].append(("op", fn, sem, inc, val))
        for r in writes:
            self.state[r] = {"w": stamp, "r": {}}
        for r in reads:
            st = self.state.setdefault(r, {"w": None, "r": {}})
            st["r"][eng] = stamp
        return stamp

    def finalize(self, block, sems):
        waited = {}
        for ename in self.ENGINES:
            for it in self.q[ename]:
                if it[0] == "wait" and not self.is_dma_sem(it[1]):
                    waited.setdefault(it[1], set()).add(it[2])
        remap = {}
        for sem, vals in waited.items():
            remap[sem] = {v: i + 1 for i, v in enumerate(sorted(vals))}

        for ename in self.ENGINES:
            items = self.q[ename]

            def make_body(items):
                def body(engine):
                    for it in items:
                        if it[0] == "wait":
                            sem, val = it[1], it[2]
                            if sem in remap:
                                val = remap[sem][val]
                            engine.wait_ge(sems[sem], val)
                        elif it[0] == "drain":
                            engine.drain()
                        else:
                            _, fn, sem, inc, val = it
                            inst = fn(engine)
                            if self.is_dma_sem(sem):
                                inst.then_inc(sems[sem], inc)
                            elif sem in remap and val in remap[sem]:
                                inst.then_inc(sems[sem], inc)
                return body
            getattr(block, ename)(make_body(items))


def build_core_program(cfg):
    DM, DC, L, LC = cfg["DM"], cfg["DC"], cfg["L"], cfg["LC"]
    N, R, CONV = cfg["N"], cfg["R"], cfg["CONV"]
    M, INNER = cfg["M"], cfg["INNER"]
    NCORES = cfg["NCORES"]
    D_MODEL = DM * 128
    D_INNER = DC * 128
    NCH = L // LC
    HCH = NCH // 2
    HALF = L // 2
    EB = 2 * DC
    LCP = LC + CONV - 1
    RN = 128

    nc = bass.Bass("TRN2", target_bir_lowering=False)

    def din(name, shape, dt=F32):
        return nc.dram_tensor(name, list(shape), dt, kind="ExternalInput").ap()

    xT = din("xT", [D_MODEL, L])
    x_res = din("x_res", [D_MODEL, HALF])
    win_t4 = din("win_t4", [EB, 128, DM, 128])
    wout_t = din("wout_t", [D_INNER, D_MODEL])
    wx_t = din("wx_t", [D_INNER, RN])
    wdt_t = din("wdt_t", [R, D_INNER])
    alog_c = din("alog_c", [128, DC * N])
    dcol = din("dcol", [128, DC])
    bdt_c = din("bdt_c", [128, DC])
    convw_c = din("convw_c", [128, DC * CONV])
    convb_r = din("convb_r", [1, D_INNER])
    gcol = din("gcol", [128, DM])
    bcol = din("bcol", [128, DM])
    winf = din("winf", [D_MODEL, M])
    woutf = din("woutf", [M, D_MODEL])
    w1 = din("w1", [M, INNER])
    b1 = din("b1", [M, INNER])
    w2 = din("w2", [M, INNER])
    b2c = din("b2c", [M, 1])
    binc = din("binc", [M, 1])
    bout_c = din("bout_c", [128, DM])
    eye16 = din("eye16", [128, 128], BF16)

    out_d = nc.dram_tensor("out", [D_MODEL, HALF], F32, kind="ExternalOutput").ap()

    win_bf = nc.dram_tensor("win_bf", [EB, 128, DM * 128], BF16).ap()
    bc_dram = [nc.dram_tensor(f"bc_dram{i}", [2 * N, LC], BF16).ap()
               for i in range(2)]
    yown_dram = nc.dram_tensor("yown_dram", [D_MODEL, HALF], BF16).ap()
    send_dram = nc.dram_tensor("send_dram", [2, D_MODEL, HALF // 2], BF16).ap()
    ar_dram = nc.dram_tensor("ar_dram", [2, D_MODEL, HALF // 2], BF16).ap()

    stack = ExitStack()

    def sbuf(name, shape, dt):
        return stack.enter_context(nc.sbuf_tensor(name, list(shape), dt))

    def psum(name, shape, dt=F32):
        return stack.enter_context(nc.psum_tensor(name, list(shape), dt))

    wout_sb = sbuf("wout_sb", [128, DC * D_MODEL], BF16)
    wx_sb = sbuf("wx_sb", [128, DC * RN], BF16)
    wdt_sb = sbuf("wdt_sb", [R, D_INNER], BF16)
    winf_sb = sbuf("winf_sb", [128, DM * M], BF16)
    woutf_sb = sbuf("woutf_sb", [M, D_MODEL], BF16)
    diag_sb = sbuf("diag_sb", [128, DC * CONV * 128], BF16)
    eye_sb = sbuf("eye_sb", [128, 128], BF16)
    A_sb = sbuf("A_sb", [128, DC * N], F32)
    dcol_sb = sbuf("dcol_sb", [128, DC], F32)
    bdt_sb = sbuf("bdt_sb", [128, DC], F32)
    convw_sb = sbuf("convw_sb", [128, DC * CONV], F32)
    convbr_sb = sbuf("convbr_sb", [1, D_INNER], BF16)
    gcol_sb = sbuf("gcol_sb", [128, DM], F32)
    bcol_sb = sbuf("bcol_sb", [128, DM], F32)
    w1_sb = sbuf("w1_sb", [M, INNER], F32)
    b1_sb = sbuf("b1_sb", [M, INNER], F32)
    w2_sb = sbuf("w2_sb", [M, INNER], F32)
    bias1_sb = sbuf("bias1_sb", [M, INNER], F32)
    b2_sb = sbuf("b2_sb", [M, 1], F32)
    binc_sb = sbuf("binc_sb", [M, 1], F32)
    bout_sb = sbuf("bout_sb", [128, DM], F32)
    ones_rf = sbuf("ones_rf", [1, 128], F32)
    cvals_sb = sbuf("cvals_sb", [1, 4], F32)
    half_c = sbuf("half_c", [128, 1], F32)
    ones_rb = sbuf("ones_rb", [1, LC], BF16)
    onec_f = sbuf("onec_f", [128, 1], F32)
    onec_b = sbuf("onec_b", [128, 1], BF16)
    c_sb = sbuf("c_sb", [128, EB], F32)
    bcolb_sb = sbuf("bcolb_sb", [128, DM], BF16)

    wbuf = [sbuf(f"wbuf{i}", [128, DM * 128], BF16) for i in range(2)]
    xt_sb = [sbuf(f"xt_sb{i}", [128, DM * LC], BF16) for i in range(2)]
    xn_sb = [sbuf(f"xn_sb{i}", [128, DM * LC], BF16) for i in range(2)]
    xzu_sb = sbuf("xzu_sb", [128, DC * LCP], BF16)
    u_sb = sbuf("u_sb", [128, DC * LC], BF16)
    gt_sb = [sbuf(f"gt_sb{i}", [128, DC * LC], BF16) for i in range(2)]
    dt_sb = [sbuf(f"dt_sb{i}", [128, DC * LC], BF16) for i in range(2)]
    uw_sb = sbuf("uw_sb", [128, DC * LC], BF16)
    a_sb = sbuf("a_sb", [128, DC * LC], BF16)
    b_sb = sbuf("b_sb", [128, DC * LC], BF16)
    y_sb = sbuf("y_sb", [128, DC * LC], BF16)
    tprod_sb = sbuf("tprod_sb", [128, DC * LC], BF16)
    bc_sb = sbuf("bc_sb", [64, LC], BF16)
    bcq_sb = [sbuf(f"bcq_sb{i}", [128, 8 * LC], BF16) for i in range(2)]
    hstate_sb = sbuf("hstate_sb", [128, N * DC], BF16)
    fix_sb = sbuf("fix_sb", [128, DC], BF16)
    outst_sb = sbuf("outst_sb", [128, 2 * LC], BF16)
    stat_sb = sbuf("stat_sb", [1, 2 * LC], F32)
    gacc_sb = [sbuf(f"gacc_sb{i}", [M, LC], BF16) for i in range(2)]
    x1_sb = sbuf("x1_sb", [128, DM * LC], F32)
    xres_sb = sbuf("xres_sb", [128, DM * LC], F32)
    ys_sb = sbuf("ys_sb", [128, 2 * LC], F32)
    s2row_sb = sbuf("s2row_sb", [1, LC], F32)
    e1t_sb = sbuf("e1t_sb", [128, LC], BF16)
    dtrt_sb = sbuf("dtrt_sb", [R, LC], BF16)
    e1_sb = e1t_sb[:, 0:LC]
    dtr_sb = dtrt_sb[0:R, 0:LC]
    if INNER * LC <= DC * LC:
        hff_sb = a_sb
    else:
        hff_sb = sbuf("hff_sb", [M, INNER * LC], BF16)

    ps_rep = psum("ps_rep", [128, 512])
    ps_st = psum("ps_st", [1, 512])
    ps_sum = ps_st[0:1, 0:256]
    ps_sq = ps_st[0:1, 256:512]
    ps_xz = [psum(f"ps_xz{i}", [128, LC]) for i in range(2)]
    ps_conv = [psum(f"ps_conv{i}", [128, LC]) for i in range(2)]
    ps_small = psum("ps_small", [128, 512])
    ps_outb = [psum("ps_outb", [128, LC])]
    ps_out = [ps_outb[0], ps_outb[0]]

    assert LC <= 256

    sem_names = ["tensor", "vector", "scalar", "gpsimd", "sync",
                 "d_st", "d_x0", "d_x1", "d_w0", "d_w1", "d_bc0", "d_bc1",
                 "d_ow", "d_pl0", "d_pl1", "d_od", "d_g", "d_gx0", "d_gx1",
                 "d_bn0", "d_bn1", "cc", "cc1", "d_g2", "d_g3", "d_g4", "d_wo", "d_sw"]
    sems = {nm: stack.enter_context(nc.semaphore(nm)) for nm in sem_names}

    p = Prog(nc)
    W = {"xt": lambda i: ("xt", i), "xn": lambda k: ("xn", k)}

    XN = [("xn", k) for k in range(DM)]
    U = [("u", d) for d in range(DC)]
    AK = [("a", d) for d in range(DC)]
    BK = [("b", d) for d in range(DC)]
    YK = [("y", d) for d in range(DC)]
    def DTK(par):
        return [("dt", par, d) for d in range(DC)]
    GT = [("gt", 0, d) for d in range(DC)]
    XZU = [("xzu", d) for d in range(DC)]
    X1 = [("x1", k) for k in range(DM)]
    YS = [("ys", k) for k in range(DM)]
    OUTST = [("outst", k) for k in range(DM)]

    def startup():
        p.op("gpsimd", lambda g: g.dma_start(
            out=wx_sb[:].rearrange("p (dc r) -> p dc r", r=RN),
            in_=wx_t.rearrange("(dc p) r -> p dc r", p=128)),
            writes=[("wx_sb",)], dma=True, sem="d_sw")
        p.op("gpsimd", lambda g: g.dma_start(out=wdt_sb[:], in_=wdt_t),
             writes=[("wdt_sb",)], dma=True, sem="d_sw")
        p.op("gpsimd", lambda g: g.dma_start(
            out=winf_sb[:].rearrange("p (km m) -> p km m", m=M),
            in_=winf.rearrange("(km p) m -> p km m", p=128)),
            writes=[("winf_sb",)], dma=True, sem="d_sw")
        p.op("gpsimd", lambda g: g.dma_start(out=woutf_sb[:], in_=woutf),
             writes=[("woutf_sb",)], dma=True, sem="d_sw")
        p.op("gpsimd", lambda g: g.dma_start(out=convbr_sb[:], in_=convb_r),
             writes=[("convbr_sb",)], dma=True, sem="d_sw")
        loads = [(A_sb, alog_c, "A_sb"), (dcol_sb, dcol, "dcol_sb"),
                 (bdt_sb, bdt_c, "bdt_sb"), (convw_sb, convw_c, "convw_sb"),
                 (gcol_sb, gcol, "gcol_sb"), (bcol_sb, bcol, "bcol_sb"),
                 (w1_sb, w1, "w1_sb"), (b1_sb, b1, "b1_sb"),
                 (w2_sb, w2, "w2_sb"), (b2_sb, b2c, "b2_sb"),
                 (binc_sb, binc, "binc_sb"), (bout_sb, bout_c, "bout_sb"),
                 (eye_sb, eye16, "eye_sb")]
        for t, src, rg in loads:
            p.op("sync", lambda s, t=t, src=src: s.dma_start(out=t[:], in_=src),
                 writes=[(rg,)], dma=True, sem="d_st")
        p.op("vector", lambda v: v.memset(ones_rf[:], 1.0), writes=[("ones_rf",)])
        p.op("vector", lambda v: v.memset(cvals_sb[0:1, 0:1], EPS * EPS),
             writes=[("cv", 0)])
        p.op("vector", lambda v: v.memset(cvals_sb[0:1, 1:2], 1.0 + EPS),
             writes=[("cv", 1)])
        p.op("vector", lambda v: v.memset(cvals_sb[0:1, 2:3], EPS),
             writes=[("cv", 2)])
        p.op("vector", lambda v: v.memset(cvals_sb[0:1, 3:4], -0.5),
             writes=[("cv", 3)])
        p.op("vector", lambda v: v.memset(half_c[:], 0.5), writes=[("half_c",)])
        p.op("vector", lambda v: v.memset(ones_rb[:], 1.0), writes=[("ones_rb",)])
        p.op("vector", lambda v: v.memset(onec_f[:], 1.0), writes=[("onec_f",)])
        p.op("vector", lambda v: v.memset(onec_b[:], 1.0), writes=[("onec_b",)])
        p.op("vector", lambda v: v.memset(hstate_sb[:], 0.0), writes=[("hstate",)])
        p.op("vector", lambda v: v.memset(xzu_sb[:], 0.0), writes=XZU)

    def startup_prep():
        p.op("scalar", lambda s: s.activation(A_sb[:], A_sb[:], AF.Exp),
             reads=[("A_sb",)], writes=[("A_sb",)])
        p.op("vector", lambda v: v.tensor_scalar(
            A_sb[:], A_sb[:], -1.0, None, ALU.mult),
            reads=[("A_sb",)], writes=[("A_sb",)])
        for dc in range(DC):
            for k in range(CONV):
                j = dc * CONV + k
                p.op("vector", lambda v, j=j: v.tensor_scalar(
                    diag_sb[:, j * 128:(j + 1) * 128], eye_sb[:],
                    convw_sb[:, j:j + 1], None, ALU.mult),
                    reads=[("eye_sb",), ("convw_sb",)], writes=[("diag", j)])
        p.op("vector", lambda v: v.tensor_scalar(
            bias1_sb[:], w1_sb[:], binc_sb[:, 0:1], None, ALU.mult),
            reads=[("w1_sb",), ("binc_sb",)], writes=[("bias1_sb",)])
        p.op("vector", lambda v: v.tensor_tensor(
            bias1_sb[:], bias1_sb[:], b1_sb[:], ALU.add),
            reads=[("bias1_sb",), ("b1_sb",)], writes=[("bias1_sb",)])

    def load_wout():
        for j in range(4):
            d0 = DC // 4 * j
            d1 = DC // 4 * (j + 1)
            p.op("gpsimd", lambda g, d0=d0, d1=d1: g.dma_start(
                out=wout_sb[:, d0 * D_MODEL:d1 * D_MODEL].rearrange(
                    "p (dc d) -> p dc d", d=D_MODEL),
                in_=wout_t.rearrange("(dc p) d -> p dc d", p=128)[:, d0:d1]),
                writes=[("wout_sb", j)], dma=True, sem="d_wo")

    def startup_transform():
        p.op("vector", lambda v: v.tensor_copy(bcolb_sb[:], bcol_sb[:]),
             reads=[("bcol_sb",)], writes=[("bcolb",)])
        WB = DM * 128
        nub = WB // LC
        stg = [u_sb[:, 0:WB], u_sb[:, WB:2 * WB]]
        stgk = [[("u", d) for d in range(nub)],
                [("u", d) for d in range(nub, 2 * nub)]]
        castq = ["d_g", "d_g2", "d_g3", "d_g4"]

        def cast(eblk):
            rd = [("win_bf", eblk - 4)] if eblk >= 4 else []
            p.op("gpsimd", lambda g, eblk=eblk: g.dma_start(
                out=win_bf[eblk], in_=win_t4[eblk]),
                reads=rd, writes=[("win_bf", eblk)], dma=True,
                sem=castq[eblk % 4])

        for eblk in range(min(4, EB)):
            cast(eblk)
        for eblk in range(EB):
            bf = eblk % 2
            p.op("sync", lambda s, eblk=eblk, bf=bf: s.dma_start(
                out=stg[bf][:], in_=win_bf[eblk]),
                reads=[("win_bf", eblk)], writes=stgk[bf], dma=True,
                sem=f"d_w{bf}")
            if eblk + 4 < EB:
                cast(eblk + 4)
            for k in range(DM):
                p.op("tensor", lambda t, k=k, bf=bf: t.matmul(
                    ps_rep[:, bf:bf + 1], stg[bf][:, k * 128:(k + 1) * 128],
                    bcolb_sb[:, k:k + 1],
                    start=(k == 0), stop=(k == DM - 1)),
                    reads=stgk[bf] + [("bcolb",)],
                    writes=[("ps_rep",)], accum=(k > 0))
            p.op("scalar", lambda s, eblk=eblk, bf=bf: s.copy(
                c_sb[:, eblk:eblk + 1], ps_rep[:, bf:bf + 1]),
                reads=[("ps_rep",)], writes=[("c_sb",)])
            for k in range(DM):
                p.op("vector", lambda v, k=k, bf=bf: v.tensor_scalar(
                    stg[bf][:, k * 128:(k + 1) * 128],
                    stg[bf][:, k * 128:(k + 1) * 128],
                    gcol_sb[:, k:k + 1], None, ALU.mult),
                    reads=stgk[bf] + [("gcol_sb",)], writes=stgk[bf])
            p.op("sync", lambda s, eblk=eblk, bf=bf: s.dma_start(
                out=win_bf[eblk], in_=stg[bf][:]),
                reads=stgk[bf], writes=[("win_bf", eblk)], dma=True,
                sem=f"d_w{bf}")
            xz_eblk(eblk, 0, src=stg[bf], srck=stgk[bf])

    def ln_stats_part(xt_ap, xt_rg, sq_buf, ones_col, ones_rg):
        for km in range(DM):
            p.op("tensor", lambda t, km=km: t.matmul(
                ps_sum[:, 0:LC], ones_col[:],
                xt_ap[:, km * LC:(km + 1) * LC],
                start=(km == 0), stop=(km == DM - 1)),
                reads=[xt_rg[km] if isinstance(xt_rg, list) else xt_rg,
                       ones_rg],
                writes=[("ps_sum",)], accum=(km > 0))
        for km in range(DM):
            sl = slice((km % 2) * LC, (km % 2 + 1) * LC)
            p.op("scalar", lambda s, km=km, sl=sl: s.activation(
                ys_sb[:, sl],
                xt_ap[:, km * LC:(km + 1) * LC], AF.Square),
                reads=[xt_rg[km] if isinstance(xt_rg, list) else xt_rg],
                writes=[("ys", km % 2)])
            p.op("tensor", lambda t, km=km, sl=sl: t.matmul(
                ps_sq[:, 0:LC], onec_f[:], ys_sb[:, sl],
                start=(km == 0), stop=(km == DM - 1)),
                reads=[("ys", km % 2), ("onec_f",)],
                writes=[("ps_sq",)], accum=(km > 0))

    def ln_stats_apply(xt_ap, xt_rg, sq_buf, combine12, ones_col=None,
                       ones_rg=("onec_f",), skip_stats=False):
        if ones_col is None:
            ones_col = onec_f
        if not skip_stats:
            ln_stats_part(xt_ap, xt_rg, sq_buf, ones_col, ones_rg)
        s0 = stat_sb[0:1, 0:LC]
        s1 = stat_sb[0:1, LC:2 * LC]
        s2 = s2row_sb[0:1, 0:LC]
        p.op("vector", lambda v: v.tensor_scalar(
            s0, ps_sum[:, 0:LC], 1.0 / D_MODEL, None, ALU.mult),
            reads=[("ps_sum",), ("ps_sq",)], writes=[("st0",)])
        p.op("vector", lambda v: v.tensor_scalar(
            s1, ps_sq[:, 0:LC], 1.0 / D_MODEL, None, ALU.mult),
            reads=[("ps_sq",), ("ps_sum",)], writes=[("st1",)])
        p.op("vector", lambda v: v.tensor_tensor(s2, s0, s0, ALU.mult),
             reads=[("st0",)], writes=[("st2",)])
        p.op("vector", lambda v: v.tensor_tensor(s1, s1, s2, ALU.subtract),
             reads=[("st1",), ("st2",)], writes=[("st1",)])
        if combine12:
            p.op("scalar", lambda s: s.activation(
                s1, s1, AF.Ln, bias=cvals_sb[0:1, 0:1],
                scale=cvals_sb[0:1, 1:2]),
                reads=[("st1",), ("cv", 0), ("cv", 1)], writes=[("st1",)])
        else:
            p.op("scalar", lambda s: s.activation(
                s1, s1, AF.Ln, bias=cvals_sb[0:1, 2:3]),
                reads=[("st1",), ("cv", 2)], writes=[("st1",)])
        p.op("scalar", lambda s: s.activation(
            s1, s1, AF.Exp, scale=cvals_sb[0:1, 3:4]),
            reads=[("st1",), ("cv", 3)], writes=[("st1",)])
        p.op("tensor", lambda t: t.matmul(
            ps_rep[:, 0:LC], ones_rf[:], s1, start=True, stop=True),
            reads=[("st1",), ("ones_rf",)], writes=[("ps_rep",)])
        p.op("vector", lambda v: v.tensor_tensor(s0, s0, s1, ALU.mult),
             reads=[("st0",), ("st1",)], writes=[("st0",)])
        p.op("tensor", lambda t: t.matmul(
            ps_rep[:, 256:256 + LC], ones_rf[:], s0, start=True, stop=True),
            reads=[("st0",), ("ones_rf",)], writes=[("ps_rep",)])

    def x_load(ch):
        par = ch % 2
        src = xT.rearrange("(km p) l -> p km l", p=128)[:, :, ch * LC:(ch + 1) * LC]
        p.op("gpsimd", lambda g, par=par, src=src: g.dma_start(
            out=xt_sb[par][:].rearrange("p (km l) -> p km l", l=LC), in_=src),
            writes=[("xt", par)], dma=True, sem=f"d_gx{par}")

    def ln12_stats(ch):
        par = ch % 2
        ln_stats_part(xt_sb[par][:], ("xt", par), None, onec_b, ("onec_b",))

    def ln12_mid(ch):
        par = ch % 2
        ln_stats_apply(xt_sb[par][:], ("xt", par), None, combine12=True,
                       ones_col=onec_b, ones_rg=("onec_b",), skip_stats=True)

    def ln12_apply(ch):
        par = ch % 2
        xt = xt_sb[par][:]
        xn = xn_sb[par][:]
        for km in range(DM):
            sl = slice(km * LC, (km + 1) * LC)
            r = km % 4
            if r < 2:
                tmp = ys_sb[:, r * LC:(r + 1) * LC]
                tkey = ("ys", r)
            else:
                tmp = x1_sb[:, (r - 2) * LC:(r - 1) * LC]
                tkey = ("x1", r - 2)
            p.op("vector", lambda v, sl=sl, tmp=tmp: v.tensor_tensor(
                tmp, xt[:, sl], ps_rep[:, 0:LC], ALU.mult),
                reads=[("xt", par), ("ps_rep",)], writes=[tkey])
            p.op("vector", lambda v, sl=sl, tmp=tmp, xn=xn: v.tensor_tensor(
                xn[:, sl], tmp, ps_rep[:, 256:256 + LC], ALU.subtract),
                reads=[tkey, ("ps_rep",)], writes=[("xn", par, km)])

    def xz_eblk(eblk, ch, src=None, srck=None):
        par = eblk % 2
        cpar = ch % 2
        xn = xn_sb[cpar][:]
        if src is None:
            p.op("sync", lambda s, par=par, eblk=eblk: s.dma_start(
                out=wbuf[par][:], in_=win_bf[eblk]),
                reads=[("win_bf", eblk)], writes=[("wbuf", par)], dma=True,
                sem=f"d_w{par}")
            src = wbuf[par]
            srck = [("wbuf", par)]
        for km in range(DM):
            p.op("tensor", lambda t, par=par, km=km, xn=xn, src=src: t.matmul(
                ps_xz[par][:],
                src[:, km * 128:(km + 1) * 128],
                xn[:, km * LC:(km + 1) * LC],
                start=(km == 0), stop=(km == DM - 1)),
                reads=srck + [("xn", cpar, km)],
                writes=[("ps_xz", par)], accum=(km > 0))
        if eblk < DC:
            dc = eblk
            p.op("scalar", lambda s, dc=dc, par=par, eblk=eblk: s.activation(
                xzu_sb[:, dc * LCP + CONV - 1: dc * LCP + CONV - 1 + LC],
                ps_xz[par][:], AF.Identity, bias=c_sb[:, eblk:eblk + 1]),
                reads=[("ps_xz", par), ("c_sb",)], writes=[("xzu", dc)])
        else:
            dc = eblk - DC
            p.op("scalar", lambda s, dc=dc, par=par, cpar=cpar, eblk=eblk:
                 s.activation(
                    gt_sb[cpar][:, dc * LC:(dc + 1) * LC], ps_xz[par][:],
                    AF.Silu, bias=c_sb[:, eblk:eblk + 1]),
                 reads=[("ps_xz", par), ("c_sb",)], writes=[("gt", cpar, dc)])

    def xz_u(ch):
        for eblk in range(DC):
            xz_eblk(eblk, ch)

    def xz_z(ch):
        for eblk in range(DC, EB):
            xz_eblk(eblk, ch)

    def conv_silu(ch):
        for dc in range(DC):
            par = dc % 2
            for k in range(CONV):
                j = dc * CONV + k
                p.op("tensor", lambda t, j=j, dc=dc, k=k, par=par: t.matmul(
                    ps_conv[par][:],
                    diag_sb[:, j * 128:(j + 1) * 128],
                    xzu_sb[:, dc * LCP + k: dc * LCP + k + LC],
                    start=(k == 0), stop=False),
                    reads=[("diag", j), ("xzu", dc)],
                    writes=[("ps_conv", par)], accum=(k > 0))
            p.op("tensor", lambda t, dc=dc, par=par: t.matmul(
                ps_conv[par][:],
                convbr_sb[0:1, dc * 128:(dc + 1) * 128], ones_rb[:],
                start=False, stop=True),
                reads=[("convbr_sb",), ("ones_rb",)],
                writes=[("ps_conv", par)], accum=True)
            p.op("scalar", lambda s, dc=dc, par=par: s.activation(
                u_sb[:, dc * LC:(dc + 1) * LC], ps_conv[par][:], AF.Silu),
                reads=[("ps_conv", par)], writes=[("u", dc)])
        xv = xzu_sb[:].rearrange("p (dc l) -> p dc l", l=LCP)
        p.op("vector", lambda v, xv=xv: v.tensor_copy(
            xv[:, :, 0:CONV - 1], xv[:, :, LC:LC + CONV - 1]),
            reads=XZU, writes=XZU)

    def xdbl_dt(ch):
        par2 = ch % 2
        dt = dt_sb[par2]
        for dc in range(DC):
            p.op("tensor", lambda t, dc=dc: t.matmul(
                ps_small[0:RN, 0:LC],
                wx_sb[:, dc * RN:(dc + 1) * RN],
                u_sb[:, dc * LC:(dc + 1) * LC],
                start=(dc == 0), stop=(dc == DC - 1)),
                reads=[("wx_sb",), ("u", dc)],
                writes=[("ps_xdbl",)], accum=(dc > 0))
        p.op("scalar", lambda s: s.copy(dtr_sb, ps_small[0:R, 0:LC]),
             reads=[("ps_xdbl",)], writes=[("dtr",)])
        p.op("scalar", lambda s: s.copy(
            bc_sb[0:N, :], ps_small[R:R + N, 0:LC]),
            reads=[("ps_xdbl",)], writes=[("bc_b",)])
        p.op("scalar", lambda s: s.copy(
            bc_sb[32:32 + N, :], ps_small[96:96 + N, 0:LC]),
            reads=[("ps_xdbl",)], writes=[("bc_c",)])
        p.op("sync", lambda s, par2=par2: s.dma_start(
            out=bc_dram[par2][0:N], in_=bc_sb[0:N, :]),
            reads=[("bc_b",)], writes=[("bc_dram", par2)],
            dma=True, sem=f"d_bc{par2}")
        p.op("sync", lambda s, par2=par2: s.dma_start(
            out=bc_dram[par2][N:2 * N], in_=bc_sb[32:32 + N, :]),
            reads=[("bc_c",)], writes=[("bc_dram", par2)],
            dma=True, sem=f"d_bc{par2}")




    def dt_chain(ch, dcs):
        par2 = ch % 2
        dt = dt_sb[par2]
        dtslots = [ps_small[:, 256:256 + LC], ps_conv[0][:], ps_conv[1][:]]
        dtkeys = [("ps_dt",), ("ps_conv", 0), ("ps_conv", 1)]
        for dc in dcs:
            slot = dtslots[dc % 3]
            skey = dtkeys[dc % 3]
            wr = [skey] + ([("ps_xdbl",)] if dc == 0 else [])
            p.op("tensor", lambda t, dc=dc, slot=slot: t.matmul(
                slot, wdt_sb[:, dc * 128:(dc + 1) * 128], dtr_sb,
                start=True, stop=True),
                reads=[("wdt_sb",), ("dtr",)], writes=wr)
            p.op("scalar", lambda s, dc=dc, slot=slot: s.activation(
                e1_sb, slot, AF.Exp, bias=bdt_sb[:, dc:dc + 1]),
                reads=[skey, ("bdt_sb",)], writes=[("e1",)])
            p.op("scalar", lambda s, dc=dc, dt=dt: s.activation(
                dt[:, dc * LC:(dc + 1) * LC], e1_sb, AF.Ln,
                bias=onec_f[:, 0:1]),
                reads=[("e1",), ("onec_f",)], writes=[("dt", par2, dc)])

    def bcq_load(ch, q):
        par2 = ch % 2
        par = q % 2
        p.op("sync", lambda s, par2=par2, par=par, q=q: s.dma_start(
            out=bcq_sb[par][:, 0:4 * LC],
            in_=bc_dram[par2][4 * q:4 * q + 4].rearrange("r l -> (r l)")[
                None, :].broadcast_to([128, 4 * LC])),
            reads=[("bc_dram", par2)], writes=[("bcq", par)], dma=True,
            sem=f"d_bn{par}")
        p.op("sync", lambda s, par2=par2, par=par, q=q: s.dma_start(
            out=bcq_sb[par][:, 4 * LC:8 * LC],
            in_=bc_dram[par2][N + 4 * q:N + 4 * q + 4].rearrange(
                "r l -> (r l)")[None, :].broadcast_to([128, 4 * LC])),
            reads=[("bc_dram", par2)], writes=[("bcq", par)], dma=True,
            sem=f"d_bn{par}")

    def uw_y_init(ch):
        par2 = ch % 2
        p.op("vector", lambda v, par2=par2: v.tensor_tensor(
            uw_sb[:], dt_sb[par2][:], u_sb[:], ALU.mult),
            reads=DTK(par2) + U, writes=[("uw",)])
        for dc in range(DC):
            sl = slice(dc * LC, (dc + 1) * LC)
            p.op("scalar", lambda s, sl=sl, dc=dc: s.activation(
                y_sb[:, sl], u_sb[:, sl], AF.Copy,
                scale=dcol_sb[:, dc:dc + 1]),
                reads=[("u", dc), ("dcol_sb",)], writes=[("y", dc)])

    def scan_phase(ch):
        first = (ch == 0)
        av = a_sb[:].rearrange("p (dc l) -> p dc l", l=LC)
        bv = b_sb[:].rearrange("p (dc l) -> p dc l", l=LC)
        H = DC // 2
        nxt = ch + 1 < NCH
        usched = {}
        csched = {}
        for j in range(DC):
            usched.setdefault(2 + j // 2, []).append(j)
            csched.setdefault(3 + j // 2, []).append(j)
        tail_n = 3 + (DC - 1) // 2 + 1
        zsched = {}
        zslots = list(range(max(N - 6, tail_n), N))
        for i in range(DC):
            zsched.setdefault(zslots[i * len(zslots) // DC], []).append(DC + i)
        cpar = ch % 2
        if first:
            bcq_load(ch, 0)
        for n in range(N):
            for h in range(2):
                hdc = range(h * H, (h + 1) * H)
                hsl = slice(h * H * LC, (h + 1) * H * LC)
                HK = [("a", d) for d in hdc]
                HB = [("b", d) for d in hdc]
                if n > 0 or first:
                    for dc in hdc:
                        p.op("scalar", lambda s, n=n, dc=dc, cpar=cpar:
                             s.activation(
                                a_sb[:, dc * LC:(dc + 1) * LC],
                                dt_sb[cpar][:, dc * LC:(dc + 1) * LC],
                                AF.Exp,
                                scale=A_sb[:, dc * N + n:dc * N + n + 1]),
                             reads=[("dt", cpar, dc), ("A_sb",)],
                             writes=[("a", dc)])
                p.op("vector", lambda v, n=n, hsl=hsl: v.tensor_tensor(
                    b_sb[:, hsl].rearrange("p (dc l) -> p dc l", l=LC),
                    uw_sb[:, hsl].rearrange("p (dc l) -> p dc l", l=LC),
                    bcq_sb[(n // 4) % 2][:, (n % 4) * LC:(n % 4 + 1) * LC][
                        :, None, :].broadcast_to([128, H, LC]),
                    ALU.mult),
                    reads=[("uw",), ("bcq", (n // 4) % 2)], writes=HB)
                avh = av[:, h * H:(h + 1) * H]
                bvh = bv[:, h * H:(h + 1) * H]
                if not first:
                    p.op("vector", lambda v, n=n, avh=avh, h=h: v.tensor_tensor(
                        fix_sb[:, 0:H], avh[:, :, 0],
                        hstate_sb[:, n * DC + h * H:n * DC + (h + 1) * H],
                        ALU.mult),
                        reads=HK + [("hstate",)], writes=[("fix",)])
                    p.op("vector", lambda v, bvh=bvh: v.tensor_tensor(
                        bvh[:, :, 0], bvh[:, :, 0], fix_sb[:, 0:H], ALU.add),
                        reads=HB + [("fix",)], writes=HB)
                p.op("vector", lambda v, avh=avh: v.memset(avh[:, :, 0], 0.0),
                     reads=HK, writes=HK)
                p.op("vector", lambda v, hsl=hsl: v.tensor_tensor_scan(
                    b_sb[:, hsl], a_sb[:, hsl], b_sb[:, hsl], 0.0,
                    ALU.mult, ALU.add),
                    reads=HK + HB, writes=HB)
                p.op("vector", lambda v, n=n, bvh=bvh, h=h: v.tensor_copy(
                    hstate_sb[:, n * DC + h * H:n * DC + (h + 1) * H],
                    bvh[:, :, LC - 1]),
                    reads=HB, writes=[("hstate",)])
                hk = ("tp", h)
                p.op("vector", lambda v, n=n, hsl=hsl, h=h: v.tensor_tensor(
                    tprod_sb[:, hsl].rearrange("p (dc l) -> p dc l", l=LC),
                    b_sb[:, hsl].rearrange("p (dc l) -> p dc l", l=LC),
                    bcq_sb[(n // 4) % 2][:, (4 + n % 4) * LC:(4 + n % 4 + 1) * LC][
                        :, None, :].broadcast_to([128, H, LC]),
                    ALU.mult),
                    reads=HB + [("bcq", (n // 4) % 2)], writes=[hk])
                p.op("vector", lambda v, hsl=hsl: v.tensor_tensor(
                    y_sb[:, hsl], y_sb[:, hsl], tprod_sb[:, hsl], ALU.add),
                    reads=[hk] + YK, writes=YK)
            if n % 4 == 1 and n // 4 + 1 < 4:
                bcq_load(ch, n // 4 + 1)
            if nxt:
                for j in usched.get(n, []):
                    xz_eblk(j, ch + 1)
                for j in csched.get(n, []):
                    conv_dc_next(j)
                if n == tail_n:
                    conv_tail_next()
                for j in zsched.get(n, []):
                    xz_eblk(j, ch + 1)
                if n == 8 and ch + 2 < NCH:
                    ln12_stats(ch + 2)
                if n == 9 and ch + 2 < NCH:
                    ln12_mid(ch + 2)
                if n == 11:
                    xdbl_dt(ch + 1)
                if 12 <= n <= 15:
                    dt_chain(ch + 1,
                             range(DC * (n - 12) // 4, DC * (n - 11) // 4))
                if n == 13:
                    bcq_load(ch + 1, 0)
        if nxt:
            cpar2 = (ch + 1) % 2
            for dc in range(DC):
                p.op("scalar", lambda s, dc=dc, cpar2=cpar2: s.activation(
                    a_sb[:, dc * LC:(dc + 1) * LC],
                    dt_sb[cpar2][:, dc * LC:(dc + 1) * LC],
                    AF.Exp, scale=A_sb[:, dc * N:dc * N + 1]),
                    reads=[("dt", cpar2, dc), ("A_sb",)], writes=[("a", dc)])

    def conv_dc_next(dc):
        par = dc % 2
        for k in range(CONV):
            j = dc * CONV + k
            p.op("tensor", lambda t, j=j, dc=dc, k=k, par=par: t.matmul(
                ps_conv[par][:],
                diag_sb[:, j * 128:(j + 1) * 128],
                xzu_sb[:, dc * LCP + k: dc * LCP + k + LC],
                start=(k == 0), stop=False),
                reads=[("diag", j), ("xzu", dc)],
                writes=[("ps_conv", par)], accum=(k > 0))
        p.op("tensor", lambda t, dc=dc, par=par: t.matmul(
            ps_conv[par][:],
            convbr_sb[0:1, dc * 128:(dc + 1) * 128], ones_rb[:],
            start=False, stop=True),
            reads=[("convbr_sb",), ("ones_rb",)],
            writes=[("ps_conv", par)], accum=True)
        p.op("scalar", lambda s, dc=dc, par=par: s.activation(
            u_sb[:, dc * LC:(dc + 1) * LC], ps_conv[par][:], AF.Silu),
            reads=[("ps_conv", par)], writes=[("u", dc)])

    def conv_tail_next():
        xv = xzu_sb[:].rearrange("p (dc l) -> p dc l", l=LCP)
        p.op("vector", lambda v, xv=xv: v.tensor_copy(
            xv[:, :, 0:CONV - 1], xv[:, :, LC:LC + CONV - 1]),
            reads=XZU, writes=XZU)

    def gate_mult(ch):
        cpar = ch % 2
        xv = xzu_sb[:].rearrange("p (dc l) -> p dc l", l=LCP)
        p.op("vector", lambda v, cpar=cpar, xv=xv: v.tensor_tensor(
            xv[:, :, CONV - 1:CONV - 1 + LC],
            y_sb[:].rearrange("p (dc l) -> p dc l", l=LC),
            gt_sb[cpar][:].rearrange("p (dc l) -> p dc l", l=LC),
            ALU.mult),
            reads=YK + [("gt", cpar, d) for d in range(DC)], writes=XZU)

    def gate_mm(ch):
        for km in range(DM):
            par = km % 2
            for dc in range(DC):
                p.op("tensor", lambda t, km=km, dc=dc, par=par: t.matmul(
                    ps_out[par][:],
                    wout_sb[:, dc * D_MODEL + km * 128:
                            dc * D_MODEL + (km + 1) * 128],
                    xzu_sb[:, dc * LCP + CONV - 1: dc * LCP + CONV - 1 + LC],
                    start=(dc == 0), stop=(dc == DC - 1)),
                    reads=[("wout_sb", dc * 4 // DC), ("xzu", dc)],
                    writes=[("ps_o",)], accum=(dc > 0))
            ks = km % 2
            if ch < HCH:
                dst = outst_sb[:, ks * LC:(ks + 1) * LC]
            else:
                dst = outst_sb[:, ks * LC:(ks + 1) * LC][:, ::-1]
            p.op("scalar", lambda s, dst=dst, par=par: s.copy(
                dst, ps_out[par][:]),
                reads=[("ps_o",)], writes=[("outst", ks)])
            if ks == 1:
                kg = km - 1
                if ch < HCH:
                    dst2 = yown_dram.rearrange("(km p) l -> p km l", p=128)[
                        :, kg:kg + 2, ch * LC:(ch + 1) * LC]
                    rg = ("yown_dram", ch)
                else:
                    cc = NCH - 1 - ch
                    base = cc * LC
                    hi = base // (HALF // 2)
                    off = base % (HALF // 2)
                    dst2 = send_dram[hi].rearrange("(km p) l -> p km l", p=128)[
                        :, kg:kg + 2, off:off + LC]
                    rg = ("send_dram", cc)
                p.op("sync", lambda s, dst2=dst2: s.dma_start(
                    out=dst2,
                    in_=outst_sb[:].rearrange("p (km l) -> p km l", l=LC)),
                    reads=[("outst", 0), ("outst", 1)],
                    writes=[rg], dma=True, sem="d_ow")

    def collective_half(hi):
        groups = [[2 * i, 2 * i + 1] for i in range(NCORES // 2)]
        chs = [c for c in range(HCH)
               if hi * (HALF // 2) <= c * LC < (hi + 1) * (HALF // 2)]
        p.op("gpsimd", lambda g, hi=hi: g.collective_compute(
            "AllReduce", ALU.add, replica_groups=groups,
            ins=[send_dram[hi].opt()],
            outs=[ar_dram[hi].opt()]),
            reads=[("send_dram", c) for c in chs],
            writes=[("ar_dram", hi)], sem="cc" if hi == 0 else "cc1",
            inc=1)

    def post_loads(pc):
        par = pc % 2
        xt = xres_sb[:]
        colsl = slice(pc * LC, (pc + 1) * LC)
        p.op("sync", lambda s: s.dma_start(
            out=xt.rearrange("p (km l) -> p km l", l=LC),
            in_=x_res.rearrange("(km p) l -> p km l", p=128)[:, :, colsl]),
            writes=[("xres",)], dma=True, sem=f"d_x{par}")
        p.op("sync", lambda s: s.dma_start(
            out=b_sb[:, 0:DM * LC].rearrange("p (km l) -> p km l", l=LC),
            in_=yown_dram.rearrange("(km p) l -> p km l", p=128)[:, :, colsl]),
            reads=[("yown_dram", pc)], writes=BK, dma=True, sem=f"d_pl{par}")
        arh = 0 if (pc + 1) * LC <= HALF // 2 else 1
        poff = pc * LC - arh * (HALF // 2)
        p.op("sync", lambda s: s.dma_start(
            out=uw_sb[:, 0:DM * LC].rearrange("p (km l) -> p km l", l=LC),
            in_=send_dram[arh].rearrange("(km p) l -> p km l", p=128)[
                :, :, poff:poff + LC]),
            reads=[("send_dram", pc)], writes=[("uw",)], dma=True,
            sem=f"d_pl{par}")
        p.op("sync", lambda s: s.dma_start(
            out=gt_sb[0][:, 0:DM * LC].rearrange("p (km l) -> p km l", l=LC),
            in_=ar_dram[arh].rearrange("(km p) l -> p km l", p=128)[
                :, :, poff:poff + LC]),
            reads=[("ar_dram", arh)], writes=GT, dma=True, sem=f"d_pl{par}")

    def post_A(pc):
        xt = xres_sb[:]
        p.op("vector", lambda v: v.tensor_tensor(
            a_sb[:, 0:DM * LC], gt_sb[0][:, 0:DM * LC], uw_sb[:, 0:DM * LC],
            ALU.subtract),
            reads=GT + [("uw",)], writes=AK)
        p.op("vector", lambda v: v.tensor_tensor(
            dt_sb[0][:, 0:DM * LC], a_sb[:, 0:DM * LC], b_sb[:, 0:DM * LC],
            ALU.add),
            reads=AK + BK, writes=DTK(0))

        ln_stats_apply(xt, ("xres",), None, combine12=False)
        for km in range(DM):
            sl = slice(km * LC, (km + 1) * LC)
            ts = slice((km % 2) * LC, (km % 2 + 1) * LC)
            p.op("vector", lambda v, sl=sl, ts=ts: v.tensor_tensor(
                ys_sb[:, ts], xt[:, sl], ps_rep[:, 0:LC], ALU.mult),
                reads=[("xres",), ("ps_rep",)], writes=[("ys", km % 2)])
            p.op("vector", lambda v, ts=ts: v.tensor_tensor(
                ys_sb[:, ts], ys_sb[:, ts], ps_rep[:, 256:256 + LC],
                ALU.subtract),
                reads=[("ys", km % 2), ("ps_rep",)],
                writes=[("ys", km % 2)])
            p.op("vector", lambda v, sl=sl, ts=ts: v.tensor_tensor(
                ys_sb[:, ts], ys_sb[:, ts], xt[:, sl], ALU.add),
                reads=[("ys", km % 2), ("xres",)],
                writes=[("ys", km % 2)])
            p.op("vector", lambda v, sl=sl, ts=ts: v.tensor_tensor(
                x1_sb[:, sl], ys_sb[:, ts], dt_sb[0][:, sl], ALU.add),
                reads=[("ys", km % 2), ("dt", 0, sl.start // LC)],
                writes=[("x1", sl.start // LC)])

    def post_B(pc, firstpost):
        colsl = slice(pc * LC, (pc + 1) * LC)
        ln_stats_apply(x1_sb[:], X1, None, combine12=False)
        for km in range(DM):
            sl = slice(km * LC, (km + 1) * LC)
            ts = slice((km % 2) * LC, (km % 2 + 1) * LC)
            p.op("vector", lambda v, sl=sl, ts=ts: v.tensor_tensor(
                ys_sb[:, ts], x1_sb[:, sl], ps_rep[:, 0:LC], ALU.mult),
                reads=[("x1", km), ("ps_rep",)], writes=[("ys", km % 2)])
            p.op("vector", lambda v, sl=sl, ts=ts, km=km: v.tensor_tensor(
                xn_sb[0][:, sl], ys_sb[:, ts], ps_rep[:, 256:256 + LC],
                ALU.subtract),
                reads=[("ys", km % 2), ("ps_rep",)], writes=[("xn", 0, km)])

        for km in range(DM):
            wr = [("ps_ffs",)] + ([("ps_xdbl",), ("ps_dt",)] if km == 0 else [])
            p.op("tensor", lambda t, km=km: t.matmul(
                ps_small[0:M, 0:LC], winf_sb[:, km * M:(km + 1) * M],
                xn_sb[0][:, km * LC:(km + 1) * LC],
                start=(km == 0), stop=(km == DM - 1)),
                reads=[("winf_sb",), ("xn", 0, km)],
                writes=wr, accum=(km > 0))
        for i in range(INNER):
            wr = [("hff", i)] + (AK if (firstpost and i == 0) else [])
            p.op("scalar", lambda s, i=i: s.activation(
                hff_sb[0:M, i * LC:(i + 1) * LC], ps_small[0:M, 0:LC], AF.Relu,
                bias=bias1_sb[:, i:i + 1], scale=w1_sb[:, i:i + 1]),
                reads=[("ps_ffs",), ("bias1_sb",), ("w1_sb",)],
                writes=wr)
        p.op("vector", lambda v: v.tensor_scalar(
            gacc_sb[0][:], hff_sb[0:M, 0:LC], w2_sb[:, 0:1], b2_sb[:, 0:1],
            ALU.mult, ALU.add),
            reads=[("hff", 0), ("w2_sb",), ("b2_sb",)], writes=[("gacc", 0)])
        for i in range(1, INNER):
            sp, dp = (i - 1) % 2, i % 2
            p.op("vector", lambda v, i=i, sp=sp, dp=dp: v.scalar_tensor_tensor(
                gacc_sb[dp][:], hff_sb[0:M, i * LC:(i + 1) * LC],
                w2_sb[:, i:i + 1], gacc_sb[sp][:], ALU.mult, ALU.add),
                reads=[("hff", i), ("w2_sb",), ("gacc", sp)],
                writes=[("gacc", dp)])
        gfin = gacc_sb[(INNER - 1) % 2]
        gfin_rg = ("gacc", (INNER - 1) % 2)
        for km in range(DM):
            par2 = km % 2
            p.op("tensor", lambda t, km=km, par2=par2: t.matmul(
                ps_out[par2][:], woutf_sb[:, km * 128:(km + 1) * 128],
                gfin[:], start=True, stop=True),
                reads=[("woutf_sb",), gfin_rg], writes=[("ps_o",)])
            p.op("vector", lambda v, km=km, par2=par2: v.scalar_tensor_tensor(
                x1_sb[:, km * LC:(km + 1) * LC], ps_out[par2][:],
                bout_sb[:, km:km + 1], x1_sb[:, km * LC:(km + 1) * LC],
                ALU.add, ALU.add),
                reads=[("ps_o",), ("bout_sb",), ("x1", km)],
                writes=[("x1", km)])
        p.op("sync", lambda s: s.dma_start(
            out=out_d.rearrange("(km p) l -> p km l", p=128)[:, :, colsl],
            in_=x1_sb[:].rearrange("p (km l) -> p km l", l=LC)),
            reads=X1, writes=[("out_d", pc)], dma=True, sem="d_od")

    x_load(0)
    startup()
    ln12_stats(0)
    ln12_mid(0)
    ln12_apply(0)
    x_load(1)
    startup_transform()
    if NCH > 1:
        ln12_stats(1)
        ln12_mid(1)
    startup_prep()
    load_wout()
    conv_silu(0)
    xdbl_dt(0)
    dt_chain(0, range(DC))
    for k in range(NCH):
        if k > 0:
            gate_mult(k - 1)
        uw_y_init(k)
        if k > 0:
            gate_mm(k - 1)
        if k == NCH - HCH // 2:
            collective_half(1)
        if k + 1 < NCH:
            ln12_apply(k + 1)
        if k + 2 < NCH:
            x_load(k + 2)
        scan_phase(k)
    gate_mult(NCH - 1)
    gate_mm(NCH - 1)
    collective_half(0)
    post_order = list(range(HCH // 2, HCH)) + list(range(HCH // 2))
    post_loads(post_order[0])
    for idx, pc in enumerate(post_order):
        post_A(pc)
        if idx + 1 < len(post_order):
            post_loads(post_order[idx + 1])
        post_B(pc, firstpost=(idx == 0))

    for snm in ["d_st", "d_x0", "d_x1", "d_w0", "d_w1", "d_bc0", "d_bc1",
                "d_ow", "d_pl0", "d_pl1", "d_od", "d_bn0", "d_bn1"]:
        p.q["sync"].append(("wait", snm, p.cnt.get(snm, 0)))
    p.q["gpsimd"].append(("wait", "d_g", p.cnt.get("d_g", 0)))
    p.q["gpsimd"].append(("wait", "d_g2", p.cnt.get("d_g2", 0)))
    p.q["gpsimd"].append(("wait", "d_g3", p.cnt.get("d_g3", 0)))
    p.q["gpsimd"].append(("wait", "d_g4", p.cnt.get("d_g4", 0)))
    p.q["gpsimd"].append(("wait", "d_wo", p.cnt.get("d_wo", 0)))
    p.q["gpsimd"].append(("wait", "d_sw", p.cnt.get("d_sw", 0)))
    p.q["gpsimd"].append(("wait", "d_gx0", p.cnt.get("d_gx0", 0)))
    p.q["gpsimd"].append(("wait", "d_gx1", p.cnt.get("d_gx1", 0)))
    p.q["gpsimd"].append(("wait", "cc", p.cnt.get("cc", 0)))
    p.q["gpsimd"].append(("wait", "cc1", p.cnt.get("cc1", 0)))

    with nc.Block() as block:
        p.finalize(block, sems)
    stack.close()
    return nc


def _prep_core_inputs(cfg, x_b, flip, w, ffn):
    DM, DC, L = cfg["DM"], cfg["DC"], cfg["L"]
    N, R, CONV, M = cfg["N"], cfg["R"], cfg["CONV"], cfg["M"]
    D_MODEL, D_INNER = DM * 128, DC * 128
    EB = 2 * DC
    HALF = L // 2
    C = np.ascontiguousarray

    xd = x_b[::-1] if flip else x_b
    ins = {}
    ins["xT"] = C(xd.T.astype(np.float32))
    ins["x_res"] = C(xd[:HALF].T.astype(np.float32))
    win = np.asarray(w["Win"], dtype=np.float32)
    ins["win_t4"] = C(win.reshape(EB, 128, DM, 128).transpose(0, 3, 2, 1))
    ins["wout_t"] = C(np.asarray(w["Wout"], np.float32).T)
    wxT = np.asarray(w["Wx"], np.float32).T
    wxp = np.zeros((wxT.shape[0], 128), np.float32)
    wxp[:, 0:R] = wxT[:, 0:R]
    wxp[:, R:R + N] = wxT[:, R:R + N]
    wxp[:, 96:96 + N] = wxT[:, R + N:R + 2 * N]
    ins["wx_t"] = C(wxp)
    ins["wdt_t"] = C(np.asarray(w["Wdt"], np.float32).T)
    ins["alog_c"] = C(np.asarray(w["Alog"], np.float32)
                      .reshape(DC, 128, N).transpose(1, 0, 2).reshape(128, DC * N))
    ins["dcol"] = C(np.asarray(w["D"], np.float32).reshape(DC, 128).T)
    ins["bdt_c"] = C(np.asarray(w["bdt"], np.float32).reshape(DC, 128).T)
    ins["convw_c"] = C(np.asarray(w["convw"], np.float32)[:, 0, :]
                       .reshape(DC, 128, CONV).transpose(1, 0, 2)
                       .reshape(128, DC * CONV))
    ins["convb_r"] = C(np.asarray(w["convb"], np.float32)[None, :])
    ins["gcol"] = C(np.asarray(w["norm_g"], np.float32).reshape(DM, 128).T)
    ins["bcol"] = C(np.asarray(w["norm_b"], np.float32).reshape(DM, 128).T)
    ins["winf"] = C(np.asarray(ffn["Win"], np.float32))
    ins["woutf"] = C(np.asarray(ffn["Wout"], np.float32))
    ins["w1"] = C(np.asarray(ffn["W1"], np.float32))
    ins["b1"] = C(np.asarray(ffn["b1"], np.float32))
    ins["w2"] = C(np.asarray(ffn["W2"], np.float32))
    ins["b2c"] = C(np.asarray(ffn["b2"], np.float32)[:, None])
    ins["binc"] = C(np.asarray(ffn["bin_"], np.float32)[:, None])
    ins["bout_c"] = C(np.asarray(ffn["bout"], np.float32).reshape(DM, 128).T)
    ins["eye16"] = np.eye(128, dtype=ml_dtypes.bfloat16)
    return ins


def _assemble(cfg, shards):
    DM, L = cfg["DM"], cfg["L"]
    D_MODEL, HALF = DM * 128, L // 2
    B = len(shards) // 2
    out = np.empty((B, L, D_MODEL), dtype=np.float32)
    for b in range(B):
        out[b, :HALF] = shards[2 * b].T
        out[b, HALF:] = shards[2 * b + 1].T[::-1]
    return out


MAMBA_KEYS = ["Win", "convw", "convb", "Wx", "Wdt", "bdt", "Alog", "D", "Wout"]


def kernel(**inputs):
    cfg = FULL_CFG
    x = np.asarray(inputs["L0_attn_x"], dtype=np.float32)
    Bn = x.shape[0]
    wf = {k: np.asarray(inputs["L0_attn_f_" + k]) for k in MAMBA_KEYS}
    wb = {k: np.asarray(inputs["L0_attn_b_" + k]) for k in MAMBA_KEYS}
    for d in (wf, wb):
        d["norm_g"] = np.asarray(inputs["L0_attn_norm_g"])
        d["norm_b"] = np.asarray(inputs["L0_attn_norm_b"])
    ffn = {k: np.asarray(inputs["L0_ffn_" + k]) for k in
           ["Win", "bin_", "W1", "b1", "W2", "b2", "Wout", "bout"]}

    nc = build_core_program(cfg)
    in_maps = []
    for b in range(Bn):
        in_maps.append(_prep_core_inputs(cfg, x[b], False, wf, ffn))
        in_maps.append(_prep_core_inputs(cfg, x[b], True, wb, ffn))
    res = run_bass_kernel_spmd(nc, in_maps, core_ids=list(range(2 * Bn)))
    shards = [np.asarray(r["out"]) for r in res.results]
    return _assemble(cfg, shards)


# version 32
# speedup vs baseline: 1.1214x; 1.0172x over previous
import numpy as np
import ml_dtypes
from contextlib import ExitStack

import concourse.bass as bass
import concourse.mybir as mybir
from concourse.bass_utils import run_bass_kernel_spmd

F32 = mybir.dt.float32
BF16 = mybir.dt.bfloat16
AF = mybir.ActivationFunctionType
ALU = mybir.AluOpType

EPS = 1e-5

FULL_CFG = dict(DM=8, DC=16, L=4096, LC=256, N=16, R=64, CONV=4, M=64,
                INNER=16, NCORES=8)


class Prog:
    ENGINES = ("tensor", "vector", "scalar", "gpsimd", "sync")

    @staticmethod
    def is_dma_sem(sem):
        return sem.startswith("d_") or sem.startswith("cc")

    def __init__(self, nc):
        self.nc = nc
        self.q = {e: [] for e in self.ENGINES}
        self.cnt = {}
        self.state = {}
        self.waited = {}
        self.opidx = {e: 0 for e in self.ENGINES}
        self.last_drain = {e: -1 for e in self.ENGINES}

    def _need(self, eng, stamp, drains):
        sem, val, peng, pidx = stamp
        if self.is_dma_sem(sem):
            val = self.cnt.get(sem, 0)
        k = (eng, sem)
        if self.waited.get(k, 0) < val:
            self.waited[k] = val
            self.q[eng].append(("wait", sem, val))

    def op(self, eng, fn, reads=(), writes=(), dma=False, sem=None, inc=None,
           accum=False):
        drains = []
        for r in reads:
            st = self.state.get(r)
            if st and st["w"]:
                self._need(eng, st["w"], drains)
        if not accum:
            for r in writes:
                st = self.state.get(r)
                if st:
                    if st["w"]:
                        sem_, val_, peng_, pidx_ = st["w"]
                        if peng_ != eng or self.is_dma_sem(sem_):
                            self._need(eng, st["w"], drains)
                    for stamp in st["r"].values():
                        self._need(eng, stamp, drains)
        if drains and max(drains) > self.last_drain[eng]:
            self.q[eng].append(("drain",))
            self.last_drain[eng] = self.opidx[eng]
        if sem is None:
            assert not dma, "DMA ops must name their semaphore"
            sem = eng
        if inc is None:
            inc = 16 if dma else 1
        self.cnt[sem] = self.cnt.get(sem, 0) + inc
        val = self.cnt[sem]
        self.opidx[eng] += 1
        stamp = (sem, val, eng, self.opidx[eng])
        self.q[eng].append(("op", fn, sem, inc, val))
        for r in writes:
            self.state[r] = {"w": stamp, "r": {}}
        for r in reads:
            st = self.state.setdefault(r, {"w": None, "r": {}})
            st["r"][eng] = stamp
        return stamp

    def finalize(self, block, sems):
        waited = {}
        for ename in self.ENGINES:
            for it in self.q[ename]:
                if it[0] == "wait" and not self.is_dma_sem(it[1]):
                    waited.setdefault(it[1], set()).add(it[2])
        remap = {}
        for sem, vals in waited.items():
            remap[sem] = {v: i + 1 for i, v in enumerate(sorted(vals))}

        for ename in self.ENGINES:
            items = self.q[ename]

            def make_body(items):
                def body(engine):
                    for it in items:
                        if it[0] == "wait":
                            sem, val = it[1], it[2]
                            if sem in remap:
                                val = remap[sem][val]
                            engine.wait_ge(sems[sem], val)
                        elif it[0] == "drain":
                            engine.drain()
                        else:
                            _, fn, sem, inc, val = it
                            inst = fn(engine)
                            if self.is_dma_sem(sem):
                                inst.then_inc(sems[sem], inc)
                            elif sem in remap and val in remap[sem]:
                                inst.then_inc(sems[sem], inc)
                return body
            getattr(block, ename)(make_body(items))


def build_core_program(cfg):
    DM, DC, L, LC = cfg["DM"], cfg["DC"], cfg["L"], cfg["LC"]
    N, R, CONV = cfg["N"], cfg["R"], cfg["CONV"]
    M, INNER = cfg["M"], cfg["INNER"]
    NCORES = cfg["NCORES"]
    D_MODEL = DM * 128
    D_INNER = DC * 128
    NCH = L // LC
    HCH = NCH // 2
    HALF = L // 2
    EB = 2 * DC
    LCP = LC + CONV - 1
    RN = 128

    nc = bass.Bass("TRN2", target_bir_lowering=False)

    def din(name, shape, dt=F32):
        return nc.dram_tensor(name, list(shape), dt, kind="ExternalInput").ap()

    xT = din("xT", [D_MODEL, L])
    x_res = din("x_res", [D_MODEL, HALF])
    win_t4 = din("win_t4", [EB, 128, DM, 128])
    wout_t = din("wout_t", [D_INNER, D_MODEL])
    wx_t = din("wx_t", [D_INNER, RN])
    wdt_t = din("wdt_t", [R, D_INNER])
    alog_c = din("alog_c", [128, DC * N])
    dcol = din("dcol", [128, DC])
    bdt_c = din("bdt_c", [128, DC])
    convw_c = din("convw_c", [128, DC * CONV])
    convb_r = din("convb_r", [1, D_INNER])
    gcol = din("gcol", [128, DM])
    bcol = din("bcol", [128, DM])
    winf = din("winf", [D_MODEL, M])
    woutf = din("woutf", [M, D_MODEL])
    w1 = din("w1", [M, INNER])
    b1 = din("b1", [M, INNER])
    w2 = din("w2", [M, INNER])
    b2c = din("b2c", [M, 1])
    binc = din("binc", [M, 1])
    bout_c = din("bout_c", [128, DM])
    eye16 = din("eye16", [128, 128], BF16)

    out_d = nc.dram_tensor("out", [D_MODEL, HALF], F32, kind="ExternalOutput").ap()

    win_bf = nc.dram_tensor("win_bf", [EB, 128, DM * 128], BF16).ap()
    bc_dram = [nc.dram_tensor(f"bc_dram{i}", [2 * N, LC], BF16).ap()
               for i in range(2)]
    yown_dram = nc.dram_tensor("yown_dram", [D_MODEL, HALF], BF16).ap()
    send_dram = nc.dram_tensor("send_dram", [2, D_MODEL, HALF // 2], BF16).ap()
    ar_dram = nc.dram_tensor("ar_dram", [2, D_MODEL, HALF // 2], BF16).ap()

    stack = ExitStack()

    def sbuf(name, shape, dt):
        return stack.enter_context(nc.sbuf_tensor(name, list(shape), dt))

    def psum(name, shape, dt=F32):
        return stack.enter_context(nc.psum_tensor(name, list(shape), dt))

    wout_sb = sbuf("wout_sb", [128, DC * D_MODEL], BF16)
    wx_sb = sbuf("wx_sb", [128, DC * RN], BF16)
    wdt_sb = sbuf("wdt_sb", [R, D_INNER], BF16)
    winf_sb = sbuf("winf_sb", [128, DM * M], BF16)
    woutf_sb = sbuf("woutf_sb", [M, D_MODEL], BF16)
    diag_sb = sbuf("diag_sb", [128, DC * CONV * 128], BF16)
    eye_sb = sbuf("eye_sb", [128, 128], BF16)
    A_sb = sbuf("A_sb", [128, DC * N], F32)
    dcol_sb = sbuf("dcol_sb", [128, DC], F32)
    bdt_sb = sbuf("bdt_sb", [128, DC], F32)
    convw_sb = sbuf("convw_sb", [128, DC * CONV], F32)
    convbr_sb = sbuf("convbr_sb", [1, D_INNER], BF16)
    gcol_sb = sbuf("gcol_sb", [128, DM], F32)
    bcol_sb = sbuf("bcol_sb", [128, DM], F32)
    w1_sb = sbuf("w1_sb", [M, INNER], F32)
    b1_sb = sbuf("b1_sb", [M, INNER], F32)
    w2_sb = sbuf("w2_sb", [M, INNER], F32)
    bias1_sb = sbuf("bias1_sb", [M, INNER], F32)
    b2_sb = sbuf("b2_sb", [M, 1], F32)
    binc_sb = sbuf("binc_sb", [M, 1], F32)
    bout_sb = sbuf("bout_sb", [128, DM], F32)
    ones_rf = sbuf("ones_rf", [1, 128], F32)
    cvals_sb = sbuf("cvals_sb", [1, 4], F32)
    half_c = sbuf("half_c", [128, 1], F32)
    ones_rb = sbuf("ones_rb", [1, LC], BF16)
    onec_f = sbuf("onec_f", [128, 1], F32)
    onec_b = sbuf("onec_b", [128, 1], BF16)
    c_sb = sbuf("c_sb", [128, EB], F32)
    bcolb_sb = sbuf("bcolb_sb", [128, DM], BF16)

    wbuf = [sbuf(f"wbuf{i}", [128, DM * 128], BF16) for i in range(2)]
    xt_sb = [sbuf(f"xt_sb{i}", [128, DM * LC], BF16) for i in range(2)]
    xn_sb = [sbuf(f"xn_sb{i}", [128, DM * LC], BF16) for i in range(2)]
    xzu_sb = sbuf("xzu_sb", [128, DC * LCP], BF16)
    u_sb = sbuf("u_sb", [128, DC * LC], BF16)
    gt_sb = [sbuf(f"gt_sb{i}", [128, DC * LC], BF16) for i in range(2)]
    dt_sb = [sbuf(f"dt_sb{i}", [128, DC * LC], BF16) for i in range(2)]
    uw_sb = sbuf("uw_sb", [128, DC * LC], BF16)
    a_sb = sbuf("a_sb", [128, DC * LC], BF16)
    b_sb = sbuf("b_sb", [128, DC * LC], BF16)
    y_sb = sbuf("y_sb", [128, DC * LC], BF16)
    tprod_sb = sbuf("tprod_sb", [128, DC * LC], BF16)
    bc_sb = sbuf("bc_sb", [64, LC], BF16)
    bcq_sb = [sbuf(f"bcq_sb{i}", [128, 8 * LC], BF16) for i in range(2)]
    hstate_sb = sbuf("hstate_sb", [128, N * DC], BF16)
    fix_sb = sbuf("fix_sb", [128, DC], BF16)
    outst_sb = sbuf("outst_sb", [128, 2 * LC], BF16)
    stat_sb = sbuf("stat_sb", [1, 2 * LC], F32)
    gacc_sb = [sbuf(f"gacc_sb{i}", [M, LC], BF16) for i in range(2)]
    x1_sb = sbuf("x1_sb", [128, DM * LC], F32)
    xres_sb = sbuf("xres_sb", [128, DM * LC], F32)
    ys_sb = sbuf("ys_sb", [128, 2 * LC], F32)
    s2row_sb = sbuf("s2row_sb", [1, LC], F32)
    e1t_sb = sbuf("e1t_sb", [128, LC], BF16)
    dtrt_sb = sbuf("dtrt_sb", [R, LC], BF16)
    e1_sb = e1t_sb[:, 0:LC]
    dtr_sb = dtrt_sb[0:R, 0:LC]
    if INNER * LC <= DC * LC:
        hff_sb = a_sb
    else:
        hff_sb = sbuf("hff_sb", [M, INNER * LC], BF16)

    ps_rep = psum("ps_rep", [128, 512])
    ps_st = psum("ps_st", [1, 512])
    ps_sum = ps_st[0:1, 0:256]
    ps_sq = ps_st[0:1, 256:512]
    ps_xz = [psum(f"ps_xz{i}", [128, LC]) for i in range(2)]
    ps_conv = [psum(f"ps_conv{i}", [128, LC]) for i in range(2)]
    ps_small = psum("ps_small", [128, 512])
    ps_outb = [psum("ps_outb", [128, LC])]
    ps_out = [ps_outb[0], ps_outb[0]]

    assert LC <= 256

    sem_names = ["tensor", "vector", "scalar", "gpsimd", "sync",
                 "d_st", "d_x0", "d_x1", "d_w0", "d_w1", "d_bc0", "d_bc1",
                 "d_ow", "d_pl0", "d_pl1", "d_od", "d_g", "d_gx0", "d_gx1",
                 "d_bn0", "d_bn1", "cc", "cc1", "d_g2", "d_g3", "d_g4", "d_wo", "d_sw"]
    sems = {nm: stack.enter_context(nc.semaphore(nm)) for nm in sem_names}

    p = Prog(nc)
    W = {"xt": lambda i: ("xt", i), "xn": lambda k: ("xn", k)}

    XN = [("xn", k) for k in range(DM)]
    U = [("u", d) for d in range(DC)]
    AK = [("a", d) for d in range(DC)]
    BK = [("b", d) for d in range(DC)]
    YK = [("y", d) for d in range(DC)]
    def DTK(par):
        return [("dt", par, d) for d in range(DC)]
    GT = [("gt", 0, d) for d in range(DC)]
    XZU = [("xzu", d) for d in range(DC)]
    X1 = [("x1", k) for k in range(DM)]
    YS = [("ys", k) for k in range(DM)]
    OUTST = [("outst", k) for k in range(DM)]

    def startup():
        p.op("gpsimd", lambda g: g.dma_start(
            out=wx_sb[:].rearrange("p (dc r) -> p dc r", r=RN),
            in_=wx_t.rearrange("(dc p) r -> p dc r", p=128)),
            writes=[("wx_sb",)], dma=True, sem="d_sw")
        p.op("gpsimd", lambda g: g.dma_start(out=wdt_sb[:], in_=wdt_t),
             writes=[("wdt_sb",)], dma=True, sem="d_sw")
        p.op("gpsimd", lambda g: g.dma_start(
            out=winf_sb[:].rearrange("p (km m) -> p km m", m=M),
            in_=winf.rearrange("(km p) m -> p km m", p=128)),
            writes=[("winf_sb",)], dma=True, sem="d_sw")
        p.op("gpsimd", lambda g: g.dma_start(out=woutf_sb[:], in_=woutf),
             writes=[("woutf_sb",)], dma=True, sem="d_sw")
        p.op("gpsimd", lambda g: g.dma_start(out=convbr_sb[:], in_=convb_r),
             writes=[("convbr_sb",)], dma=True, sem="d_sw")
        loads = [(A_sb, alog_c, "A_sb"), (dcol_sb, dcol, "dcol_sb"),
                 (bdt_sb, bdt_c, "bdt_sb"), (convw_sb, convw_c, "convw_sb"),
                 (gcol_sb, gcol, "gcol_sb"), (bcol_sb, bcol, "bcol_sb"),
                 (w1_sb, w1, "w1_sb"), (b1_sb, b1, "b1_sb"),
                 (w2_sb, w2, "w2_sb"), (b2_sb, b2c, "b2_sb"),
                 (binc_sb, binc, "binc_sb"), (bout_sb, bout_c, "bout_sb"),
                 (eye_sb, eye16, "eye_sb")]
        for t, src, rg in loads:
            p.op("sync", lambda s, t=t, src=src: s.dma_start(out=t[:], in_=src),
                 writes=[(rg,)], dma=True, sem="d_st")
        p.op("vector", lambda v: v.memset(ones_rf[:], 1.0), writes=[("ones_rf",)])
        p.op("vector", lambda v: v.memset(cvals_sb[0:1, 0:1], EPS * EPS),
             writes=[("cv", 0)])
        p.op("vector", lambda v: v.memset(cvals_sb[0:1, 1:2], 1.0 + EPS),
             writes=[("cv", 1)])
        p.op("vector", lambda v: v.memset(cvals_sb[0:1, 2:3], EPS),
             writes=[("cv", 2)])
        p.op("vector", lambda v: v.memset(cvals_sb[0:1, 3:4], -0.5),
             writes=[("cv", 3)])
        p.op("vector", lambda v: v.memset(half_c[:], 0.5), writes=[("half_c",)])
        p.op("vector", lambda v: v.memset(ones_rb[:], 1.0), writes=[("ones_rb",)])
        p.op("vector", lambda v: v.memset(onec_f[:], 1.0), writes=[("onec_f",)])
        p.op("vector", lambda v: v.memset(onec_b[:], 1.0), writes=[("onec_b",)])
        p.op("vector", lambda v: v.memset(hstate_sb[:], 0.0), writes=[("hstate",)])
        p.op("vector", lambda v: v.memset(xzu_sb[:], 0.0), writes=XZU)

    def startup_prep():
        p.op("scalar", lambda s: s.activation(A_sb[:], A_sb[:], AF.Exp),
             reads=[("A_sb",)], writes=[("A_sb",)])
        p.op("vector", lambda v: v.tensor_scalar(
            A_sb[:], A_sb[:], -1.0, None, ALU.mult),
            reads=[("A_sb",)], writes=[("A_sb",)])
        for dc in range(DC):
            for k in range(CONV):
                j = dc * CONV + k
                p.op("vector", lambda v, j=j: v.tensor_scalar(
                    diag_sb[:, j * 128:(j + 1) * 128], eye_sb[:],
                    convw_sb[:, j:j + 1], None, ALU.mult),
                    reads=[("eye_sb",), ("convw_sb",)], writes=[("diag", j)])
        p.op("vector", lambda v: v.tensor_scalar(
            bias1_sb[:], w1_sb[:], binc_sb[:, 0:1], None, ALU.mult),
            reads=[("w1_sb",), ("binc_sb",)], writes=[("bias1_sb",)])
        p.op("vector", lambda v: v.tensor_tensor(
            bias1_sb[:], bias1_sb[:], b1_sb[:], ALU.add),
            reads=[("bias1_sb",), ("b1_sb",)], writes=[("bias1_sb",)])

    def load_wout():
        for j in range(4):
            d0 = DC // 4 * j
            d1 = DC // 4 * (j + 1)
            p.op("gpsimd", lambda g, d0=d0, d1=d1: g.dma_start(
                out=wout_sb[:, d0 * D_MODEL:d1 * D_MODEL].rearrange(
                    "p (dc d) -> p dc d", d=D_MODEL),
                in_=wout_t.rearrange("(dc p) d -> p dc d", p=128)[:, d0:d1]),
                writes=[("wout_sb", j)], dma=True, sem="d_wo")

    def startup_transform():
        WB = DM * 128
        nub = WB // LC
        stg = [u_sb[:, 0:WB], u_sb[:, WB:2 * WB]]
        stgk = [[("u", d) for d in range(nub)],
                [("u", d) for d in range(nub, 2 * nub)]]
        xst = [xres_sb[:, 0:WB], x1_sb[:, 0:WB]]
        xstk = [("xst", 0), ("xst", 1)]
        for eblk in range(EB):
            bf = eblk % 2
            p.op("sync", lambda s, eblk=eblk, bf=bf: s.dma_start(
                out=xst[bf][:].rearrange("p (k q) -> p k q", q=128),
                in_=win_t4[eblk]),
                writes=[xstk[bf]], dma=True, sem=f"d_w{bf}")
            for k in range(DM):
                p.op("tensor", lambda t, k=k, bf=bf: t.matmul(
                    ps_rep[:, bf:bf + 1], xst[bf][:, k * 128:(k + 1) * 128],
                    bcol_sb[:, k:k + 1],
                    start=(k == 0), stop=(k == DM - 1)),
                    reads=[xstk[bf], ("bcol_sb",)],
                    writes=[("ps_rep",)], accum=(k > 0))
            p.op("scalar", lambda s, eblk=eblk, bf=bf: s.copy(
                c_sb[:, eblk:eblk + 1], ps_rep[:, bf:bf + 1]),
                reads=[("ps_rep",)], writes=[("c_sb",)])
            for k in range(DM):
                p.op("vector", lambda v, k=k, bf=bf: v.tensor_scalar(
                    stg[bf][:, k * 128:(k + 1) * 128],
                    xst[bf][:, k * 128:(k + 1) * 128],
                    gcol_sb[:, k:k + 1], None, ALU.mult),
                    reads=[xstk[bf], ("gcol_sb",)], writes=stgk[bf])
            p.op("sync", lambda s, eblk=eblk, bf=bf: s.dma_start(
                out=win_bf[eblk], in_=stg[bf][:]),
                reads=stgk[bf], writes=[("win_bf", eblk)], dma=True,
                sem=f"d_w{bf}")
            xz_eblk(eblk, 0, src=stg[bf], srck=stgk[bf])

    def ln_stats_part(xt_ap, xt_rg, sq_buf, ones_col, ones_rg):
        for km in range(DM):
            p.op("tensor", lambda t, km=km: t.matmul(
                ps_sum[:, 0:LC], ones_col[:],
                xt_ap[:, km * LC:(km + 1) * LC],
                start=(km == 0), stop=(km == DM - 1)),
                reads=[xt_rg[km] if isinstance(xt_rg, list) else xt_rg,
                       ones_rg],
                writes=[("ps_sum",)], accum=(km > 0))
        for km in range(DM):
            sl = slice((km % 2) * LC, (km % 2 + 1) * LC)
            p.op("scalar", lambda s, km=km, sl=sl: s.activation(
                ys_sb[:, sl],
                xt_ap[:, km * LC:(km + 1) * LC], AF.Square),
                reads=[xt_rg[km] if isinstance(xt_rg, list) else xt_rg],
                writes=[("ys", km % 2)])
            p.op("tensor", lambda t, km=km, sl=sl: t.matmul(
                ps_sq[:, 0:LC], onec_f[:], ys_sb[:, sl],
                start=(km == 0), stop=(km == DM - 1)),
                reads=[("ys", km % 2), ("onec_f",)],
                writes=[("ps_sq",)], accum=(km > 0))

    def ln_stats_apply(xt_ap, xt_rg, sq_buf, combine12, ones_col=None,
                       ones_rg=("onec_f",), skip_stats=False):
        if ones_col is None:
            ones_col = onec_f
        if not skip_stats:
            ln_stats_part(xt_ap, xt_rg, sq_buf, ones_col, ones_rg)
        s0 = stat_sb[0:1, 0:LC]
        s1 = stat_sb[0:1, LC:2 * LC]
        s2 = s2row_sb[0:1, 0:LC]
        p.op("vector", lambda v: v.tensor_scalar(
            s0, ps_sum[:, 0:LC], 1.0 / D_MODEL, None, ALU.mult),
            reads=[("ps_sum",), ("ps_sq",)], writes=[("st0",)])
        p.op("vector", lambda v: v.tensor_scalar(
            s1, ps_sq[:, 0:LC], 1.0 / D_MODEL, None, ALU.mult),
            reads=[("ps_sq",), ("ps_sum",)], writes=[("st1",)])
        p.op("vector", lambda v: v.tensor_tensor(s2, s0, s0, ALU.mult),
             reads=[("st0",)], writes=[("st2",)])
        p.op("vector", lambda v: v.tensor_tensor(s1, s1, s2, ALU.subtract),
             reads=[("st1",), ("st2",)], writes=[("st1",)])
        if combine12:
            p.op("scalar", lambda s: s.activation(
                s1, s1, AF.Ln, bias=cvals_sb[0:1, 0:1],
                scale=cvals_sb[0:1, 1:2]),
                reads=[("st1",), ("cv", 0), ("cv", 1)], writes=[("st1",)])
        else:
            p.op("scalar", lambda s: s.activation(
                s1, s1, AF.Ln, bias=cvals_sb[0:1, 2:3]),
                reads=[("st1",), ("cv", 2)], writes=[("st1",)])
        p.op("scalar", lambda s: s.activation(
            s1, s1, AF.Exp, scale=cvals_sb[0:1, 3:4]),
            reads=[("st1",), ("cv", 3)], writes=[("st1",)])
        p.op("tensor", lambda t: t.matmul(
            ps_rep[:, 0:LC], ones_rf[:], s1, start=True, stop=True),
            reads=[("st1",), ("ones_rf",)], writes=[("ps_rep",)])
        p.op("vector", lambda v: v.tensor_tensor(s0, s0, s1, ALU.mult),
             reads=[("st0",), ("st1",)], writes=[("st0",)])
        p.op("tensor", lambda t: t.matmul(
            ps_rep[:, 256:256 + LC], ones_rf[:], s0, start=True, stop=True),
            reads=[("st0",), ("ones_rf",)], writes=[("ps_rep",)])

    def x_load(ch):
        par = ch % 2
        src = xT.rearrange("(km p) l -> p km l", p=128)[:, :, ch * LC:(ch + 1) * LC]
        p.op("gpsimd", lambda g, par=par, src=src: g.dma_start(
            out=xt_sb[par][:].rearrange("p (km l) -> p km l", l=LC), in_=src),
            writes=[("xt", par)], dma=True, sem=f"d_gx{par}")

    def ln12_stats(ch):
        par = ch % 2
        ln_stats_part(xt_sb[par][:], ("xt", par), None, onec_b, ("onec_b",))

    def ln12_mid(ch):
        par = ch % 2
        ln_stats_apply(xt_sb[par][:], ("xt", par), None, combine12=True,
                       ones_col=onec_b, ones_rg=("onec_b",), skip_stats=True)

    def ln12_apply(ch):
        par = ch % 2
        xt = xt_sb[par][:]
        xn = xn_sb[par][:]
        for km in range(DM):
            sl = slice(km * LC, (km + 1) * LC)
            r = km % 4
            if r < 2:
                tmp = ys_sb[:, r * LC:(r + 1) * LC]
                tkey = ("ys", r)
            else:
                tmp = x1_sb[:, (r - 2) * LC:(r - 1) * LC]
                tkey = ("x1", r - 2)
            p.op("vector", lambda v, sl=sl, tmp=tmp: v.tensor_tensor(
                tmp, xt[:, sl], ps_rep[:, 0:LC], ALU.mult),
                reads=[("xt", par), ("ps_rep",)], writes=[tkey])
            p.op("vector", lambda v, sl=sl, tmp=tmp, xn=xn: v.tensor_tensor(
                xn[:, sl], tmp, ps_rep[:, 256:256 + LC], ALU.subtract),
                reads=[tkey, ("ps_rep",)], writes=[("xn", par, km)])

    def xz_eblk(eblk, ch, src=None, srck=None):
        par = eblk % 2
        cpar = ch % 2
        xn = xn_sb[cpar][:]
        if src is None:
            p.op("sync", lambda s, par=par, eblk=eblk: s.dma_start(
                out=wbuf[par][:], in_=win_bf[eblk]),
                reads=[("win_bf", eblk)], writes=[("wbuf", par)], dma=True,
                sem=f"d_w{par}")
            src = wbuf[par]
            srck = [("wbuf", par)]
        for km in range(DM):
            p.op("tensor", lambda t, par=par, km=km, xn=xn, src=src: t.matmul(
                ps_xz[par][:],
                src[:, km * 128:(km + 1) * 128],
                xn[:, km * LC:(km + 1) * LC],
                start=(km == 0), stop=(km == DM - 1)),
                reads=srck + [("xn", cpar, km)],
                writes=[("ps_xz", par)], accum=(km > 0))
        if eblk < DC:
            dc = eblk
            p.op("scalar", lambda s, dc=dc, par=par, eblk=eblk: s.activation(
                xzu_sb[:, dc * LCP + CONV - 1: dc * LCP + CONV - 1 + LC],
                ps_xz[par][:], AF.Identity, bias=c_sb[:, eblk:eblk + 1]),
                reads=[("ps_xz", par), ("c_sb",)], writes=[("xzu", dc)])
        else:
            dc = eblk - DC
            p.op("scalar", lambda s, dc=dc, par=par, cpar=cpar, eblk=eblk:
                 s.activation(
                    gt_sb[cpar][:, dc * LC:(dc + 1) * LC], ps_xz[par][:],
                    AF.Silu, bias=c_sb[:, eblk:eblk + 1]),
                 reads=[("ps_xz", par), ("c_sb",)], writes=[("gt", cpar, dc)])

    def xz_u(ch):
        for eblk in range(DC):
            xz_eblk(eblk, ch)

    def xz_z(ch):
        for eblk in range(DC, EB):
            xz_eblk(eblk, ch)

    def conv_silu(ch):
        for dc in range(DC):
            par = dc % 2
            for k in range(CONV):
                j = dc * CONV + k
                p.op("tensor", lambda t, j=j, dc=dc, k=k, par=par: t.matmul(
                    ps_conv[par][:],
                    diag_sb[:, j * 128:(j + 1) * 128],
                    xzu_sb[:, dc * LCP + k: dc * LCP + k + LC],
                    start=(k == 0), stop=False),
                    reads=[("diag", j), ("xzu", dc)],
                    writes=[("ps_conv", par)], accum=(k > 0))
            p.op("tensor", lambda t, dc=dc, par=par: t.matmul(
                ps_conv[par][:],
                convbr_sb[0:1, dc * 128:(dc + 1) * 128], ones_rb[:],
                start=False, stop=True),
                reads=[("convbr_sb",), ("ones_rb",)],
                writes=[("ps_conv", par)], accum=True)
            p.op("scalar", lambda s, dc=dc, par=par: s.activation(
                u_sb[:, dc * LC:(dc + 1) * LC], ps_conv[par][:], AF.Silu),
                reads=[("ps_conv", par)], writes=[("u", dc)])
        xv = xzu_sb[:].rearrange("p (dc l) -> p dc l", l=LCP)
        p.op("vector", lambda v, xv=xv: v.tensor_copy(
            xv[:, :, 0:CONV - 1], xv[:, :, LC:LC + CONV - 1]),
            reads=XZU, writes=XZU)

    def xdbl_dt(ch):
        par2 = ch % 2
        dt = dt_sb[par2]
        for dc in range(DC):
            p.op("tensor", lambda t, dc=dc: t.matmul(
                ps_small[0:RN, 0:LC],
                wx_sb[:, dc * RN:(dc + 1) * RN],
                u_sb[:, dc * LC:(dc + 1) * LC],
                start=(dc == 0), stop=(dc == DC - 1)),
                reads=[("wx_sb",), ("u", dc)],
                writes=[("ps_xdbl",)], accum=(dc > 0))
        p.op("scalar", lambda s: s.copy(dtr_sb, ps_small[0:R, 0:LC]),
             reads=[("ps_xdbl",)], writes=[("dtr",)])
        p.op("scalar", lambda s: s.copy(
            bc_sb[0:N, :], ps_small[R:R + N, 0:LC]),
            reads=[("ps_xdbl",)], writes=[("bc_b",)])
        p.op("scalar", lambda s: s.copy(
            bc_sb[32:32 + N, :], ps_small[96:96 + N, 0:LC]),
            reads=[("ps_xdbl",)], writes=[("bc_c",)])
        p.op("sync", lambda s, par2=par2: s.dma_start(
            out=bc_dram[par2][0:N], in_=bc_sb[0:N, :]),
            reads=[("bc_b",)], writes=[("bc_dram", par2)],
            dma=True, sem=f"d_bc{par2}")
        p.op("sync", lambda s, par2=par2: s.dma_start(
            out=bc_dram[par2][N:2 * N], in_=bc_sb[32:32 + N, :]),
            reads=[("bc_c",)], writes=[("bc_dram", par2)],
            dma=True, sem=f"d_bc{par2}")




    def dt_chain(ch, dcs):
        par2 = ch % 2
        dt = dt_sb[par2]
        dtslots = [ps_small[:, 256:256 + LC], ps_conv[0][:], ps_conv[1][:]]
        dtkeys = [("ps_dt",), ("ps_conv", 0), ("ps_conv", 1)]
        for dc in dcs:
            slot = dtslots[dc % 3]
            skey = dtkeys[dc % 3]
            wr = [skey] + ([("ps_xdbl",)] if dc == 0 else [])
            p.op("tensor", lambda t, dc=dc, slot=slot: t.matmul(
                slot, wdt_sb[:, dc * 128:(dc + 1) * 128], dtr_sb,
                start=True, stop=True),
                reads=[("wdt_sb",), ("dtr",)], writes=wr)
            p.op("scalar", lambda s, dc=dc, slot=slot: s.activation(
                e1_sb, slot, AF.Exp, bias=bdt_sb[:, dc:dc + 1]),
                reads=[skey, ("bdt_sb",)], writes=[("e1",)])
            p.op("scalar", lambda s, dc=dc, dt=dt: s.activation(
                dt[:, dc * LC:(dc + 1) * LC], e1_sb, AF.Ln,
                bias=onec_f[:, 0:1]),
                reads=[("e1",), ("onec_f",)], writes=[("dt", par2, dc)])

    def bcq_load(ch, q):
        par2 = ch % 2
        par = q % 2
        p.op("sync", lambda s, par2=par2, par=par, q=q: s.dma_start(
            out=bcq_sb[par][:, 0:4 * LC],
            in_=bc_dram[par2][4 * q:4 * q + 4].rearrange("r l -> (r l)")[
                None, :].broadcast_to([128, 4 * LC])),
            reads=[("bc_dram", par2)], writes=[("bcq", par)], dma=True,
            sem=f"d_bn{par}")
        p.op("sync", lambda s, par2=par2, par=par, q=q: s.dma_start(
            out=bcq_sb[par][:, 4 * LC:8 * LC],
            in_=bc_dram[par2][N + 4 * q:N + 4 * q + 4].rearrange(
                "r l -> (r l)")[None, :].broadcast_to([128, 4 * LC])),
            reads=[("bc_dram", par2)], writes=[("bcq", par)], dma=True,
            sem=f"d_bn{par}")

    def uw_y_init(ch):
        par2 = ch % 2
        p.op("vector", lambda v, par2=par2: v.tensor_tensor(
            uw_sb[:], dt_sb[par2][:], u_sb[:], ALU.mult),
            reads=DTK(par2) + U, writes=[("uw",)])
        for dc in range(DC):
            sl = slice(dc * LC, (dc + 1) * LC)
            p.op("scalar", lambda s, sl=sl, dc=dc: s.activation(
                y_sb[:, sl], u_sb[:, sl], AF.Copy,
                scale=dcol_sb[:, dc:dc + 1]),
                reads=[("u", dc), ("dcol_sb",)], writes=[("y", dc)])

    def scan_phase(ch):
        first = (ch == 0)
        av = a_sb[:].rearrange("p (dc l) -> p dc l", l=LC)
        bv = b_sb[:].rearrange("p (dc l) -> p dc l", l=LC)
        H = DC // 2
        nxt = ch + 1 < NCH
        usched = {}
        csched = {}
        for j in range(DC):
            usched.setdefault(2 + j // 2, []).append(j)
            csched.setdefault(3 + j // 2, []).append(j)
        tail_n = 3 + (DC - 1) // 2 + 1
        zsched = {}
        zslots = list(range(max(N - 6, tail_n), N))
        for i in range(DC):
            zsched.setdefault(zslots[i * len(zslots) // DC], []).append(DC + i)
        cpar = ch % 2
        if first:
            bcq_load(ch, 0)
        for n in range(N):
            for h in range(2):
                hdc = range(h * H, (h + 1) * H)
                hsl = slice(h * H * LC, (h + 1) * H * LC)
                HK = [("a", d) for d in hdc]
                HB = [("b", d) for d in hdc]
                if n > 0 or first:
                    for dc in hdc:
                        p.op("scalar", lambda s, n=n, dc=dc, cpar=cpar:
                             s.activation(
                                a_sb[:, dc * LC:(dc + 1) * LC],
                                dt_sb[cpar][:, dc * LC:(dc + 1) * LC],
                                AF.Exp,
                                scale=A_sb[:, dc * N + n:dc * N + n + 1]),
                             reads=[("dt", cpar, dc), ("A_sb",)],
                             writes=[("a", dc)])
                p.op("vector", lambda v, n=n, hsl=hsl: v.tensor_tensor(
                    b_sb[:, hsl].rearrange("p (dc l) -> p dc l", l=LC),
                    uw_sb[:, hsl].rearrange("p (dc l) -> p dc l", l=LC),
                    bcq_sb[(n // 4) % 2][:, (n % 4) * LC:(n % 4 + 1) * LC][
                        :, None, :].broadcast_to([128, H, LC]),
                    ALU.mult),
                    reads=[("uw",), ("bcq", (n // 4) % 2)], writes=HB)
                avh = av[:, h * H:(h + 1) * H]
                bvh = bv[:, h * H:(h + 1) * H]
                if not first:
                    p.op("vector", lambda v, n=n, avh=avh, h=h: v.tensor_tensor(
                        fix_sb[:, 0:H], avh[:, :, 0],
                        hstate_sb[:, n * DC + h * H:n * DC + (h + 1) * H],
                        ALU.mult),
                        reads=HK + [("hstate",)], writes=[("fix",)])
                    p.op("vector", lambda v, bvh=bvh: v.tensor_tensor(
                        bvh[:, :, 0], bvh[:, :, 0], fix_sb[:, 0:H], ALU.add),
                        reads=HB + [("fix",)], writes=HB)
                p.op("vector", lambda v, avh=avh: v.memset(avh[:, :, 0], 0.0),
                     reads=HK, writes=HK)
                p.op("vector", lambda v, hsl=hsl: v.tensor_tensor_scan(
                    b_sb[:, hsl], a_sb[:, hsl], b_sb[:, hsl], 0.0,
                    ALU.mult, ALU.add),
                    reads=HK + HB, writes=HB)
                p.op("vector", lambda v, n=n, bvh=bvh, h=h: v.tensor_copy(
                    hstate_sb[:, n * DC + h * H:n * DC + (h + 1) * H],
                    bvh[:, :, LC - 1]),
                    reads=HB, writes=[("hstate",)])
                hk = ("tp", h)
                p.op("vector", lambda v, n=n, hsl=hsl, h=h: v.tensor_tensor(
                    tprod_sb[:, hsl].rearrange("p (dc l) -> p dc l", l=LC),
                    b_sb[:, hsl].rearrange("p (dc l) -> p dc l", l=LC),
                    bcq_sb[(n // 4) % 2][:, (4 + n % 4) * LC:(4 + n % 4 + 1) * LC][
                        :, None, :].broadcast_to([128, H, LC]),
                    ALU.mult),
                    reads=HB + [("bcq", (n // 4) % 2)], writes=[hk])
                p.op("vector", lambda v, hsl=hsl: v.tensor_tensor(
                    y_sb[:, hsl], y_sb[:, hsl], tprod_sb[:, hsl], ALU.add),
                    reads=[hk] + YK, writes=YK)
            if n % 4 == 1 and n // 4 + 1 < 4:
                bcq_load(ch, n // 4 + 1)
            if nxt:
                for j in usched.get(n, []):
                    xz_eblk(j, ch + 1)
                for j in csched.get(n, []):
                    conv_dc_next(j)
                if n == tail_n:
                    conv_tail_next()
                for j in zsched.get(n, []):
                    xz_eblk(j, ch + 1)
                if n == 8 and ch + 2 < NCH:
                    ln12_stats(ch + 2)
                if n == 9 and ch + 2 < NCH:
                    ln12_mid(ch + 2)
                if n == 11:
                    xdbl_dt(ch + 1)
                if 12 <= n <= 15:
                    dt_chain(ch + 1,
                             range(DC * (n - 12) // 4, DC * (n - 11) // 4))
                if n == 13:
                    bcq_load(ch + 1, 0)
        if nxt:
            cpar2 = (ch + 1) % 2
            for dc in range(DC):
                p.op("scalar", lambda s, dc=dc, cpar2=cpar2: s.activation(
                    a_sb[:, dc * LC:(dc + 1) * LC],
                    dt_sb[cpar2][:, dc * LC:(dc + 1) * LC],
                    AF.Exp, scale=A_sb[:, dc * N:dc * N + 1]),
                    reads=[("dt", cpar2, dc), ("A_sb",)], writes=[("a", dc)])

    def conv_dc_next(dc):
        par = dc % 2
        for k in range(CONV):
            j = dc * CONV + k
            p.op("tensor", lambda t, j=j, dc=dc, k=k, par=par: t.matmul(
                ps_conv[par][:],
                diag_sb[:, j * 128:(j + 1) * 128],
                xzu_sb[:, dc * LCP + k: dc * LCP + k + LC],
                start=(k == 0), stop=False),
                reads=[("diag", j), ("xzu", dc)],
                writes=[("ps_conv", par)], accum=(k > 0))
        p.op("tensor", lambda t, dc=dc, par=par: t.matmul(
            ps_conv[par][:],
            convbr_sb[0:1, dc * 128:(dc + 1) * 128], ones_rb[:],
            start=False, stop=True),
            reads=[("convbr_sb",), ("ones_rb",)],
            writes=[("ps_conv", par)], accum=True)
        p.op("scalar", lambda s, dc=dc, par=par: s.activation(
            u_sb[:, dc * LC:(dc + 1) * LC], ps_conv[par][:], AF.Silu),
            reads=[("ps_conv", par)], writes=[("u", dc)])

    def conv_tail_next():
        xv = xzu_sb[:].rearrange("p (dc l) -> p dc l", l=LCP)
        p.op("vector", lambda v, xv=xv: v.tensor_copy(
            xv[:, :, 0:CONV - 1], xv[:, :, LC:LC + CONV - 1]),
            reads=XZU, writes=XZU)

    def gate_mult(ch):
        cpar = ch % 2
        xv = xzu_sb[:].rearrange("p (dc l) -> p dc l", l=LCP)
        p.op("vector", lambda v, cpar=cpar, xv=xv: v.tensor_tensor(
            xv[:, :, CONV - 1:CONV - 1 + LC],
            y_sb[:].rearrange("p (dc l) -> p dc l", l=LC),
            gt_sb[cpar][:].rearrange("p (dc l) -> p dc l", l=LC),
            ALU.mult),
            reads=YK + [("gt", cpar, d) for d in range(DC)], writes=XZU)

    def gate_mm(ch):
        for km in range(DM):
            par = km % 2
            for dc in range(DC):
                p.op("tensor", lambda t, km=km, dc=dc, par=par: t.matmul(
                    ps_out[par][:],
                    wout_sb[:, dc * D_MODEL + km * 128:
                            dc * D_MODEL + (km + 1) * 128],
                    xzu_sb[:, dc * LCP + CONV - 1: dc * LCP + CONV - 1 + LC],
                    start=(dc == 0), stop=(dc == DC - 1)),
                    reads=[("wout_sb", dc * 4 // DC), ("xzu", dc)],
                    writes=[("ps_o",)], accum=(dc > 0))
            ks = km % 2
            if ch < HCH:
                dst = outst_sb[:, ks * LC:(ks + 1) * LC]
            else:
                dst = outst_sb[:, ks * LC:(ks + 1) * LC][:, ::-1]
            p.op("scalar", lambda s, dst=dst, par=par: s.copy(
                dst, ps_out[par][:]),
                reads=[("ps_o",)], writes=[("outst", ks)])
            if ks == 1:
                kg = km - 1
                if ch < HCH:
                    dst2 = yown_dram.rearrange("(km p) l -> p km l", p=128)[
                        :, kg:kg + 2, ch * LC:(ch + 1) * LC]
                    rg = ("yown_dram", ch)
                else:
                    cc = NCH - 1 - ch
                    base = cc * LC
                    hi = base // (HALF // 2)
                    off = base % (HALF // 2)
                    dst2 = send_dram[hi].rearrange("(km p) l -> p km l", p=128)[
                        :, kg:kg + 2, off:off + LC]
                    rg = ("send_dram", cc)
                p.op("sync", lambda s, dst2=dst2: s.dma_start(
                    out=dst2,
                    in_=outst_sb[:].rearrange("p (km l) -> p km l", l=LC)),
                    reads=[("outst", 0), ("outst", 1)],
                    writes=[rg], dma=True, sem="d_ow")

    def collective_half(hi):
        groups = [[2 * i, 2 * i + 1] for i in range(NCORES // 2)]
        chs = [c for c in range(HCH)
               if hi * (HALF // 2) <= c * LC < (hi + 1) * (HALF // 2)]
        p.op("gpsimd", lambda g, hi=hi: g.collective_compute(
            "AllReduce", ALU.add, replica_groups=groups,
            ins=[send_dram[hi].opt()],
            outs=[ar_dram[hi].opt()]),
            reads=[("send_dram", c) for c in chs],
            writes=[("ar_dram", hi)], sem="cc" if hi == 0 else "cc1",
            inc=1)

    def post_loads(pc):
        par = pc % 2
        xt = xres_sb[:]
        colsl = slice(pc * LC, (pc + 1) * LC)
        p.op("sync", lambda s: s.dma_start(
            out=xt.rearrange("p (km l) -> p km l", l=LC),
            in_=x_res.rearrange("(km p) l -> p km l", p=128)[:, :, colsl]),
            writes=[("xres",), ("xst", 0)], dma=True, sem=f"d_x{par}")
        p.op("sync", lambda s: s.dma_start(
            out=b_sb[:, 0:DM * LC].rearrange("p (km l) -> p km l", l=LC),
            in_=yown_dram.rearrange("(km p) l -> p km l", p=128)[:, :, colsl]),
            reads=[("yown_dram", pc)], writes=BK, dma=True, sem=f"d_pl{par}")
        arh = 0 if (pc + 1) * LC <= HALF // 2 else 1
        poff = pc * LC - arh * (HALF // 2)
        p.op("sync", lambda s: s.dma_start(
            out=uw_sb[:, 0:DM * LC].rearrange("p (km l) -> p km l", l=LC),
            in_=send_dram[arh].rearrange("(km p) l -> p km l", p=128)[
                :, :, poff:poff + LC]),
            reads=[("send_dram", pc)], writes=[("uw",)], dma=True,
            sem=f"d_pl{par}")
        p.op("sync", lambda s: s.dma_start(
            out=gt_sb[0][:, 0:DM * LC].rearrange("p (km l) -> p km l", l=LC),
            in_=ar_dram[arh].rearrange("(km p) l -> p km l", p=128)[
                :, :, poff:poff + LC]),
            reads=[("ar_dram", arh)], writes=GT, dma=True, sem=f"d_pl{par}")

    def post_A(pc):
        xt = xres_sb[:]
        p.op("vector", lambda v: v.tensor_tensor(
            a_sb[:, 0:DM * LC], gt_sb[0][:, 0:DM * LC], uw_sb[:, 0:DM * LC],
            ALU.subtract),
            reads=GT + [("uw",)], writes=AK)
        p.op("vector", lambda v: v.tensor_tensor(
            dt_sb[0][:, 0:DM * LC], a_sb[:, 0:DM * LC], b_sb[:, 0:DM * LC],
            ALU.add),
            reads=AK + BK, writes=DTK(0))

        ln_stats_apply(xt, ("xres",), None, combine12=False)
        for km in range(DM):
            sl = slice(km * LC, (km + 1) * LC)
            ts = slice((km % 2) * LC, (km % 2 + 1) * LC)
            p.op("vector", lambda v, sl=sl, ts=ts: v.tensor_tensor(
                ys_sb[:, ts], xt[:, sl], ps_rep[:, 0:LC], ALU.mult),
                reads=[("xres",), ("ps_rep",)], writes=[("ys", km % 2)])
            p.op("vector", lambda v, ts=ts: v.tensor_tensor(
                ys_sb[:, ts], ys_sb[:, ts], ps_rep[:, 256:256 + LC],
                ALU.subtract),
                reads=[("ys", km % 2), ("ps_rep",)],
                writes=[("ys", km % 2)])
            p.op("vector", lambda v, sl=sl, ts=ts: v.tensor_tensor(
                ys_sb[:, ts], ys_sb[:, ts], xt[:, sl], ALU.add),
                reads=[("ys", km % 2), ("xres",)],
                writes=[("ys", km % 2)])
            p.op("vector", lambda v, sl=sl, ts=ts: v.tensor_tensor(
                x1_sb[:, sl], ys_sb[:, ts], dt_sb[0][:, sl], ALU.add),
                reads=[("ys", km % 2), ("dt", 0, sl.start // LC)],
                writes=[("x1", sl.start // LC)] + ([("xst", 1)] if km == 0
                                                  else []))

    def post_B(pc, firstpost):
        colsl = slice(pc * LC, (pc + 1) * LC)
        ln_stats_apply(x1_sb[:], X1, None, combine12=False)
        for km in range(DM):
            sl = slice(km * LC, (km + 1) * LC)
            ts = slice((km % 2) * LC, (km % 2 + 1) * LC)
            p.op("vector", lambda v, sl=sl, ts=ts: v.tensor_tensor(
                ys_sb[:, ts], x1_sb[:, sl], ps_rep[:, 0:LC], ALU.mult),
                reads=[("x1", km), ("ps_rep",)], writes=[("ys", km % 2)])
            p.op("vector", lambda v, sl=sl, ts=ts, km=km: v.tensor_tensor(
                xn_sb[0][:, sl], ys_sb[:, ts], ps_rep[:, 256:256 + LC],
                ALU.subtract),
                reads=[("ys", km % 2), ("ps_rep",)], writes=[("xn", 0, km)])

        for km in range(DM):
            wr = [("ps_ffs",)] + ([("ps_xdbl",), ("ps_dt",)] if km == 0 else [])
            p.op("tensor", lambda t, km=km: t.matmul(
                ps_small[0:M, 0:LC], winf_sb[:, km * M:(km + 1) * M],
                xn_sb[0][:, km * LC:(km + 1) * LC],
                start=(km == 0), stop=(km == DM - 1)),
                reads=[("winf_sb",), ("xn", 0, km)],
                writes=wr, accum=(km > 0))
        for i in range(INNER):
            wr = [("hff", i)] + (AK if (firstpost and i == 0) else [])
            p.op("scalar", lambda s, i=i: s.activation(
                hff_sb[0:M, i * LC:(i + 1) * LC], ps_small[0:M, 0:LC], AF.Relu,
                bias=bias1_sb[:, i:i + 1], scale=w1_sb[:, i:i + 1]),
                reads=[("ps_ffs",), ("bias1_sb",), ("w1_sb",)],
                writes=wr)
        p.op("vector", lambda v: v.tensor_scalar(
            gacc_sb[0][:], hff_sb[0:M, 0:LC], w2_sb[:, 0:1], b2_sb[:, 0:1],
            ALU.mult, ALU.add),
            reads=[("hff", 0), ("w2_sb",), ("b2_sb",)], writes=[("gacc", 0)])
        for i in range(1, INNER):
            sp, dp = (i - 1) % 2, i % 2
            p.op("vector", lambda v, i=i, sp=sp, dp=dp: v.scalar_tensor_tensor(
                gacc_sb[dp][:], hff_sb[0:M, i * LC:(i + 1) * LC],
                w2_sb[:, i:i + 1], gacc_sb[sp][:], ALU.mult, ALU.add),
                reads=[("hff", i), ("w2_sb",), ("gacc", sp)],
                writes=[("gacc", dp)])
        gfin = gacc_sb[(INNER - 1) % 2]
        gfin_rg = ("gacc", (INNER - 1) % 2)
        for km in range(DM):
            par2 = km % 2
            p.op("tensor", lambda t, km=km, par2=par2: t.matmul(
                ps_out[par2][:], woutf_sb[:, km * 128:(km + 1) * 128],
                gfin[:], start=True, stop=True),
                reads=[("woutf_sb",), gfin_rg], writes=[("ps_o",)])
            p.op("vector", lambda v, km=km, par2=par2: v.scalar_tensor_tensor(
                x1_sb[:, km * LC:(km + 1) * LC], ps_out[par2][:],
                bout_sb[:, km:km + 1], x1_sb[:, km * LC:(km + 1) * LC],
                ALU.add, ALU.add),
                reads=[("ps_o",), ("bout_sb",), ("x1", km)],
                writes=[("x1", km)])
        p.op("sync", lambda s: s.dma_start(
            out=out_d.rearrange("(km p) l -> p km l", p=128)[:, :, colsl],
            in_=x1_sb[:].rearrange("p (km l) -> p km l", l=LC)),
            reads=X1, writes=[("out_d", pc)], dma=True, sem="d_od")

    x_load(0)
    startup()
    ln12_stats(0)
    ln12_mid(0)
    ln12_apply(0)
    x_load(1)
    startup_transform()
    if NCH > 1:
        ln12_stats(1)
        ln12_mid(1)
    startup_prep()
    load_wout()
    conv_silu(0)
    xdbl_dt(0)
    dt_chain(0, range(DC))
    for k in range(NCH):
        if k > 0:
            gate_mult(k - 1)
        uw_y_init(k)
        if k > 0:
            gate_mm(k - 1)
        if k == NCH - HCH // 2:
            collective_half(1)
        if k + 1 < NCH:
            ln12_apply(k + 1)
        if k + 2 < NCH:
            x_load(k + 2)
        scan_phase(k)
    gate_mult(NCH - 1)
    gate_mm(NCH - 1)
    collective_half(0)
    post_order = list(range(HCH // 2, HCH)) + list(range(HCH // 2))
    post_loads(post_order[0])
    for idx, pc in enumerate(post_order):
        post_A(pc)
        if idx + 1 < len(post_order):
            post_loads(post_order[idx + 1])
        post_B(pc, firstpost=(idx == 0))

    for snm in ["d_st", "d_x0", "d_x1", "d_w0", "d_w1", "d_bc0", "d_bc1",
                "d_ow", "d_pl0", "d_pl1", "d_od", "d_bn0", "d_bn1"]:
        p.q["sync"].append(("wait", snm, p.cnt.get(snm, 0)))
    p.q["gpsimd"].append(("wait", "d_g", p.cnt.get("d_g", 0)))
    p.q["gpsimd"].append(("wait", "d_g2", p.cnt.get("d_g2", 0)))
    p.q["gpsimd"].append(("wait", "d_g3", p.cnt.get("d_g3", 0)))
    p.q["gpsimd"].append(("wait", "d_g4", p.cnt.get("d_g4", 0)))
    p.q["gpsimd"].append(("wait", "d_wo", p.cnt.get("d_wo", 0)))
    p.q["gpsimd"].append(("wait", "d_sw", p.cnt.get("d_sw", 0)))
    p.q["gpsimd"].append(("wait", "d_gx0", p.cnt.get("d_gx0", 0)))
    p.q["gpsimd"].append(("wait", "d_gx1", p.cnt.get("d_gx1", 0)))
    p.q["gpsimd"].append(("wait", "cc", p.cnt.get("cc", 0)))
    p.q["gpsimd"].append(("wait", "cc1", p.cnt.get("cc1", 0)))

    with nc.Block() as block:
        p.finalize(block, sems)
    stack.close()
    return nc


def _prep_core_inputs(cfg, x_b, flip, w, ffn):
    DM, DC, L = cfg["DM"], cfg["DC"], cfg["L"]
    N, R, CONV, M = cfg["N"], cfg["R"], cfg["CONV"], cfg["M"]
    D_MODEL, D_INNER = DM * 128, DC * 128
    EB = 2 * DC
    HALF = L // 2
    C = np.ascontiguousarray

    xd = x_b[::-1] if flip else x_b
    ins = {}
    ins["xT"] = C(xd.T.astype(np.float32))
    ins["x_res"] = C(xd[:HALF].T.astype(np.float32))
    win = np.asarray(w["Win"], dtype=np.float32)
    ins["win_t4"] = C(win.reshape(EB, 128, DM, 128).transpose(0, 3, 2, 1))
    ins["wout_t"] = C(np.asarray(w["Wout"], np.float32).T)
    wxT = np.asarray(w["Wx"], np.float32).T
    wxp = np.zeros((wxT.shape[0], 128), np.float32)
    wxp[:, 0:R] = wxT[:, 0:R]
    wxp[:, R:R + N] = wxT[:, R:R + N]
    wxp[:, 96:96 + N] = wxT[:, R + N:R + 2 * N]
    ins["wx_t"] = C(wxp)
    ins["wdt_t"] = C(np.asarray(w["Wdt"], np.float32).T)
    ins["alog_c"] = C(np.asarray(w["Alog"], np.float32)
                      .reshape(DC, 128, N).transpose(1, 0, 2).reshape(128, DC * N))
    ins["dcol"] = C(np.asarray(w["D"], np.float32).reshape(DC, 128).T)
    ins["bdt_c"] = C(np.asarray(w["bdt"], np.float32).reshape(DC, 128).T)
    ins["convw_c"] = C(np.asarray(w["convw"], np.float32)[:, 0, :]
                       .reshape(DC, 128, CONV).transpose(1, 0, 2)
                       .reshape(128, DC * CONV))
    ins["convb_r"] = C(np.asarray(w["convb"], np.float32)[None, :])
    ins["gcol"] = C(np.asarray(w["norm_g"], np.float32).reshape(DM, 128).T)
    ins["bcol"] = C(np.asarray(w["norm_b"], np.float32).reshape(DM, 128).T)
    ins["winf"] = C(np.asarray(ffn["Win"], np.float32))
    ins["woutf"] = C(np.asarray(ffn["Wout"], np.float32))
    ins["w1"] = C(np.asarray(ffn["W1"], np.float32))
    ins["b1"] = C(np.asarray(ffn["b1"], np.float32))
    ins["w2"] = C(np.asarray(ffn["W2"], np.float32))
    ins["b2c"] = C(np.asarray(ffn["b2"], np.float32)[:, None])
    ins["binc"] = C(np.asarray(ffn["bin_"], np.float32)[:, None])
    ins["bout_c"] = C(np.asarray(ffn["bout"], np.float32).reshape(DM, 128).T)
    ins["eye16"] = np.eye(128, dtype=ml_dtypes.bfloat16)
    return ins


def _assemble(cfg, shards):
    DM, L = cfg["DM"], cfg["L"]
    D_MODEL, HALF = DM * 128, L // 2
    B = len(shards) // 2
    out = np.empty((B, L, D_MODEL), dtype=np.float32)
    for b in range(B):
        out[b, :HALF] = shards[2 * b].T
        out[b, HALF:] = shards[2 * b + 1].T[::-1]
    return out


MAMBA_KEYS = ["Win", "convw", "convb", "Wx", "Wdt", "bdt", "Alog", "D", "Wout"]


def kernel(**inputs):
    cfg = FULL_CFG
    x = np.asarray(inputs["L0_attn_x"], dtype=np.float32)
    Bn = x.shape[0]
    wf = {k: np.asarray(inputs["L0_attn_f_" + k]) for k in MAMBA_KEYS}
    wb = {k: np.asarray(inputs["L0_attn_b_" + k]) for k in MAMBA_KEYS}
    for d in (wf, wb):
        d["norm_g"] = np.asarray(inputs["L0_attn_norm_g"])
        d["norm_b"] = np.asarray(inputs["L0_attn_norm_b"])
    ffn = {k: np.asarray(inputs["L0_ffn_" + k]) for k in
           ["Win", "bin_", "W1", "b1", "W2", "b2", "Wout", "bout"]}

    nc = build_core_program(cfg)
    in_maps = []
    for b in range(Bn):
        in_maps.append(_prep_core_inputs(cfg, x[b], False, wf, ffn))
        in_maps.append(_prep_core_inputs(cfg, x[b], True, wb, ffn))
    res = run_bass_kernel_spmd(nc, in_maps, core_ids=list(range(2 * Bn)))
    shards = [np.asarray(r["out"]) for r in res.results]
    return _assemble(cfg, shards)
